# version 01
# baseline (speedup 1.0000x reference)
import re
import numpy as np
import ml_dtypes
import concourse.bass as bass
import concourse.mybir as mybir
from concourse.bass_utils import run_bass_kernel_spmd
from concourse.bass_core_rust import IndirectOffsetOnAxis

F32 = mybir.dt.float32
BF16 = mybir.dt.bfloat16
U32 = mybir.dt.uint32
AF = mybir.ActivationFunctionType
ALU = mybir.AluOpType
AX = mybir.AxisListType

B, S, D, DI, DS, E, H = 4, 8192, 1024, 2048, 16, 16, 4096
NTOK = B * S
CAP = NTOK // E
NC = 8
TPC = NTOK // NC
NTILE = TPC // 128
TC = 512
NCH = TPC // TC
NG = 4
DT_PER_G = 4
TRASH = NTOK
TL = 36992
BIGF = 65535.0
EPS = 1e-5

PCF_COREBASE = 0
PCF_PARITY = 1
PCF_SLOTBASE = 2
PCF_G0 = 4
PCU_PARTNER_ROW = 0
PCU_TABROW = 1


def _u32(x):
    return np.asarray(x, dtype=np.uint32)


class Ctx:

    def __init__(self, nc):
        self.nc = nc
        self.prog = []
        self.sems = {}
        self.cnt = {}
        self._cm = []
        self.bar_k = 0

    _ALIAS = {
        "m1ld0": "Fld0", "m1ld1": "Fld1", "m1ac": "Fac", "m1vs": "Fvs",
        "m1st0": "Fst0", "m1st1": "Fst1", "m1wl": "Fcl2", "m1trh": "Ftrx",
        "m2ac": "Fac", "m2vs": "Fvs", "m2pe": "Fpe",
        "m3vs": "Fvs", "m3pe": "Fpe", "m3wl": "Fcl2",
        "m3yl0": "Fld0", "m3yl1": "Fld1", "m3xl0": "Fxl0", "m3xl1": "Fxl1",
    }

    @staticmethod
    def canon(name):
        n2 = name
        if n2.startswith("f0") or n2.startswith("f1"):
            n2 = "F" + n2[2:]
            if n2.startswith("Fe0") or n2.startswith("Fe1"):
                n2 = "FE" + n2[3:]
        m = re.fullmatch(r"m2(stg|ag|si)\d", n2)
        if m:
            n2 = "m2" + m.group(1)
        return Ctx._ALIAS.get(n2, n2)

    def sem(self, name):
        name = self.canon(name)
        if name not in self.sems:
            cm = self.nc.semaphore(name)
            h = cm.__enter__()
            self._cm.append(cm)
            self.sems[name] = h
            self.cnt[name] = 0
        return self.sems[name]

    def bump(self, name, k):
        name = self.canon(name)
        self.sem(name)
        self.cnt[name] += k
        return self.cnt[name]

    def val(self, name):
        name = self.canon(name)
        self.sem(name)
        return self.cnt[name]

    def on(self, eng):
        def deco(fn):
            self.prog.append((eng, fn))
            return None
        return deco

    def sb(self, name, shape, dtype):
        cm = self.nc.sbuf_tensor(name, shape, dtype)
        h = cm.__enter__()
        return h, cm

    def ps(self, name, shape, dtype):
        cm = self.nc.psum_tensor(name, shape, dtype)
        h = cm.__enter__()
        return h, cm

    def close_tensors(self, cms):
        for cm in reversed(cms):
            cm.__exit__(None, None, None)

    def barrier(self, tag, engines=("sync", "vector", "scalar", "tensor", "gpsimd")):
        bar = self.sem("bar")
        n = len(engines)
        base = self.cnt["bar"]
        tgt = base + n
        self.cnt["bar"] = tgt
        for e in engines:
            k = self.bar_k
            if e == "scalar":
                self.bar_k += 1
            @self.on(e)
            def _(eng, bar=bar, tgt=tgt, e=e, k=k):
                if e in ("vector", "gpsimd"):
                    eng.engine_nop().then_inc(bar, 1)
                elif e == "scalar":
                    eng.wait_ge(self.sem("dumz"), 1)
                    eng.activation(out=self.dum[0:1, 2 + k:3 + k], in_=self.dum[0:1, 1:2],
                                   func=mybir.ActivationFunctionType.Copy).then_inc(bar, 1)
                else:
                    eng.nop().then_inc(bar, 1)
                eng.wait_ge(bar, tgt)

    def finalize(self):
        for cm in reversed(self._cm):
            cm.__exit__(None, None, None)


def stage_inputs(inp):
    x = np.asarray(inp["L0_attn_x"], np.float32).reshape(NTOK, D)
    W_in = np.asarray(inp["L0_attn_W_in"], np.float32)
    conv_w = np.asarray(inp["L0_attn_conv_w"], np.float32).reshape(DI, 4)
    conv_b = np.asarray(inp["L0_attn_conv_b"], np.float32)
    W_x = np.asarray(inp["L0_attn_W_x"], np.float32)
    W_dt = np.asarray(inp["L0_attn_W_dt"], np.float32)
    b_dt = np.asarray(inp["L0_attn_b_dt"], np.float32)
    A_log = np.asarray(inp["L0_attn_A_log"], np.float32)
    Dvec = np.asarray(inp["L0_attn_D"], np.float32)
    W_out = np.asarray(inp["L0_attn_W_out"], np.float32)
    Wg = np.asarray(inp["L0_ffn_Wg"], np.float32)
    bg = np.asarray(inp["L0_ffn_bg"], np.float32)
    W1 = np.asarray(inp["L0_ffn_W1"], np.float32)
    b1 = np.asarray(inp["L0_ffn_b1"], np.float32)
    W2 = np.asarray(inp["L0_ffn_W2"], np.float32)
    b2 = np.asarray(inp["L0_ffn_b2"], np.float32)

    winT = np.ascontiguousarray(W_in.T)
    wdtT = np.ascontiguousarray(W_dt.T)
    woutT = np.ascontiguousarray(W_out.T)
    wxT = np.ascontiguousarray(W_x.T)

    in_maps = []
    for c in range(NC):
        t0 = c * TPC
        idx = np.clip(np.arange(t0 - 3, t0 + TPC), 0, NTOK - 1)
        pc_f = np.zeros((1, 32), np.float32)
        pc_f[0, PCF_COREBASE] = t0
        pc_f[0, PCF_PARITY] = c % 2
        pc_f[0, PCF_SLOTBASE] = c * TPC
        pc_f[0, 5] = (c - 1) * 128 if c % 2 == 1 else NC * 128
        pc_u = np.zeros((1, 32), np.uint32)
        pc_u[0, PCU_PARTNER_ROW] = (c - 1) * 128 if c % 2 == 1 else NC * 128
        pc_u[0, PCU_TABROW] = (c * TPC) // 32
        for gg in range(NG):
            pc_u[0, 8 + gg] = 0 if c % 2 == 1 else gg + 1
            pc_u[0, 12 + gg] = gg + (c % 2)
            pc_u[0, 16 + gg] = (gg + (c % 2)) * 16
            pc_u[0, 20 + gg] = (c % 2) * (gg + 1) * 16
        in_maps.append({
            "x_buf": np.ascontiguousarray(x[idx]),
            "winT": winT, "wdtT": wdtT, "woutT": woutT, "wxT": wxT,
            "convw": np.ascontiguousarray(conv_w.reshape(16, 128, 4).transpose(1, 2, 0)),
            "convb": np.ascontiguousarray(conv_b.reshape(16, 128).T),
            "bdt": np.ascontiguousarray(b_dt.reshape(16, 128).T),
            "alog": np.ascontiguousarray(A_log.reshape(16, 128, DS).transpose(1, 0, 2)),
            "dvec": np.ascontiguousarray(Dvec.reshape(16, 128).T),
            "wg": Wg, "bgv": bg.reshape(1, E),
            "w1": np.ascontiguousarray(W1[2 * c:2 * c + 2]),
            "w2": np.ascontiguousarray(W2[2 * c:2 * c + 2]),
            "b1v": np.ascontiguousarray(b1[2 * c:2 * c + 2].reshape(2, 32, 128).transpose(0, 2, 1)),
            "b2v": np.ascontiguousarray(b2[2 * c:2 * c + 2]),
            "pc_f": pc_f, "pc_u": pc_u,
        })
    return in_maps


def build_program(phases=("w", "f1", "m1", "m2", "m3", "f2"), dbg=()):
    nc = bass.Bass("TRN2", target_bir_lowering=False)
    ctx = Ctx(nc)
    g = {}
    _dum_cm = nc.sbuf_tensor("bar_dum", [1, 64], F32)
    ctx.dum = _dum_cm.__enter__()
    @ctx.on("scalar")
    def _(sc):
        sc.memzero(ctx.dum[:]).then_inc(ctx.sem("dumz"), 1)
    ctx.bump("dumz", 1)

    def ein(name, shape, dt=F32):
        g[name] = nc.dram_tensor(name, shape, dt, kind="ExternalInput").ap()
    ein("x_buf", [TPC + 3, D])
    ein("winT", [D, 2 * DI]); ein("wdtT", [DI, DI]); ein("woutT", [DI, D])
    ein("wxT", [DI, 2 * DS])
    ein("convw", [128, 4, 16]); ein("convb", [128, 16]); ein("bdt", [128, 16])
    ein("alog", [128, 16, DS]); ein("dvec", [128, 16])
    ein("wg", [D, E]); ein("bgv", [1, E])
    ein("w1", [2, D, H]); ein("w2", [2, H, D])
    ein("b1v", [2, 128, 32]); ein("b2v", [2, D])
    ein("pc_f", [1, 32]); ein("pc_u", [1, 32], U32)

    g["out"] = nc.dram_tensor("out", [TPC, D], F32, kind="ExternalOutput").ap()

    def dscr(name, shape, dt):
        g[name] = nc.dram_tensor(name, shape, dt, kind="Internal").ap()
    dscr("winT_b", [32, 8, 128, 128], BF16)
    dscr("wdtT_b", [16, 16, 128, 128], BF16)
    dscr("woutT_b", [DI, D], BF16)
    dscr("wxT_b", [16, 128, 2 * DS], BF16)
    dscr("wg_b", [8, 128, E], BF16)
    dscr("w1_b", [2, 32, 8, 128, 128], BF16)
    dscr("w2_b", [2, H, D], BF16)
    dscr("xln_ag_in", [TPC, D], BF16); dscr("xln_full", [NTOK, D], BF16)
    dscr("tab_in", [TL, 1], U32); dscr("tab_out", [(NC * TL) // 16, 16], U32)
    dscr("cnt_in", [1, 32], F32); dscr("cnt_out", [NC, 32], F32)
    dscr("y_in", [TPC, D], F32); dscr("y_full", [NTOK, D], F32)
    dscr("xg", [CAP, D], BF16)
    dscr("x1b", [TPC + 3, D], F32)
    dscr("xln1", [TPC + 3, D], BF16)
    dscr("u_h", [DI, TPC], F32); dscr("sg_h", [DI, TPC], F32)
    dscr("ubf_h", [DI, TPC], BF16); dscr("yg_h", [DI, TPC], BF16)
    dscr("x2b", [TPC, D], F32)
    dscr("hmx", [1, 6], F32)
    dscr("route_dbg", [128, 4 * NTILE], F32)
    dscr("hmst", [2, 128], F32)
    for gg in range(NG):
        dscr(f"st_in_{gg}", [128, DT_PER_G * DS], F32)
        dscr(f"st_out_{gg}", [(NC + 1) * 128, DT_PER_G * DS], F32)

    for name, shape, dt in dbg:
        g["dbg_" + name] = nc.dram_tensor("dbg_" + name, shape, dt, kind="ExternalOutput").ap()

    sb_keep = []
    def SB(name, shape, dt):
        h, cm = ctx.sb(name, shape, dt)
        sb_keep.append(cm)
        g[name] = h
        return h

    SB("pcf", [1, 32], F32)
    SB("pcu", [1, 32], U32)
    SB("pcf_col", [128, 32], F32)
    SB("iotaP", [128, 1], F32)
    SB("iotaE", [128, E], F32)
    SB("ltU", [128, 128], BF16)
    SB("ones128", [128, 1], BF16)
    SB("onesK1", [1, 128], BF16)
    SB("idN", [128, 128], BF16)
    SB("seln", [E, 128 * E], BF16)
    SB("trashc", [128, NTILE], F32)
    SB("Acols", [128, 16 * DS], F32)
    SB("dcol", [128, 16], F32)
    SB("bdtcol", [128, 16], F32)
    SB("cbcol", [128, 16], F32)
    SB("cwcol", [128, 16 * 4], F32)
    SB("zero128", [128, 64], F32)
    SB("cstF", [128, 8], F32)

    g["r_ntok"] = nc.gpsimd.alloc_register("r_ntok")

    def emit_setup():
        @ctx.on("gpsimd")
        def _(gp):
            gp.reg_mov(g["r_ntok"], NTOK - 1)
        consts = [1e-5, 1.0, -0.5, -1.0, 0.5, 2.0]
        @ctx.on("vector")
        def _(v):
            for j, cv in enumerate(consts):
                v.memset(g["cstF"][:, j:j + 1], float(cv))
            v.engine_nop().then_inc(ctx.sem("setup_v"), 1)
        ctx.bump("setup_v", 1)
        for j, cv in enumerate(consts):
            nc.const_aps.register(F32, float(cv), g["cstF"][:, j:j + 1])

        ld = "s_ld"
        @ctx.on("sync")
        def _(s):
            s.dma_start(out=g["pcf"][:], in_=g["pc_f"]).then_inc(ctx.sem(ld), 16)
            s.dma_start(out=g["pcu"][:], in_=g["pc_u"]).then_inc(ctx.sem(ld), 16)
            s.dma_start(out=g["pcf_col"][:], in_=g["pc_f"].broadcast_to([128, 32])).then_inc(ctx.sem(ld), 16)
            s.dma_start(out=g["bdtcol"][:], in_=g["bdt"]).then_inc(ctx.sem(ld), 16)
            s.dma_start(out=g["cbcol"][:], in_=g["convb"]).then_inc(ctx.sem(ld), 16)
            s.dma_start(out=g["dcol"][:], in_=g["dvec"]).then_inc(ctx.sem(ld), 16)
            s.dma_start(out=g["cwcol"][:].rearrange("p (k a) -> p k a", k=4), in_=g["convw"]).then_inc(ctx.sem(ld), 16)
            s.dma_start(out=g["Acols"][:].rearrange("p (a n) -> p a n", a=16), in_=g["alog"]).then_inc(ctx.sem(ld), 16)
        ctx.bump(ld, 16 * 8)
        ldv = ctx.val(ld)

        @ctx.on("gpsimd")
        def _(gp):
            gp.iota(g["iotaP"][:], pattern=[[0, 1]], base=0, channel_multiplier=1,
                    allow_small_or_imprecise_dtypes=True)
            gp.iota(g["iotaE"][:], pattern=[[1, E]], base=0, channel_multiplier=0,
                    allow_small_or_imprecise_dtypes=True)
            gp.iota(g["trashc"][:], pattern=[[128, NTILE]], base=TRASH, channel_multiplier=1,
                    allow_small_or_imprecise_dtypes=True)
            gp.memset(g["zero128"][:], 0.0)
            gp.engine_nop().then_inc(ctx.sem("setup_g"), 1)
        ctx.bump("setup_g", 1)

        tmpi, cmi = ctx.sb("tmp_ii", [128, 128], F32)
        tmpj, cmj = ctx.sb("tmp_jj", [128, 128], F32)
        @ctx.on("gpsimd")
        def _(gp):
            gp.iota(tmpi[:], pattern=[[0, 128]], base=0, channel_multiplier=1,
                    allow_small_or_imprecise_dtypes=True)
            gp.iota(tmpj[:], pattern=[[1, 128]], base=0, channel_multiplier=0,
                    allow_small_or_imprecise_dtypes=True)
            gp.engine_nop().then_inc(ctx.sem("setup_g"), 1)
        ctx.bump("setup_g", 1)
        sg = ctx.val("setup_g")

        @ctx.on("vector")
        def _(v):
            v.wait_ge(ctx.sem("setup_g"), sg)
            v.wait_ge(ctx.sem(ld), ldv)
            v.tensor_tensor(out=g["idN"][:], in0=tmpi[:], in1=tmpj[:], op=ALU.is_equal)
            v.tensor_tensor(out=g["ltU"][:], in0=tmpi[:], in1=tmpj[:], op=ALU.is_lt)
            v.memset(g["ones128"][:], 1.0)
            v.memset(g["onesK1"][:], 1.0)
            v.engine_nop().then_inc(ctx.sem("setup_v"), 1)
        ctx.bump("setup_v", 1)
        si, cmsi = ctx.sb("tmp_si", [16, 128 * E], F32)
        sj, cmsj = ctx.sb("tmp_sj", [16, 128 * E], F32)
        @ctx.on("gpsimd")
        def _(gp):
            gp.iota(si[:], pattern=[[0, 128 * E]], base=0, channel_multiplier=1,
                    allow_small_or_imprecise_dtypes=True)
            gp.iota(sj[:], pattern=[[1, E], [0, 128]], base=0, channel_multiplier=0,
                    allow_small_or_imprecise_dtypes=True)
            gp.engine_nop().then_inc(ctx.sem("setup_g"), 1)
        ctx.bump("setup_g", 1)
        sg2 = ctx.val("setup_g")
        @ctx.on("vector")
        def _(v):
            v.wait_ge(ctx.sem("setup_g"), sg2)
            v.tensor_tensor(out=g["seln"][:], in0=si[:], in1=sj[:], op=ALU.is_equal)
            v.engine_nop().then_inc(ctx.sem("setup_v"), 1)
        ctx.bump("setup_v", 1)
        ctx.close_tensors([cmsi, cmsj])

        @ctx.on("scalar")
        def _(sc):
            sc.wait_ge(ctx.sem(ld), ldv)
            sc.activation(out=g["Acols"][:], in_=g["Acols"][:], func=AF.Exp).then_inc(ctx.sem("setup_a"), 1)
        ctx.bump("setup_a", 1)
        sa = ctx.val("setup_a")
        @ctx.on("vector")
        def _(v):
            v.wait_ge(ctx.sem("setup_a"), sa)
            v.tensor_scalar_mul(g["Acols"][:], g["Acols"][:], -1.0)
            v.engine_nop().then_inc(ctx.sem("setup_v"), 1)
        ctx.bump("setup_v", 1)

        n_sg_for_zero = ctx.val("setup_g")
        @ctx.on("sync")
        def _(s):
            s.wait_ge(ctx.sem("setup_g"), n_sg_for_zero)
            for gg in range(NG):
                s.dma_start(out=g[f"st_out_{gg}"][NC * 128:(NC + 1) * 128, :], in_=g["zero128"][:])\
                    .then_inc(ctx.sem("setup_z"), 16)
        ctx.bump("setup_z", 16 * NG)
        ctx.close_tensors([cmi, cmj])

    def emit_wconv():
        wtile, cmw = ctx.sb("wconv_tile", [128, 2, 4096], BF16)
        jobs = []
        jobs.append((g["wg"].rearrange("(a p) n -> p a n", p=128),
                     g["wg_b"].rearrange("a p n -> p a n")))
        jobs.append((g["wxT"].rearrange("(a p) n -> p a n", p=128),
                     g["wxT_b"].rearrange("a p n -> p a n")))
        for o in range(32):
            jobs.append((g["winT"][:, o * 128:(o + 1) * 128].rearrange("(a p) h -> p a h", p=128),
                         g["winT_b"][o].rearrange("a p h -> p a h")))
        for o in range(16):
            jobs.append((g["wdtT"][:, o * 128:(o + 1) * 128].rearrange("(a p) h -> p a h", p=128),
                         g["wdtT_b"][o].rearrange("a p h -> p a h")))
        for r0 in range(0, DI, 128):
            jobs.append((g["woutT"][r0:r0 + 128, :], g["woutT_b"][r0:r0 + 128, :]))
        for e in range(2):
            for hc in range(32):
                jobs.append((g["w1"][e][:, hc * 128:(hc + 1) * 128].rearrange("(a p) h -> p a h", p=128),
                             g["w1_b"][e, hc].rearrange("a p h -> p a h")))
            for r0 in range(0, H, 128):
                jobs.append((g["w2"][e][r0:r0 + 128, :], g["w2_b"][e][r0:r0 + 128, :]))
        buf = 0
        for (src, dst) in jobs:
            fs = int(np.prod(src.shape[1:]))
            sview = wtile[:, buf % 2, 0:fs]
            if len(src.shape) == 3:
                sview = sview.rearrange("p (a n) -> p a n", a=src.shape[1])
            bi = buf % 2
            j = buf
            buf += 1
            st_need = (j // 2) * 16
            n_cast = ctx.bump(f"wc_cast{bi}", 16)
            @ctx.on("gpsimd")
            def _(gp, src=src, sview=sview, st_need=st_need, bi=bi):
                gp.wait_ge(ctx.sem(f"wc_st{bi}"), st_need)
                gp.dma_start(out=sview, in_=src).then_inc(ctx.sem(f"wc_cast{bi}"), 16)
            ctx.bump(f"wc_st{bi}", 16)
            @ctx.on("sync")
            def _(s, dst=dst, sview=sview, n_cast=n_cast, bi=bi):
                s.wait_ge(ctx.sem(f"wc_cast{bi}"), n_cast)
                s.dma_start(out=dst, in_=sview).then_inc(ctx.sem(f"wc_st{bi}"), 16)
        @ctx.on("sync")
        def _(s, a=ctx.val("wc_st0"), b=ctx.val("wc_st1")):
            s.wait_ge(ctx.sem("wc_st0"), a)
            s.wait_ge(ctx.sem("wc_st1"), b)
        @ctx.on("gpsimd")
        def _(gp, a=ctx.val("wc_st0"), b=ctx.val("wc_st1")):
            gp.wait_ge(ctx.sem("wc_st0"), a)
            gp.wait_ge(ctx.sem("wc_st1"), b)
        ctx.close_tensors([cmw])

    emit_setup()
    ctx.barrier("after_setup")
    if "w" in phases:
        emit_wconv()
    ctx.barrier("after_w")

    if "f1" in phases:
        emit_ffn(ctx, g, call=0)
    ctx.barrier("after_f1")
    if "m1" in phases:
        emit_m1(ctx, g)
    ctx.barrier("after_m1")
    if "m2" in phases:
        emit_m2(ctx, g)
    ctx.barrier("after_m2")
    if "m3" in phases:
        emit_m3(ctx, g)
    ctx.barrier("after_m3")
    if "f2" in phases:
        emit_ffn(ctx, g, call=1)
    ctx.barrier("after_all")

    if dbg:
        for name, shape, dt in dbg:
            ctx.bump("dbgs", 16)
            @ctx.on("sync")
            def _(s, name=name):
                s.dma_start(out=g["dbg_" + name], in_=g[name]).then_inc(ctx.sem("dbgs"), 16)
        n_dbg = ctx.val("dbgs")
        @ctx.on("sync")
        def _(s):
            s.wait_ge(ctx.sem("dbgs"), n_dbg)

    with nc.Block() as block:
        @block.sync
        def _(eng):
            for e, fn in ctx.prog:
                if e == "sync":
                    fn(eng)
        @block.vector
        def _(eng):
            for e, fn in ctx.prog:
                if e == "vector":
                    fn(eng)
        @block.scalar
        def _(eng):
            for e, fn in ctx.prog:
                if e == "scalar":
                    fn(eng)
        @block.tensor
        def _(eng):
            for e, fn in ctx.prog:
                if e == "tensor":
                    fn(eng)
        @block.gpsimd
        def _(eng):
            for e, fn in ctx.prog:
                if e == "gpsimd":
                    fn(eng)

    ctx.finalize()
    return nc


def emit_ffn(ctx, g, call):
    nc = ctx.nc
    q = f"f{call}"
    src = g["x_buf"] if call == 0 else g["x2b"]
    soff = 3 if call == 0 else 0
    dst = g["x1b"] if call == 0 else g["out"]
    doff = 3 if call == 0 else 0

    cms = []
    cms2 = []
    def sb(name, shape, dt, grp=None):
        h, cm = ctx.sb(q + name, shape, dt)
        (cms if grp is None else grp).append(cm)
        return h

    idxf_a = sb("idxf_a", [128, NTILE], F32)
    pos2_a = sb("pos2_a", [128, NTILE], F32)
    gate_a = sb("gate_a", [128, NTILE], F32)
    comb_a = sb("comb_a", [128, NTILE], F32)
    slkf_a = sb("slkf_a", [128, NTILE], F32)
    slot_u = sb("slot_u", [128, NTILE], U32)
    tabs_u = sb("tabs_u", [128, NTILE], U32)
    payl_u = sb("payl_u", [128, NTILE], U32)
    payl_f = sb("payl_f", [128, NTILE], F32)
    cnt_sb = sb("cnt_sb", [1, 32], F32)
    iota8 = sb("iota8", [8, 1], F32)
    cld = sb("cld", [8, 32], F32)
    mask8 = sb("mask8", [8, 2], F32)
    baser = sb("baser", [1, E], F32)
    bases = sb("bases", [128, E], F32)
    mgi = sb("mgi", [128, 2, 8], U32)
    mgf = sb("mgf", [128, 2, 8, 16], F32)
    tos_f = sb("tos_f", [128, 2, 16], F32)
    tos_u = sb("tos_u", [128, 2, 16], U32)
    hcol = sb("hcol", [128, 2], F32)
    hcu = sb("hcu", [128, 1], U32)
    gbuf = sb("gbuf", [128, 2, D], BF16)
    yrow = sb("yrow", [128, 2, D], F32)
    x1row = sb("x1row", [128, 2, D], F32)
    xr = sb("xr", [128, 2, D], F32, cms2)
    scr = sb("scr", [128, D], F32, cms2)
    col = sb("col", [128, 2, 8], F32, cms2)
    xlt = sb("xlt", [128, 2, D], BF16, cms2)
    xlnT = sb("xlnT", [128, 8, TPC], BF16, cms2)
    wgb = sb("wgb", [128, 8, E], BF16, cms2)
    bgbc = sb("bgbc", [128, E], F32, cms2)
    lg = sb("lg", [128, 2, E], F32, cms2)
    tcol = sb("tcol", [128, 2, 16], F32, cms2)
    idx8 = sb("idx8", [128, 8], U32, cms2)
    ex = sb("ex", [128, 2, E], F32, cms2)
    oh = sb("oh", [128, 2, E], F32, cms2)
    base_r = sb("base_r", [1, 2, E], F32, cms2)
    posS = sb("posS", [128, E], F32, cms2)
    ohs = sb("ohs", [128, E], F32, cms2)
    ltUf = sb("ltUf", [128, 128], F32, cms2)
    base_run = sb("base_run", [1, E], F32, cms2)
    onesK1f = sb("onesK1f", [1, 128], F32, cms2)
    ones128f = sb("ones128f", [128, 1], F32, cms2)
    ffff = sb("ffff", [128, TL // 128], U32, cms2)
    tbf = sb("tbf", [128, NTILE], F32, cms2)

    S_LD = q + "ld"; S_VS = q + "vs"; S_AC = q + "ac"; S_PE = q + "pe"
    S_LG = q + "lg"; S_ST = q + "st"; S_AGX = q + "agx"; S_TRX = q + "trx"
    S_SC = q + "sc"; S_AGT = q + "agt"; S_AGC = q + "agc"; S_MG = q + "mg"
    S_AGY = q + "agy"; S_GA = q + "ga"; S_EX = q + "ex"; S_YS = q + "ys"
    S_OUT = q + "out"
    base_st = [ctx.val(S_ST + "0"), ctx.val(S_ST + "1")]
    base_gst = [ctx.val(q + "gst0"), ctx.val(q + "gst1")]
    base_yst = [ctx.val(q + "yst0"), ctx.val(q + "yst1")]
    base_out = [ctx.val(S_OUT + "0"), ctx.val(S_OUT + "1")]
    base_ev = ctx.val(q + "ev")
    base_peB = ctx.val(S_PE)

    @ctx.on("sync")
    def _(s):
        s.dma_start(out=wgb[:], in_=g["wg_b"].rearrange("a p n -> p a n")).then_inc(ctx.sem(q + "cl2"), 16)
        s.dma_start(out=bgbc[:], in_=g["bgv"].broadcast_to([128, E])).then_inc(ctx.sem(q + "cl2"), 16)
    ctx.bump(q + "cl2", 32)

    @ctx.on("gpsimd")
    def _(gp):
        gp.iota(ffff[:], pattern=[[0, TL // 128]], base=-1, channel_multiplier=0,
                allow_small_or_imprecise_dtypes=True)
        gp.iota(iota8[:], pattern=[[0, 1]], base=0, channel_multiplier=1,
                allow_small_or_imprecise_dtypes=True)
        gp.iota(payl_f[:], pattern=[[128, NTILE]], base=0, channel_multiplier=1,
                allow_small_or_imprecise_dtypes=True)
        gp.engine_nop().then_inc(ctx.sem(q + "setg"), 1)
    ctx.bump(q + "setg", 1)
    v_setg = ctx.val(q + "setg")

    @ctx.on("vector")
    def _(v):
        v.wait_ge(ctx.sem(q + "setg"), v_setg)
        v.tensor_copy(ltUf[:], g["ltU"][:])
        v.memset(onesK1f[:], 1.0)
        v.memset(ones128f[:], 1.0)
        v.tensor_scalar_add(payl_f[:], payl_f[:], g["pcf_col"][:, PCF_COREBASE:PCF_COREBASE + 1])
        v.engine_nop().then_inc(ctx.sem(S_VS), 1)
    n_vs0 = ctx.bump(S_VS, 1)
    @ctx.on("vector")
    def _(v):
        v.wait_ge(ctx.sem(S_VS), n_vs0)
        v.tensor_copy(payl_u[:], payl_f[:])
        v.engine_nop().then_inc(ctx.sem(S_VS), 1)
    ctx.bump(S_VS, 1)

    @ctx.on("sync")
    def _(s):
        s.wait_ge(ctx.sem(q + "setg"), v_setg)
        s.dma_start(out=g["tab_in"].rearrange("(p f) 1 -> p f", p=128), in_=ffff[:])\
            .then_inc(ctx.sem(q + "tz"), 16)
    ctx.bump(q + "tz", 16)

    for i in range(NTILE):
        b = i % 2
        n_ld = ctx.bump(S_LD + str(b), 16)
        @ctx.on("sync")
        def _(s, i=i, b=b):
            if i >= 2:
                s.wait_ge(ctx.sem(S_AC), ac_rel[i - 2])
            s.dma_start(out=xr[:, b, :], in_=src[soff + i * 128: soff + (i + 1) * 128, :])\
                .then_inc(ctx.sem(S_LD + str(b)), 16)
        if i == 0:
            ac_rel = {}
        n_ac = ctx.bump(S_AC, 1)
        @ctx.on("scalar")
        def _(sc, i=i, b=b, n_ld=n_ld):
            sc.wait_ge(ctx.sem(S_LD + str(b)), n_ld)
            sc.activation(out=scr[:], in_=xr[:, b, :], func=AF.Square,
                          accum_out=col[:, b, 1:2]).then_inc(ctx.sem(S_AC), 1)
        n_v1 = ctx.bump(S_VS, 1)
        @ctx.on("vector")
        def _(v, i=i, b=b, n_ld=n_ld, n_ac=n_ac):
            v.wait_ge(ctx.sem(S_LD + str(b)), n_ld)
            v.wait_ge(ctx.sem(S_AC), n_ac)
            v.reduce_sum(col[:, b, 0:1], xr[:, b, :], axis=AX.X)
            v.engine_nop().then_inc(ctx.sem(S_VS), 1)
        n_v2 = ctx.bump(S_VS, 1)
        @ctx.on("vector")
        def _(v, b=b, n_v1=n_v1):
            v.wait_ge(ctx.sem(S_VS), n_v1)
            v.tensor_scalar_mul(col[:, b, 2:3], col[:, b, 0:1], 1.0 / D)
            v.tensor_scalar_mul(col[:, b, 3:4], col[:, b, 1:2], 1.0 / D)
            v.engine_nop().then_inc(ctx.sem(S_VS), 1)
        n_v3 = ctx.bump(S_VS, 1)
        @ctx.on("vector")
        def _(v, b=b, n_v2=n_v2):
            v.wait_ge(ctx.sem(S_VS), n_v2)
            v.tensor_tensor(col[:, b, 6:7], col[:, b, 2:3], col[:, b, 2:3], op=ALU.mult)
            v.engine_nop().then_inc(ctx.sem(S_VS), 1)
        n_v4 = ctx.bump(S_VS, 1)
        @ctx.on("vector")
        def _(v, b=b, n_v3=n_v3):
            v.wait_ge(ctx.sem(S_VS), n_v3)
            v.tensor_sub(col[:, b, 3:4], col[:, b, 3:4], col[:, b, 6:7])
            v.engine_nop().then_inc(ctx.sem(S_VS), 1)
        n_ac1b = ctx.bump(S_AC, 1)
        n_ac2 = ctx.bump(S_AC, 1)
        @ctx.on("scalar")
        def _(sc, b=b, n_v4=n_v4, n_ac1b=n_ac1b):
            sc.wait_ge(ctx.sem(S_VS), n_v4)
            sc.activation(out=col[:, b, 4:5], in_=col[:, b, 3:4], func=AF.Ln, bias=EPS)\
                .then_inc(ctx.sem(S_AC), 1)
            sc.wait_ge(ctx.sem(S_AC), n_ac1b)
            sc.activation(out=col[:, b, 4:5], in_=col[:, b, 4:5], func=AF.Exp, scale=-0.5)\
                .then_inc(ctx.sem(S_AC), 1)
        n_v5 = ctx.bump(S_VS, 1)
        @ctx.on("vector")
        def _(v, b=b, n_ac2=n_ac2):
            v.wait_ge(ctx.sem(S_AC), n_ac2)
            v.scalar_tensor_tensor(col[:, b, 5:6], col[:, b, 2:3], -1.0, col[:, b, 4:5],
                                   op0=ALU.mult, op1=ALU.mult)
            v.engine_nop().then_inc(ctx.sem(S_VS), 1)
        n_ac3 = ctx.bump(S_AC, 1)
        ac_rel[i] = n_ac3
        @ctx.on("scalar")
        def _(sc, i=i, b=b, n_v5=n_v5):
            sc.wait_ge(ctx.sem(S_VS), n_v5)
            if i >= 2:
                sc.wait_ge(ctx.sem(S_ST + str(b)), base_st[b] + (i // 2) * 16)
            sc.activation(out=xlt[:, b, :], in_=xr[:, b, :], func=AF.Identity,
                          scale=col[:, b, 4:5], bias=col[:, b, 5:6]).then_inc(ctx.sem(S_AC), 1)
        ctx.bump(S_ST + str(b), 16)
        @ctx.on("sync")
        def _(s, i=i, b=b, n_ac3=n_ac3):
            s.wait_ge(ctx.sem(S_AC), n_ac3)
            s.dma_start(out=g["xln_ag_in"][i * 128:(i + 1) * 128, :], in_=xlt[:, b, :])\
                .then_inc(ctx.sem(S_ST + str(b)), 16)

    n_st_all0 = ctx.val(S_ST + "0")
    n_st_all1 = ctx.val(S_ST + "1")


    @ctx.on("gpsimd")
    def _(gp):
        gp.wait_ge(ctx.sem(S_ST + "0"), n_st_all0)
        gp.wait_ge(ctx.sem(S_ST + "1"), n_st_all1)
        gp.collective_compute("AllGather", ALU.bypass,
                              replica_groups=[list(range(NC))],
                              ins=[g["xln_ag_in"].opt()], outs=[g["xln_full"].opt()])\
            .then_inc(ctx.sem(S_AGX), 1)
    ctx.bump(S_AGX, 1)
    n_agx = ctx.val(S_AGX)

    for kc in range(8):
        ctx.bump(S_TRX, 16)
        @ctx.on("scalar")
        def _(sc, kc=kc):
            if kc == 0:
                sc.wait_ge(ctx.sem(S_ST + "0"), n_st_all0)
                sc.wait_ge(ctx.sem(S_ST + "1"), n_st_all1)
            sc.dma_start_transpose(out=xlnT[:, kc, :],
                                   in_=g["xln_ag_in"][:, kc * 128:(kc + 1) * 128])\
                .then_inc(ctx.sem(S_TRX), 16)
    n_trx = ctx.val(S_TRX)

    lgP, cm1 = ctx.ps(q + "lgP", [128, 2, E], F32)
    posP, cm2 = ctx.ps(q + "posP", [128, E], F32)
    cntP, cm3 = ctx.ps(q + "cntP", [1, 2, E], F32)
    cms += [cm1, cm2, cm3]
    n_vbr = ctx.bump(S_VS, 1)
    @ctx.on("vector")
    def _(v):
        v.memset(base_run[:], 0.0)
        v.engine_nop().then_inc(ctx.sem(S_VS), 1)

    n_cl2 = ctx.val(q + "cl2")
    for i in range(NTILE):
        b = i % 2
        n_lg = ctx.bump(S_LG, 1)
        @ctx.on("tensor")
        def _(t, i=i, b=b):
            if i == 0:
                t.wait_ge(ctx.sem(S_TRX), n_trx)
                t.wait_ge(ctx.sem(q + "cl2"), n_cl2)
            if i >= 2:
                t.wait_ge(ctx.sem(S_VS), lg_rel[i - 2])
            last = None
            for kc in range(8):
                last = t.matmul(lgP[:, b, :], xlnT[:, kc, i * 128:(i + 1) * 128], wgb[:, kc, :],
                                start=(kc == 0), stop=(kc == 7))
            last.then_inc(ctx.sem(S_LG), 1)
        if i == 0:
            lg_rel = {}
        n_v1 = ctx.bump(S_VS, 1)
        @ctx.on("vector")
        def _(v, i=i, b=b, n_lg=n_lg):
            v.wait_ge(ctx.sem(S_LG), n_lg)
            v.scalar_tensor_tensor(lg[:, b, :], lgP[:, b, :], 0.0, bgbc[:],
                                   op0=ALU.bypass, op1=ALU.add)
            v.engine_nop().then_inc(ctx.sem(S_VS), 1)
        lg_rel[i] = n_v1
        n_v2 = ctx.bump(S_VS, 1)
        @ctx.on("vector")
        def _(v, b=b, n_v1=n_v1):
            v.wait_ge(ctx.sem(S_VS), n_v1)
            v.reduce_max(tcol[:, b, 0:1], lg[:, b, :], axis=AX.X)
            v.engine_nop().then_inc(ctx.sem(S_VS), 1)
        n_v3 = ctx.bump(S_VS, 1)
        @ctx.on("vector")
        def _(v, i=i, b=b, n_v2=n_v2):
            v.wait_ge(ctx.sem(S_VS), n_v2)
            v.max_index(idx8[:], tcol[:, b, 0:1].broadcast_to([128, 8]), lg[:, b, :])
            v.tensor_scalar_mul(tcol[:, b, 1:2], tcol[:, b, 0:1], -1.0)
            v.engine_nop().then_inc(ctx.sem(S_VS), 1)
        n_v4 = ctx.bump(S_VS, 1)
        @ctx.on("vector")
        def _(v, i=i, n_v3=n_v3):
            v.wait_ge(ctx.sem(S_VS), n_v3)
            v.tensor_copy(idxf_a[:, i:i + 1], idx8[:, 0:1])
            v.engine_nop().then_inc(ctx.sem(S_VS), 1)
        n_ac = ctx.bump(S_AC, 1)
        @ctx.on("scalar")
        def _(sc, b=b, n_v3=n_v3):
            sc.wait_ge(ctx.sem(S_VS), n_v3)
            sc.activation(out=ex[:, b, :], in_=lg[:, b, :], func=AF.Exp,
                          bias=tcol[:, b, 1:2]).then_inc(ctx.sem(S_AC), 1)
        n_v5 = ctx.bump(S_VS, 1)
        @ctx.on("vector")
        def _(v, i=i, b=b, n_ac=n_ac, n_v4=n_v4):
            v.wait_ge(ctx.sem(S_AC), n_ac)
            v.wait_ge(ctx.sem(S_VS), n_v4)
            v.reduce_sum(tcol[:, b, 2:3], ex[:, b, :], axis=AX.X)
            v.tensor_tensor(oh[:, b, :], idxf_a[:, i:i + 1].broadcast_to([128, E]),
                            g["iotaE"][:], op=ALU.is_equal)
            v.engine_nop().then_inc(ctx.sem(S_VS), 1)
        n_v6 = ctx.bump(S_VS, 1)
        @ctx.on("vector")
        def _(v, i=i, b=b, n_v5=n_v5):
            v.wait_ge(ctx.sem(S_VS), n_v5)
            v.reciprocal(gate_a[:, i:i + 1], tcol[:, b, 2:3])
            v.tensor_copy(base_r[:, b, :], base_run[:])
            v.engine_nop().then_inc(ctx.sem(S_VS), 1)
        ctx.bump(S_PE, 2)
        @ctx.on("tensor")
        def _(t, i=i, b=b, n_v6=n_v6):
            t.wait_ge(ctx.sem(S_VS), n_v6)
            t.matmul(posP[:], onesK1f[:], base_r[:, b, :], start=True, stop=False)
            t.matmul(posP[:], ltUf[:], oh[:, b, :], start=False, stop=True)\
                .then_inc(ctx.sem(S_PE), 1)
            t.matmul(cntP[:, b, :], ones128f[:], oh[:, b, :], start=True,
                     stop=True).then_inc(ctx.sem(S_PE), 1)
        n_v7 = ctx.bump(S_VS, 1)
        @ctx.on("vector")
        def _(v, i=i, n_pe=base_peB + 2 * i + 1):
            v.wait_ge(ctx.sem(S_PE), n_pe)
            v.tensor_copy(posS[:], posP[:])
            v.engine_nop().then_inc(ctx.sem(S_VS), 1)
        n_v8 = ctx.bump(S_VS, 1)
        @ctx.on("vector")
        def _(v, b=b, n_v7=n_v7):
            v.wait_ge(ctx.sem(S_VS), n_v7)
            v.tensor_mul(posS[:], posS[:], oh[:, b, :])
            v.engine_nop().then_inc(ctx.sem(S_VS), 1)
        n_v9x = ctx.bump(S_VS, 1)
        @ctx.on("vector")
        def _(v, i=i, n_v8=n_v8):
            v.wait_ge(ctx.sem(S_VS), n_v8)
            v.reduce_sum(pos2_a[:, i:i + 1], posS[:], axis=AX.X)
            v.engine_nop().then_inc(ctx.sem(S_VS), 1)
        ctx.bump(S_VS, 1)
        @ctx.on("vector")
        def _(v, i=i, b=b, n_v9x=n_v9x, n_pe=base_peB + 2 * i + 2):
            v.wait_ge(ctx.sem(S_VS), n_v9x)
            v.wait_ge(ctx.sem(S_PE), n_pe)
            v.tensor_add(base_run[:], base_run[:], cntP[:, b, :])
            v.engine_nop().then_inc(ctx.sem(S_VS), 1)

    n_vcnt = ctx.bump(S_VS, 1)
    @ctx.on("vector")
    def _(v, w=ctx.val(S_VS) - 1):
        v.wait_ge(ctx.sem(S_VS), w)
        v.tensor_copy(cnt_sb[:, 0:E], base_run[:])
        v.engine_nop().then_inc(ctx.sem(S_VS), 1)

    @ctx.on("sync")
    def _(s, n_vcnt=n_vcnt):
        s.wait_ge(ctx.sem(S_VS), n_vcnt)
        s.dma_start(out=g["cnt_in"][:, 0:E], in_=cnt_sb[:, 0:E]).then_inc(ctx.sem(q + "cst"), 16)
        s.dma_start(out=g["cnt_in"][:, 16:32], in_=g["zero128"][0:1, 0:16]).then_inc(ctx.sem(q + "cst"), 16)
    ctx.bump(q + "cst", 32)
    n_cst = ctx.val(q + "cst")
    @ctx.on("gpsimd")
    def _(gp):
        gp.wait_ge(ctx.sem(q + "cst"), n_cst)
        gp.collective_compute("AllGather", ALU.bypass,
                              replica_groups=[list(range(NC))],
                              ins=[g["cnt_in"].opt()], outs=[g["cnt_out"].opt()])\
            .then_inc(ctx.sem(S_AGC), 1)
    ctx.bump(S_AGC, 1)
    n_agc = ctx.val(S_AGC)

    @ctx.on("sync")
    def _(s):
        s.wait_ge(ctx.sem(S_AGC), n_agc)
        s.dma_start(out=cld[:], in_=g["cnt_out"][:, :]).then_inc(ctx.sem(q + "cl"), 16)
    ctx.bump(q + "cl", 16)
    n_cl = ctx.val(q + "cl")
    n_v9 = ctx.bump(S_VS, 1)
    @ctx.on("vector")
    def _(v):
        v.wait_ge(ctx.sem(q + "cl"), n_cl)
        v.scalar_tensor_tensor(mask8[:, 0:1], g["pcf_col"][0:8, PCF_COREBASE:PCF_COREBASE + 1],
                               1.0 / TPC, iota8[:], op0=ALU.mult, op1=ALU.is_gt)
        v.scalar_tensor_tensor(mask8[:, 1:2], g["pcf_col"][0:8, PCF_COREBASE:PCF_COREBASE + 1],
                               1.0 / TPC, iota8[:], op0=ALU.mult, op1=ALU.is_equal)
        v.engine_nop().then_inc(ctx.sem(S_VS), 1)
    ctx.bump(S_PE, 1)
    @ctx.on("tensor")
    def _(t, n_v9=n_v9):
        t.wait_ge(ctx.sem(S_VS), n_v9)
        t.matmul(posP[0:1, 0:E], mask8[:, 0:1], cld[0:8, 0:E], start=True, stop=True)\
            .then_inc(ctx.sem(S_PE), 1)
    n_v10 = ctx.bump(S_VS, 1)
    @ctx.on("vector")
    def _(v, n_pe=ctx.val(S_PE)):
        v.wait_ge(ctx.sem(S_PE), n_pe)
        v.tensor_copy(baser[:], posP[0:1, 0:E])
        v.engine_nop().then_inc(ctx.sem(S_VS), 1)
    ctx.bump(S_PE, 1)
    @ctx.on("tensor")
    def _(t, n_v10=n_v10):
        t.wait_ge(ctx.sem(S_VS), n_v10)
        t.matmul(posP[:, 0:E], onesK1f[:], baser[:], start=True, stop=True)\
            .then_inc(ctx.sem(S_PE), 1)
    n_v11 = ctx.bump(S_VS, 1)
    @ctx.on("vector")
    def _(v, n_pe=ctx.val(S_PE)):
        v.wait_ge(ctx.sem(S_PE), n_pe)
        v.tensor_copy(bases[:], posP[:, 0:E])
        v.engine_nop().then_inc(ctx.sem(S_VS), 1)

    for i in range(NTILE):
        b = i % 2
        n_c1 = ctx.bump(S_VS, 1)
        @ctx.on("vector")
        def _(v, i=i, n_v11=n_v11):
            if i == 0:
                v.wait_ge(ctx.sem(S_VS), n_v11)
            v.tensor_tensor(ohs[:], idxf_a[:, i:i + 1].broadcast_to([128, E]),
                            g["iotaE"][:], op=ALU.is_equal)
            v.engine_nop().then_inc(ctx.sem(S_VS), 1)
        n_c2 = ctx.bump(S_VS, 1)
        @ctx.on("vector")
        def _(v, n_c1=n_c1):
            v.wait_ge(ctx.sem(S_VS), n_c1)
            v.tensor_mul(ohs[:], ohs[:], bases[:])
            v.engine_nop().then_inc(ctx.sem(S_VS), 1)
        n_c3 = ctx.bump(S_VS, 1)
        @ctx.on("vector")
        def _(v, b=b, n_c2=n_c2):
            v.wait_ge(ctx.sem(S_VS), n_c2)
            v.reduce_sum(tcol[:, b, 3:4], ohs[:], axis=AX.X)
            v.engine_nop().then_inc(ctx.sem(S_VS), 1)
        n_c4 = ctx.bump(S_VS, 1)
        @ctx.on("vector")
        def _(v, i=i, b=b, n_c3=n_c3):
            v.wait_ge(ctx.sem(S_VS), n_c3)
            v.tensor_add(tcol[:, b, 4:5], tcol[:, b, 3:4], pos2_a[:, i:i + 1])
            v.engine_nop().then_inc(ctx.sem(S_VS), 1)
        n_c5 = ctx.bump(S_VS, 1)
        @ctx.on("vector")
        def _(v, b=b, n_c4=n_c4):
            v.wait_ge(ctx.sem(S_VS), n_c4)
            v.tensor_single_scalar(tcol[:, b, 5:6], tcol[:, b, 4:5], float(CAP), op=ALU.is_lt)
            v.engine_nop().then_inc(ctx.sem(S_VS), 1)
        n_c6 = ctx.bump(S_VS, 1)
        @ctx.on("vector")
        def _(v, i=i, b=b, n_c5=n_c5):
            v.wait_ge(ctx.sem(S_VS), n_c5)
            v.scalar_tensor_tensor(tcol[:, b, 6:7], idxf_a[:, i:i + 1], float(CAP),
                                   tcol[:, b, 4:5], op0=ALU.mult, op1=ALU.add)
            v.scalar_tensor_tensor(comb_a[:, i:i + 1], gate_a[:, i:i + 1], 0.5,
                                   tcol[:, b, 5:6], op0=ALU.mult, op1=ALU.mult)
            v.engine_nop().then_inc(ctx.sem(S_VS), 1)
        n_c7 = ctx.bump(S_VS, 1)
        @ctx.on("vector")
        def _(v, i=i, b=b, n_c6=n_c6):
            v.wait_ge(ctx.sem(S_VS), n_c6)
            v.tensor_mul(slkf_a[:, i:i + 1], tcol[:, b, 6:7], tcol[:, b, 5:6])
            v.tensor_sub(tcol[:, b, 7:8], tcol[:, b, 6:7], g["trashc"][:, i:i + 1])
            v.engine_nop().then_inc(ctx.sem(S_VS), 1)
        n_c8 = ctx.bump(S_VS, 1)
        @ctx.on("vector")
        def _(v, i=i, b=b, n_c7=n_c7):
            v.wait_ge(ctx.sem(S_VS), n_c7)
            v.scalar_tensor_tensor(tbf[:, i:i + 1], tcol[:, b, 7:8], tcol[:, b, 5:6],
                                   g["trashc"][:, i:i + 1], op0=ALU.mult, op1=ALU.add)
            v.engine_nop().then_inc(ctx.sem(S_VS), 1)
        ctx.bump(S_VS, 1)
        @ctx.on("vector")
        def _(v, i=i, n_c8=n_c8):
            v.wait_ge(ctx.sem(S_VS), n_c8)
            v.tensor_copy(slot_u[:, i:i + 1], slkf_a[:, i:i + 1])
            v.tensor_copy(tabs_u[:, i:i + 1], tbf[:, i:i + 1])
            v.engine_nop().then_inc(ctx.sem(S_VS), 1)
    n_vs_passc = ctx.val(S_VS)

    if call == 0:
        @ctx.on("sync")
        def _(s):
            s.wait_ge(ctx.sem(S_VS), n_vs_passc)
            s.dma_start(out=g["route_dbg"][:, 0:NTILE], in_=idxf_a[:]).then_inc(ctx.sem(q + "rd"), 16)
            s.dma_start(out=g["route_dbg"][:, NTILE:2 * NTILE], in_=pos2_a[:]).then_inc(ctx.sem(q + "rd"), 16)
            s.dma_start(out=g["route_dbg"][:, 2 * NTILE:3 * NTILE], in_=slkf_a[:]).then_inc(ctx.sem(q + "rd"), 16)
            s.dma_start(out=g["route_dbg"][:, 3 * NTILE:4 * NTILE], in_=comb_a[:]).then_inc(ctx.sem(q + "rd"), 16)
            s.wait_ge(ctx.sem(q + "rd"), 64)
        ctx.bump(q + "rd", 64)

    n_tz = ctx.val(q + "tz")
    for i in range(NTILE):
        ctx.bump(S_SC, 16)
        @ctx.on("gpsimd")
        def _(gp, i=i):
            if i == 0:
                gp.wait_ge(ctx.sem(S_VS), n_vs_passc)
                gp.wait_ge(ctx.sem(q + "tz"), n_tz)
            gp.indirect_dma_start(
                out=g["tab_in"],
                out_offset=IndirectOffsetOnAxis(ap=tabs_u[:, i:i + 1], axis=0),
                in_=payl_u[:, i:i + 1],
                in_offset=None,
            ).then_inc(ctx.sem(S_SC), 16)
    n_sc = ctx.val(S_SC)

    @ctx.on("sync")
    def _(s):
        s.wait_ge(ctx.sem(S_VS), n_vs_passc)
        s.dma_start(out=g["hmst"][0:1, :].rearrange("1 p -> p 1"),
                    in_=slkf_a[:, NTILE - 1:NTILE]).then_inc(ctx.sem(q + "hm"), 16)
        s.dma_start(out=g["hmst"][1:2, :].rearrange("1 p -> p 1"),
                    in_=comb_a[:, NTILE - 1:NTILE]).then_inc(ctx.sem(q + "hm"), 16)
    ctx.bump(q + "hm", 32)
    n_hm0 = ctx.val(q + "hm")
    @ctx.on("sync")
    def _(s):
        s.wait_ge(ctx.sem(q + "hm"), n_hm0)
        s.dma_start(out=g["cnt_in"][0:1, 16:19], in_=g["hmst"][0:1, 125:128])\
            .then_inc(ctx.sem(q + "hm"), 16)
        s.dma_start(out=g["cnt_in"][0:1, 19:22], in_=g["hmst"][1:2, 125:128])\
            .then_inc(ctx.sem(q + "hm"), 16)
    ctx.bump(q + "hm", 32)

    @ctx.on("gpsimd")
    def _(gp):
        gp.wait_ge(ctx.sem(S_SC), n_sc)
        gp.collective_compute("AllGather", ALU.bypass,
                              replica_groups=[list(range(NC))],
                              ins=[g["tab_in"].opt()], outs=[g["tab_out"].opt()])\
            .then_inc(ctx.sem(S_AGT), 1)
    ctx.bump(S_AGT, 1)
    n_agt = ctx.val(S_AGT)

    n_vmg = ctx.bump(S_VS, 2)
    @ctx.on("gpsimd")
    def _(gp):
        gp.iota(tbf[:, 0:16].rearrange("p (a b) -> p a b", a=2),
                pattern=[[128, 2], [TL // 16, 8]], base=0,
                channel_multiplier=1, allow_small_or_imprecise_dtypes=True)
        gp.engine_nop().then_inc(ctx.sem(q + "mgi"), 1)
    ctx.bump(q + "mgi", 1)
    n_mgi = ctx.val(q + "mgi")
    @ctx.on("vector")
    def _(v):
        v.wait_ge(ctx.sem(q + "mgi"), n_mgi)
        v.scalar_tensor_tensor(tbf[:, 0:16],
                               g["pcf_col"][:, PCF_SLOTBASE:PCF_SLOTBASE + 1].broadcast_to([128, 16]),
                               1.0 / 16.0, tbf[:, 0:16], op0=ALU.mult, op1=ALU.add)
        v.engine_nop().then_inc(ctx.sem(S_VS), 1)
    @ctx.on("vector")
    def _(v, tgt=n_vmg - 1):
        v.wait_ge(ctx.sem(S_VS), tgt)
        v.tensor_copy(mgi[:].rearrange("p a b -> p (a b)"), tbf[:, 0:16])
        v.engine_nop().then_inc(ctx.sem(S_VS), 1)
    n_vmg_done = ctx.val(S_VS)

    for e in range(2):
        for k in range(NC):
            ctx.bump(S_MG, 16)
            @ctx.on("gpsimd")
            def _(gp, e=e, k=k):
                if e == 0 and k == 0:
                    gp.wait_ge(ctx.sem(S_AGT), n_agt)
                    gp.wait_ge(ctx.sem(S_VS), n_vmg_done)
                gp.indirect_dma_start(
                    out=mgf[:, e, k, :],
                    out_offset=None,
                    in_=g["tab_out"],
                    in_offset=IndirectOffsetOnAxis(ap=mgi[:, e, k:k + 1], axis=0),
                ).then_inc(ctx.sem(S_MG), 16)
    n_mg = ctx.val(S_MG)
    n_vm1 = ctx.bump(S_VS, 1)
    @ctx.on("vector")
    def _(v):
        v.wait_ge(ctx.sem(S_MG), n_mg)
        v.tensor_tensor(tos_f[:], mgf[:, :, 0, :], mgf[:, :, 1, :], op=ALU.min)
        v.engine_nop().then_inc(ctx.sem(S_VS), 1)
    for k in range(2, NC):
        n_vmk = ctx.bump(S_VS, 1)
        @ctx.on("vector")
        def _(v, k=k, w=n_vm1 + k - 2):
            v.wait_ge(ctx.sem(S_VS), w)
            v.tensor_tensor(tos_f[:], tos_f[:], mgf[:, :, k, :], op=ALU.min)
            v.engine_nop().then_inc(ctx.sem(S_VS), 1)
    n_vm2 = ctx.bump(S_VS, 1)
    @ctx.on("vector")
    def _(v, w=ctx.val(S_VS) - 1):
        v.wait_ge(ctx.sem(S_VS), w)
        v.tensor_scalar_min(tos_f[:], tos_f[:], BIGF)
        v.engine_nop().then_inc(ctx.sem(S_VS), 1)
    ctx.bump(S_VS, 1)
    @ctx.on("vector")
    def _(v, w=n_vm2):
        v.wait_ge(ctx.sem(S_VS), w)
        v.tensor_copy(tos_u[:], tos_f[:])
        v.engine_nop().then_inc(ctx.sem(S_VS), 1)
    n_tos = ctx.val(S_VS)

    ctx.close_tensors([cm1, cm2, cm3])
    cms.remove(cm1); cms.remove(cm2); cms.remove(cm3)
    ctx.close_tensors(cms2)

    cms3 = []
    bufT = sb("bufT", [128, 8, CAP], BF16, cms3)
    w1t = sb("w1t", [128, 2, 8, 128], BF16, cms3)
    w2sb = sb("w2sb", [128, 32, D], BF16, cms3)
    hbf = sb("hbf", [128, 32, CAP // 2], BF16, cms3)
    b1sb = sb("b1sb", [128, 32], F32, cms3)
    b2bc = sb("b2bc", [128, D], F32, cms3)
    ysb = sb("ysb", [128, 2, D], F32, cms3)

    hP, cm4 = ctx.ps(q + "hP", [128, 2, 1024], F32)
    yP, cm5 = ctx.ps(q + "yP", [128, 4, 512], F32)
    cms += [cm4, cm5]

    for e in range(2):
        eq = f"{q}e{e}"
        eb_peh = ctx.val(eq + "peh")
        eb_hev = ctx.val(eq + "hev")
        eb_yev = ctx.val(eq + "yev")
        for f in range(16):
            fg = e * 16 + f
            fb = fg % 2
            n_ga = ctx.bump(q + "ga" + str(fb), 16)
            @ctx.on("gpsimd")
            def _(gp, f=f, fg=fg, fb=fb, e=e):
                if fg == 0:
                    gp.wait_ge(ctx.sem(S_VS), n_tos)
                    gp.wait_ge(ctx.sem(S_AGX), n_agx)
                gp.wait_ge(ctx.sem(q + "gst" + str(fb)), base_gst[fb] + (fg // 2) * 16)
                gp.indirect_dma_start(
                    out=gbuf[:, fb, :],
                    out_offset=None,
                    in_=g["xln_full"],
                    in_offset=IndirectOffsetOnAxis(ap=tos_u[:, e, f:f + 1], axis=0),
                    bounds_check=g["r_ntok"],
                    oob_is_err=False,
                ).then_inc(ctx.sem(q + "ga" + str(fb)), 16)
            ctx.bump(q + "gst" + str(fb), 16)
            @ctx.on("sync")
            def _(s, f=f, fb=fb, n_ga=n_ga):
                s.wait_ge(ctx.sem(q + "ga" + str(fb)), n_ga)
                s.dma_start(out=g["xg"][f * 128:(f + 1) * 128, :], in_=gbuf[:, fb, :])\
                    .then_inc(ctx.sem(q + "gst" + str(fb)), 16)
        n_gst0 = ctx.val(q + "gst0")
        n_gst1 = ctx.val(q + "gst1")
        for kc in range(8):
            ctx.bump(eq + "trb", 16)
            @ctx.on("scalar")
            def _(sc, kc=kc, eq=eq, n_gst0=n_gst0, n_gst1=n_gst1):
                if kc == 0:
                    sc.wait_ge(ctx.sem(q + "gst0"), n_gst0)
                    sc.wait_ge(ctx.sem(q + "gst1"), n_gst1)
                sc.dma_start_transpose(out=bufT[:, kc, :],
                                       in_=g["xg"][:, kc * 128:(kc + 1) * 128])\
                    .then_inc(ctx.sem(eq + "trb"), 16)
        n_trb = ctx.val(eq + "trb")
        @ctx.on("sync")
        def _(s, e=e, eq=eq):
            s.dma_start(out=b1sb[:], in_=g["b1v"][e]).then_inc(ctx.sem(eq + "wl"), 16)
            s.dma_start(out=b2bc[:], in_=g["b2v"][e:e + 1, :].broadcast_to([128, D]))\
                .then_inc(ctx.sem(eq + "wl"), 16)
            for hc in range(32):
                s.dma_start(out=w2sb[:, hc, :], in_=g["w2_b"][e][hc * 128:(hc + 1) * 128, :])\
                    .then_inc(ctx.sem(eq + "wl"), 16)
        n_wl = ctx.bump(eq + "wl", 16 * 34)

        for half in range(2):
            s0 = half * 1024
            for hc in range(32):
                j = half * 32 + hc
                wb = hc % 2
                n_w1l = ctx.bump(eq + "w1l" + str(wb), 16)
                @ctx.on("sync")
                def _(s, e=e, hc=hc, wb=wb, j=j, eq=eq, eb_peh=eb_peh):
                    s.wait_ge(ctx.sem(eq + "peh"), eb_peh + max(0, j - 1))
                    s.dma_start(out=w1t[:, wb, :, :],
                                in_=g["w1_b"][e, hc].rearrange("a p h -> p a h"))\
                        .then_inc(ctx.sem(eq + "w1l" + str(wb)), 16)
                n_peh = ctx.bump(eq + "peh", 1)
                @ctx.on("tensor")
                def _(t, hc=hc, wb=wb, s0=s0, j=j, n_w1l=n_w1l, first=(j == 0),
                      eq=eq, n_trb=n_trb, eb_hev=eb_hev):
                    t.wait_ge(ctx.sem(eq + "w1l" + str(wb)), n_w1l)
                    if first:
                        t.wait_ge(ctx.sem(eq + "trb"), n_trb)
                    t.wait_ge(ctx.sem(eq + "hev"), eb_hev + max(0, j - 1))
                    last = None
                    for kc in range(8):
                        for nb in range(2):
                            last = t.matmul(hP[:, wb, nb * 512:(nb + 1) * 512],
                                            w1t[:, wb, kc, :],
                                            bufT[:, kc, s0 + nb * 512: s0 + (nb + 1) * 512],
                                            start=(kc == 0), stop=(kc == 7))
                    last.then_inc(ctx.sem(eq + "peh"), 1)
                ctx.bump(eq + "hev", 1)
                @ctx.on("scalar")
                def _(sc, hc=hc, wb=wb, n_peh=n_peh, n_wl=n_wl, eq=eq):
                    sc.wait_ge(ctx.sem(eq + "peh"), n_peh)
                    sc.wait_ge(ctx.sem(eq + "wl"), n_wl)
                    sc.activation(out=hbf[:, hc, :], in_=hP[:, wb, :], func=AF.Relu,
                                  bias=b1sb[:, hc:hc + 1]).then_inc(ctx.sem(eq + "hev"), 1)
            n_hev_half = ctx.val(eq + "hev")
            for st in range(8):
                jj = half * 8 + st
                sb2 = st % 2
                for dh in range(2):
                    yb = (jj * 2 + dh) % 4
                    n_pey = ctx.bump(eq + "pey", 1)
                    @ctx.on("tensor")
                    def _(t, st=st, dh=dh, yb=yb, jj=jj, n_hev_half=n_hev_half, eq=eq, eb_yev=eb_yev):
                        if st == 0 and dh == 0:
                            t.wait_ge(ctx.sem(eq + "hev"), n_hev_half)
                        t.wait_ge(ctx.sem(eq + "yev"), eb_yev + max(0, jj * 2 + dh - 3))
                        last = None
                        for hc in range(32):
                            last = t.matmul(yP[:, yb, :],
                                            hbf[:, hc, st * 128:(st + 1) * 128],
                                            w2sb[:, hc, dh * 512:(dh + 1) * 512],
                                            start=(hc == 0), stop=(hc == 31))
                        last.then_inc(ctx.sem(eq + "pey"), 1)
                    ctx.bump(eq + "yev", 1)
                    @ctx.on("vector")
                    def _(v, st=st, dh=dh, yb=yb, sb2=sb2, jj=jj, n_pey=n_pey, n_wl=n_wl, eq=eq, e=e, base_yst=base_yst):
                        v.wait_ge(ctx.sem(eq + "pey"), n_pey)
                        if jj == 0 and dh == 0:
                            v.wait_ge(ctx.sem(eq + "wl"), n_wl)
                        jjg = e * 16 + jj
                        if jjg >= 2 and dh == 0:
                            v.wait_ge(ctx.sem(q + "yst" + str(sb2)), base_yst[sb2] + (jjg // 2) * 16)
                        v.scalar_tensor_tensor(ysb[:, sb2, dh * 512:(dh + 1) * 512],
                                               yP[:, yb, :], 0.0, b2bc[:, dh * 512:(dh + 1) * 512],
                                               op0=ALU.bypass, op1=ALU.add)
                        v.engine_nop().then_inc(ctx.sem(eq + "yev"), 1)
                n_yev = ctx.val(eq + "yev")
                fl = half * 8 + st
                ctx.bump(q + "yst" + str(sb2), 16)
                @ctx.on("sync")
                def _(s, sb2=sb2, fl=fl, e=e, n_yev=n_yev, eq=eq):
                    s.wait_ge(ctx.sem(eq + "yev"), n_yev)
                    s.dma_start(out=g["y_in"][e * CAP:(e + 1) * CAP, :]
                                .rearrange("(p f) d -> p f d", f=16)[:, fl, :],
                                in_=ysb[:, sb2, :]).then_inc(ctx.sem(q + "yst" + str(sb2)), 16)
    ywaits = [(q + f"yst{p}", ctx.val(q + f"yst{p}")) for p in range(2)]
    @ctx.on("gpsimd")
    def _(gp):
        for nm, vv in ywaits:
            gp.wait_ge(ctx.sem(nm), vv)
        gp.collective_compute("AllGather", ALU.bypass,
                              replica_groups=[list(range(NC))],
                              ins=[g["y_in"].opt()], outs=[g["y_full"].opt()])\
            .then_inc(ctx.sem(S_AGY), 1)
    ctx.bump(S_AGY, 1)
    n_agy = ctx.val(S_AGY)

    ctx.close_tensors([cm4, cm5])
    cms.remove(cm4); cms.remove(cm5)
    ctx.close_tensors(cms3)

    if call == 0:
        n_hm = ctx.val(q + "hm")
        @ctx.on("gpsimd")
        def _(gp):
            gp.wait_ge(ctx.sem(q + "hm"), n_hm)
            gp.collective_compute("AllGather", ALU.bypass,
                                  replica_groups=[list(range(NC))],
                                  ins=[g["cnt_in"].opt()], outs=[g["cnt_out"].opt()])\
                .then_inc(ctx.sem(q + "agh"), 1)
        ctx.bump(q + "agh", 1)
        n_agh = ctx.val(q + "agh")
        @ctx.on("sync")
        def _(s):
            s.wait_ge(ctx.sem(q + "agh"), n_agh)
            s.dma_start(out=cld[:], in_=g["cnt_out"][:, :]).then_inc(ctx.sem(q + "hl"), 16)
        ctx.bump(q + "hl", 16)
        n_hl = ctx.val(q + "hl")
        n_hv2 = ctx.bump(S_VS, 1)
        @ctx.on("vector")
        def _(v):
            v.wait_ge(ctx.sem(q + "hl"), n_hl)
            v.tensor_scalar_add(mask8[:, 1:2], iota8[:], 1.0)
            v.engine_nop().then_inc(ctx.sem(S_VS), 1)
        n_hv3 = ctx.bump(S_VS, 1)
        @ctx.on("vector")
        def _(v, n_hv2=n_hv2):
            v.wait_ge(ctx.sem(S_VS), n_hv2)
            v.tensor_scalar_mul(mask8[:, 1:2], mask8[:, 1:2], float(TPC))
            v.engine_nop().then_inc(ctx.sem(S_VS), 1)
        n_hv4 = ctx.bump(S_VS, 1)
        @ctx.on("vector")
        def _(v, n_hv3=n_hv3):
            v.wait_ge(ctx.sem(S_VS), n_hv3)
            v.tensor_tensor(mask8[:, 1:2], mask8[:, 1:2],
                            g["pcf_col"][0:8, PCF_COREBASE:PCF_COREBASE + 1], op=ALU.is_equal)
            v.engine_nop().then_inc(ctx.sem(S_VS), 1)
        hmP, cmh = ctx.ps(q + "hmP", [1, 8], F32)
        cms.append(cmh)
        ctx.bump(S_PE, 1)
        @ctx.on("tensor")
        def _(t, n_hv4=n_hv4):
            t.wait_ge(ctx.sem(S_VS), n_hv4)
            t.matmul(hmP[:, 0:6], mask8[:, 1:2], cld[0:8, 16:22], start=True, stop=True)\
                .then_inc(ctx.sem(S_PE), 1)
        n_hv5 = ctx.bump(S_VS, 1)
        @ctx.on("vector")
        def _(v, n_pe=ctx.val(S_PE)):
            v.wait_ge(ctx.sem(S_PE), n_pe)
            v.tensor_copy(cnt_sb[:, 24:30], hmP[:, 0:6])
            v.engine_nop().then_inc(ctx.sem(S_VS), 1)
        @ctx.on("sync")
        def _(s, n_hv5=n_hv5):
            s.wait_ge(ctx.sem(S_VS), n_hv5)
            s.dma_start(out=g["hmx"], in_=cnt_sb[:, 24:30]).then_inc(ctx.sem(q + "hx"), 16)
        ctx.bump(q + "hx", 16)
        n_hx = ctx.val(q + "hx")
        @ctx.on("sync")
        def _(s):
            s.wait_ge(ctx.sem(q + "hx"), n_hx)
            s.dma_start(out=hcol[0:3, 0:1], in_=g["hmx"][0:1, 0:3].rearrange("1 a -> a 1"))\
                .then_inc(ctx.sem(q + "hx"), 16)
            s.dma_start(out=hcol[0:3, 1:2], in_=g["hmx"][0:1, 3:6].rearrange("1 a -> a 1"))\
                .then_inc(ctx.sem(q + "hx"), 16)
        ctx.bump(q + "hx", 32)
        n_hx2 = ctx.val(q + "hx")
        n_hv6 = ctx.bump(S_VS, 2)
        @ctx.on("vector")
        def _(v):
            v.wait_ge(ctx.sem(q + "hx"), n_hx2)
            v.memset(hcu[:], 0)
            v.engine_nop().then_inc(ctx.sem(S_VS), 1)
        @ctx.on("vector")
        def _(v, w=n_hv6 - 1):
            v.wait_ge(ctx.sem(S_VS), w)
            v.tensor_copy(hcu[0:3, :], hcol[0:3, 0:1])
            v.engine_nop().then_inc(ctx.sem(S_VS), 1)
        n_hcu = ctx.val(S_VS)
        ctx.bump(q + "hg", 16)
        @ctx.on("gpsimd")
        def _(gp):
            gp.wait_ge(ctx.sem(S_VS), n_hcu)
            gp.wait_ge(ctx.sem(S_AGY), n_agy)
            gp.indirect_dma_start(
                out=yrow[:, 0, :], out_offset=None,
                in_=g["y_full"],
                in_offset=IndirectOffsetOnAxis(ap=hcu[:], axis=0),
                bounds_check=g["r_ntok"],
                oob_is_err=False,
            ).then_inc(ctx.sem(q + "hg"), 16)
        n_hg = ctx.val(q + "hg")
        @ctx.on("sync")
        def _(s):
            s.dma_start(out=x1row[:, 0, :][0:3, :], in_=src[0:3, :]).then_inc(ctx.sem(q + "hgx"), 16)
        ctx.bump(q + "hgx", 16)
        n_hg2 = ctx.val(q + "hgx")
        n_hv7 = ctx.bump(S_VS, 2)
        @ctx.on("vector")
        def _(v):
            v.wait_ge(ctx.sem(q + "hg"), n_hg)
            v.wait_ge(ctx.sem(q + "hgx"), n_hg2)
            v.scalar_tensor_tensor(yrow[0:3, 0, :], yrow[0:3, 0, :], hcol[0:3, 1:2],
                                   x1row[0:3, 0, :], op0=ALU.mult, op1=ALU.add)
            v.engine_nop().then_inc(ctx.sem(S_VS), 1)
        @ctx.on("vector")
        def _(v, w=n_hv7 - 1):
            v.wait_ge(ctx.sem(S_VS), w)
            v.tensor_scalar_mul(yrow[0:3, 0, :], yrow[0:3, 0, :],
                                g["pcf_col"][0:3, PCF_PARITY:PCF_PARITY + 1])
            v.engine_nop().then_inc(ctx.sem(S_VS), 1)
        n_hv8 = ctx.val(S_VS)
        ctx.bump(q + "outh", 16)
        @ctx.on("sync")
        def _(s, n_hv8=n_hv8):
            s.wait_ge(ctx.sem(S_VS), n_hv8)
            s.dma_start(out=dst[0:3, :], in_=yrow[0:3, 0, :]).then_inc(ctx.sem(q + "outh"), 16)

    for i in range(NTILE):
        b = i % 2
        n_yg = ctx.bump(q + "yg" + str(b), 16)
        @ctx.on("gpsimd")
        def _(gp, i=i, b=b):
            if i == 0:
                gp.wait_ge(ctx.sem(S_AGY), n_agy)
                if call == 0:
                    gp.wait_ge(ctx.sem(q + "outh"), 16)
            if i >= 2:
                gp.wait_ge(ctx.sem(S_OUT + str(b)), base_out[b] + (i // 2) * 16)
            gp.indirect_dma_start(
                out=yrow[:, b, :], out_offset=None,
                in_=g["y_full"],
                in_offset=IndirectOffsetOnAxis(ap=slot_u[:, i:i + 1], axis=0),
                bounds_check=g["r_ntok"],
                oob_is_err=False,
            ).then_inc(ctx.sem(q + "yg" + str(b)), 16)
        n_xl = ctx.bump(q + "xl" + str(b), 16)
        @ctx.on("sync")
        def _(s, i=i, b=b):
            s.wait_ge(ctx.sem(q + "ev"), base_ev + max(0, i - 1))
            s.dma_start(out=x1row[:, b, :], in_=src[soff + i * 128: soff + (i + 1) * 128, :])\
                .then_inc(ctx.sem(q + "xl" + str(b)), 16)
        n_ev = ctx.bump(q + "ev", 1)
        @ctx.on("vector")
        def _(v, i=i, b=b, n_yg=n_yg, n_xl=n_xl):
            v.wait_ge(ctx.sem(q + "yg" + str(b)), n_yg)
            v.wait_ge(ctx.sem(q + "xl" + str(b)), n_xl)
            v.scalar_tensor_tensor(yrow[:, b, :], yrow[:, b, :], comb_a[:, i:i + 1],
                                   x1row[:, b, :], op0=ALU.mult, op1=ALU.add)
            v.engine_nop().then_inc(ctx.sem(q + "ev"), 1)
        ctx.bump(S_OUT + str(b), 16)
        @ctx.on("sync")
        def _(s, i=i, b=b, n_ev=n_ev):
            s.wait_ge(ctx.sem(q + "ev"), n_ev)
            s.dma_start(out=dst[doff + i * 128: doff + (i + 1) * 128, :], in_=yrow[:, b, :])\
                .then_inc(ctx.sem(S_OUT + str(b)), 16)
    @ctx.on("sync")
    def _(s, a=ctx.val(S_OUT + "0"), bb=ctx.val(S_OUT + "1"),
          hh=(ctx.val(q + "outh") if call == 0 else 0)):
        s.wait_ge(ctx.sem(S_OUT + "0"), a)
        s.wait_ge(ctx.sem(S_OUT + "1"), bb)
        if call == 0:
            s.wait_ge(ctx.sem(q + "outh"), hh)

    ctx.close_tensors(cms)


def emit_m1(ctx, g):
    q = "m1"
    cms = []
    def sb(name, shape, dt):
        h, cm = ctx.sb(q + name, shape, dt)
        cms.append(cm)
        return h

    xr = sb("xr", [128, 2, D], F32)
    scr = sb("scr", [128, D], F32)
    col = sb("col", [128, 2, 8], F32)
    xlt = sb("xlt", [128, 2, D], BF16)
    winb = sb("winb", [128, 32, 8, 128], BF16)
    xlnT = sb("xlnT", [128, 2, 8, TC], BF16)
    xlnTh = sb("xlnTh", [128, 8, 16], BF16)
    xi1 = sb("xi1", [128, 2, TC + 3], F32)
    carry = sb("carry", [128, 16, 3], F32)
    u1 = sb("u1", [128, 2, TC], F32)
    ub1 = sb("ub1", [128, 2, TC], BF16)
    sg1 = sb("sg1", [128, 2, TC], F32)
    xc_sb = sb("xc_sb", [128, TC], F32)

    S_AC = q + "ac"; S_VS = q + "vs"
    base_st = [ctx.val(q + "st0"), ctx.val(q + "st1")]

    def ln_tile(i, nrow, src_off, dst_off):
        b = i % 2
        n_ld = ctx.bump(q + "ld" + str(b), 16)
        @ctx.on("sync")
        def _(s, i=i, b=b, nrow=nrow, src_off=src_off):
            if i >= 2:
                s.wait_ge(ctx.sem(S_AC), lnac_rel[i - 2])
            s.dma_start(out=xr[0:nrow, b, :], in_=g["x1b"][src_off:src_off + nrow, :])\
                .then_inc(ctx.sem(q + "ld" + str(b)), 16)
        n_ac = ctx.bump(S_AC, 1)
        @ctx.on("scalar")
        def _(sc, b=b, nrow=nrow, n_ld=n_ld):
            sc.wait_ge(ctx.sem(q + "ld" + str(b)), n_ld)
            sc.activation(out=scr[0:nrow, :], in_=xr[0:nrow, b, :], func=AF.Square,
                          accum_out=col[0:nrow, b, 1:2]).then_inc(ctx.sem(S_AC), 1)
        n_v1 = ctx.bump(S_VS, 1)
        @ctx.on("vector")
        def _(v, b=b, nrow=nrow, n_ld=n_ld, n_ac=n_ac):
            v.wait_ge(ctx.sem(q + "ld" + str(b)), n_ld)
            v.wait_ge(ctx.sem(S_AC), n_ac)
            v.reduce_sum(col[0:nrow, b, 0:1], xr[0:nrow, b, :], axis=AX.X)
            v.engine_nop().then_inc(ctx.sem(S_VS), 1)
        n_v2 = ctx.bump(S_VS, 1)
        @ctx.on("vector")
        def _(v, b=b, nrow=nrow, n_v1=n_v1):
            v.wait_ge(ctx.sem(S_VS), n_v1)
            v.tensor_scalar_mul(col[0:nrow, b, 2:3], col[0:nrow, b, 0:1], 1.0 / D)
            v.tensor_scalar_mul(col[0:nrow, b, 3:4], col[0:nrow, b, 1:2], 1.0 / D)
            v.engine_nop().then_inc(ctx.sem(S_VS), 1)
        n_v3 = ctx.bump(S_VS, 1)
        @ctx.on("vector")
        def _(v, b=b, nrow=nrow, n_v2=n_v2):
            v.wait_ge(ctx.sem(S_VS), n_v2)
            v.tensor_tensor(col[0:nrow, b, 6:7], col[0:nrow, b, 2:3], col[0:nrow, b, 2:3], op=ALU.mult)
            v.engine_nop().then_inc(ctx.sem(S_VS), 1)
        n_v4 = ctx.bump(S_VS, 1)
        @ctx.on("vector")
        def _(v, b=b, nrow=nrow, n_v3=n_v3):
            v.wait_ge(ctx.sem(S_VS), n_v3)
            v.tensor_sub(col[0:nrow, b, 3:4], col[0:nrow, b, 3:4], col[0:nrow, b, 6:7])
            v.engine_nop().then_inc(ctx.sem(S_VS), 1)
        n_ac1 = ctx.bump(S_AC, 1)
        n_ac2 = ctx.bump(S_AC, 1)
        @ctx.on("scalar")
        def _(sc, b=b, nrow=nrow, n_v4=n_v4, n_ac1=n_ac1):
            sc.wait_ge(ctx.sem(S_VS), n_v4)
            sc.activation(out=col[0:nrow, b, 4:5], in_=col[0:nrow, b, 3:4], func=AF.Ln, bias=EPS)\
                .then_inc(ctx.sem(S_AC), 1)
            sc.wait_ge(ctx.sem(S_AC), n_ac1)
            sc.activation(out=col[0:nrow, b, 4:5], in_=col[0:nrow, b, 4:5], func=AF.Exp, scale=-0.5)\
                .then_inc(ctx.sem(S_AC), 1)
        n_v5 = ctx.bump(S_VS, 1)
        @ctx.on("vector")
        def _(v, b=b, nrow=nrow, n_ac2=n_ac2):
            v.wait_ge(ctx.sem(S_AC), n_ac2)
            v.scalar_tensor_tensor(col[0:nrow, b, 5:6], col[0:nrow, b, 2:3], -1.0,
                                   col[0:nrow, b, 4:5], op0=ALU.mult, op1=ALU.mult)
            v.engine_nop().then_inc(ctx.sem(S_VS), 1)
        n_ac3 = ctx.bump(S_AC, 1)
        lnac_rel[i] = n_ac3
        @ctx.on("scalar")
        def _(sc, i=i, b=b, nrow=nrow, n_v5=n_v5):
            sc.wait_ge(ctx.sem(S_VS), n_v5)
            if i >= 2:
                sc.wait_ge(ctx.sem(q + "st" + str(b)), base_st[b] + (i // 2) * 16)
            sc.activation(out=xlt[0:nrow, b, :], in_=xr[0:nrow, b, :], func=AF.Identity,
                          scale=col[0:nrow, b, 4:5], bias=col[0:nrow, b, 5:6]).then_inc(ctx.sem(S_AC), 1)
        ctx.bump(q + "st" + str(b), 16)
        @ctx.on("sync")
        def _(s, i=i, b=b, nrow=nrow, dst_off=dst_off, n_ac3=n_ac3):
            s.wait_ge(ctx.sem(S_AC), n_ac3)
            s.dma_start(out=g["xln1"][dst_off:dst_off + nrow, :], in_=xlt[0:nrow, b, :])\
                .then_inc(ctx.sem(q + "st" + str(b)), 16)

    lnac_rel = {}
    for i in range(NTILE):
        ln_tile(i, 128, 3 + i * 128, 3 + i * 128)
    ln_tile(NTILE, 3, 0, 0)
    n_st0 = ctx.val(q + "st0")
    n_st1 = ctx.val(q + "st1")

    @ctx.on("sync")
    def _(s):
        for o in range(32):
            s.dma_start(out=winb[:, o, :, :], in_=g["winT_b"][o].rearrange("a p h -> p a h"))\
                .then_inc(ctx.sem(q + "wl"), 16)
    n_wl = ctx.bump(q + "wl", 16 * 32)

    @ctx.on("scalar")
    def _(sc):
        sc.wait_ge(ctx.sem(q + "st0"), n_st0)
        sc.wait_ge(ctx.sem(q + "st1"), n_st1)
        for kc in range(8):
            sc.dma_start_transpose(out=xlnTh[:, kc, :], in_=g["xln1"][0:16, kc * 128:(kc + 1) * 128])\
                .then_inc(ctx.sem(q + "trh"), 16)
    n_trh = ctx.bump(q + "trh", 16 * 8)

    xzP, cmp1 = ctx.ps(q + "xzP", [128, 4, TC], F32)
    hP, cmp2 = ctx.ps(q + "hP", [128, 16], F32)
    cms += [cmp1, cmp2]

    for o in range(16):
        n_hpe = ctx.bump(q + "hpe", 1)
        @ctx.on("tensor")
        def _(t, o=o):
            if o == 0:
                t.wait_ge(ctx.sem(q + "trh"), n_trh)
                t.wait_ge(ctx.sem(q + "wl"), n_wl)
            if o >= 1:
                t.wait_ge(ctx.sem(q + "hev"), o)
            last = None
            for kc in range(8):
                last = t.matmul(hP[:, 0:16], winb[:, o, kc, :], xlnTh[:, kc, :],
                                start=(kc == 0), stop=(kc == 7))
            last.then_inc(ctx.sem(q + "hpe"), 1)
        ctx.bump(q + "hev", 1)
        @ctx.on("scalar")
        def _(sc, o=o, n_hpe=n_hpe):
            sc.wait_ge(ctx.sem(q + "hpe"), n_hpe)
            sc.activation(out=carry[:, o, :], in_=hP[:, 0:3], func=AF.Copy)\
                .then_inc(ctx.sem(q + "hev"), 1)
    n_hev = ctx.val(q + "hev")

    cv_rel = {}
    last_uac = [0]
    for c in range(NCH):
        b = c % 2
        ctx.bump(q + "tr" + str(b), 16 * 8)
        @ctx.on("scalar")
        def _(sc, c=c, b=b):
            if c >= 2:
                sc.wait_ge(ctx.sem(q + "pe"), pe_rel[c - 2])
            for kc in range(8):
                sc.dma_start_transpose(out=xlnT[:, b, kc, :],
                                       in_=g["xln1"][3 + c * TC: 3 + (c + 1) * TC,
                                                     kc * 128:(kc + 1) * 128])\
                    .then_inc(ctx.sem(q + "tr" + str(b)), 16)
        if c == 0:
            pe_rel = {}
        n_tr = ctx.val(q + "tr" + str(b))
        for o in range(32):
            pb = o % 4
            ob = o % 2
            n_pe = ctx.bump(q + "pe", 1)
            @ctx.on("tensor")
            def _(t, c=c, b=b, o=o, pb=pb, n_tr=n_tr):
                if o == 0:
                    t.wait_ge(ctx.sem(q + "tr" + str(b)), n_tr)
                    if c == 0:
                        t.wait_ge(ctx.sem(q + "hev"), n_hev)
                j = c * 32 + o
                if j >= 4:
                    t.wait_ge(ctx.sem(q + "ev"), j - 3)
                last = None
                for kc in range(8):
                    last = t.matmul(xzP[:, pb, :], winb[:, o, kc, :], xlnT[:, b, kc, :],
                                    start=(kc == 0), stop=(kc == 7))
                last.then_inc(ctx.sem(q + "pe"), 1)
            pe_rel[c] = ctx.val(q + "pe")
            n_ev = ctx.bump(q + "ev", 1)
            @ctx.on("scalar")
            def _(sc, c=c, o=o, pb=pb, ob=ob, n_pe=n_pe):
                sc.wait_ge(ctx.sem(q + "pe"), n_pe)
                if o < 16:
                    jx = c * 16 + o
                    if jx >= 2:
                        sc.wait_ge(ctx.sem(S_VS), cv_rel[jx - 2])
                    sc.activation(out=xi1[:, ob, 3:3 + TC], in_=xzP[:, pb, :], func=AF.Copy)\
                        .then_inc(ctx.sem(q + "ev"), 1)
                else:
                    jz = c * 16 + (o - 16)
                    if jz >= 2:
                        sc.wait_ge(ctx.sem(q + "zs" + str(ob)), (jz // 2) * 16)
                    sc.activation(out=sg1[:, ob, :], in_=xzP[:, pb, :], func=AF.Silu)\
                        .then_inc(ctx.sem(q + "ev"), 1)
            if o >= 16:
                ctx.bump(q + "zs" + str(ob), 16)
                @ctx.on("sync")
                def _(s, c=c, o=o, ob=ob, n_ev=n_ev):
                    s.wait_ge(ctx.sem(q + "ev"), n_ev)
                    s.dma_start(out=g["sg_h"][(o - 16) * 128:(o - 15) * 128, c * TC:(c + 1) * TC],
                                in_=sg1[:, ob, :]).then_inc(ctx.sem(q + "zs" + str(ob)), 16)
            else:
                n_hd = ctx.bump(S_VS, 1)
                @ctx.on("vector")
                def _(v, o=o, ob=ob, n_ev=n_ev):
                    v.wait_ge(ctx.sem(q + "ev"), n_ev)
                    v.tensor_copy(xi1[:, ob, 0:3], carry[:, o, :])
                    v.engine_nop().then_inc(ctx.sem(S_VS), 1)
                n_vc = ctx.bump(S_VS, 4)
                @ctx.on("vector")
                def _(v, c=c, o=o, ob=ob, base=n_vc - 4, n_hd=n_hd, w_uac=last_uac[0]):
                    v.wait_ge(ctx.sem(S_VS), n_hd)
                    v.wait_ge(ctx.sem(S_AC), w_uac)
                    v.scalar_tensor_tensor(xc_sb[:], xi1[:, ob, 0:TC], g["cwcol"][:, 0 * 16 + o:0 * 16 + o + 1],
                                           g["cbcol"][:, o:o + 1].broadcast_to([128, TC]),
                                           op0=ALU.mult, op1=ALU.add)
                    v.engine_nop().then_inc(ctx.sem(S_VS), 1)
                    for k in range(1, 4):
                        v.wait_ge(ctx.sem(S_VS), base + k)
                        v.scalar_tensor_tensor(xc_sb[:], xi1[:, ob, k:k + TC], g["cwcol"][:, k * 16 + o:k * 16 + o + 1],
                                               xc_sb[:], op0=ALU.mult, op1=ALU.add)
                        v.engine_nop().then_inc(ctx.sem(S_VS), 1)
                n_cr = ctx.bump(S_VS, 1)
                cv_rel[c * 16 + o] = n_cr
                @ctx.on("vector")
                def _(v, o=o, ob=ob, n_vc=n_vc):
                    v.wait_ge(ctx.sem(S_VS), n_vc)
                    v.tensor_copy(carry[:, o, :], xi1[:, ob, TC:TC + 3])
                    v.engine_nop().then_inc(ctx.sem(S_VS), 1)
                n_uac = ctx.bump(S_AC, 1)
                last_uac[0] = n_uac
                @ctx.on("scalar")
                def _(sc, c=c, o=o, ob=ob, n_vc=n_vc):
                    sc.wait_ge(ctx.sem(S_VS), n_vc)
                    jx = c * 16 + o
                    if jx >= 2:
                        sc.wait_ge(ctx.sem(q + "us" + str(ob)), (jx // 2) * 32)
                    sc.activation(out=u1[:, ob, :], in_=xc_sb[:], func=AF.Silu).then_inc(ctx.sem(S_AC), 1)
                n_vcc = ctx.bump(S_VS, 1)
                @ctx.on("vector")
                def _(v, o=o, ob=ob, n_uac=n_uac):
                    v.wait_ge(ctx.sem(S_AC), n_uac)
                    v.tensor_copy(ub1[:, ob, :], u1[:, ob, :])
                    v.engine_nop().then_inc(ctx.sem(S_VS), 1)
                ctx.bump(q + "us" + str(ob), 32)
                @ctx.on("sync")
                def _(s, c=c, o=o, ob=ob, n_vcc=n_vcc):
                    s.wait_ge(ctx.sem(S_VS), n_vcc)
                    s.dma_start(out=g["u_h"][o * 128:(o + 1) * 128, c * TC:(c + 1) * TC],
                                in_=u1[:, ob, :]).then_inc(ctx.sem(q + "us" + str(ob)), 16)
                    s.dma_start(out=g["ubf_h"][o * 128:(o + 1) * 128, c * TC:(c + 1) * TC],
                                in_=ub1[:, ob, :]).then_inc(ctx.sem(q + "us" + str(ob)), 16)
    @ctx.on("sync")
    def _(s, a=ctx.val(q + "us0"), bb=ctx.val(q + "us1"),
          cz=ctx.val(q + "zs0"), dz=ctx.val(q + "zs1")):
        s.wait_ge(ctx.sem(q + "us0"), a)
        s.wait_ge(ctx.sem(q + "us1"), bb)
        s.wait_ge(ctx.sem(q + "zs0"), cz)
        s.wait_ge(ctx.sem(q + "zs1"), dz)
    ctx.close_tensors(cms)


def emit_m2(ctx, g):
    q = "m2"
    cms = []
    def sb(name, shape, dt):
        h, cm = ctx.sb(q + name, shape, dt)
        cms.append(cm)
        return h

    wxb = sb("wxb", [128, 16, 2 * DS], BF16)
    wdtg = sb("wdtg", [128, 4, 16, 128], BF16)
    ub2 = sb("ub2", [128, 2, 16, TC], BF16)
    u2 = sb("u2", [128, 2, 4, TC], F32)
    sg2 = sb("sg2", [128, 2, 4, TC], F32)
    dt_sb = sb("dt_sb", [128, 4, TC], F32)
    du_sb = sb("du_sb", [128, 4, TC], F32)
    r_sb = sb("r_sb", [128, TC], F32)
    a_sb = sb("a_sb", [128, TC], F32)
    bhi = sb("bhi", [16, TPC], BF16)
    blo = sb("blo", [16, TPC], BF16)
    chi = sb("chi", [16, TPC], BF16)
    clo = sb("clo", [16, TPC], BF16)
    Bbq = sb("Bbq", [128, 4, TC], F32)
    Cbq = sb("Cbq", [128, 4, TC], F32)
    dAq = sb("dAq", [128, 4, TC], F32)
    dBq = sb("dBq", [128, 4, TC], F32)
    sq = sb("sq", [128, 4, TC], F32)
    csq = sb("csq", [128, 4, TC], F32)
    t01 = sb("t01", [128, TC], F32)
    t23 = sb("t23", [128, TC], F32)
    y_acc = sb("y_acc", [128, 4, TC], F32)
    ygb = sb("ygb", [128, 4, TC], BF16)
    state_sb = sb("state_sb", [128, 256], F32)
    st_imp = sb("st_imp", [128, 256], F32)
    st_exp = sb("st_exp", [128, 256], F32)
    pr_f = sb("pr_f", [128, 1], F32)
    pr_u = sb("pr_u", [128, 1], U32)

    S_VS = q + "vs"; S_AC = q + "ac"; S_PE = q + "pe"; S_PL = q + "pl"

    r_ag = ctx.nc.gpsimd.alloc_register(q + "rag")
    r_gp = ctx.nc.gpsimd.alloc_register(q + "rgp")
    r_stg = ctx.nc.vector.alloc_register(q + "rstg")
    r_si = ctx.nc.vector.alloc_register(q + "rsi")

    @ctx.on("sync")
    def _(s):
        for kc in range(16):
            s.dma_start(out=wxb[:, kc, :], in_=g["wxT_b"][kc]).then_inc(ctx.sem(q + "wx"), 16)
    n_wx = ctx.bump(q + "wx", 16 * 16)

    n_vz = ctx.bump(S_VS, 1)
    @ctx.on("vector")
    def _(v):
        v.memset(state_sb[:], 0.0)
        v.memset(st_imp[:], 0.0)
        v.memset(st_exp[:], 0.0)
        v.engine_nop().then_inc(ctx.sem(S_VS), 1)
    @ctx.on("gpsimd")
    def _(gp):
        gp.iota(pr_f[:], pattern=[[0, 1]], base=0, channel_multiplier=1,
                allow_small_or_imprecise_dtypes=True)
        gp.engine_nop().then_inc(ctx.sem(S_PL), 1)
    n_pl0 = ctx.bump(S_PL, 1)
    n_vpr1 = ctx.bump(S_VS, 1)
    @ctx.on("vector")
    def _(v):
        v.wait_ge(ctx.sem(S_PL), n_pl0)
        v.tensor_scalar_add(pr_f[:], pr_f[:], g["pcf_col"][:, 5:6])
        v.engine_nop().then_inc(ctx.sem(S_VS), 1)
    ctx.bump(S_VS, 1)
    @ctx.on("vector")
    def _(v, w=n_vpr1):
        v.wait_ge(ctx.sem(S_VS), w)
        v.tensor_copy(pr_u[:], pr_f[:])
        v.engine_nop().then_inc(ctx.sem(S_VS), 1)
    n_pr = ctx.val(S_VS)

    dtP, cmp1 = ctx.ps(q + "dtP", [128, 2, TC], F32)
    bP, cmp2 = ctx.ps(q + "bP", [16, TC], F32)
    cP, cmp3 = ctx.ps(q + "cP", [16, TC], F32)
    bcP, cmp4 = ctx.ps(q + "bcP", [128, 4, TC], F32)
    cms += [cmp1, cmp2, cmp3, cmp4]

    da_hist = []
    pe_group_rel = {}
    gt_idx = [0]

    for gi in range(NG):
        @ctx.on("gpsimd")
        def _(gp, gi=gi):
            if gi == 0:
                gp.wait_ge(ctx.sem(S_VS), n_pr)
            gp.reg_load(r_gp, g["pcu"][0:1, 8 + gi:9 + gi])
            gp.wait_ge(ctx.sem(q + "gp"), r_gp)
            gp.dma_start(out=g[f"st_in_{gi}"], in_=st_exp[:, gi * 64:(gi + 1) * 64])\
                .then_inc(ctx.sem(q + f"stg{gi}"), 16)
        n_stg = ctx.bump(q + f"stg{gi}", 16)
        @ctx.on("gpsimd")
        def _(gp, gi=gi, n_stg=n_stg):
            gp.wait_ge(ctx.sem(q + f"stg{gi}"), n_stg)
            gp.collective_compute(
                "AllGather", ALU.bypass,
                replica_groups=[list(range(NC))],
                ins=[g[f"st_in_{gi}"].opt()],
                outs=[g[f"st_out_{gi}"][0:NC * 128, :].opt()],
            ).then_inc(ctx.sem(q + f"ag{gi}"), 1)
        ctx.bump(q + f"ag{gi}", 1)
        @ctx.on("gpsimd")
        def _(gp, gi=gi):
            gp.wait_ge(ctx.sem("setup_z"), 16 * NG)
            gp.wait_ge(ctx.sem(q + f"si{gi}"), gi * 16)
            gp.reg_load(r_ag, g["pcu"][0:1, 12 + gi:13 + gi])
            gp.wait_ge(ctx.sem(q + f"ag{gi}"), r_ag)
            gp.indirect_dma_start(
                out=st_imp[:, gi * 64:(gi + 1) * 64],
                out_offset=None,
                in_=g[f"st_out_{gi}"],
                in_offset=IndirectOffsetOnAxis(ap=pr_u[:], axis=0),
            ).then_inc(ctx.sem(q + f"si{gi}"), 16)
        n_si = ctx.bump(q + f"si{gi}", 16)

        @ctx.on("sync")
        def _(s, gi=gi):
            if gi >= 1:
                s.wait_ge(ctx.sem(S_PE), pe_group_rel[gi - 1])
            for ot in range(4):
                s.dma_start(out=wdtg[:, ot, :, :],
                            in_=g["wdtT_b"][gi * 4 + ot].rearrange("a p h -> p a h"))\
                    .then_inc(ctx.sem(q + "wd"), 16)
        n_wd = ctx.bump(q + "wd", 16 * 4)

        for c in range(NCH):
            b = c % 2
            pair = gi * NCH + c
            n_ub = ctx.bump(q + "ub" + str(b), 16 * 16)
            n_ug = ctx.bump(q + "ug" + str(b), 16 * 8)
            @ctx.on("sync")
            def _(s, gi=gi, c=c, b=b, pair=pair):
                if pair >= 2:
                    s.wait_ge(ctx.sem(q + "gt"), pair - 1)
                for kc in range(16):
                    s.dma_start(out=ub2[:, b, kc, :],
                                in_=g["ubf_h"][kc * 128:(kc + 1) * 128, c * TC:(c + 1) * TC])\
                        .then_inc(ctx.sem(q + "ub" + str(b)), 16)
                for dt_i in range(4):
                    r0 = (gi * 4 + dt_i) * 128
                    s.dma_start(out=u2[:, b, dt_i, :],
                                in_=g["u_h"][r0:r0 + 128, c * TC:(c + 1) * TC])\
                        .then_inc(ctx.sem(q + "ug" + str(b)), 16)
                    s.dma_start(out=sg2[:, b, dt_i, :],
                                in_=g["sg_h"][r0:r0 + 128, c * TC:(c + 1) * TC])\
                        .then_inc(ctx.sem(q + "ug" + str(b)), 16)
            for dt_i in range(4):
                pb = (c * 4 + dt_i) % 2
                j = pair * 4 + dt_i
                n_pe = ctx.bump(S_PE, 1)
                @ctx.on("tensor")
                def _(t, gi=gi, c=c, b=b, dt_i=dt_i, pb=pb, j=j, n_ub=n_ub, n_wd=n_wd):
                    if c == 0 and dt_i == 0:
                        t.wait_ge(ctx.sem(q + "wd"), n_wd)
                        if gi == 0:
                            t.wait_ge(ctx.sem(q + "wx"), n_wx)
                    if dt_i == 0:
                        t.wait_ge(ctx.sem(q + "ub" + str(b)), n_ub)
                    if j >= 2:
                        t.wait_ge(ctx.sem(S_AC), spab_hist[j - 2])
                    last = None
                    for kc in range(16):
                        last = t.matmul(dtP[:, pb, :], wdtg[:, dt_i, kc, :], ub2[:, b, kc, :],
                                        start=(kc == 0), stop=(kc == 15))
                    last.then_inc(ctx.sem(S_PE), 1)
                n_sp1 = ctx.bump(S_AC, 2)
                if not (gi == 0 and c == 0 and dt_i == 0):
                    pass
                else:
                    spab_hist = {}
                spab_hist[j] = n_sp1
                w_spl = spl_hist[-1] if (gi + c + dt_i) > 0 else 0
                @ctx.on("scalar")
                def _(sc, gi=gi, dt_i=dt_i, pb=pb, n_pe=n_pe, j=j, w_spl=w_spl):
                    sc.wait_ge(ctx.sem(S_PE), n_pe)
                    if j >= 1:
                        sc.wait_ge(ctx.sem(q + "spl"), w_spl)
                    bcol = g["bdtcol"][:, gi * 4 + dt_i:gi * 4 + dt_i + 1]
                    sc.activation(out=r_sb[:], in_=dtP[:, pb, :], func=AF.Relu, bias=bcol)
                    sc.activation(out=a_sb[:], in_=dtP[:, pb, :], func=AF.Abs, bias=bcol)\
                        .then_inc(ctx.sem(S_AC), 2)
                n_sp2 = ctx.bump(S_AC, 1)
                n_sp3 = ctx.bump(S_AC, 1)
                @ctx.on("scalar")
                def _(sc, n_sp1=n_sp1, n_sp2=n_sp2):
                    sc.wait_ge(ctx.sem(S_AC), n_sp1)
                    sc.activation(out=a_sb[:], in_=a_sb[:], func=AF.Exp, scale=-1.0)\
                        .then_inc(ctx.sem(S_AC), 1)
                    sc.wait_ge(ctx.sem(S_AC), n_sp2)
                    sc.activation(out=a_sb[:], in_=a_sb[:], func=AF.Ln, bias=1.0)\
                        .then_inc(ctx.sem(S_AC), 1)
                if pair == 0 and dt_i == 0:
                    spl_hist = []
                    spl_v_hist = {}
                    da_chunk_hist = {}
                n_spl = ctx.bump(q + "spl", 1)
                n_vsp = ctx.bump(S_VS, 2)
                spl_v_hist[(gi, c, dt_i)] = n_vsp - 1
                @ctx.on("vector")
                def _(v, gi=gi, c=c, b=b, dt_i=dt_i, pair=pair, n_sp3=n_sp3, n_ug=n_ug,
                      base=n_vsp - 2):
                    v.wait_ge(ctx.sem(S_AC), n_sp3)
                    if pair >= 1:
                        v.wait_ge(ctx.sem(S_AC), da_chunk_hist[pair - 1])
                    v.tensor_add(dt_sb[:, dt_i, :], r_sb[:], a_sb[:])
                    v.engine_nop().then_inc(ctx.sem(S_VS), 1)
                    v.engine_nop().then_inc(ctx.sem(q + "spl"), 1)
                    v.wait_ge(ctx.sem(S_VS), base + 1)
                    v.wait_ge(ctx.sem(q + "ug" + str(b)), n_ug)
                    v.tensor_mul(du_sb[:, dt_i, :], dt_sb[:, dt_i, :], u2[:, b, dt_i, :])
                    v.engine_nop().then_inc(ctx.sem(S_VS), 1)
                spl_hist.append(n_spl)
            n_du_all = ctx.val(S_VS)

            if gi == 0:
                n_bc = ctx.bump(S_PE, 2)
                @ctx.on("tensor")
                def _(t, c=c, b=b):
                    if c >= 2:
                        t.wait_ge(ctx.sem(q + "bcev"), (c - 1) * 4)
                    last = None
                    for kc in range(16):
                        last = t.matmul(bP[:], wxb[:, kc, 0:DS], ub2[:, b, kc, :],
                                        start=(kc == 0), stop=(kc == 15))
                    last.then_inc(ctx.sem(S_PE), 1)
                    last = None
                    for kc in range(16):
                        last = t.matmul(cP[:], wxb[:, kc, DS:2 * DS], ub2[:, b, kc, :],
                                        start=(kc == 0), stop=(kc == 15))
                    last.then_inc(ctx.sem(S_PE), 1)
                n_bhi = ctx.bump(S_AC, 2)
                @ctx.on("scalar")
                def _(sc, c=c, n_bc=n_bc):
                    sc.wait_ge(ctx.sem(S_PE), n_bc)
                    sc.activation(out=bhi[:, c * TC:(c + 1) * TC], in_=bP[:], func=AF.Copy)
                    sc.activation(out=chi[:, c * TC:(c + 1) * TC], in_=cP[:], func=AF.Copy)\
                        .then_inc(ctx.sem(S_AC), 2)
                n_bcev = ctx.bump(q + "bcev", 4)
                @ctx.on("vector")
                def _(v, c=c, n_bhi=n_bhi):
                    v.wait_ge(ctx.sem(S_AC), n_bhi)
                    v.scalar_tensor_tensor(blo[:, c * TC:(c + 1) * TC], bP[:], 0.0,
                                           bhi[:, c * TC:(c + 1) * TC], op0=ALU.bypass, op1=ALU.subtract)
                    v.scalar_tensor_tensor(clo[:, c * TC:(c + 1) * TC], cP[:], 0.0,
                                           chi[:, c * TC:(c + 1) * TC], op0=ALU.bypass, op1=ALU.subtract)
                    v.engine_nop().then_inc(ctx.sem(q + "bcev"), 4)
            n_bcev_need = 4 * (c + 1) if gi == 0 else 4 * NCH

            for qq in range(4):
                it = (pair * 4 + qq)
                n_bb = ctx.bump(S_PE, 2)
                bb_par = it % 2
                @ctx.on("tensor")
                def _(t, gi=gi, c=c, qq=qq, bb_par=bb_par, n_bcev_need=n_bcev_need, it=it):
                    if gi == 0:
                        t.wait_ge(ctx.sem(q + "bcev"), n_bcev_need)
                    elif c == 0 and qq == 0:
                        t.wait_ge(ctx.sem(q + "bcev"), 4 * NCH)
                    if it >= 1:
                        t.wait_ge(ctx.sem(q + "bbev"), it * 2)
                    last = None
                    for nl in range(4):
                        n = qq * 4 + nl
                        t.matmul(bcP[:, nl, :], g["seln"][:, n * 128:(n + 1) * 128],
                                 bhi[:, c * TC:(c + 1) * TC], start=True, stop=False)
                        last = t.matmul(bcP[:, nl, :], g["seln"][:, n * 128:(n + 1) * 128],
                                        blo[:, c * TC:(c + 1) * TC], start=False, stop=True)
                    last.then_inc(ctx.sem(S_PE), 1)
                    t.wait_ge(ctx.sem(q + "bbev"), it * 2 + 1)
                    last = None
                    for nl in range(4):
                        n = qq * 4 + nl
                        t.matmul(bcP[:, nl, :], g["seln"][:, n * 128:(n + 1) * 128],
                                 chi[:, c * TC:(c + 1) * TC], start=True, stop=False)
                        last = t.matmul(bcP[:, nl, :], g["seln"][:, n * 128:(n + 1) * 128],
                                        clo[:, c * TC:(c + 1) * TC], start=False, stop=True)
                    last.then_inc(ctx.sem(S_PE), 1)
                n_bbev1 = ctx.bump(q + "bbev", 1)
                n_bbev2 = ctx.bump(q + "bbev", 1)
                @ctx.on("scalar")
                def _(sc, bb_par=bb_par, w1=n_bb - 1, w2=n_bb, it=it):
                    sc.wait_ge(ctx.sem(S_PE), w1)
                    if it >= 1:
                        sc.wait_ge(ctx.sem(S_VS), da_hist[(it - 1) * 4 + 3])
                    sc.activation(out=Bbq[:], in_=bcP[:, :, :], func=AF.Copy)\
                        .then_inc(ctx.sem(q + "bbev"), 1)
                    sc.wait_ge(ctx.sem(S_PE), w2)
                    sc.activation(out=Cbq[:], in_=bcP[:, :, :], func=AF.Copy)\
                        .then_inc(ctx.sem(q + "bbev"), 1)

                for dt_i in range(4):
                    db = (qq * 4 + dt_i) % 2
                    dcol = gi * 4 + dt_i
                    itd = it * 4 + dt_i
                    n_da = ctx.bump(S_AC, 4)
                    @ctx.on("scalar")
                    def _(sc, gi=gi, c=c, qq=qq, dt_i=dt_i, db=db, dcol=dcol, b=b, itd=itd):
                        sc.wait_ge(ctx.sem(S_VS), spl_v_hist[(gi, c, dt_i)])
                        if itd >= 1:
                            sc.wait_ge(ctx.sem(S_VS), da_hist[itd - 1])
                        for nl in range(4):
                            acol = g["Acols"][:, dcol * 16 + qq * 4 + nl: dcol * 16 + qq * 4 + nl + 1]
                            sc.activation(out=dAq[:, nl, :], in_=dt_sb[:, dt_i, :],
                                          func=AF.Exp, scale=acol).then_inc(ctx.sem(S_AC), 1)
                    n_db = ctx.bump(S_VS, 1)
                    @ctx.on("vector")
                    def _(v, b=b, qq=qq, dt_i=dt_i, db=db, bb_par=bb_par,
                          n_bbev1=n_bbev1, itd=itd):
                        v.wait_ge(ctx.sem(q + "bbev"), n_bbev1)
                        v.tensor_tensor(dBq[:],
                                        du_sb[:, dt_i, :].unsqueeze(1).broadcast_to([128, 4, TC]),
                                        Bbq[:], op=ALU.mult)
                        v.engine_nop().then_inc(ctx.sem(S_VS), 1)
                    n_sc = ctx.bump(S_VS, 4)
                    @ctx.on("vector")
                    def _(v, gi=gi, c=c, qq=qq, dt_i=dt_i, db=db, dcol=dcol,
                          n_da=n_da, n_db=n_db, n_si=n_si, base=n_sc - 4):
                        v.wait_ge(ctx.sem(S_AC), n_da)
                        v.wait_ge(ctx.sem(S_VS), n_db)
                        if c == 0 and qq == 0 and dt_i == 0:
                            v.reg_load(r_si, g["pcu"][0:1, 20 + gi:21 + gi])
                            v.wait_ge(ctx.sem(q + f"si{gi}"), r_si)
                        for nl in range(4):
                            scol = dcol * 16 + qq * 4 + nl
                            init = st_imp[:, scol:scol + 1] if c == 0 else state_sb[:, scol:scol + 1]
                            v.tensor_tensor_scan(out=sq[:, nl, :], data0=dAq[:, nl, :],
                                                 data1=dBq[:, nl, :], initial=init,
                                                 op0=ALU.mult, op1=ALU.add)
                            v.engine_nop().then_inc(ctx.sem(S_VS), 1)
                    n_st = ctx.bump(S_VS, 1)
                    @ctx.on("vector")
                    def _(v, gi=gi, c=c, qq=qq, dt_i=dt_i, dcol=dcol, n_sc=n_sc):
                        v.wait_ge(ctx.sem(S_VS), n_sc)
                        scol = dcol * 16 + qq * 4
                        if c == NCH - 1:
                            v.reg_load(r_stg, g["pcu"][0:1, 16 + gi:17 + gi])
                            v.wait_ge(ctx.sem(q + f"stg{gi}"), r_stg)
                            v.tensor_copy(st_exp[:, scol:scol + 4],
                                          sq[:, 0:4, TC - 1:TC].rearrange("p a 1 -> p a"))
                        else:
                            v.tensor_copy(state_sb[:, scol:scol + 4],
                                          sq[:, 0:4, TC - 1:TC].rearrange("p a 1 -> p a"))
                        v.engine_nop().then_inc(ctx.sem(S_VS), 1)
                    n_cm = ctx.bump(S_VS, 1)
                    @ctx.on("vector")
                    def _(v, db=db, bb_par=bb_par, n_st=n_st, itd=itd):
                        v.wait_ge(ctx.sem(S_VS), n_st)
                        v.tensor_mul(csq[:], sq[:], Cbq[:])
                        v.engine_nop().then_inc(ctx.sem(S_VS), 1)
                    da_hist.append(ctx.val(S_VS))
                    n_red = ctx.bump(S_VS, 3 if qq == 0 else 4)
                    @ctx.on("vector")
                    def _(v, qq=qq, dt_i=dt_i, db=db, n_cm=n_cm, base=n_red - (3 if qq == 0 else 4)):
                        v.wait_ge(ctx.sem(S_VS), n_cm)
                        v.tensor_add(t01[:], csq[:, 0, :], csq[:, 1, :])
                        v.tensor_add(t23[:], csq[:, 2, :], csq[:, 3, :])
                        v.engine_nop().then_inc(ctx.sem(S_VS), 2)
                        v.wait_ge(ctx.sem(S_VS), base + 2)
                        if qq == 0:
                            v.tensor_add(y_acc[:, dt_i, :], t01[:], t23[:])
                            v.engine_nop().then_inc(ctx.sem(S_VS), 1)
                        else:
                            v.tensor_add(t01[:], t01[:], t23[:])
                            v.engine_nop().then_inc(ctx.sem(S_VS), 1)
                            v.wait_ge(ctx.sem(S_VS), base + 3)
                            v.tensor_add(y_acc[:, dt_i, :], y_acc[:, dt_i, :], t01[:])
                            v.engine_nop().then_inc(ctx.sem(S_VS), 1)
            da_chunk_hist[pair] = ctx.val(S_AC)
            if c == NCH - 1:
                ctx.bump(q + "gp", 1)
                @ctx.on("vector")
                def _(v, w=ctx.val(S_VS)):
                    v.wait_ge(ctx.sem(S_VS), w)
                    v.engine_nop().then_inc(ctx.sem(q + "gp"), 1)
            for dt_i in range(4):
                n_g1 = ctx.bump(S_VS, 3)
                @ctx.on("vector")
                def _(v, gi=gi, c=c, b=b, dt_i=dt_i, dcol=gi * 4 + dt_i, pair=pair,
                      base=n_g1 - 3, n_ug=n_ug):
                    v.wait_ge(ctx.sem(q + "ug" + str(b)), n_ug)
                    if pair >= 1:
                        v.wait_ge(ctx.sem(q + "ys"), pair * 64)
                    v.scalar_tensor_tensor(y_acc[:, dt_i, :], u2[:, b, dt_i, :],
                                           g["dcol"][:, dcol:dcol + 1], y_acc[:, dt_i, :],
                                           op0=ALU.mult, op1=ALU.add)
                    v.engine_nop().then_inc(ctx.sem(S_VS), 1)
                    v.wait_ge(ctx.sem(S_VS), base + 1)
                    v.tensor_mul(y_acc[:, dt_i, :], y_acc[:, dt_i, :], sg2[:, b, dt_i, :])
                    v.engine_nop().then_inc(ctx.sem(S_VS), 1)
                    v.wait_ge(ctx.sem(S_VS), base + 2)
                    v.tensor_copy(ygb[:, dt_i, :], y_acc[:, dt_i, :])
                    v.engine_nop().then_inc(ctx.sem(S_VS), 1)
                ctx.bump(q + "ys", 16)
                @ctx.on("sync")
                def _(s, gi=gi, c=c, dt_i=dt_i, n_g1=n_g1):
                    s.wait_ge(ctx.sem(S_VS), n_g1)
                    r0 = (gi * 4 + dt_i) * 128
                    s.dma_start(out=g["yg_h"][r0:r0 + 128, c * TC:(c + 1) * TC],
                                in_=ygb[:, dt_i, :]).then_inc(ctx.sem(q + "ys"), 16)
            ctx.bump(q + "gt", 1)
            @ctx.on("vector")
            def _(v, w=ctx.val(S_VS)):
                v.wait_ge(ctx.sem(S_VS), w)
                v.engine_nop().then_inc(ctx.sem(q + "gt"), 1)
        pe_group_rel[gi] = ctx.val(S_PE)

    n_ys_all = ctx.val(q + "ys")
    n_si_all = ctx.val(q + "si3")
    n_stg_all = ctx.val(q + "stg3")
    n_ag_all = ctx.val(q + "ag3")
    @ctx.on("sync")
    def _(s):
        s.wait_ge(ctx.sem(q + "ys"), n_ys_all)
    @ctx.on("gpsimd")
    def _(gp):
        gp.wait_ge(ctx.sem(q + "si3"), n_si_all)
        gp.wait_ge(ctx.sem(q + "stg3"), n_stg_all)
        gp.wait_ge(ctx.sem(q + "ag3"), n_ag_all)
    ctx.close_tensors(cms)


def emit_m3(ctx, g):
    q = "m3"
    cms = []
    def sb(name, shape, dt):
        h, cm = ctx.sb(q + name, shape, dt)
        cms.append(cm)
        return h
    woutb = sb("woutb", [128, 16, D], BF16)
    ygc = sb("ygc", [128, 2, 16, TC], BF16)
    x1r = sb("x1r", [128, 2, D], F32)
    x2r = sb("x2r", [128, 2, D], F32)
    S_VS = q + "vs"

    @ctx.on("sync")
    def _(s):
        for kc in range(16):
            s.dma_start(out=woutb[:, kc, :], in_=g["woutT_b"][kc * 128:(kc + 1) * 128, :])\
                .then_inc(ctx.sem(q + "wl"), 16)
    n_wl = ctx.bump(q + "wl", 16 * 16)

    yoP, cmp = ctx.ps(q + "yoP", [128, 4, 512], F32)
    cms.append(cmp)

    ev_rel = {}
    for c in range(NCH):
        b = c % 2
        n_yl = ctx.bump(q + "yl" + str(b), 16 * 16)
        @ctx.on("sync")
        def _(s, c=c, b=b):
            if c >= 2:
                s.wait_ge(ctx.sem(q + "pe"), pe_rel[c - 2])
            for kc in range(16):
                s.dma_start(out=ygc[:, b, kc, :],
                            in_=g["yg_h"][kc * 128:(kc + 1) * 128, c * TC:(c + 1) * TC])\
                    .then_inc(ctx.sem(q + "yl" + str(b)), 16)
        if c == 0:
            pe_rel = {}
        for ts in range(4):
            tb = ts % 2
            jt = c * 4 + ts
            row0 = c * TC + ts * 128
            n_xl = ctx.bump(q + "xl" + str(tb), 16)
            @ctx.on("sync")
            def _(s, row0=row0, tb=tb, jt=jt):
                if jt >= 2:
                    s.wait_ge(ctx.sem(S_VS), ev_rel[jt - 2])
                s.dma_start(out=x1r[:, tb, :], in_=g["x1b"][3 + row0:3 + row0 + 128, :])\
                    .then_inc(ctx.sem(q + "xl" + str(tb)), 16)
            n_pe = ctx.bump(q + "pe", 2)
            @ctx.on("tensor")
            def _(t, c=c, b=b, ts=ts, jt=jt, n_yl=n_yl, base=n_pe - 2):
                if c == 0 and ts == 0:
                    t.wait_ge(ctx.sem(q + "wl"), n_wl)
                if ts == 0:
                    t.wait_ge(ctx.sem(q + "yl" + str(b)), n_yl)
                for dh in range(2):
                    pb = (jt * 2 + dh) % 4
                    if jt * 2 + dh >= 4:
                        t.wait_ge(ctx.sem(S_VS), ev_rel2[jt * 2 + dh - 4])
                    last = None
                    for kc in range(16):
                        last = t.matmul(yoP[:, pb, :],
                                        ygc[:, b, kc, ts * 128:(ts + 1) * 128],
                                        woutb[:, kc, dh * 512:(dh + 1) * 512],
                                        start=(kc == 0), stop=(kc == 15))
                    last.then_inc(ctx.sem(q + "pe"), 1)
            if c == 0 and ts == 0:
                ev_rel2 = {}
            n_ev = ctx.bump(S_VS, 2)
            @ctx.on("vector")
            def _(v, jt=jt, tb=tb, n_pe=n_pe, n_xl=n_xl, base=n_ev - 2):
                v.wait_ge(ctx.sem(q + "xl" + str(tb)), n_xl)
                for dh in range(2):
                    pb = (jt * 2 + dh) % 4
                    v.wait_ge(ctx.sem(q + "pe"), n_pe - 1 + dh)
                    if jt >= 2:
                        v.wait_ge(ctx.sem(q + "xs" + str(tb)), (jt // 2) * 16)
                    v.scalar_tensor_tensor(x2r[:, tb, dh * 512:(dh + 1) * 512],
                                           yoP[:, pb, :], 0.0, x1r[:, tb, dh * 512:(dh + 1) * 512],
                                           op0=ALU.bypass, op1=ALU.add)
                    v.engine_nop().then_inc(ctx.sem(S_VS), 1)
            ev_rel[jt] = n_ev
            ev_rel2[jt * 2] = n_ev - 1
            ev_rel2[jt * 2 + 1] = n_ev
            ctx.bump(q + "xs" + str(tb), 16)
            @ctx.on("sync")
            def _(s, row0=row0, tb=tb, n_ev=n_ev):
                s.wait_ge(ctx.sem(S_VS), n_ev)
                s.dma_start(out=g["x2b"][row0:row0 + 128, :], in_=x2r[:, tb, :])\
                    .then_inc(ctx.sem(q + "xs" + str(tb)), 16)
        pe_rel[c] = ctx.val(q + "pe")
    @ctx.on("sync")
    def _(s, a=ctx.val(q + "xs0"), bb=ctx.val(q + "xs1")):
        s.wait_ge(ctx.sem(q + "xs0"), a)
        s.wait_ge(ctx.sem(q + "xs1"), bb)
    ctx.close_tensors(cms)


def kernel(**inputs):
    in_maps = stage_inputs(inputs)
    nc = build_program()
    res = run_bass_kernel_spmd(nc, in_maps, core_ids=list(range(NC)))
    shards = [np.asarray(r["out"], np.float32) for r in res.results]
    return np.concatenate(shards, 0).reshape(B, S, D)


# version 02
# speedup vs baseline: 1.0125x; 1.0125x over previous
import re
import numpy as np
import ml_dtypes
import concourse.bass as bass
import concourse.mybir as mybir
from concourse.bass_utils import run_bass_kernel_spmd
from concourse.bass_core_rust import IndirectOffsetOnAxis

F32 = mybir.dt.float32
BF16 = mybir.dt.bfloat16
U32 = mybir.dt.uint32
AF = mybir.ActivationFunctionType
ALU = mybir.AluOpType
AX = mybir.AxisListType

B, S, D, DI, DS, E, H = 4, 8192, 1024, 2048, 16, 16, 4096
NTOK = B * S
CAP = NTOK // E
NC = 8
TPC = NTOK // NC
NTILE = TPC // 128
TC = 512
NCH = TPC // TC
NG = 4
DT_PER_G = 4
TRASH = NTOK
TL = 36992
BIGF = 65535.0
EPS = 1e-5

PCF_COREBASE = 0
PCF_PARITY = 1
PCF_SLOTBASE = 2
PCF_G0 = 4
PCU_PARTNER_ROW = 0
PCU_TABROW = 1


def _u32(x):
    return np.asarray(x, dtype=np.uint32)


class Ctx:

    def __init__(self, nc):
        self.nc = nc
        self.prog = []
        self.sems = {}
        self.cnt = {}
        self._cm = []
        self.bar_k = 0

    _ALIAS = {
        "m1ld0": "Fld0", "m1ld1": "Fld1", "m1ac": "Fac", "m1vs": "Fvs",
        "m1st0": "Fst0", "m1st1": "Fst1", "m1wl": "Fcl2", "m1trh": "Ftrx",
        "m2ac": "Fac", "m2vs": "Fvs", "m2pe": "Fpe",
        "m3vs": "Fvs", "m3pe": "Fpe", "m3wl": "Fcl2",
        "m3yl0": "Fld0", "m3yl1": "Fld1", "m3xl0": "Fxl0", "m3xl1": "Fxl1",
    }

    @staticmethod
    def canon(name):
        n2 = name
        if n2.startswith("f0") or n2.startswith("f1"):
            n2 = "F" + n2[2:]
            if n2.startswith("Fe0") or n2.startswith("Fe1"):
                n2 = "FE" + n2[3:]
        m = re.fullmatch(r"m2(stg|ag|si)\d", n2)
        if m:
            n2 = "m2" + m.group(1)
        return Ctx._ALIAS.get(n2, n2)

    def sem(self, name):
        name = self.canon(name)
        if name not in self.sems:
            cm = self.nc.semaphore(name)
            h = cm.__enter__()
            self._cm.append(cm)
            self.sems[name] = h
            self.cnt[name] = 0
        return self.sems[name]

    def bump(self, name, k):
        name = self.canon(name)
        self.sem(name)
        self.cnt[name] += k
        return self.cnt[name]

    def val(self, name):
        name = self.canon(name)
        self.sem(name)
        return self.cnt[name]

    def on(self, eng):
        def deco(fn):
            self.prog.append((eng, fn))
            return None
        return deco

    def sb(self, name, shape, dtype):
        cm = self.nc.sbuf_tensor(name, shape, dtype)
        h = cm.__enter__()
        return h, cm

    def ps(self, name, shape, dtype):
        cm = self.nc.psum_tensor(name, shape, dtype)
        h = cm.__enter__()
        return h, cm

    def close_tensors(self, cms):
        for cm in reversed(cms):
            cm.__exit__(None, None, None)

    def barrier(self, tag, engines=("sync", "vector", "scalar", "tensor", "gpsimd")):
        bar = self.sem("bar")
        n = len(engines)
        base = self.cnt["bar"]
        tgt = base + n
        self.cnt["bar"] = tgt
        for e in engines:
            k = self.bar_k
            if e == "scalar":
                self.bar_k += 1
            @self.on(e)
            def _(eng, bar=bar, tgt=tgt, e=e, k=k):
                if e in ("vector", "gpsimd"):
                    eng.engine_nop().then_inc(bar, 1)
                elif e == "scalar":
                    eng.wait_ge(self.sem("dumz"), 1)
                    eng.activation(out=self.dum[0:1, 2 + k:3 + k], in_=self.dum[0:1, 1:2],
                                   func=mybir.ActivationFunctionType.Copy).then_inc(bar, 1)
                else:
                    eng.nop().then_inc(bar, 1)
                eng.wait_ge(bar, tgt)

    def finalize(self):
        for cm in reversed(self._cm):
            cm.__exit__(None, None, None)


def stage_inputs(inp):
    x = np.asarray(inp["L0_attn_x"], np.float32).reshape(NTOK, D)
    W_in = np.asarray(inp["L0_attn_W_in"], np.float32)
    conv_w = np.asarray(inp["L0_attn_conv_w"], np.float32).reshape(DI, 4)
    conv_b = np.asarray(inp["L0_attn_conv_b"], np.float32)
    W_x = np.asarray(inp["L0_attn_W_x"], np.float32)
    W_dt = np.asarray(inp["L0_attn_W_dt"], np.float32)
    b_dt = np.asarray(inp["L0_attn_b_dt"], np.float32)
    A_log = np.asarray(inp["L0_attn_A_log"], np.float32)
    Dvec = np.asarray(inp["L0_attn_D"], np.float32)
    W_out = np.asarray(inp["L0_attn_W_out"], np.float32)
    Wg = np.asarray(inp["L0_ffn_Wg"], np.float32)
    bg = np.asarray(inp["L0_ffn_bg"], np.float32)
    W1 = np.asarray(inp["L0_ffn_W1"], np.float32)
    b1 = np.asarray(inp["L0_ffn_b1"], np.float32)
    W2 = np.asarray(inp["L0_ffn_W2"], np.float32)
    b2 = np.asarray(inp["L0_ffn_b2"], np.float32)

    winT = np.ascontiguousarray(W_in.T)
    wdtT = np.ascontiguousarray(W_dt.T)
    woutT = np.ascontiguousarray(W_out.T)
    wxT = np.ascontiguousarray(W_x.T)

    in_maps = []
    for c in range(NC):
        t0 = c * TPC
        idx = np.clip(np.arange(t0 - 3, t0 + TPC), 0, NTOK - 1)
        pc_f = np.zeros((1, 32), np.float32)
        pc_f[0, PCF_COREBASE] = t0
        pc_f[0, PCF_PARITY] = c % 2
        pc_f[0, PCF_SLOTBASE] = c * TPC
        pc_f[0, 5] = (c - 1) * 128 if c % 2 == 1 else NC * 128
        pc_u = np.zeros((1, 32), np.uint32)
        pc_u[0, PCU_PARTNER_ROW] = (c - 1) * 128 if c % 2 == 1 else NC * 128
        pc_u[0, PCU_TABROW] = (c * TPC) // 32
        for gg in range(NG):
            pc_u[0, 8 + gg] = 0 if c % 2 == 1 else gg + 1
            pc_u[0, 12 + gg] = gg + (c % 2)
            pc_u[0, 16 + gg] = (gg + (c % 2)) * 16
            pc_u[0, 20 + gg] = (c % 2) * (gg + 1) * 16
        in_maps.append({
            "x_buf": np.ascontiguousarray(x[idx]),
            "winT": winT, "wdtT": wdtT, "woutT": woutT, "wxT": wxT,
            "convw": np.ascontiguousarray(conv_w.reshape(16, 128, 4).transpose(1, 2, 0)),
            "convb": np.ascontiguousarray(conv_b.reshape(16, 128).T),
            "bdt": np.ascontiguousarray(b_dt.reshape(16, 128).T),
            "alog": np.ascontiguousarray(A_log.reshape(16, 128, DS).transpose(1, 0, 2)),
            "dvec": np.ascontiguousarray(Dvec.reshape(16, 128).T),
            "wg": Wg, "bgv": bg.reshape(1, E),
            "w1": np.ascontiguousarray(W1[2 * c:2 * c + 2]),
            "w2": np.ascontiguousarray(W2[2 * c:2 * c + 2]),
            "b1v": np.ascontiguousarray(b1[2 * c:2 * c + 2].reshape(2, 32, 128).transpose(0, 2, 1)),
            "b2v": np.ascontiguousarray(b2[2 * c:2 * c + 2]),
            "pc_f": pc_f, "pc_u": pc_u,
        })
    return in_maps


def build_program(phases=("w", "f1", "m1", "m2", "m3", "f2"), dbg=()):
    nc = bass.Bass("TRN2", target_bir_lowering=False)
    ctx = Ctx(nc)
    g = {}
    _dum_cm = nc.sbuf_tensor("bar_dum", [1, 64], F32)
    ctx.dum = _dum_cm.__enter__()
    @ctx.on("scalar")
    def _(sc):
        sc.memzero(ctx.dum[:]).then_inc(ctx.sem("dumz"), 1)
    ctx.bump("dumz", 1)

    def ein(name, shape, dt=F32):
        g[name] = nc.dram_tensor(name, shape, dt, kind="ExternalInput").ap()
    ein("x_buf", [TPC + 3, D])
    ein("winT", [D, 2 * DI]); ein("wdtT", [DI, DI]); ein("woutT", [DI, D])
    ein("wxT", [DI, 2 * DS])
    ein("convw", [128, 4, 16]); ein("convb", [128, 16]); ein("bdt", [128, 16])
    ein("alog", [128, 16, DS]); ein("dvec", [128, 16])
    ein("wg", [D, E]); ein("bgv", [1, E])
    ein("w1", [2, D, H]); ein("w2", [2, H, D])
    ein("b1v", [2, 128, 32]); ein("b2v", [2, D])
    ein("pc_f", [1, 32]); ein("pc_u", [1, 32], U32)

    g["out"] = nc.dram_tensor("out", [TPC, D], F32, kind="ExternalOutput").ap()

    def dscr(name, shape, dt):
        g[name] = nc.dram_tensor(name, shape, dt, kind="Internal").ap()
    dscr("winT_b", [32, 8, 128, 128], BF16)
    dscr("wdtT_b", [16, 16, 128, 128], BF16)
    dscr("woutT_b", [DI, D], BF16)
    dscr("wxT_b", [16, 128, 2 * DS], BF16)
    dscr("wg_b", [8, 128, E], BF16)
    dscr("w1_b", [2, 32, 8, 128, 128], BF16)
    dscr("w2_b", [2, H, D], BF16)
    dscr("xln_ag_in", [TPC, D], BF16); dscr("xln_full", [NTOK, D], BF16)
    dscr("tab_in", [TL, 1], U32); dscr("tab_out", [(NC * TL) // 16, 16], U32)
    dscr("cnt_in", [1, 32], F32); dscr("cnt_out", [NC, 32], F32)
    dscr("y_in", [TPC, D], F32); dscr("y_full", [NTOK, D], F32)
    dscr("xg", [CAP, D], BF16)
    dscr("x1b", [TPC + 3, D], F32)
    dscr("xln1", [TPC + 3, D], BF16)
    dscr("u_h", [DI, TPC], F32); dscr("sg_h", [DI, TPC], F32)
    dscr("ubf_h", [DI, TPC], BF16); dscr("yg_h", [DI, TPC], BF16)
    dscr("x2b", [TPC, D], F32)
    dscr("hmx", [1, 6], F32)
    dscr("route_dbg", [128, 4 * NTILE], F32)
    dscr("hmst", [2, 128], F32)
    for gg in range(NG):
        dscr(f"st_in_{gg}", [128, DT_PER_G * DS], F32)
        dscr(f"st_out_{gg}", [(NC + 1) * 128, DT_PER_G * DS], F32)

    for name, shape, dt in dbg:
        g["dbg_" + name] = nc.dram_tensor("dbg_" + name, shape, dt, kind="ExternalOutput").ap()

    sb_keep = []
    def SB(name, shape, dt):
        h, cm = ctx.sb(name, shape, dt)
        sb_keep.append(cm)
        g[name] = h
        return h

    SB("pcf", [1, 32], F32)
    SB("pcu", [1, 32], U32)
    SB("pcf_col", [128, 32], F32)
    SB("iotaP", [128, 1], F32)
    SB("iotaE", [128, E], F32)
    SB("ltU", [128, 128], BF16)
    SB("ones128", [128, 1], BF16)
    SB("onesK1", [1, 128], BF16)
    SB("idN", [128, 128], BF16)
    SB("seln", [E, 128 * E], BF16)
    SB("trashc", [128, NTILE], F32)
    SB("Acols", [128, 16 * DS], F32)
    SB("dcol", [128, 16], F32)
    SB("bdtcol", [128, 16], F32)
    SB("cbcol", [128, 16], F32)
    SB("cwcol", [128, 16 * 4], F32)
    SB("zero128", [128, 64], F32)
    SB("cstF", [128, 8], F32)

    g["r_ntok"] = nc.gpsimd.alloc_register("r_ntok")

    def emit_setup():
        @ctx.on("gpsimd")
        def _(gp):
            gp.reg_mov(g["r_ntok"], NTOK - 1)
        consts = [1e-5, 1.0, -0.5, -1.0, 0.5, 2.0]
        @ctx.on("vector")
        def _(v):
            for j, cv in enumerate(consts):
                v.memset(g["cstF"][:, j:j + 1], float(cv))
            v.engine_nop().then_inc(ctx.sem("setup_v"), 1)
        ctx.bump("setup_v", 1)
        for j, cv in enumerate(consts):
            nc.const_aps.register(F32, float(cv), g["cstF"][:, j:j + 1])

        ld = "s_ld"
        @ctx.on("sync")
        def _(s):
            s.dma_start(out=g["pcf"][:], in_=g["pc_f"]).then_inc(ctx.sem(ld), 16)
            s.dma_start(out=g["pcu"][:], in_=g["pc_u"]).then_inc(ctx.sem(ld), 16)
            s.dma_start(out=g["pcf_col"][:], in_=g["pc_f"].broadcast_to([128, 32])).then_inc(ctx.sem(ld), 16)
            s.dma_start(out=g["bdtcol"][:], in_=g["bdt"]).then_inc(ctx.sem(ld), 16)
            s.dma_start(out=g["cbcol"][:], in_=g["convb"]).then_inc(ctx.sem(ld), 16)
            s.dma_start(out=g["dcol"][:], in_=g["dvec"]).then_inc(ctx.sem(ld), 16)
            s.dma_start(out=g["cwcol"][:].rearrange("p (k a) -> p k a", k=4), in_=g["convw"]).then_inc(ctx.sem(ld), 16)
            s.dma_start(out=g["Acols"][:].rearrange("p (a n) -> p a n", a=16), in_=g["alog"]).then_inc(ctx.sem(ld), 16)
        ctx.bump(ld, 16 * 8)
        ldv = ctx.val(ld)

        @ctx.on("gpsimd")
        def _(gp):
            gp.iota(g["iotaP"][:], pattern=[[0, 1]], base=0, channel_multiplier=1,
                    allow_small_or_imprecise_dtypes=True)
            gp.iota(g["iotaE"][:], pattern=[[1, E]], base=0, channel_multiplier=0,
                    allow_small_or_imprecise_dtypes=True)
            gp.iota(g["trashc"][:], pattern=[[128, NTILE]], base=TRASH, channel_multiplier=1,
                    allow_small_or_imprecise_dtypes=True)
            gp.memset(g["zero128"][:], 0.0)
            gp.engine_nop().then_inc(ctx.sem("setup_g"), 1)
        ctx.bump("setup_g", 1)

        tmpi, cmi = ctx.sb("tmp_ii", [128, 128], F32)
        tmpj, cmj = ctx.sb("tmp_jj", [128, 128], F32)
        @ctx.on("gpsimd")
        def _(gp):
            gp.iota(tmpi[:], pattern=[[0, 128]], base=0, channel_multiplier=1,
                    allow_small_or_imprecise_dtypes=True)
            gp.iota(tmpj[:], pattern=[[1, 128]], base=0, channel_multiplier=0,
                    allow_small_or_imprecise_dtypes=True)
            gp.engine_nop().then_inc(ctx.sem("setup_g"), 1)
        ctx.bump("setup_g", 1)
        sg = ctx.val("setup_g")

        @ctx.on("vector")
        def _(v):
            v.wait_ge(ctx.sem("setup_g"), sg)
            v.wait_ge(ctx.sem(ld), ldv)
            v.tensor_tensor(out=g["idN"][:], in0=tmpi[:], in1=tmpj[:], op=ALU.is_equal)
            v.tensor_tensor(out=g["ltU"][:], in0=tmpi[:], in1=tmpj[:], op=ALU.is_lt)
            v.memset(g["ones128"][:], 1.0)
            v.memset(g["onesK1"][:], 1.0)
            v.engine_nop().then_inc(ctx.sem("setup_v"), 1)
        ctx.bump("setup_v", 1)
        si, cmsi = ctx.sb("tmp_si", [16, 128 * E], F32)
        sj, cmsj = ctx.sb("tmp_sj", [16, 128 * E], F32)
        @ctx.on("gpsimd")
        def _(gp):
            gp.iota(si[:], pattern=[[0, 128 * E]], base=0, channel_multiplier=1,
                    allow_small_or_imprecise_dtypes=True)
            gp.iota(sj[:], pattern=[[1, E], [0, 128]], base=0, channel_multiplier=0,
                    allow_small_or_imprecise_dtypes=True)
            gp.engine_nop().then_inc(ctx.sem("setup_g"), 1)
        ctx.bump("setup_g", 1)
        sg2 = ctx.val("setup_g")
        @ctx.on("vector")
        def _(v):
            v.wait_ge(ctx.sem("setup_g"), sg2)
            v.tensor_tensor(out=g["seln"][:], in0=si[:], in1=sj[:], op=ALU.is_equal)
            v.engine_nop().then_inc(ctx.sem("setup_v"), 1)
        ctx.bump("setup_v", 1)
        ctx.close_tensors([cmsi, cmsj])

        @ctx.on("scalar")
        def _(sc):
            sc.wait_ge(ctx.sem(ld), ldv)
            sc.activation(out=g["Acols"][:], in_=g["Acols"][:], func=AF.Exp).then_inc(ctx.sem("setup_a"), 1)
        ctx.bump("setup_a", 1)
        sa = ctx.val("setup_a")
        @ctx.on("vector")
        def _(v):
            v.wait_ge(ctx.sem("setup_a"), sa)
            v.tensor_scalar_mul(g["Acols"][:], g["Acols"][:], -1.0)
            v.engine_nop().then_inc(ctx.sem("setup_v"), 1)
        ctx.bump("setup_v", 1)

        n_sg_for_zero = ctx.val("setup_g")
        @ctx.on("sync")
        def _(s):
            s.wait_ge(ctx.sem("setup_g"), n_sg_for_zero)
            for gg in range(NG):
                s.dma_start(out=g[f"st_out_{gg}"][NC * 128:(NC + 1) * 128, :], in_=g["zero128"][:])\
                    .then_inc(ctx.sem("setup_z"), 16)
        ctx.bump("setup_z", 16 * NG)
        ctx.close_tensors([cmi, cmj])

    def emit_wconv():
        wtile, cmw = ctx.sb("wconv_tile", [128, 2, 4096], BF16)
        jobs = []
        jobs.append((g["wg"].rearrange("(a p) n -> p a n", p=128),
                     g["wg_b"].rearrange("a p n -> p a n")))
        jobs.append((g["wxT"].rearrange("(a p) n -> p a n", p=128),
                     g["wxT_b"].rearrange("a p n -> p a n")))
        for o in range(32):
            jobs.append((g["winT"][:, o * 128:(o + 1) * 128].rearrange("(a p) h -> p a h", p=128),
                         g["winT_b"][o].rearrange("a p h -> p a h")))
        for o in range(16):
            jobs.append((g["wdtT"][:, o * 128:(o + 1) * 128].rearrange("(a p) h -> p a h", p=128),
                         g["wdtT_b"][o].rearrange("a p h -> p a h")))
        for r0 in range(0, DI, 128):
            jobs.append((g["woutT"][r0:r0 + 128, :], g["woutT_b"][r0:r0 + 128, :]))
        for e in range(2):
            for hc in range(32):
                jobs.append((g["w1"][e][:, hc * 128:(hc + 1) * 128].rearrange("(a p) h -> p a h", p=128),
                             g["w1_b"][e, hc].rearrange("a p h -> p a h")))
            for r0 in range(0, H, 128):
                jobs.append((g["w2"][e][r0:r0 + 128, :], g["w2_b"][e][r0:r0 + 128, :]))
        buf = 0
        for (src, dst) in jobs:
            fs = int(np.prod(src.shape[1:]))
            sview = wtile[:, buf % 2, 0:fs]
            if len(src.shape) == 3:
                sview = sview.rearrange("p (a n) -> p a n", a=src.shape[1])
            bi = buf % 2
            j = buf
            buf += 1
            st_need = (j // 2) * 16
            n_cast = ctx.bump(f"wc_cast{bi}", 16)
            @ctx.on("gpsimd")
            def _(gp, src=src, sview=sview, st_need=st_need, bi=bi):
                gp.wait_ge(ctx.sem(f"wc_st{bi}"), st_need)
                gp.dma_start(out=sview, in_=src).then_inc(ctx.sem(f"wc_cast{bi}"), 16)
            ctx.bump(f"wc_st{bi}", 16)
            @ctx.on("sync")
            def _(s, dst=dst, sview=sview, n_cast=n_cast, bi=bi):
                s.wait_ge(ctx.sem(f"wc_cast{bi}"), n_cast)
                s.dma_start(out=dst, in_=sview).then_inc(ctx.sem(f"wc_st{bi}"), 16)
        @ctx.on("sync")
        def _(s, a=ctx.val("wc_st0"), b=ctx.val("wc_st1")):
            s.wait_ge(ctx.sem("wc_st0"), a)
            s.wait_ge(ctx.sem("wc_st1"), b)
        @ctx.on("gpsimd")
        def _(gp, a=ctx.val("wc_st0"), b=ctx.val("wc_st1")):
            gp.wait_ge(ctx.sem("wc_st0"), a)
            gp.wait_ge(ctx.sem("wc_st1"), b)
        ctx.close_tensors([cmw])

    emit_setup()
    ctx.barrier("after_setup")
    if "w" in phases:
        emit_wconv()
    ctx.barrier("after_w")

    if "f1" in phases:
        emit_ffn(ctx, g, call=0)
    ctx.barrier("after_f1")
    if "m1" in phases:
        emit_m1(ctx, g)
    ctx.barrier("after_m1")
    if "m2" in phases:
        emit_m2(ctx, g)
    ctx.barrier("after_m2")
    if "m3" in phases:
        emit_m3(ctx, g)
    ctx.barrier("after_m3")
    if "f2" in phases:
        emit_ffn(ctx, g, call=1)
    ctx.barrier("after_all")

    if dbg:
        for name, shape, dt in dbg:
            ctx.bump("dbgs", 16)
            @ctx.on("sync")
            def _(s, name=name):
                s.dma_start(out=g["dbg_" + name], in_=g[name]).then_inc(ctx.sem("dbgs"), 16)
        n_dbg = ctx.val("dbgs")
        @ctx.on("sync")
        def _(s):
            s.wait_ge(ctx.sem("dbgs"), n_dbg)

    with nc.Block() as block:
        @block.sync
        def _(eng):
            for e, fn in ctx.prog:
                if e == "sync":
                    fn(eng)
        @block.vector
        def _(eng):
            for e, fn in ctx.prog:
                if e == "vector":
                    fn(eng)
        @block.scalar
        def _(eng):
            for e, fn in ctx.prog:
                if e == "scalar":
                    fn(eng)
        @block.tensor
        def _(eng):
            for e, fn in ctx.prog:
                if e == "tensor":
                    fn(eng)
        @block.gpsimd
        def _(eng):
            for e, fn in ctx.prog:
                if e == "gpsimd":
                    fn(eng)

    ctx.finalize()
    return nc


def emit_ffn(ctx, g, call):
    nc = ctx.nc
    q = f"f{call}"
    src = g["x_buf"] if call == 0 else g["x2b"]
    soff = 3 if call == 0 else 0
    dst = g["x1b"] if call == 0 else g["out"]
    doff = 3 if call == 0 else 0

    cms = []
    cms2 = []
    def sb(name, shape, dt, grp=None):
        h, cm = ctx.sb(q + name, shape, dt)
        (cms if grp is None else grp).append(cm)
        return h

    idxf_a = sb("idxf_a", [128, NTILE], F32)
    pos2_a = sb("pos2_a", [128, NTILE], F32)
    gate_a = sb("gate_a", [128, NTILE], F32)
    comb_a = sb("comb_a", [128, NTILE], F32)
    slkf_a = sb("slkf_a", [128, NTILE], F32)
    slot_u = sb("slot_u", [128, NTILE], U32)
    tabs_u = sb("tabs_u", [128, NTILE], U32)
    payl_u = sb("payl_u", [128, NTILE], U32)
    payl_f = sb("payl_f", [128, NTILE], F32)
    cnt_sb = sb("cnt_sb", [1, 32], F32)
    iota8 = sb("iota8", [8, 1], F32)
    cld = sb("cld", [8, 32], F32)
    mask8 = sb("mask8", [8, 2], F32)
    baser = sb("baser", [1, E], F32)
    bases = sb("bases", [128, E], F32)
    mgi = sb("mgi", [128, 2, 8], U32)
    mgf = sb("mgf", [128, 2, 8, 16], F32)
    tos_f = sb("tos_f", [128, 2, 16], F32)
    tos_u = sb("tos_u", [128, 2, 16], U32)
    hcol = sb("hcol", [128, 2], F32)
    hcu = sb("hcu", [128, 1], U32)
    gbuf = sb("gbuf", [128, 2, D], BF16)
    yrow = sb("yrow", [128, 2, D], F32)
    x1row = sb("x1row", [128, 2, D], F32)
    xr = sb("xr", [128, 2, D], F32, cms2)
    scr = sb("scr", [128, D], F32, cms2)
    col = sb("col", [128, 2, 8], F32, cms2)
    xlt = sb("xlt", [128, 2, D], BF16, cms2)
    xlnT = sb("xlnT", [128, 8, TPC], BF16, cms2)
    wgb = sb("wgb", [128, 8, E], BF16, cms2)
    bgbc = sb("bgbc", [128, E], F32, cms2)
    lg = sb("lg", [128, 2, E], F32, cms2)
    tcol = sb("tcol", [128, 2, 16], F32, cms2)
    idx8 = sb("idx8", [128, 8], U32, cms2)
    ex = sb("ex", [128, 2, E], F32, cms2)
    oh = sb("oh", [128, 2, E], F32, cms2)
    base_r = sb("base_r", [1, 2, E], F32, cms2)
    posS = sb("posS", [128, E], F32, cms2)
    ohs = sb("ohs", [128, E], F32, cms2)
    ltUf = sb("ltUf", [128, 128], F32, cms2)
    base_run = sb("base_run", [1, E], F32, cms2)
    onesK1f = sb("onesK1f", [1, 128], F32, cms2)
    ones128f = sb("ones128f", [128, 1], F32, cms2)
    ffff = sb("ffff", [128, TL // 128], U32, cms2)
    tbf = sb("tbf", [128, NTILE], F32, cms2)

    S_LD = q + "ld"; S_VS = q + "vs"; S_AC = q + "ac"; S_PE = q + "pe"
    S_LG = q + "lg"; S_ST = q + "st"; S_AGX = q + "agx"; S_TRX = q + "trx"
    S_SC = q + "sc"; S_AGT = q + "agt"; S_AGC = q + "agc"; S_MG = q + "mg"
    S_AGY = q + "agy"; S_GA = q + "ga"; S_EX = q + "ex"; S_YS = q + "ys"
    S_OUT = q + "out"
    base_st = [ctx.val(S_ST + "0"), ctx.val(S_ST + "1")]
    base_gst = [ctx.val(q + "gst0"), ctx.val(q + "gst1")]
    base_yst = [ctx.val(q + "yst0"), ctx.val(q + "yst1")]
    base_out = [ctx.val(S_OUT + "0"), ctx.val(S_OUT + "1")]
    base_ev = ctx.val(q + "ev")
    base_peB = ctx.val(S_PE)

    @ctx.on("sync")
    def _(s):
        s.dma_start(out=wgb[:], in_=g["wg_b"].rearrange("a p n -> p a n")).then_inc(ctx.sem(q + "cl2"), 16)
        s.dma_start(out=bgbc[:], in_=g["bgv"].broadcast_to([128, E])).then_inc(ctx.sem(q + "cl2"), 16)
    ctx.bump(q + "cl2", 32)

    @ctx.on("gpsimd")
    def _(gp):
        gp.iota(ffff[:], pattern=[[0, TL // 128]], base=-1, channel_multiplier=0,
                allow_small_or_imprecise_dtypes=True)
        gp.iota(iota8[:], pattern=[[0, 1]], base=0, channel_multiplier=1,
                allow_small_or_imprecise_dtypes=True)
        gp.iota(payl_f[:], pattern=[[128, NTILE]], base=0, channel_multiplier=1,
                allow_small_or_imprecise_dtypes=True)
        gp.engine_nop().then_inc(ctx.sem(q + "setg"), 1)
    ctx.bump(q + "setg", 1)
    v_setg = ctx.val(q + "setg")

    @ctx.on("vector")
    def _(v):
        v.wait_ge(ctx.sem(q + "setg"), v_setg)
        v.tensor_copy(ltUf[:], g["ltU"][:])
        v.memset(onesK1f[:], 1.0)
        v.memset(ones128f[:], 1.0)
        v.tensor_scalar_add(payl_f[:], payl_f[:], g["pcf_col"][:, PCF_COREBASE:PCF_COREBASE + 1])
        v.engine_nop().then_inc(ctx.sem(S_VS), 1)
    n_vs0 = ctx.bump(S_VS, 1)
    @ctx.on("vector")
    def _(v):
        v.wait_ge(ctx.sem(S_VS), n_vs0)
        v.tensor_copy(payl_u[:], payl_f[:])
        v.engine_nop().then_inc(ctx.sem(S_VS), 1)
    ctx.bump(S_VS, 1)

    @ctx.on("sync")
    def _(s):
        s.wait_ge(ctx.sem(q + "setg"), v_setg)
        s.dma_start(out=g["tab_in"].rearrange("(p f) 1 -> p f", p=128), in_=ffff[:])\
            .then_inc(ctx.sem(q + "tz"), 16)
    ctx.bump(q + "tz", 16)

    for i in range(NTILE):
        b = i % 2
        n_ld = ctx.bump(S_LD + str(b), 16)
        @ctx.on("sync")
        def _(s, i=i, b=b):
            if i >= 2:
                s.wait_ge(ctx.sem(S_AC), ac_rel[i - 2])
            s.dma_start(out=xr[:, b, :], in_=src[soff + i * 128: soff + (i + 1) * 128, :])\
                .then_inc(ctx.sem(S_LD + str(b)), 16)
        if i == 0:
            ac_rel = {}
        n_ac = ctx.bump(S_AC, 1)
        @ctx.on("scalar")
        def _(sc, i=i, b=b, n_ld=n_ld):
            sc.wait_ge(ctx.sem(S_LD + str(b)), n_ld)
            sc.activation(out=scr[:], in_=xr[:, b, :], func=AF.Square,
                          accum_out=col[:, b, 1:2]).then_inc(ctx.sem(S_AC), 1)
        n_v1 = ctx.bump(S_VS, 1)
        @ctx.on("vector")
        def _(v, i=i, b=b, n_ld=n_ld, n_ac=n_ac):
            v.wait_ge(ctx.sem(S_LD + str(b)), n_ld)
            v.wait_ge(ctx.sem(S_AC), n_ac)
            v.reduce_sum(col[:, b, 0:1], xr[:, b, :], axis=AX.X)
            v.engine_nop().then_inc(ctx.sem(S_VS), 1)
        n_v2 = ctx.bump(S_VS, 1)
        @ctx.on("vector")
        def _(v, b=b, n_v1=n_v1):
            v.wait_ge(ctx.sem(S_VS), n_v1)
            v.tensor_scalar_mul(col[:, b, 2:3], col[:, b, 0:1], 1.0 / D)
            v.tensor_scalar_mul(col[:, b, 3:4], col[:, b, 1:2], 1.0 / D)
            v.engine_nop().then_inc(ctx.sem(S_VS), 1)
        n_v3 = ctx.bump(S_VS, 1)
        @ctx.on("vector")
        def _(v, b=b, n_v2=n_v2):
            v.wait_ge(ctx.sem(S_VS), n_v2)
            v.tensor_tensor(col[:, b, 6:7], col[:, b, 2:3], col[:, b, 2:3], op=ALU.mult)
            v.engine_nop().then_inc(ctx.sem(S_VS), 1)
        n_v4 = ctx.bump(S_VS, 1)
        @ctx.on("vector")
        def _(v, b=b, n_v3=n_v3):
            v.wait_ge(ctx.sem(S_VS), n_v3)
            v.tensor_sub(col[:, b, 3:4], col[:, b, 3:4], col[:, b, 6:7])
            v.engine_nop().then_inc(ctx.sem(S_VS), 1)
        n_ac1b = ctx.bump(S_AC, 1)
        n_ac2 = ctx.bump(S_AC, 1)
        @ctx.on("scalar")
        def _(sc, b=b, n_v4=n_v4, n_ac1b=n_ac1b):
            sc.wait_ge(ctx.sem(S_VS), n_v4)
            sc.activation(out=col[:, b, 4:5], in_=col[:, b, 3:4], func=AF.Ln, bias=EPS)\
                .then_inc(ctx.sem(S_AC), 1)
            sc.wait_ge(ctx.sem(S_AC), n_ac1b)
            sc.activation(out=col[:, b, 4:5], in_=col[:, b, 4:5], func=AF.Exp, scale=-0.5)\
                .then_inc(ctx.sem(S_AC), 1)
        n_v5 = ctx.bump(S_VS, 1)
        @ctx.on("vector")
        def _(v, b=b, n_ac2=n_ac2):
            v.wait_ge(ctx.sem(S_AC), n_ac2)
            v.scalar_tensor_tensor(col[:, b, 5:6], col[:, b, 2:3], -1.0, col[:, b, 4:5],
                                   op0=ALU.mult, op1=ALU.mult)
            v.engine_nop().then_inc(ctx.sem(S_VS), 1)
        n_ac3 = ctx.bump(S_AC, 1)
        ac_rel[i] = n_ac3
        @ctx.on("scalar")
        def _(sc, i=i, b=b, n_v5=n_v5):
            sc.wait_ge(ctx.sem(S_VS), n_v5)
            if i >= 2:
                sc.wait_ge(ctx.sem(S_ST + str(b)), base_st[b] + (i // 2) * 16)
            sc.activation(out=xlt[:, b, :], in_=xr[:, b, :], func=AF.Identity,
                          scale=col[:, b, 4:5], bias=col[:, b, 5:6]).then_inc(ctx.sem(S_AC), 1)
        ctx.bump(S_ST + str(b), 16)
        @ctx.on("sync")
        def _(s, i=i, b=b, n_ac3=n_ac3):
            s.wait_ge(ctx.sem(S_AC), n_ac3)
            s.dma_start(out=g["xln_ag_in"][i * 128:(i + 1) * 128, :], in_=xlt[:, b, :])\
                .then_inc(ctx.sem(S_ST + str(b)), 16)

    n_st_all0 = ctx.val(S_ST + "0")
    n_st_all1 = ctx.val(S_ST + "1")


    @ctx.on("gpsimd")
    def _(gp):
        gp.wait_ge(ctx.sem(S_ST + "0"), n_st_all0)
        gp.wait_ge(ctx.sem(S_ST + "1"), n_st_all1)
        gp.collective_compute("AllGather", ALU.bypass,
                              replica_groups=[list(range(NC))],
                              ins=[g["xln_ag_in"].opt()], outs=[g["xln_full"].opt()])\
            .then_inc(ctx.sem(S_AGX), 1)
    ctx.bump(S_AGX, 1)
    n_agx = ctx.val(S_AGX)

    for kc in range(8):
        ctx.bump(S_TRX, 16)
        @ctx.on("scalar")
        def _(sc, kc=kc):
            if kc == 0:
                sc.wait_ge(ctx.sem(S_ST + "0"), n_st_all0)
                sc.wait_ge(ctx.sem(S_ST + "1"), n_st_all1)
            sc.dma_start_transpose(out=xlnT[:, kc, :],
                                   in_=g["xln_ag_in"][:, kc * 128:(kc + 1) * 128])\
                .then_inc(ctx.sem(S_TRX), 16)
    n_trx = ctx.val(S_TRX)

    lgP, cm1 = ctx.ps(q + "lgP", [128, 2, E], F32)
    posP, cm2 = ctx.ps(q + "posP", [128, E], F32)
    cntP, cm3 = ctx.ps(q + "cntP", [1, 2, E], F32)
    cms += [cm1, cm2, cm3]
    n_vbr = ctx.bump(S_VS, 1)
    @ctx.on("vector")
    def _(v):
        v.memset(base_run[:], 0.0)
        v.engine_nop().then_inc(ctx.sem(S_VS), 1)

    n_cl2 = ctx.val(q + "cl2")
    for i in range(NTILE):
        b = i % 2
        n_lg = ctx.bump(S_LG, 1)
        @ctx.on("tensor")
        def _(t, i=i, b=b):
            if i == 0:
                t.wait_ge(ctx.sem(S_TRX), n_trx)
                t.wait_ge(ctx.sem(q + "cl2"), n_cl2)
            if i >= 2:
                t.wait_ge(ctx.sem(S_VS), lg_rel[i - 2])
            last = None
            for kc in range(8):
                last = t.matmul(lgP[:, b, :], xlnT[:, kc, i * 128:(i + 1) * 128], wgb[:, kc, :],
                                start=(kc == 0), stop=(kc == 7))
            last.then_inc(ctx.sem(S_LG), 1)
        if i == 0:
            lg_rel = {}
        n_v1 = ctx.bump(S_VS, 1)
        @ctx.on("vector")
        def _(v, i=i, b=b, n_lg=n_lg):
            v.wait_ge(ctx.sem(S_LG), n_lg)
            v.scalar_tensor_tensor(lg[:, b, :], lgP[:, b, :], 0.0, bgbc[:],
                                   op0=ALU.bypass, op1=ALU.add)
            v.engine_nop().then_inc(ctx.sem(S_VS), 1)
        lg_rel[i] = n_v1
        n_v2 = ctx.bump(S_VS, 1)
        @ctx.on("vector")
        def _(v, b=b, n_v1=n_v1):
            v.wait_ge(ctx.sem(S_VS), n_v1)
            v.reduce_max(tcol[:, b, 0:1], lg[:, b, :], axis=AX.X)
            v.engine_nop().then_inc(ctx.sem(S_VS), 1)
        n_v3 = ctx.bump(S_VS, 1)
        @ctx.on("vector")
        def _(v, i=i, b=b, n_v2=n_v2):
            v.wait_ge(ctx.sem(S_VS), n_v2)
            v.max_index(idx8[:], tcol[:, b, 0:1].broadcast_to([128, 8]), lg[:, b, :])
            v.tensor_scalar_mul(tcol[:, b, 1:2], tcol[:, b, 0:1], -1.0)
            v.engine_nop().then_inc(ctx.sem(S_VS), 1)
        n_v4 = ctx.bump(S_VS, 1)
        @ctx.on("vector")
        def _(v, i=i, n_v3=n_v3):
            v.wait_ge(ctx.sem(S_VS), n_v3)
            v.tensor_copy(idxf_a[:, i:i + 1], idx8[:, 0:1])
            v.engine_nop().then_inc(ctx.sem(S_VS), 1)
        n_ac = ctx.bump(S_AC, 1)
        @ctx.on("scalar")
        def _(sc, b=b, n_v3=n_v3):
            sc.wait_ge(ctx.sem(S_VS), n_v3)
            sc.activation(out=ex[:, b, :], in_=lg[:, b, :], func=AF.Exp,
                          bias=tcol[:, b, 1:2]).then_inc(ctx.sem(S_AC), 1)
        n_v5 = ctx.bump(S_VS, 1)
        @ctx.on("vector")
        def _(v, i=i, b=b, n_ac=n_ac, n_v4=n_v4):
            v.wait_ge(ctx.sem(S_AC), n_ac)
            v.wait_ge(ctx.sem(S_VS), n_v4)
            v.reduce_sum(tcol[:, b, 2:3], ex[:, b, :], axis=AX.X)
            v.tensor_tensor(oh[:, b, :], idxf_a[:, i:i + 1].broadcast_to([128, E]),
                            g["iotaE"][:], op=ALU.is_equal)
            v.engine_nop().then_inc(ctx.sem(S_VS), 1)
        n_v6 = ctx.bump(S_VS, 1)
        @ctx.on("vector")
        def _(v, i=i, b=b, n_v5=n_v5):
            v.wait_ge(ctx.sem(S_VS), n_v5)
            v.reciprocal(gate_a[:, i:i + 1], tcol[:, b, 2:3])
            v.tensor_copy(base_r[:, b, :], base_run[:])
            v.engine_nop().then_inc(ctx.sem(S_VS), 1)
        ctx.bump(S_PE, 2)
        @ctx.on("tensor")
        def _(t, i=i, b=b, n_v6=n_v6):
            t.wait_ge(ctx.sem(S_VS), n_v6)
            t.matmul(posP[:], onesK1f[:], base_r[:, b, :], start=True, stop=False)
            t.matmul(posP[:], ltUf[:], oh[:, b, :], start=False, stop=True)\
                .then_inc(ctx.sem(S_PE), 1)
            t.matmul(cntP[:, b, :], ones128f[:], oh[:, b, :], start=True,
                     stop=True).then_inc(ctx.sem(S_PE), 1)
        n_v7 = ctx.bump(S_VS, 1)
        @ctx.on("vector")
        def _(v, i=i, n_pe=base_peB + 2 * i + 1):
            v.wait_ge(ctx.sem(S_PE), n_pe)
            v.tensor_copy(posS[:], posP[:])
            v.engine_nop().then_inc(ctx.sem(S_VS), 1)
        n_v8 = ctx.bump(S_VS, 1)
        @ctx.on("vector")
        def _(v, b=b, n_v7=n_v7):
            v.wait_ge(ctx.sem(S_VS), n_v7)
            v.tensor_mul(posS[:], posS[:], oh[:, b, :])
            v.engine_nop().then_inc(ctx.sem(S_VS), 1)
        n_v9x = ctx.bump(S_VS, 1)
        @ctx.on("vector")
        def _(v, i=i, n_v8=n_v8):
            v.wait_ge(ctx.sem(S_VS), n_v8)
            v.reduce_sum(pos2_a[:, i:i + 1], posS[:], axis=AX.X)
            v.engine_nop().then_inc(ctx.sem(S_VS), 1)
        ctx.bump(S_VS, 1)
        @ctx.on("vector")
        def _(v, i=i, b=b, n_v9x=n_v9x, n_pe=base_peB + 2 * i + 2):
            v.wait_ge(ctx.sem(S_VS), n_v9x)
            v.wait_ge(ctx.sem(S_PE), n_pe)
            v.tensor_add(base_run[:], base_run[:], cntP[:, b, :])
            v.engine_nop().then_inc(ctx.sem(S_VS), 1)

    n_vcnt = ctx.bump(S_VS, 1)
    @ctx.on("vector")
    def _(v, w=ctx.val(S_VS) - 1):
        v.wait_ge(ctx.sem(S_VS), w)
        v.tensor_copy(cnt_sb[:, 0:E], base_run[:])
        v.engine_nop().then_inc(ctx.sem(S_VS), 1)

    @ctx.on("sync")
    def _(s, n_vcnt=n_vcnt):
        s.wait_ge(ctx.sem(S_VS), n_vcnt)
        s.dma_start(out=g["cnt_in"][:, 0:E], in_=cnt_sb[:, 0:E]).then_inc(ctx.sem(q + "cst"), 16)
        s.dma_start(out=g["cnt_in"][:, 16:32], in_=g["zero128"][0:1, 0:16]).then_inc(ctx.sem(q + "cst"), 16)
    ctx.bump(q + "cst", 32)
    n_cst = ctx.val(q + "cst")
    @ctx.on("gpsimd")
    def _(gp):
        gp.wait_ge(ctx.sem(q + "cst"), n_cst)
        gp.collective_compute("AllGather", ALU.bypass,
                              replica_groups=[list(range(NC))],
                              ins=[g["cnt_in"].opt()], outs=[g["cnt_out"].opt()])\
            .then_inc(ctx.sem(S_AGC), 1)
    ctx.bump(S_AGC, 1)
    n_agc = ctx.val(S_AGC)

    @ctx.on("sync")
    def _(s):
        s.wait_ge(ctx.sem(S_AGC), n_agc)
        s.dma_start(out=cld[:], in_=g["cnt_out"][:, :]).then_inc(ctx.sem(q + "cl"), 16)
    ctx.bump(q + "cl", 16)
    n_cl = ctx.val(q + "cl")
    n_v9 = ctx.bump(S_VS, 1)
    @ctx.on("vector")
    def _(v):
        v.wait_ge(ctx.sem(q + "cl"), n_cl)
        v.scalar_tensor_tensor(mask8[:, 0:1], g["pcf_col"][0:8, PCF_COREBASE:PCF_COREBASE + 1],
                               1.0 / TPC, iota8[:], op0=ALU.mult, op1=ALU.is_gt)
        v.scalar_tensor_tensor(mask8[:, 1:2], g["pcf_col"][0:8, PCF_COREBASE:PCF_COREBASE + 1],
                               1.0 / TPC, iota8[:], op0=ALU.mult, op1=ALU.is_equal)
        v.engine_nop().then_inc(ctx.sem(S_VS), 1)
    ctx.bump(S_PE, 1)
    @ctx.on("tensor")
    def _(t, n_v9=n_v9):
        t.wait_ge(ctx.sem(S_VS), n_v9)
        t.matmul(posP[0:1, 0:E], mask8[:, 0:1], cld[0:8, 0:E], start=True, stop=True)\
            .then_inc(ctx.sem(S_PE), 1)
    n_v10 = ctx.bump(S_VS, 1)
    @ctx.on("vector")
    def _(v, n_pe=ctx.val(S_PE)):
        v.wait_ge(ctx.sem(S_PE), n_pe)
        v.tensor_copy(baser[:], posP[0:1, 0:E])
        v.engine_nop().then_inc(ctx.sem(S_VS), 1)
    ctx.bump(S_PE, 1)
    @ctx.on("tensor")
    def _(t, n_v10=n_v10):
        t.wait_ge(ctx.sem(S_VS), n_v10)
        t.matmul(posP[:, 0:E], onesK1f[:], baser[:], start=True, stop=True)\
            .then_inc(ctx.sem(S_PE), 1)
    n_v11 = ctx.bump(S_VS, 1)
    @ctx.on("vector")
    def _(v, n_pe=ctx.val(S_PE)):
        v.wait_ge(ctx.sem(S_PE), n_pe)
        v.tensor_copy(bases[:], posP[:, 0:E])
        v.engine_nop().then_inc(ctx.sem(S_VS), 1)

    for i in range(NTILE):
        b = i % 2
        n_c1 = ctx.bump(S_VS, 1)
        @ctx.on("vector")
        def _(v, i=i, n_v11=n_v11):
            if i == 0:
                v.wait_ge(ctx.sem(S_VS), n_v11)
            v.tensor_tensor(ohs[:], idxf_a[:, i:i + 1].broadcast_to([128, E]),
                            g["iotaE"][:], op=ALU.is_equal)
            v.engine_nop().then_inc(ctx.sem(S_VS), 1)
        n_c2 = ctx.bump(S_VS, 1)
        @ctx.on("vector")
        def _(v, n_c1=n_c1):
            v.wait_ge(ctx.sem(S_VS), n_c1)
            v.tensor_mul(ohs[:], ohs[:], bases[:])
            v.engine_nop().then_inc(ctx.sem(S_VS), 1)
        n_c3 = ctx.bump(S_VS, 1)
        @ctx.on("vector")
        def _(v, b=b, n_c2=n_c2):
            v.wait_ge(ctx.sem(S_VS), n_c2)
            v.reduce_sum(tcol[:, b, 3:4], ohs[:], axis=AX.X)
            v.engine_nop().then_inc(ctx.sem(S_VS), 1)
        n_c4 = ctx.bump(S_VS, 1)
        @ctx.on("vector")
        def _(v, i=i, b=b, n_c3=n_c3):
            v.wait_ge(ctx.sem(S_VS), n_c3)
            v.tensor_add(tcol[:, b, 4:5], tcol[:, b, 3:4], pos2_a[:, i:i + 1])
            v.engine_nop().then_inc(ctx.sem(S_VS), 1)
        n_c5 = ctx.bump(S_VS, 1)
        @ctx.on("vector")
        def _(v, b=b, n_c4=n_c4):
            v.wait_ge(ctx.sem(S_VS), n_c4)
            v.tensor_single_scalar(tcol[:, b, 5:6], tcol[:, b, 4:5], float(CAP), op=ALU.is_lt)
            v.engine_nop().then_inc(ctx.sem(S_VS), 1)
        n_c6 = ctx.bump(S_VS, 1)
        @ctx.on("vector")
        def _(v, i=i, b=b, n_c5=n_c5):
            v.wait_ge(ctx.sem(S_VS), n_c5)
            v.scalar_tensor_tensor(tcol[:, b, 6:7], idxf_a[:, i:i + 1], float(CAP),
                                   tcol[:, b, 4:5], op0=ALU.mult, op1=ALU.add)
            v.scalar_tensor_tensor(comb_a[:, i:i + 1], gate_a[:, i:i + 1], 0.5,
                                   tcol[:, b, 5:6], op0=ALU.mult, op1=ALU.mult)
            v.engine_nop().then_inc(ctx.sem(S_VS), 1)
        n_c7 = ctx.bump(S_VS, 1)
        @ctx.on("vector")
        def _(v, i=i, b=b, n_c6=n_c6):
            v.wait_ge(ctx.sem(S_VS), n_c6)
            v.tensor_mul(slkf_a[:, i:i + 1], tcol[:, b, 6:7], tcol[:, b, 5:6])
            v.tensor_sub(tcol[:, b, 7:8], tcol[:, b, 6:7], g["trashc"][:, i:i + 1])
            v.engine_nop().then_inc(ctx.sem(S_VS), 1)
        n_c8 = ctx.bump(S_VS, 1)
        @ctx.on("vector")
        def _(v, i=i, b=b, n_c7=n_c7):
            v.wait_ge(ctx.sem(S_VS), n_c7)
            v.scalar_tensor_tensor(tbf[:, i:i + 1], tcol[:, b, 7:8], tcol[:, b, 5:6],
                                   g["trashc"][:, i:i + 1], op0=ALU.mult, op1=ALU.add)
            v.engine_nop().then_inc(ctx.sem(S_VS), 1)
        ctx.bump(S_VS, 1)
        @ctx.on("vector")
        def _(v, i=i, n_c8=n_c8):
            v.wait_ge(ctx.sem(S_VS), n_c8)
            v.tensor_copy(slot_u[:, i:i + 1], slkf_a[:, i:i + 1])
            v.tensor_copy(tabs_u[:, i:i + 1], tbf[:, i:i + 1])
            v.engine_nop().then_inc(ctx.sem(S_VS), 1)
    n_vs_passc = ctx.val(S_VS)

    if call == 0:
        @ctx.on("sync")
        def _(s):
            s.wait_ge(ctx.sem(S_VS), n_vs_passc)
            s.dma_start(out=g["route_dbg"][:, 0:NTILE], in_=idxf_a[:]).then_inc(ctx.sem(q + "rd"), 16)
            s.dma_start(out=g["route_dbg"][:, NTILE:2 * NTILE], in_=pos2_a[:]).then_inc(ctx.sem(q + "rd"), 16)
            s.dma_start(out=g["route_dbg"][:, 2 * NTILE:3 * NTILE], in_=slkf_a[:]).then_inc(ctx.sem(q + "rd"), 16)
            s.dma_start(out=g["route_dbg"][:, 3 * NTILE:4 * NTILE], in_=comb_a[:]).then_inc(ctx.sem(q + "rd"), 16)
            s.wait_ge(ctx.sem(q + "rd"), 64)
        ctx.bump(q + "rd", 64)

    n_tz = ctx.val(q + "tz")
    for i in range(NTILE):
        ctx.bump(S_SC, 16)
        @ctx.on("gpsimd")
        def _(gp, i=i):
            if i == 0:
                gp.wait_ge(ctx.sem(S_VS), n_vs_passc)
                gp.wait_ge(ctx.sem(q + "tz"), n_tz)
            gp.indirect_dma_start(
                out=g["tab_in"],
                out_offset=IndirectOffsetOnAxis(ap=tabs_u[:, i:i + 1], axis=0),
                in_=payl_u[:, i:i + 1],
                in_offset=None,
            ).then_inc(ctx.sem(S_SC), 16)
    n_sc = ctx.val(S_SC)

    @ctx.on("sync")
    def _(s):
        s.wait_ge(ctx.sem(S_VS), n_vs_passc)
        s.dma_start(out=g["hmst"][0:1, :].rearrange("1 p -> p 1"),
                    in_=slkf_a[:, NTILE - 1:NTILE]).then_inc(ctx.sem(q + "hm"), 16)
        s.dma_start(out=g["hmst"][1:2, :].rearrange("1 p -> p 1"),
                    in_=comb_a[:, NTILE - 1:NTILE]).then_inc(ctx.sem(q + "hm"), 16)
    ctx.bump(q + "hm", 32)
    n_hm0 = ctx.val(q + "hm")
    @ctx.on("sync")
    def _(s):
        s.wait_ge(ctx.sem(q + "hm"), n_hm0)
        s.dma_start(out=g["cnt_in"][0:1, 16:19], in_=g["hmst"][0:1, 125:128])\
            .then_inc(ctx.sem(q + "hm"), 16)
        s.dma_start(out=g["cnt_in"][0:1, 19:22], in_=g["hmst"][1:2, 125:128])\
            .then_inc(ctx.sem(q + "hm"), 16)
    ctx.bump(q + "hm", 32)

    @ctx.on("gpsimd")
    def _(gp):
        gp.wait_ge(ctx.sem(S_SC), n_sc)
        gp.collective_compute("AllGather", ALU.bypass,
                              replica_groups=[list(range(NC))],
                              ins=[g["tab_in"].opt()], outs=[g["tab_out"].opt()])\
            .then_inc(ctx.sem(S_AGT), 1)
    ctx.bump(S_AGT, 1)
    n_agt = ctx.val(S_AGT)

    n_vmg = ctx.bump(S_VS, 2)
    @ctx.on("gpsimd")
    def _(gp):
        gp.iota(tbf[:, 0:16].rearrange("p (a b) -> p a b", a=2),
                pattern=[[128, 2], [TL // 16, 8]], base=0,
                channel_multiplier=1, allow_small_or_imprecise_dtypes=True)
        gp.engine_nop().then_inc(ctx.sem(q + "mgi"), 1)
    ctx.bump(q + "mgi", 1)
    n_mgi = ctx.val(q + "mgi")
    @ctx.on("vector")
    def _(v):
        v.wait_ge(ctx.sem(q + "mgi"), n_mgi)
        v.scalar_tensor_tensor(tbf[:, 0:16],
                               g["pcf_col"][:, PCF_SLOTBASE:PCF_SLOTBASE + 1].broadcast_to([128, 16]),
                               1.0 / 16.0, tbf[:, 0:16], op0=ALU.mult, op1=ALU.add)
        v.engine_nop().then_inc(ctx.sem(S_VS), 1)
    @ctx.on("vector")
    def _(v, tgt=n_vmg - 1):
        v.wait_ge(ctx.sem(S_VS), tgt)
        v.tensor_copy(mgi[:].rearrange("p a b -> p (a b)"), tbf[:, 0:16])
        v.engine_nop().then_inc(ctx.sem(S_VS), 1)
    n_vmg_done = ctx.val(S_VS)

    for e in range(2):
        for k in range(NC):
            ctx.bump(S_MG, 16)
            @ctx.on("gpsimd")
            def _(gp, e=e, k=k):
                if e == 0 and k == 0:
                    gp.wait_ge(ctx.sem(S_AGT), n_agt)
                    gp.wait_ge(ctx.sem(S_VS), n_vmg_done)
                gp.indirect_dma_start(
                    out=mgf[:, e, k, :],
                    out_offset=None,
                    in_=g["tab_out"],
                    in_offset=IndirectOffsetOnAxis(ap=mgi[:, e, k:k + 1], axis=0),
                ).then_inc(ctx.sem(S_MG), 16)
    n_mg = ctx.val(S_MG)
    n_vm1 = ctx.bump(S_VS, 1)
    @ctx.on("vector")
    def _(v):
        v.wait_ge(ctx.sem(S_MG), n_mg)
        v.tensor_tensor(tos_f[:], mgf[:, :, 0, :], mgf[:, :, 1, :], op=ALU.min)
        v.engine_nop().then_inc(ctx.sem(S_VS), 1)
    for k in range(2, NC):
        n_vmk = ctx.bump(S_VS, 1)
        @ctx.on("vector")
        def _(v, k=k, w=n_vm1 + k - 2):
            v.wait_ge(ctx.sem(S_VS), w)
            v.tensor_tensor(tos_f[:], tos_f[:], mgf[:, :, k, :], op=ALU.min)
            v.engine_nop().then_inc(ctx.sem(S_VS), 1)
    n_vm2 = ctx.bump(S_VS, 1)
    @ctx.on("vector")
    def _(v, w=ctx.val(S_VS) - 1):
        v.wait_ge(ctx.sem(S_VS), w)
        v.tensor_scalar_min(tos_f[:], tos_f[:], BIGF)
        v.engine_nop().then_inc(ctx.sem(S_VS), 1)
    ctx.bump(S_VS, 1)
    @ctx.on("vector")
    def _(v, w=n_vm2):
        v.wait_ge(ctx.sem(S_VS), w)
        v.tensor_copy(tos_u[:], tos_f[:])
        v.engine_nop().then_inc(ctx.sem(S_VS), 1)
    n_tos = ctx.val(S_VS)

    ctx.close_tensors([cm1, cm2, cm3])
    cms.remove(cm1); cms.remove(cm2); cms.remove(cm3)
    ctx.close_tensors(cms2)

    cms3 = []
    bufT = sb("bufT", [128, 8, CAP], BF16, cms3)
    w1t = sb("w1t", [128, 2, 8, 128], BF16, cms3)
    w2sb = sb("w2sb", [128, 32, D], BF16, cms3)
    hbf = sb("hbf", [128, 32, CAP // 2], BF16, cms3)
    b1sb = sb("b1sb", [128, 32], F32, cms3)
    b2bc = sb("b2bc", [128, D], F32, cms3)
    ysb = sb("ysb", [128, 2, D], F32, cms3)

    hP, cm4 = ctx.ps(q + "hP", [128, 2, 1024], F32)
    yP, cm5 = ctx.ps(q + "yP", [128, 4, 512], F32)
    cms += [cm4, cm5]

    for e in range(2):
        eq = f"{q}e{e}"
        eb_peh = ctx.val(eq + "peh")
        eb_hev = ctx.val(eq + "hev")
        eb_yev = ctx.val(eq + "yev")
        for f in range(16):
            fg = e * 16 + f
            fb = fg % 2
            n_ga = ctx.bump(q + "ga" + str(fb), 16)
            @ctx.on("gpsimd")
            def _(gp, f=f, fg=fg, fb=fb, e=e):
                if fg == 0:
                    gp.wait_ge(ctx.sem(S_VS), n_tos)
                    gp.wait_ge(ctx.sem(S_AGX), n_agx)
                gp.wait_ge(ctx.sem(q + "gst" + str(fb)), base_gst[fb] + (fg // 2) * 16)
                gp.indirect_dma_start(
                    out=gbuf[:, fb, :],
                    out_offset=None,
                    in_=g["xln_full"],
                    in_offset=IndirectOffsetOnAxis(ap=tos_u[:, e, f:f + 1], axis=0),
                    bounds_check=g["r_ntok"],
                    oob_is_err=False,
                ).then_inc(ctx.sem(q + "ga" + str(fb)), 16)
            ctx.bump(q + "gst" + str(fb), 16)
            @ctx.on("sync")
            def _(s, f=f, fb=fb, n_ga=n_ga):
                s.wait_ge(ctx.sem(q + "ga" + str(fb)), n_ga)
                s.dma_start(out=g["xg"][f * 128:(f + 1) * 128, :], in_=gbuf[:, fb, :])\
                    .then_inc(ctx.sem(q + "gst" + str(fb)), 16)
        n_gst0 = ctx.val(q + "gst0")
        n_gst1 = ctx.val(q + "gst1")
        for kc in range(8):
            ctx.bump(eq + "trb", 16)
            @ctx.on("scalar")
            def _(sc, kc=kc, eq=eq, n_gst0=n_gst0, n_gst1=n_gst1):
                if kc == 0:
                    sc.wait_ge(ctx.sem(q + "gst0"), n_gst0)
                    sc.wait_ge(ctx.sem(q + "gst1"), n_gst1)
                sc.dma_start_transpose(out=bufT[:, kc, :],
                                       in_=g["xg"][:, kc * 128:(kc + 1) * 128])\
                    .then_inc(ctx.sem(eq + "trb"), 16)
        n_trb = ctx.val(eq + "trb")
        @ctx.on("sync")
        def _(s, e=e, eq=eq):
            s.dma_start(out=b1sb[:], in_=g["b1v"][e]).then_inc(ctx.sem(eq + "wl"), 16)
            s.dma_start(out=b2bc[:], in_=g["b2v"][e:e + 1, :].broadcast_to([128, D]))\
                .then_inc(ctx.sem(eq + "wl"), 16)
            for hc in range(32):
                s.dma_start(out=w2sb[:, hc, :], in_=g["w2_b"][e][hc * 128:(hc + 1) * 128, :])\
                    .then_inc(ctx.sem(eq + "wl"), 16)
        n_wl = ctx.bump(eq + "wl", 16 * 34)

        for half in range(2):
            s0 = half * 1024
            for hc in range(32):
                j = half * 32 + hc
                wb = hc % 2
                n_w1l = ctx.bump(eq + "w1l" + str(wb), 16)
                @ctx.on("sync")
                def _(s, e=e, hc=hc, wb=wb, j=j, eq=eq, eb_peh=eb_peh):
                    s.wait_ge(ctx.sem(eq + "peh"), eb_peh + max(0, j - 1))
                    s.dma_start(out=w1t[:, wb, :, :],
                                in_=g["w1_b"][e, hc].rearrange("a p h -> p a h"))\
                        .then_inc(ctx.sem(eq + "w1l" + str(wb)), 16)
                n_peh = ctx.bump(eq + "peh", 1)
                @ctx.on("tensor")
                def _(t, hc=hc, wb=wb, s0=s0, j=j, n_w1l=n_w1l, first=(j == 0),
                      eq=eq, n_trb=n_trb, eb_hev=eb_hev):
                    t.wait_ge(ctx.sem(eq + "w1l" + str(wb)), n_w1l)
                    if first:
                        t.wait_ge(ctx.sem(eq + "trb"), n_trb)
                    t.wait_ge(ctx.sem(eq + "hev"), eb_hev + max(0, j - 1))
                    last = None
                    for kc in range(8):
                        for nb in range(2):
                            last = t.matmul(hP[:, wb, nb * 512:(nb + 1) * 512],
                                            w1t[:, wb, kc, :],
                                            bufT[:, kc, s0 + nb * 512: s0 + (nb + 1) * 512],
                                            start=(kc == 0), stop=(kc == 7))
                    last.then_inc(ctx.sem(eq + "peh"), 1)
                ctx.bump(eq + "hev", 1)
                @ctx.on("scalar")
                def _(sc, hc=hc, wb=wb, n_peh=n_peh, n_wl=n_wl, eq=eq):
                    sc.wait_ge(ctx.sem(eq + "peh"), n_peh)
                    sc.wait_ge(ctx.sem(eq + "wl"), n_wl)
                    sc.activation(out=hbf[:, hc, :], in_=hP[:, wb, :], func=AF.Relu,
                                  bias=b1sb[:, hc:hc + 1]).then_inc(ctx.sem(eq + "hev"), 1)
            n_hev_half = ctx.val(eq + "hev")
            for st in range(8):
                jj = half * 8 + st
                sb2 = st % 2
                for dh in range(2):
                    yb = (jj * 2 + dh) % 4
                    n_pey = ctx.bump(eq + "pey", 1)
                    @ctx.on("tensor")
                    def _(t, st=st, dh=dh, yb=yb, jj=jj, n_hev_half=n_hev_half, eq=eq, eb_yev=eb_yev):
                        if st == 0 and dh == 0:
                            t.wait_ge(ctx.sem(eq + "hev"), n_hev_half)
                        t.wait_ge(ctx.sem(eq + "yev"), eb_yev + max(0, jj * 2 + dh - 3))
                        last = None
                        for hc in range(32):
                            last = t.matmul(yP[:, yb, :],
                                            hbf[:, hc, st * 128:(st + 1) * 128],
                                            w2sb[:, hc, dh * 512:(dh + 1) * 512],
                                            start=(hc == 0), stop=(hc == 31))
                        last.then_inc(ctx.sem(eq + "pey"), 1)
                    ctx.bump(eq + "yev", 1)
                    @ctx.on("vector")
                    def _(v, st=st, dh=dh, yb=yb, sb2=sb2, jj=jj, n_pey=n_pey, n_wl=n_wl, eq=eq, e=e, base_yst=base_yst):
                        v.wait_ge(ctx.sem(eq + "pey"), n_pey)
                        if jj == 0 and dh == 0:
                            v.wait_ge(ctx.sem(eq + "wl"), n_wl)
                        jjg = e * 16 + jj
                        if jjg >= 2 and dh == 0:
                            v.wait_ge(ctx.sem(q + "yst" + str(sb2)), base_yst[sb2] + (jjg // 2) * 16)
                        v.scalar_tensor_tensor(ysb[:, sb2, dh * 512:(dh + 1) * 512],
                                               yP[:, yb, :], 0.0, b2bc[:, dh * 512:(dh + 1) * 512],
                                               op0=ALU.bypass, op1=ALU.add)
                        v.engine_nop().then_inc(ctx.sem(eq + "yev"), 1)
                n_yev = ctx.val(eq + "yev")
                fl = half * 8 + st
                ctx.bump(q + "yst" + str(sb2), 16)
                @ctx.on("sync")
                def _(s, sb2=sb2, fl=fl, e=e, n_yev=n_yev, eq=eq):
                    s.wait_ge(ctx.sem(eq + "yev"), n_yev)
                    s.dma_start(out=g["y_in"][e * CAP:(e + 1) * CAP, :]
                                .rearrange("(p f) d -> p f d", f=16)[:, fl, :],
                                in_=ysb[:, sb2, :]).then_inc(ctx.sem(q + "yst" + str(sb2)), 16)
    ywaits = [(q + f"yst{p}", ctx.val(q + f"yst{p}")) for p in range(2)]
    @ctx.on("gpsimd")
    def _(gp):
        for nm, vv in ywaits:
            gp.wait_ge(ctx.sem(nm), vv)
        gp.collective_compute("AllGather", ALU.bypass,
                              replica_groups=[list(range(NC))],
                              ins=[g["y_in"].opt()], outs=[g["y_full"].opt()])\
            .then_inc(ctx.sem(S_AGY), 1)
    ctx.bump(S_AGY, 1)
    n_agy = ctx.val(S_AGY)

    ctx.close_tensors([cm4, cm5])
    cms.remove(cm4); cms.remove(cm5)
    ctx.close_tensors(cms3)

    if call == 0:
        n_hm = ctx.val(q + "hm")
        @ctx.on("gpsimd")
        def _(gp):
            gp.wait_ge(ctx.sem(q + "hm"), n_hm)
            gp.collective_compute("AllGather", ALU.bypass,
                                  replica_groups=[list(range(NC))],
                                  ins=[g["cnt_in"].opt()], outs=[g["cnt_out"].opt()])\
                .then_inc(ctx.sem(q + "agh"), 1)
        ctx.bump(q + "agh", 1)
        n_agh = ctx.val(q + "agh")
        @ctx.on("sync")
        def _(s):
            s.wait_ge(ctx.sem(q + "agh"), n_agh)
            s.dma_start(out=cld[:], in_=g["cnt_out"][:, :]).then_inc(ctx.sem(q + "hl"), 16)
        ctx.bump(q + "hl", 16)
        n_hl = ctx.val(q + "hl")
        n_hv2 = ctx.bump(S_VS, 1)
        @ctx.on("vector")
        def _(v):
            v.wait_ge(ctx.sem(q + "hl"), n_hl)
            v.tensor_scalar_add(mask8[:, 1:2], iota8[:], 1.0)
            v.engine_nop().then_inc(ctx.sem(S_VS), 1)
        n_hv3 = ctx.bump(S_VS, 1)
        @ctx.on("vector")
        def _(v, n_hv2=n_hv2):
            v.wait_ge(ctx.sem(S_VS), n_hv2)
            v.tensor_scalar_mul(mask8[:, 1:2], mask8[:, 1:2], float(TPC))
            v.engine_nop().then_inc(ctx.sem(S_VS), 1)
        n_hv4 = ctx.bump(S_VS, 1)
        @ctx.on("vector")
        def _(v, n_hv3=n_hv3):
            v.wait_ge(ctx.sem(S_VS), n_hv3)
            v.tensor_tensor(mask8[:, 1:2], mask8[:, 1:2],
                            g["pcf_col"][0:8, PCF_COREBASE:PCF_COREBASE + 1], op=ALU.is_equal)
            v.engine_nop().then_inc(ctx.sem(S_VS), 1)
        hmP, cmh = ctx.ps(q + "hmP", [1, 8], F32)
        cms.append(cmh)
        ctx.bump(S_PE, 1)
        @ctx.on("tensor")
        def _(t, n_hv4=n_hv4):
            t.wait_ge(ctx.sem(S_VS), n_hv4)
            t.matmul(hmP[:, 0:6], mask8[:, 1:2], cld[0:8, 16:22], start=True, stop=True)\
                .then_inc(ctx.sem(S_PE), 1)
        n_hv5 = ctx.bump(S_VS, 1)
        @ctx.on("vector")
        def _(v, n_pe=ctx.val(S_PE)):
            v.wait_ge(ctx.sem(S_PE), n_pe)
            v.tensor_copy(cnt_sb[:, 24:30], hmP[:, 0:6])
            v.engine_nop().then_inc(ctx.sem(S_VS), 1)
        @ctx.on("sync")
        def _(s, n_hv5=n_hv5):
            s.wait_ge(ctx.sem(S_VS), n_hv5)
            s.dma_start(out=g["hmx"], in_=cnt_sb[:, 24:30]).then_inc(ctx.sem(q + "hx"), 16)
        ctx.bump(q + "hx", 16)
        n_hx = ctx.val(q + "hx")
        @ctx.on("sync")
        def _(s):
            s.wait_ge(ctx.sem(q + "hx"), n_hx)
            s.dma_start(out=hcol[0:3, 0:1], in_=g["hmx"][0:1, 0:3].rearrange("1 a -> a 1"))\
                .then_inc(ctx.sem(q + "hx"), 16)
            s.dma_start(out=hcol[0:3, 1:2], in_=g["hmx"][0:1, 3:6].rearrange("1 a -> a 1"))\
                .then_inc(ctx.sem(q + "hx"), 16)
        ctx.bump(q + "hx", 32)
        n_hx2 = ctx.val(q + "hx")
        n_hv6 = ctx.bump(S_VS, 2)
        @ctx.on("vector")
        def _(v):
            v.wait_ge(ctx.sem(q + "hx"), n_hx2)
            v.memset(hcu[:], 0)
            v.engine_nop().then_inc(ctx.sem(S_VS), 1)
        @ctx.on("vector")
        def _(v, w=n_hv6 - 1):
            v.wait_ge(ctx.sem(S_VS), w)
            v.tensor_copy(hcu[0:3, :], hcol[0:3, 0:1])
            v.engine_nop().then_inc(ctx.sem(S_VS), 1)
        n_hcu = ctx.val(S_VS)
        ctx.bump(q + "hg", 16)
        @ctx.on("gpsimd")
        def _(gp):
            gp.wait_ge(ctx.sem(S_VS), n_hcu)
            gp.wait_ge(ctx.sem(S_AGY), n_agy)
            gp.indirect_dma_start(
                out=yrow[:, 0, :], out_offset=None,
                in_=g["y_full"],
                in_offset=IndirectOffsetOnAxis(ap=hcu[:], axis=0),
                bounds_check=g["r_ntok"],
                oob_is_err=False,
            ).then_inc(ctx.sem(q + "hg"), 16)
        n_hg = ctx.val(q + "hg")
        @ctx.on("sync")
        def _(s):
            s.dma_start(out=x1row[:, 0, :][0:3, :], in_=src[0:3, :]).then_inc(ctx.sem(q + "hgx"), 16)
        ctx.bump(q + "hgx", 16)
        n_hg2 = ctx.val(q + "hgx")
        n_hv7 = ctx.bump(S_VS, 2)
        @ctx.on("vector")
        def _(v):
            v.wait_ge(ctx.sem(q + "hg"), n_hg)
            v.wait_ge(ctx.sem(q + "hgx"), n_hg2)
            v.scalar_tensor_tensor(yrow[0:3, 0, :], yrow[0:3, 0, :], hcol[0:3, 1:2],
                                   x1row[0:3, 0, :], op0=ALU.mult, op1=ALU.add)
            v.engine_nop().then_inc(ctx.sem(S_VS), 1)
        @ctx.on("vector")
        def _(v, w=n_hv7 - 1):
            v.wait_ge(ctx.sem(S_VS), w)
            v.tensor_scalar_mul(yrow[0:3, 0, :], yrow[0:3, 0, :],
                                g["pcf_col"][0:3, PCF_PARITY:PCF_PARITY + 1])
            v.engine_nop().then_inc(ctx.sem(S_VS), 1)
        n_hv8 = ctx.val(S_VS)
        ctx.bump(q + "outh", 16)
        @ctx.on("sync")
        def _(s, n_hv8=n_hv8):
            s.wait_ge(ctx.sem(S_VS), n_hv8)
            s.dma_start(out=dst[0:3, :], in_=yrow[0:3, 0, :]).then_inc(ctx.sem(q + "outh"), 16)

    for i in range(NTILE):
        b = i % 2
        n_yg = ctx.bump(q + "yg" + str(b), 16)
        @ctx.on("gpsimd")
        def _(gp, i=i, b=b):
            if i == 0:
                gp.wait_ge(ctx.sem(S_AGY), n_agy)
                if call == 0:
                    gp.wait_ge(ctx.sem(q + "outh"), 16)
            if i >= 2:
                gp.wait_ge(ctx.sem(S_OUT + str(b)), base_out[b] + (i // 2) * 16)
            gp.indirect_dma_start(
                out=yrow[:, b, :], out_offset=None,
                in_=g["y_full"],
                in_offset=IndirectOffsetOnAxis(ap=slot_u[:, i:i + 1], axis=0),
                bounds_check=g["r_ntok"],
                oob_is_err=False,
            ).then_inc(ctx.sem(q + "yg" + str(b)), 16)
        n_xl = ctx.bump(q + "xl" + str(b), 16)
        @ctx.on("sync")
        def _(s, i=i, b=b):
            s.wait_ge(ctx.sem(q + "ev"), base_ev + max(0, i - 1))
            s.dma_start(out=x1row[:, b, :], in_=src[soff + i * 128: soff + (i + 1) * 128, :])\
                .then_inc(ctx.sem(q + "xl" + str(b)), 16)
        n_ev = ctx.bump(q + "ev", 1)
        @ctx.on("vector")
        def _(v, i=i, b=b, n_yg=n_yg, n_xl=n_xl):
            v.wait_ge(ctx.sem(q + "yg" + str(b)), n_yg)
            v.wait_ge(ctx.sem(q + "xl" + str(b)), n_xl)
            v.scalar_tensor_tensor(yrow[:, b, :], yrow[:, b, :], comb_a[:, i:i + 1],
                                   x1row[:, b, :], op0=ALU.mult, op1=ALU.add)
            v.engine_nop().then_inc(ctx.sem(q + "ev"), 1)
        ctx.bump(S_OUT + str(b), 16)
        @ctx.on("sync")
        def _(s, i=i, b=b, n_ev=n_ev):
            s.wait_ge(ctx.sem(q + "ev"), n_ev)
            s.dma_start(out=dst[doff + i * 128: doff + (i + 1) * 128, :], in_=yrow[:, b, :])\
                .then_inc(ctx.sem(S_OUT + str(b)), 16)
    @ctx.on("sync")
    def _(s, a=ctx.val(S_OUT + "0"), bb=ctx.val(S_OUT + "1"),
          hh=(ctx.val(q + "outh") if call == 0 else 0)):
        s.wait_ge(ctx.sem(S_OUT + "0"), a)
        s.wait_ge(ctx.sem(S_OUT + "1"), bb)
        if call == 0:
            s.wait_ge(ctx.sem(q + "outh"), hh)

    ctx.close_tensors(cms)


def emit_m1(ctx, g):
    q = "m1"
    cms = []
    def sb(name, shape, dt):
        h, cm = ctx.sb(q + name, shape, dt)
        cms.append(cm)
        return h

    xr = sb("xr", [128, 2, D], F32)
    scr = sb("scr", [128, D], F32)
    col = sb("col", [128, 2, 8], F32)
    xlt = sb("xlt", [128, 2, D], BF16)
    winb = sb("winb", [128, 32, 8, 128], BF16)
    xlnT = sb("xlnT", [128, 2, 8, TC], BF16)
    xlnTh = sb("xlnTh", [128, 8, 16], BF16)
    xi1 = sb("xi1", [128, 2, TC + 3], F32)
    carry = sb("carry", [128, 16, 3], F32)
    u1 = sb("u1", [128, 2, TC], F32)
    ub1 = sb("ub1", [128, 2, TC], BF16)
    sg1 = sb("sg1", [128, 2, TC], F32)
    xc_sb = sb("xc_sb", [128, TC], F32)

    S_AC = q + "ac"; S_VS = q + "vs"
    base_st = [ctx.val(q + "st0"), ctx.val(q + "st1")]

    def ln_tile(i, nrow, src_off, dst_off):
        b = i % 2
        n_ld = ctx.bump(q + "ld" + str(b), 16)
        @ctx.on("sync")
        def _(s, i=i, b=b, nrow=nrow, src_off=src_off):
            if i >= 2:
                s.wait_ge(ctx.sem(S_AC), lnac_rel[i - 2])
            s.dma_start(out=xr[0:nrow, b, :], in_=g["x1b"][src_off:src_off + nrow, :])\
                .then_inc(ctx.sem(q + "ld" + str(b)), 16)
        n_ac = ctx.bump(S_AC, 1)
        @ctx.on("scalar")
        def _(sc, b=b, nrow=nrow, n_ld=n_ld):
            sc.wait_ge(ctx.sem(q + "ld" + str(b)), n_ld)
            sc.activation(out=scr[0:nrow, :], in_=xr[0:nrow, b, :], func=AF.Square,
                          accum_out=col[0:nrow, b, 1:2]).then_inc(ctx.sem(S_AC), 1)
        n_v1 = ctx.bump(S_VS, 1)
        @ctx.on("vector")
        def _(v, b=b, nrow=nrow, n_ld=n_ld, n_ac=n_ac):
            v.wait_ge(ctx.sem(q + "ld" + str(b)), n_ld)
            v.wait_ge(ctx.sem(S_AC), n_ac)
            v.reduce_sum(col[0:nrow, b, 0:1], xr[0:nrow, b, :], axis=AX.X)
            v.engine_nop().then_inc(ctx.sem(S_VS), 1)
        n_v2 = ctx.bump(S_VS, 1)
        @ctx.on("vector")
        def _(v, b=b, nrow=nrow, n_v1=n_v1):
            v.wait_ge(ctx.sem(S_VS), n_v1)
            v.tensor_scalar_mul(col[0:nrow, b, 2:3], col[0:nrow, b, 0:1], 1.0 / D)
            v.tensor_scalar_mul(col[0:nrow, b, 3:4], col[0:nrow, b, 1:2], 1.0 / D)
            v.engine_nop().then_inc(ctx.sem(S_VS), 1)
        n_v3 = ctx.bump(S_VS, 1)
        @ctx.on("vector")
        def _(v, b=b, nrow=nrow, n_v2=n_v2):
            v.wait_ge(ctx.sem(S_VS), n_v2)
            v.tensor_tensor(col[0:nrow, b, 6:7], col[0:nrow, b, 2:3], col[0:nrow, b, 2:3], op=ALU.mult)
            v.engine_nop().then_inc(ctx.sem(S_VS), 1)
        n_v4 = ctx.bump(S_VS, 1)
        @ctx.on("vector")
        def _(v, b=b, nrow=nrow, n_v3=n_v3):
            v.wait_ge(ctx.sem(S_VS), n_v3)
            v.tensor_sub(col[0:nrow, b, 3:4], col[0:nrow, b, 3:4], col[0:nrow, b, 6:7])
            v.engine_nop().then_inc(ctx.sem(S_VS), 1)
        n_ac1 = ctx.bump(S_AC, 1)
        n_ac2 = ctx.bump(S_AC, 1)
        @ctx.on("scalar")
        def _(sc, b=b, nrow=nrow, n_v4=n_v4, n_ac1=n_ac1):
            sc.wait_ge(ctx.sem(S_VS), n_v4)
            sc.activation(out=col[0:nrow, b, 4:5], in_=col[0:nrow, b, 3:4], func=AF.Ln, bias=EPS)\
                .then_inc(ctx.sem(S_AC), 1)
            sc.wait_ge(ctx.sem(S_AC), n_ac1)
            sc.activation(out=col[0:nrow, b, 4:5], in_=col[0:nrow, b, 4:5], func=AF.Exp, scale=-0.5)\
                .then_inc(ctx.sem(S_AC), 1)
        n_v5 = ctx.bump(S_VS, 1)
        @ctx.on("vector")
        def _(v, b=b, nrow=nrow, n_ac2=n_ac2):
            v.wait_ge(ctx.sem(S_AC), n_ac2)
            v.scalar_tensor_tensor(col[0:nrow, b, 5:6], col[0:nrow, b, 2:3], -1.0,
                                   col[0:nrow, b, 4:5], op0=ALU.mult, op1=ALU.mult)
            v.engine_nop().then_inc(ctx.sem(S_VS), 1)
        n_ac3 = ctx.bump(S_AC, 1)
        lnac_rel[i] = n_ac3
        @ctx.on("scalar")
        def _(sc, i=i, b=b, nrow=nrow, n_v5=n_v5):
            sc.wait_ge(ctx.sem(S_VS), n_v5)
            if i >= 2:
                sc.wait_ge(ctx.sem(q + "st" + str(b)), base_st[b] + (i // 2) * 16)
            sc.activation(out=xlt[0:nrow, b, :], in_=xr[0:nrow, b, :], func=AF.Identity,
                          scale=col[0:nrow, b, 4:5], bias=col[0:nrow, b, 5:6]).then_inc(ctx.sem(S_AC), 1)
        ctx.bump(q + "st" + str(b), 16)
        @ctx.on("sync")
        def _(s, i=i, b=b, nrow=nrow, dst_off=dst_off, n_ac3=n_ac3):
            s.wait_ge(ctx.sem(S_AC), n_ac3)
            s.dma_start(out=g["xln1"][dst_off:dst_off + nrow, :], in_=xlt[0:nrow, b, :])\
                .then_inc(ctx.sem(q + "st" + str(b)), 16)

    lnac_rel = {}
    for i in range(NTILE):
        ln_tile(i, 128, 3 + i * 128, 3 + i * 128)
    ln_tile(NTILE, 3, 0, 0)
    n_st0 = ctx.val(q + "st0")
    n_st1 = ctx.val(q + "st1")

    @ctx.on("sync")
    def _(s):
        for o in range(32):
            s.dma_start(out=winb[:, o, :, :], in_=g["winT_b"][o].rearrange("a p h -> p a h"))\
                .then_inc(ctx.sem(q + "wl"), 16)
    n_wl = ctx.bump(q + "wl", 16 * 32)

    @ctx.on("scalar")
    def _(sc):
        sc.wait_ge(ctx.sem(q + "st0"), n_st0)
        sc.wait_ge(ctx.sem(q + "st1"), n_st1)
        for kc in range(8):
            sc.dma_start_transpose(out=xlnTh[:, kc, :], in_=g["xln1"][0:16, kc * 128:(kc + 1) * 128])\
                .then_inc(ctx.sem(q + "trh"), 16)
    n_trh = ctx.bump(q + "trh", 16 * 8)

    xzP, cmp1 = ctx.ps(q + "xzP", [128, 4, TC], F32)
    hP, cmp2 = ctx.ps(q + "hP", [128, 16], F32)
    cms += [cmp1, cmp2]

    for o in range(16):
        n_hpe = ctx.bump(q + "hpe", 1)
        @ctx.on("tensor")
        def _(t, o=o):
            if o == 0:
                t.wait_ge(ctx.sem(q + "trh"), n_trh)
                t.wait_ge(ctx.sem(q + "wl"), n_wl)
            if o >= 1:
                t.wait_ge(ctx.sem(q + "hev"), o)
            last = None
            for kc in range(8):
                last = t.matmul(hP[:, 0:16], winb[:, o, kc, :], xlnTh[:, kc, :],
                                start=(kc == 0), stop=(kc == 7))
            last.then_inc(ctx.sem(q + "hpe"), 1)
        ctx.bump(q + "hev", 1)
        @ctx.on("scalar")
        def _(sc, o=o, n_hpe=n_hpe):
            sc.wait_ge(ctx.sem(q + "hpe"), n_hpe)
            sc.activation(out=carry[:, o, :], in_=hP[:, 0:3], func=AF.Copy)\
                .then_inc(ctx.sem(q + "hev"), 1)
    n_hev = ctx.val(q + "hev")

    cv_rel = {}
    last_uac = [0]
    for c in range(NCH):
        b = c % 2
        ctx.bump(q + "tr" + str(b), 16 * 8)
        @ctx.on("scalar")
        def _(sc, c=c, b=b):
            if c >= 2:
                sc.wait_ge(ctx.sem(q + "pe"), pe_rel[c - 2])
            for kc in range(8):
                sc.dma_start_transpose(out=xlnT[:, b, kc, :],
                                       in_=g["xln1"][3 + c * TC: 3 + (c + 1) * TC,
                                                     kc * 128:(kc + 1) * 128])\
                    .then_inc(ctx.sem(q + "tr" + str(b)), 16)
        if c == 0:
            pe_rel = {}
        n_tr = ctx.val(q + "tr" + str(b))
        for o in range(32):
            pb = o % 4
            ob = o % 2
            n_pe = ctx.bump(q + "pe", 1)
            @ctx.on("tensor")
            def _(t, c=c, b=b, o=o, pb=pb, n_tr=n_tr):
                if o == 0:
                    t.wait_ge(ctx.sem(q + "tr" + str(b)), n_tr)
                    if c == 0:
                        t.wait_ge(ctx.sem(q + "hev"), n_hev)
                j = c * 32 + o
                if j >= 4:
                    t.wait_ge(ctx.sem(q + "ev"), j - 3)
                last = None
                for kc in range(8):
                    last = t.matmul(xzP[:, pb, :], winb[:, o, kc, :], xlnT[:, b, kc, :],
                                    start=(kc == 0), stop=(kc == 7))
                last.then_inc(ctx.sem(q + "pe"), 1)
            pe_rel[c] = ctx.val(q + "pe")
            n_ev = ctx.bump(q + "ev", 1)
            @ctx.on("scalar")
            def _(sc, c=c, o=o, pb=pb, ob=ob, n_pe=n_pe):
                sc.wait_ge(ctx.sem(q + "pe"), n_pe)
                if o < 16:
                    jx = c * 16 + o
                    if jx >= 2:
                        sc.wait_ge(ctx.sem(S_VS), cv_rel[jx - 2])
                    sc.activation(out=xi1[:, ob, 3:3 + TC], in_=xzP[:, pb, :], func=AF.Copy)\
                        .then_inc(ctx.sem(q + "ev"), 1)
                else:
                    jz = c * 16 + (o - 16)
                    if jz >= 2:
                        sc.wait_ge(ctx.sem(q + "zs" + str(ob)), (jz // 2) * 16)
                    sc.activation(out=sg1[:, ob, :], in_=xzP[:, pb, :], func=AF.Silu)\
                        .then_inc(ctx.sem(q + "ev"), 1)
            if o >= 16:
                ctx.bump(q + "zs" + str(ob), 16)
                @ctx.on("sync")
                def _(s, c=c, o=o, ob=ob, n_ev=n_ev):
                    s.wait_ge(ctx.sem(q + "ev"), n_ev)
                    s.dma_start(out=g["sg_h"][(o - 16) * 128:(o - 15) * 128, c * TC:(c + 1) * TC],
                                in_=sg1[:, ob, :]).then_inc(ctx.sem(q + "zs" + str(ob)), 16)
            else:
                n_hd = ctx.bump(S_VS, 1)
                @ctx.on("vector")
                def _(v, o=o, ob=ob, n_ev=n_ev):
                    v.wait_ge(ctx.sem(q + "ev"), n_ev)
                    v.tensor_copy(xi1[:, ob, 0:3], carry[:, o, :])
                    v.engine_nop().then_inc(ctx.sem(S_VS), 1)
                n_vc = ctx.bump(S_VS, 4)
                @ctx.on("vector")
                def _(v, c=c, o=o, ob=ob, base=n_vc - 4, n_hd=n_hd, w_uac=last_uac[0]):
                    v.wait_ge(ctx.sem(S_VS), n_hd)
                    v.wait_ge(ctx.sem(S_AC), w_uac)
                    v.scalar_tensor_tensor(xc_sb[:], xi1[:, ob, 0:TC], g["cwcol"][:, 0 * 16 + o:0 * 16 + o + 1],
                                           g["cbcol"][:, o:o + 1].broadcast_to([128, TC]),
                                           op0=ALU.mult, op1=ALU.add)
                    v.engine_nop().then_inc(ctx.sem(S_VS), 1)
                    for k in range(1, 4):
                        v.wait_ge(ctx.sem(S_VS), base + k)
                        v.scalar_tensor_tensor(xc_sb[:], xi1[:, ob, k:k + TC], g["cwcol"][:, k * 16 + o:k * 16 + o + 1],
                                               xc_sb[:], op0=ALU.mult, op1=ALU.add)
                        v.engine_nop().then_inc(ctx.sem(S_VS), 1)
                n_cr = ctx.bump(S_VS, 1)
                cv_rel[c * 16 + o] = n_cr
                @ctx.on("vector")
                def _(v, o=o, ob=ob, n_vc=n_vc):
                    v.wait_ge(ctx.sem(S_VS), n_vc)
                    v.tensor_copy(carry[:, o, :], xi1[:, ob, TC:TC + 3])
                    v.engine_nop().then_inc(ctx.sem(S_VS), 1)
                n_uac = ctx.bump(S_AC, 1)
                last_uac[0] = n_uac
                @ctx.on("scalar")
                def _(sc, c=c, o=o, ob=ob, n_vc=n_vc):
                    sc.wait_ge(ctx.sem(S_VS), n_vc)
                    jx = c * 16 + o
                    if jx >= 2:
                        sc.wait_ge(ctx.sem(q + "us" + str(ob)), (jx // 2) * 32)
                    sc.activation(out=u1[:, ob, :], in_=xc_sb[:], func=AF.Silu).then_inc(ctx.sem(S_AC), 1)
                n_vcc = ctx.bump(S_VS, 1)
                @ctx.on("vector")
                def _(v, o=o, ob=ob, n_uac=n_uac):
                    v.wait_ge(ctx.sem(S_AC), n_uac)
                    v.tensor_copy(ub1[:, ob, :], u1[:, ob, :])
                    v.engine_nop().then_inc(ctx.sem(S_VS), 1)
                ctx.bump(q + "us" + str(ob), 32)
                @ctx.on("sync")
                def _(s, c=c, o=o, ob=ob, n_vcc=n_vcc):
                    s.wait_ge(ctx.sem(S_VS), n_vcc)
                    s.dma_start(out=g["u_h"][o * 128:(o + 1) * 128, c * TC:(c + 1) * TC],
                                in_=u1[:, ob, :]).then_inc(ctx.sem(q + "us" + str(ob)), 16)
                    s.dma_start(out=g["ubf_h"][o * 128:(o + 1) * 128, c * TC:(c + 1) * TC],
                                in_=ub1[:, ob, :]).then_inc(ctx.sem(q + "us" + str(ob)), 16)
    @ctx.on("sync")
    def _(s, a=ctx.val(q + "us0"), bb=ctx.val(q + "us1"),
          cz=ctx.val(q + "zs0"), dz=ctx.val(q + "zs1")):
        s.wait_ge(ctx.sem(q + "us0"), a)
        s.wait_ge(ctx.sem(q + "us1"), bb)
        s.wait_ge(ctx.sem(q + "zs0"), cz)
        s.wait_ge(ctx.sem(q + "zs1"), dz)
    ctx.close_tensors(cms)


def emit_m2(ctx, g):
    q = "m2"
    cms = []
    def sb(name, shape, dt):
        h, cm = ctx.sb(q + name, shape, dt)
        cms.append(cm)
        return h

    wxb = sb("wxb", [128, 16, 2 * DS], BF16)
    wdtg = sb("wdtg", [128, 4, 16, 128], BF16)
    ub2 = sb("ub2", [128, 2, 16, TC], BF16)
    u2 = sb("u2", [128, 2, 4, TC], F32)
    sg2 = sb("sg2", [128, 2, 4, TC], F32)
    dt_sb = sb("dt_sb", [128, 4, TC], F32)
    du_sb = sb("du_sb", [128, 4, TC], F32)
    a_sb = sb("a_sb", [128, TC], F32)
    bhi = sb("bhi", [16, TPC], BF16)
    blo = sb("blo", [16, TPC], BF16)
    chi = sb("chi", [16, TPC], BF16)
    clo = sb("clo", [16, TPC], BF16)
    Bbq = sb("Bbq", [128, 2, 4, TC], F32)
    Cbq = sb("Cbq", [128, 4, TC], F32)
    dAq = sb("dAq", [128, 2, 4, TC], F32)
    dBq = sb("dBq", [128, 4, TC], F32)
    sq = sb("sq", [128, 4, TC], F32)
    y_acc = sb("y_acc", [128, 4, TC], F32)
    ygb = sb("ygb", [128, 2, TC], BF16)
    state_sb = sb("state_sb", [128, 256], F32)
    st_imp = sb("st_imp", [128, 256], F32)
    pr_f = sb("pr_f", [128, 1], F32)
    pr_u = sb("pr_u", [128, 1], U32)

    S_VS = q + "vs"; S_AC = q + "ac"; S_PE = q + "pe"; S_PL = q + "pl"

    r_ag = ctx.nc.gpsimd.alloc_register(q + "rag")
    r_gp = ctx.nc.gpsimd.alloc_register(q + "rgp")
    r_stg = ctx.nc.vector.alloc_register(q + "rstg")
    r_si = ctx.nc.vector.alloc_register(q + "rsi")

    @ctx.on("sync")
    def _(s):
        for kc in range(16):
            s.dma_start(out=wxb[:, kc, :], in_=g["wxT_b"][kc]).then_inc(ctx.sem(q + "wx"), 16)
    n_wx = ctx.bump(q + "wx", 16 * 16)

    n_vz = ctx.bump(S_VS, 1)
    @ctx.on("vector")
    def _(v):
        v.memset(state_sb[:], 0.0)
        v.memset(st_imp[:], 0.0)
        v.engine_nop().then_inc(ctx.sem(S_VS), 1)
    @ctx.on("gpsimd")
    def _(gp):
        gp.iota(pr_f[:], pattern=[[0, 1]], base=0, channel_multiplier=1,
                allow_small_or_imprecise_dtypes=True)
        gp.engine_nop().then_inc(ctx.sem(S_PL), 1)
    n_pl0 = ctx.bump(S_PL, 1)
    n_vpr1 = ctx.bump(S_VS, 1)
    @ctx.on("vector")
    def _(v):
        v.wait_ge(ctx.sem(S_PL), n_pl0)
        v.tensor_scalar_add(pr_f[:], pr_f[:], g["pcf_col"][:, 5:6])
        v.engine_nop().then_inc(ctx.sem(S_VS), 1)
    ctx.bump(S_VS, 1)
    @ctx.on("vector")
    def _(v, w=n_vpr1):
        v.wait_ge(ctx.sem(S_VS), w)
        v.tensor_copy(pr_u[:], pr_f[:])
        v.engine_nop().then_inc(ctx.sem(S_VS), 1)
    n_pr = ctx.val(S_VS)

    dtP, cmp1 = ctx.ps(q + "dtP", [128, 2, TC], F32)
    bP, cmp2 = ctx.ps(q + "bP", [16, TC], F32)
    cP, cmp3 = ctx.ps(q + "cP", [16, TC], F32)
    bcP, cmp4 = ctx.ps(q + "bcP", [128, 4, TC], F32)
    cms += [cmp1, cmp2, cmp3, cmp4]

    scan_hist = {}
    dbu_hist = {}
    cm_hist = {}
    pe_group_rel = {}
    gt_idx = [0]

    for gi in range(NG):
        @ctx.on("gpsimd")
        def _(gp, gi=gi):
            if gi == 0:
                gp.wait_ge(ctx.sem(S_VS), n_pr)
            gp.reg_load(r_gp, g["pcu"][0:1, 8 + gi:9 + gi])
            gp.wait_ge(ctx.sem(q + "gp"), r_gp)
            gp.dma_start(out=g[f"st_in_{gi}"], in_=st_imp[:, gi * 64:(gi + 1) * 64])\
                .then_inc(ctx.sem(q + f"stg{gi}"), 16)
        n_stg = ctx.bump(q + f"stg{gi}", 16)
        @ctx.on("gpsimd")
        def _(gp, gi=gi, n_stg=n_stg):
            gp.wait_ge(ctx.sem(q + f"stg{gi}"), n_stg)
            gp.collective_compute(
                "AllGather", ALU.bypass,
                replica_groups=[list(range(NC))],
                ins=[g[f"st_in_{gi}"].opt()],
                outs=[g[f"st_out_{gi}"][0:NC * 128, :].opt()],
            ).then_inc(ctx.sem(q + f"ag{gi}"), 1)
        ctx.bump(q + f"ag{gi}", 1)
        @ctx.on("gpsimd")
        def _(gp, gi=gi):
            gp.wait_ge(ctx.sem("setup_z"), 16 * NG)
            gp.wait_ge(ctx.sem(q + f"si{gi}"), gi * 16)
            gp.reg_load(r_ag, g["pcu"][0:1, 12 + gi:13 + gi])
            gp.wait_ge(ctx.sem(q + f"ag{gi}"), r_ag)
            gp.indirect_dma_start(
                out=st_imp[:, gi * 64:(gi + 1) * 64],
                out_offset=None,
                in_=g[f"st_out_{gi}"],
                in_offset=IndirectOffsetOnAxis(ap=pr_u[:], axis=0),
            ).then_inc(ctx.sem(q + f"si{gi}"), 16)
        n_si = ctx.bump(q + f"si{gi}", 16)

        @ctx.on("sync")
        def _(s, gi=gi):
            if gi >= 1:
                s.wait_ge(ctx.sem(S_PE), pe_group_rel[gi - 1])
            for ot in range(4):
                s.dma_start(out=wdtg[:, ot, :, :],
                            in_=g["wdtT_b"][gi * 4 + ot].rearrange("a p h -> p a h"))\
                    .then_inc(ctx.sem(q + "wd"), 16)
        n_wd = ctx.bump(q + "wd", 16 * 4)

        for c in range(NCH):
            b = c % 2
            pair = gi * NCH + c
            n_ub = ctx.bump(q + "ub" + str(b), 16 * 16)
            n_ug = ctx.bump(q + "ug" + str(b), 16 * 8)
            @ctx.on("sync")
            def _(s, gi=gi, c=c, b=b, pair=pair):
                if pair >= 2:
                    s.wait_ge(ctx.sem(q + "gt"), pair - 1)
                for kc in range(16):
                    s.dma_start(out=ub2[:, b, kc, :],
                                in_=g["ubf_h"][kc * 128:(kc + 1) * 128, c * TC:(c + 1) * TC])\
                        .then_inc(ctx.sem(q + "ub" + str(b)), 16)
                for dt_i in range(4):
                    r0 = (gi * 4 + dt_i) * 128
                    s.dma_start(out=u2[:, b, dt_i, :],
                                in_=g["u_h"][r0:r0 + 128, c * TC:(c + 1) * TC])\
                        .then_inc(ctx.sem(q + "ug" + str(b)), 16)
                    s.dma_start(out=sg2[:, b, dt_i, :],
                                in_=g["sg_h"][r0:r0 + 128, c * TC:(c + 1) * TC])\
                        .then_inc(ctx.sem(q + "ug" + str(b)), 16)
            for dt_i in range(4):
                pb = (c * 4 + dt_i) % 2
                j = pair * 4 + dt_i
                n_pe = ctx.bump(S_PE, 1)
                @ctx.on("tensor")
                def _(t, gi=gi, c=c, b=b, dt_i=dt_i, pb=pb, j=j, n_ub=n_ub, n_wd=n_wd):
                    if c == 0 and dt_i == 0:
                        t.wait_ge(ctx.sem(q + "wd"), n_wd)
                        if gi == 0:
                            t.wait_ge(ctx.sem(q + "wx"), n_wx)
                    if dt_i == 0:
                        t.wait_ge(ctx.sem(q + "ub" + str(b)), n_ub)
                    if j >= 2:
                        t.wait_ge(ctx.sem(S_AC), spab_hist[j - 2])
                    last = None
                    for kc in range(16):
                        last = t.matmul(dtP[:, pb, :], wdtg[:, dt_i, kc, :], ub2[:, b, kc, :],
                                        start=(kc == 0), stop=(kc == 15))
                    last.then_inc(ctx.sem(S_PE), 1)
                n_sp1 = ctx.bump(S_AC, 2)
                if not (gi == 0 and c == 0 and dt_i == 0):
                    pass
                else:
                    spab_hist = {}
                spab_hist[j] = n_sp1
                w_spl = spl_hist[-1] if (gi + c + dt_i) > 0 else 0
                @ctx.on("scalar")
                def _(sc, gi=gi, dt_i=dt_i, pb=pb, n_pe=n_pe, j=j, w_spl=w_spl):
                    sc.wait_ge(ctx.sem(S_PE), n_pe)
                    if j >= 1:
                        sc.wait_ge(ctx.sem(S_VS), w_spl)
                    bcol = g["bdtcol"][:, gi * 4 + dt_i:gi * 4 + dt_i + 1]
                    sc.activation(out=dt_sb[:, dt_i, :], in_=dtP[:, pb, :], func=AF.Relu, bias=bcol)
                    sc.activation(out=a_sb[:], in_=dtP[:, pb, :], func=AF.Abs, bias=bcol)\
                        .then_inc(ctx.sem(S_AC), 2)
                n_sp2 = ctx.bump(S_AC, 1)
                n_sp3 = ctx.bump(S_AC, 1)
                @ctx.on("scalar")
                def _(sc, n_sp1=n_sp1, n_sp2=n_sp2):
                    sc.wait_ge(ctx.sem(S_AC), n_sp1)
                    sc.activation(out=a_sb[:], in_=a_sb[:], func=AF.Exp, scale=-1.0)\
                        .then_inc(ctx.sem(S_AC), 1)
                    sc.wait_ge(ctx.sem(S_AC), n_sp2)
                    sc.activation(out=a_sb[:], in_=a_sb[:], func=AF.Ln, bias=1.0)\
                        .then_inc(ctx.sem(S_AC), 1)
                if pair == 0 and dt_i == 0:
                    spl_hist = []
                    spl_v_hist = {}
                    da_chunk_hist = {}
                n_vsp = ctx.bump(S_VS, 2)
                spl_v_hist[(gi, c, dt_i)] = n_vsp - 1
                @ctx.on("vector")
                def _(v, gi=gi, c=c, b=b, dt_i=dt_i, pair=pair, n_sp3=n_sp3, n_ug=n_ug,
                      base=n_vsp - 2):
                    v.wait_ge(ctx.sem(S_AC), n_sp3)
                    if pair >= 1:
                        v.wait_ge(ctx.sem(S_AC), da_chunk_hist[pair - 1])
                    v.tensor_add(dt_sb[:, dt_i, :], dt_sb[:, dt_i, :], a_sb[:])\
                        .then_inc(ctx.sem(S_VS), 1)
                    v.wait_ge(ctx.sem(S_VS), base + 1)
                    v.wait_ge(ctx.sem(q + "ug" + str(b)), n_ug)
                    v.tensor_mul(du_sb[:, dt_i, :], dt_sb[:, dt_i, :], u2[:, b, dt_i, :])\
                        .then_inc(ctx.sem(S_VS), 1)
                spl_hist.append(n_vsp - 1)
            n_du_all = ctx.val(S_VS)

            if gi == 0:
                n_bc = ctx.bump(S_PE, 2)
                @ctx.on("tensor")
                def _(t, c=c, b=b):
                    if c >= 2:
                        t.wait_ge(ctx.sem(q + "bcev"), (c - 1) * 4)
                    last = None
                    for kc in range(16):
                        last = t.matmul(bP[:], wxb[:, kc, 0:DS], ub2[:, b, kc, :],
                                        start=(kc == 0), stop=(kc == 15))
                    last.then_inc(ctx.sem(S_PE), 1)
                    last = None
                    for kc in range(16):
                        last = t.matmul(cP[:], wxb[:, kc, DS:2 * DS], ub2[:, b, kc, :],
                                        start=(kc == 0), stop=(kc == 15))
                    last.then_inc(ctx.sem(S_PE), 1)
                n_bhi = ctx.bump(S_AC, 2)
                @ctx.on("scalar")
                def _(sc, c=c, n_bc=n_bc):
                    sc.wait_ge(ctx.sem(S_PE), n_bc)
                    sc.activation(out=bhi[:, c * TC:(c + 1) * TC], in_=bP[:], func=AF.Copy)
                    sc.activation(out=chi[:, c * TC:(c + 1) * TC], in_=cP[:], func=AF.Copy)\
                        .then_inc(ctx.sem(S_AC), 2)
                n_bcev = ctx.bump(q + "bcev", 4)
                @ctx.on("vector")
                def _(v, c=c, n_bhi=n_bhi):
                    v.wait_ge(ctx.sem(S_AC), n_bhi)
                    v.scalar_tensor_tensor(blo[:, c * TC:(c + 1) * TC], bP[:], 0.0,
                                           bhi[:, c * TC:(c + 1) * TC], op0=ALU.bypass, op1=ALU.subtract)
                    v.scalar_tensor_tensor(clo[:, c * TC:(c + 1) * TC], cP[:], 0.0,
                                           chi[:, c * TC:(c + 1) * TC], op0=ALU.bypass, op1=ALU.subtract)
                    v.engine_nop().then_inc(ctx.sem(q + "bcev"), 4)
            n_bcev_need = 4 * (c + 1) if gi == 0 else 4 * NCH

            for qq in range(4):
                it = (pair * 4 + qq)
                n_bb = ctx.bump(S_PE, 2)
                bb_par = it % 2
                @ctx.on("tensor")
                def _(t, gi=gi, c=c, qq=qq, bb_par=bb_par, n_bcev_need=n_bcev_need, it=it):
                    if gi == 0:
                        t.wait_ge(ctx.sem(q + "bcev"), n_bcev_need)
                    elif c == 0 and qq == 0:
                        t.wait_ge(ctx.sem(q + "bcev"), 4 * NCH)
                    if it >= 1:
                        t.wait_ge(ctx.sem(q + "bbev"), it * 2)
                    last = None
                    for nl in range(4):
                        n = qq * 4 + nl
                        t.matmul(bcP[:, nl, :], g["seln"][:, n * 128:(n + 1) * 128],
                                 bhi[:, c * TC:(c + 1) * TC], start=True, stop=False)
                        last = t.matmul(bcP[:, nl, :], g["seln"][:, n * 128:(n + 1) * 128],
                                        blo[:, c * TC:(c + 1) * TC], start=False, stop=True)
                    last.then_inc(ctx.sem(S_PE), 1)
                    t.wait_ge(ctx.sem(q + "bbev"), it * 2 + 1)
                    last = None
                    for nl in range(4):
                        n = qq * 4 + nl
                        t.matmul(bcP[:, nl, :], g["seln"][:, n * 128:(n + 1) * 128],
                                 chi[:, c * TC:(c + 1) * TC], start=True, stop=False)
                        last = t.matmul(bcP[:, nl, :], g["seln"][:, n * 128:(n + 1) * 128],
                                        clo[:, c * TC:(c + 1) * TC], start=False, stop=True)
                    last.then_inc(ctx.sem(S_PE), 1)
                n_bbev1 = ctx.bump(q + "bbev", 1)
                n_bbev2 = ctx.bump(q + "bbev", 1)
                @ctx.on("scalar")
                def _(sc, qb=it % 2, w1=n_bb - 1, w2=n_bb, it=it):
                    sc.wait_ge(ctx.sem(S_PE), w1)
                    if it >= 2:
                        sc.wait_ge(ctx.sem(S_VS), dbu_hist[(it - 2) * 4 + 3])
                    sc.activation(out=Bbq[:, qb], in_=bcP[:, :, :], func=AF.Copy)\
                        .then_inc(ctx.sem(q + "bbev"), 1)
                    sc.wait_ge(ctx.sem(S_PE), w2)
                    if it >= 1:
                        sc.wait_ge(ctx.sem(S_VS), cm_hist[(it - 1) * 4 + 3])
                    sc.activation(out=Cbq[:], in_=bcP[:, :, :], func=AF.Copy)\
                        .then_inc(ctx.sem(q + "bbev"), 1)

                for dt_i in range(4):
                    dcol = gi * 4 + dt_i
                    itd = it * 4 + dt_i
                    ib = itd % 2
                    n_da = ctx.bump(S_AC, 4)
                    @ctx.on("scalar")
                    def _(sc, gi=gi, c=c, qq=qq, dt_i=dt_i, ib=ib, dcol=dcol, itd=itd):
                        sc.wait_ge(ctx.sem(S_VS), spl_v_hist[(gi, c, dt_i)])
                        if itd >= 2:
                            sc.wait_ge(ctx.sem(S_VS), scan_hist[itd - 2])
                        for nl in range(4):
                            acol = g["Acols"][:, dcol * 16 + qq * 4 + nl: dcol * 16 + qq * 4 + nl + 1]
                            sc.activation(out=dAq[:, ib, nl, :], in_=dt_sb[:, dt_i, :],
                                          func=AF.Exp, scale=acol).then_inc(ctx.sem(S_AC), 1)
                    n_db = ctx.bump(S_VS, 1)
                    dbu_hist[itd] = n_db
                    @ctx.on("vector")
                    def _(v, qb=it % 2, dt_i=dt_i, n_bbev1=n_bbev1):
                        v.wait_ge(ctx.sem(q + "bbev"), n_bbev1)
                        v.tensor_tensor(dBq[:],
                                        du_sb[:, dt_i, :].unsqueeze(1).broadcast_to([128, 4, TC]),
                                        Bbq[:, qb], op=ALU.mult)\
                            .then_inc(ctx.sem(S_VS), 1)
                    n_sc = ctx.bump(S_VS, 4)
                    scan_hist[itd] = n_sc
                    @ctx.on("vector")
                    def _(v, gi=gi, c=c, qq=qq, dt_i=dt_i, ib=ib, dcol=dcol,
                          n_da=n_da, n_db=n_db, n_si=n_si):
                        v.wait_ge(ctx.sem(S_AC), n_da)
                        v.wait_ge(ctx.sem(S_VS), n_db)
                        if c == 0 and qq == 0 and dt_i == 0:
                            v.reg_load(r_si, g["pcu"][0:1, 20 + gi:21 + gi])
                            v.wait_ge(ctx.sem(q + f"si{gi}"), r_si)
                        for nl in range(4):
                            scol = dcol * 16 + qq * 4 + nl
                            init = st_imp[:, scol:scol + 1] if c == 0 else state_sb[:, scol:scol + 1]
                            v.tensor_tensor_scan(out=sq[:, nl, :], data0=dAq[:, ib, nl, :],
                                                 data1=dBq[:, nl, :], initial=init,
                                                 op0=ALU.mult, op1=ALU.add)\
                                .then_inc(ctx.sem(S_VS), 1)
                    n_st = ctx.bump(S_VS, 1)
                    @ctx.on("vector")
                    def _(v, gi=gi, c=c, qq=qq, dt_i=dt_i, dcol=dcol, n_sc=n_sc):
                        v.wait_ge(ctx.sem(S_VS), n_sc)
                        scol = dcol * 16 + qq * 4
                        if c == NCH - 1:
                            v.reg_load(r_stg, g["pcu"][0:1, 16 + gi:17 + gi])
                            v.wait_ge(ctx.sem(q + f"stg{gi}"), r_stg)
                            v.tensor_copy(st_imp[:, scol:scol + 4],
                                          sq[:, 0:4, TC - 1:TC].rearrange("p a 1 -> p a"))\
                                .then_inc(ctx.sem(S_VS), 1)
                        else:
                            v.tensor_copy(state_sb[:, scol:scol + 4],
                                          sq[:, 0:4, TC - 1:TC].rearrange("p a 1 -> p a"))\
                                .then_inc(ctx.sem(S_VS), 1)
                    n_cm = ctx.bump(S_VS, 1)
                    cm_hist[itd] = n_cm
                    @ctx.on("vector")
                    def _(v, n_st=n_st):
                        v.wait_ge(ctx.sem(S_VS), n_st)
                        v.tensor_mul(dBq[:], sq[:], Cbq[:])\
                            .then_inc(ctx.sem(S_VS), 1)
                    n_red = ctx.bump(S_VS, 3 if qq == 0 else 4)
                    @ctx.on("vector")
                    def _(v, qq=qq, dt_i=dt_i, n_cm=n_cm, base=n_red - (3 if qq == 0 else 4)):
                        v.wait_ge(ctx.sem(S_VS), n_cm)
                        v.tensor_add(sq[:, 0, :], dBq[:, 0, :], dBq[:, 1, :])\
                            .then_inc(ctx.sem(S_VS), 1)
                        v.tensor_add(sq[:, 1, :], dBq[:, 2, :], dBq[:, 3, :])\
                            .then_inc(ctx.sem(S_VS), 1)
                        v.wait_ge(ctx.sem(S_VS), base + 2)
                        if qq == 0:
                            v.tensor_add(y_acc[:, dt_i, :], sq[:, 0, :], sq[:, 1, :])\
                                .then_inc(ctx.sem(S_VS), 1)
                        else:
                            v.tensor_add(sq[:, 0, :], sq[:, 0, :], sq[:, 1, :])\
                                .then_inc(ctx.sem(S_VS), 1)
                            v.wait_ge(ctx.sem(S_VS), base + 3)
                            v.tensor_add(y_acc[:, dt_i, :], y_acc[:, dt_i, :], sq[:, 0, :])\
                                .then_inc(ctx.sem(S_VS), 1)
            da_chunk_hist[pair] = ctx.val(S_AC)
            if c == NCH - 1:
                ctx.bump(q + "gp", 1)
                @ctx.on("vector")
                def _(v, w=ctx.val(S_VS)):
                    v.wait_ge(ctx.sem(S_VS), w)
                    v.engine_nop().then_inc(ctx.sem(q + "gp"), 1)
            for dt_i in range(4):
                n_g1 = ctx.bump(S_VS, 3)
                ysn = pair * 4 + dt_i
                @ctx.on("vector")
                def _(v, gi=gi, c=c, b=b, dt_i=dt_i, dcol=gi * 4 + dt_i, ysn=ysn,
                      base=n_g1 - 3, n_ug=n_ug):
                    v.wait_ge(ctx.sem(q + "ug" + str(b)), n_ug)
                    if ysn >= 2:
                        v.wait_ge(ctx.sem(q + "ys" + str(ysn % 2)), (ysn // 2) * 16)
                    v.scalar_tensor_tensor(y_acc[:, dt_i, :], u2[:, b, dt_i, :],
                                           g["dcol"][:, dcol:dcol + 1], y_acc[:, dt_i, :],
                                           op0=ALU.mult, op1=ALU.add)\
                        .then_inc(ctx.sem(S_VS), 1)
                    v.wait_ge(ctx.sem(S_VS), base + 1)
                    v.tensor_mul(y_acc[:, dt_i, :], y_acc[:, dt_i, :], sg2[:, b, dt_i, :])\
                        .then_inc(ctx.sem(S_VS), 1)
                    v.wait_ge(ctx.sem(S_VS), base + 2)
                    v.tensor_copy(ygb[:, ysn % 2, :], y_acc[:, dt_i, :])\
                        .then_inc(ctx.sem(S_VS), 1)
                ctx.bump(q + "ys" + str(ysn % 2), 16)
                @ctx.on("sync")
                def _(s, gi=gi, c=c, dt_i=dt_i, ysn=ysn, n_g1=n_g1):
                    s.wait_ge(ctx.sem(S_VS), n_g1)
                    r0 = (gi * 4 + dt_i) * 128
                    s.dma_start(out=g["yg_h"][r0:r0 + 128, c * TC:(c + 1) * TC],
                                in_=ygb[:, ysn % 2, :]).then_inc(ctx.sem(q + "ys" + str(ysn % 2)), 16)
            ctx.bump(q + "gt", 1)
            @ctx.on("vector")
            def _(v, w=ctx.val(S_VS)):
                v.wait_ge(ctx.sem(S_VS), w)
                v.engine_nop().then_inc(ctx.sem(q + "gt"), 1)
        pe_group_rel[gi] = ctx.val(S_PE)

    n_ys0_all = ctx.val(q + "ys0")
    n_ys1_all = ctx.val(q + "ys1")
    n_si_all = ctx.val(q + "si3")
    n_stg_all = ctx.val(q + "stg3")
    n_ag_all = ctx.val(q + "ag3")
    @ctx.on("sync")
    def _(s):
        s.wait_ge(ctx.sem(q + "ys0"), n_ys0_all)
        s.wait_ge(ctx.sem(q + "ys1"), n_ys1_all)
    @ctx.on("gpsimd")
    def _(gp):
        gp.wait_ge(ctx.sem(q + "si3"), n_si_all)
        gp.wait_ge(ctx.sem(q + "stg3"), n_stg_all)
        gp.wait_ge(ctx.sem(q + "ag3"), n_ag_all)
    ctx.close_tensors(cms)


def emit_m3(ctx, g):
    q = "m3"
    cms = []
    def sb(name, shape, dt):
        h, cm = ctx.sb(q + name, shape, dt)
        cms.append(cm)
        return h
    woutb = sb("woutb", [128, 16, D], BF16)
    ygc = sb("ygc", [128, 2, 16, TC], BF16)
    x1r = sb("x1r", [128, 2, D], F32)
    x2r = sb("x2r", [128, 2, D], F32)
    S_VS = q + "vs"

    @ctx.on("sync")
    def _(s):
        for kc in range(16):
            s.dma_start(out=woutb[:, kc, :], in_=g["woutT_b"][kc * 128:(kc + 1) * 128, :])\
                .then_inc(ctx.sem(q + "wl"), 16)
    n_wl = ctx.bump(q + "wl", 16 * 16)

    yoP, cmp = ctx.ps(q + "yoP", [128, 4, 512], F32)
    cms.append(cmp)

    ev_rel = {}
    for c in range(NCH):
        b = c % 2
        n_yl = ctx.bump(q + "yl" + str(b), 16 * 16)
        @ctx.on("sync")
        def _(s, c=c, b=b):
            if c >= 2:
                s.wait_ge(ctx.sem(q + "pe"), pe_rel[c - 2])
            for kc in range(16):
                s.dma_start(out=ygc[:, b, kc, :],
                            in_=g["yg_h"][kc * 128:(kc + 1) * 128, c * TC:(c + 1) * TC])\
                    .then_inc(ctx.sem(q + "yl" + str(b)), 16)
        if c == 0:
            pe_rel = {}
        for ts in range(4):
            tb = ts % 2
            jt = c * 4 + ts
            row0 = c * TC + ts * 128
            n_xl = ctx.bump(q + "xl" + str(tb), 16)
            @ctx.on("sync")
            def _(s, row0=row0, tb=tb, jt=jt):
                if jt >= 2:
                    s.wait_ge(ctx.sem(S_VS), ev_rel[jt - 2])
                s.dma_start(out=x1r[:, tb, :], in_=g["x1b"][3 + row0:3 + row0 + 128, :])\
                    .then_inc(ctx.sem(q + "xl" + str(tb)), 16)
            n_pe = ctx.bump(q + "pe", 2)
            @ctx.on("tensor")
            def _(t, c=c, b=b, ts=ts, jt=jt, n_yl=n_yl, base=n_pe - 2):
                if c == 0 and ts == 0:
                    t.wait_ge(ctx.sem(q + "wl"), n_wl)
                if ts == 0:
                    t.wait_ge(ctx.sem(q + "yl" + str(b)), n_yl)
                for dh in range(2):
                    pb = (jt * 2 + dh) % 4
                    if jt * 2 + dh >= 4:
                        t.wait_ge(ctx.sem(S_VS), ev_rel2[jt * 2 + dh - 4])
                    last = None
                    for kc in range(16):
                        last = t.matmul(yoP[:, pb, :],
                                        ygc[:, b, kc, ts * 128:(ts + 1) * 128],
                                        woutb[:, kc, dh * 512:(dh + 1) * 512],
                                        start=(kc == 0), stop=(kc == 15))
                    last.then_inc(ctx.sem(q + "pe"), 1)
            if c == 0 and ts == 0:
                ev_rel2 = {}
            n_ev = ctx.bump(S_VS, 2)
            @ctx.on("vector")
            def _(v, jt=jt, tb=tb, n_pe=n_pe, n_xl=n_xl, base=n_ev - 2):
                v.wait_ge(ctx.sem(q + "xl" + str(tb)), n_xl)
                for dh in range(2):
                    pb = (jt * 2 + dh) % 4
                    v.wait_ge(ctx.sem(q + "pe"), n_pe - 1 + dh)
                    if jt >= 2:
                        v.wait_ge(ctx.sem(q + "xs" + str(tb)), (jt // 2) * 16)
                    v.scalar_tensor_tensor(x2r[:, tb, dh * 512:(dh + 1) * 512],
                                           yoP[:, pb, :], 0.0, x1r[:, tb, dh * 512:(dh + 1) * 512],
                                           op0=ALU.bypass, op1=ALU.add)
                    v.engine_nop().then_inc(ctx.sem(S_VS), 1)
            ev_rel[jt] = n_ev
            ev_rel2[jt * 2] = n_ev - 1
            ev_rel2[jt * 2 + 1] = n_ev
            ctx.bump(q + "xs" + str(tb), 16)
            @ctx.on("sync")
            def _(s, row0=row0, tb=tb, n_ev=n_ev):
                s.wait_ge(ctx.sem(S_VS), n_ev)
                s.dma_start(out=g["x2b"][row0:row0 + 128, :], in_=x2r[:, tb, :])\
                    .then_inc(ctx.sem(q + "xs" + str(tb)), 16)
        pe_rel[c] = ctx.val(q + "pe")
    @ctx.on("sync")
    def _(s, a=ctx.val(q + "xs0"), bb=ctx.val(q + "xs1")):
        s.wait_ge(ctx.sem(q + "xs0"), a)
        s.wait_ge(ctx.sem(q + "xs1"), bb)
    ctx.close_tensors(cms)


def kernel(**inputs):
    in_maps = stage_inputs(inputs)
    nc = build_program()
    res = run_bass_kernel_spmd(nc, in_maps, core_ids=list(range(NC)))
    shards = [np.asarray(r["out"], np.float32) for r in res.results]
    return np.concatenate(shards, 0).reshape(B, S, D)


# version 05
# speedup vs baseline: 1.0303x; 1.0176x over previous
import re
import numpy as np
import ml_dtypes
import concourse.bass as bass
import concourse.mybir as mybir
from concourse.bass_utils import run_bass_kernel_spmd
from concourse.bass_core_rust import IndirectOffsetOnAxis

F32 = mybir.dt.float32
BF16 = mybir.dt.bfloat16
U32 = mybir.dt.uint32
AF = mybir.ActivationFunctionType
ALU = mybir.AluOpType
AX = mybir.AxisListType

B, S, D, DI, DS, E, H = 4, 8192, 1024, 2048, 16, 16, 4096
NTOK = B * S
CAP = NTOK // E
NC = 8
TPC = NTOK // NC
NTILE = TPC // 128
TC = 512
NCH = TPC // TC
NG = 4
DT_PER_G = 4
TRASH = NTOK
TL = 36992
BIGF = 65535.0
EPS = 1e-5

PCF_COREBASE = 0
PCF_PARITY = 1
PCF_SLOTBASE = 2
PCF_G0 = 4
PCU_PARTNER_ROW = 0
PCU_TABROW = 1


def _u32(x):
    return np.asarray(x, dtype=np.uint32)


class Ctx:

    def __init__(self, nc):
        self.nc = nc
        self.prog = []
        self.sems = {}
        self.cnt = {}
        self._cm = []
        self.bar_k = 0

    _ALIAS = {
        "m1ld0": "Fld0", "m1ld1": "Fld1", "m1ac": "Fac", "m1vs": "Fvs",
        "m1st0": "Fst0", "m1st1": "Fst1", "m1wl": "Fcl2", "m1trh": "Ftrx",
        "m2ac": "Fac", "m2vs": "Fvs", "m2pe": "Fpe",
        "m3vs": "Fvs", "m3pe": "Fpe", "m3wl": "Fcl2",
        "m3yl0": "Fld0", "m3yl1": "Fld1", "m3xl0": "Fxl0", "m3xl1": "Fxl1",
    }

    @staticmethod
    def canon(name):
        n2 = name
        if n2.startswith("f0") or n2.startswith("f1"):
            n2 = "F" + n2[2:]
            if n2.startswith("Fe0") or n2.startswith("Fe1"):
                n2 = "FE" + n2[3:]
        m = re.fullmatch(r"m2(stg|ag|si)\d", n2)
        if m:
            n2 = "m2" + m.group(1)
        return Ctx._ALIAS.get(n2, n2)

    def sem(self, name):
        name = self.canon(name)
        if name not in self.sems:
            cm = self.nc.semaphore(name)
            h = cm.__enter__()
            self._cm.append(cm)
            self.sems[name] = h
            self.cnt[name] = 0
        return self.sems[name]

    def bump(self, name, k):
        name = self.canon(name)
        self.sem(name)
        self.cnt[name] += k
        return self.cnt[name]

    def val(self, name):
        name = self.canon(name)
        self.sem(name)
        return self.cnt[name]

    def on(self, eng):
        def deco(fn):
            self.prog.append((eng, fn))
            return None
        return deco

    def sb(self, name, shape, dtype):
        cm = self.nc.sbuf_tensor(name, shape, dtype)
        h = cm.__enter__()
        return h, cm

    def ps(self, name, shape, dtype):
        cm = self.nc.psum_tensor(name, shape, dtype)
        h = cm.__enter__()
        return h, cm

    def close_tensors(self, cms):
        for cm in reversed(cms):
            cm.__exit__(None, None, None)

    def barrier(self, tag, engines=("sync", "vector", "scalar", "tensor", "gpsimd")):
        bar = self.sem("bar")
        n = len(engines)
        base = self.cnt["bar"]
        tgt = base + n
        self.cnt["bar"] = tgt
        for e in engines:
            k = self.bar_k
            if e == "scalar":
                self.bar_k += 1
            @self.on(e)
            def _(eng, bar=bar, tgt=tgt, e=e, k=k):
                if e in ("vector", "gpsimd"):
                    eng.engine_nop().then_inc(bar, 1)
                elif e == "scalar":
                    eng.wait_ge(self.sem("dumz"), 1)
                    eng.activation(out=self.dum[0:1, 2 + k:3 + k], in_=self.dum[0:1, 1:2],
                                   func=mybir.ActivationFunctionType.Copy).then_inc(bar, 1)
                else:
                    eng.nop().then_inc(bar, 1)
                eng.wait_ge(bar, tgt)

    def finalize(self):
        for cm in reversed(self._cm):
            cm.__exit__(None, None, None)


def stage_inputs(inp):
    x = np.asarray(inp["L0_attn_x"], np.float32).reshape(NTOK, D)
    W_in = np.asarray(inp["L0_attn_W_in"], np.float32)
    conv_w = np.asarray(inp["L0_attn_conv_w"], np.float32).reshape(DI, 4)
    conv_b = np.asarray(inp["L0_attn_conv_b"], np.float32)
    W_x = np.asarray(inp["L0_attn_W_x"], np.float32)
    W_dt = np.asarray(inp["L0_attn_W_dt"], np.float32)
    b_dt = np.asarray(inp["L0_attn_b_dt"], np.float32)
    A_log = np.asarray(inp["L0_attn_A_log"], np.float32)
    Dvec = np.asarray(inp["L0_attn_D"], np.float32)
    W_out = np.asarray(inp["L0_attn_W_out"], np.float32)
    Wg = np.asarray(inp["L0_ffn_Wg"], np.float32)
    bg = np.asarray(inp["L0_ffn_bg"], np.float32)
    W1 = np.asarray(inp["L0_ffn_W1"], np.float32)
    b1 = np.asarray(inp["L0_ffn_b1"], np.float32)
    W2 = np.asarray(inp["L0_ffn_W2"], np.float32)
    b2 = np.asarray(inp["L0_ffn_b2"], np.float32)

    winT = np.ascontiguousarray(W_in.T)
    wdtT = np.ascontiguousarray(W_dt.T)
    woutT = np.ascontiguousarray(W_out.T)
    wxT = np.ascontiguousarray(W_x.T)

    in_maps = []
    for c in range(NC):
        t0 = c * TPC
        idx = np.clip(np.arange(t0 - 3, t0 + TPC), 0, NTOK - 1)
        pc_f = np.zeros((1, 32), np.float32)
        pc_f[0, PCF_COREBASE] = t0
        pc_f[0, PCF_PARITY] = c % 2
        pc_f[0, PCF_SLOTBASE] = c * TPC
        pc_f[0, 5] = (c - 1) * 128 if c % 2 == 1 else NC * 128
        pc_u = np.zeros((1, 32), np.uint32)
        pc_u[0, PCU_PARTNER_ROW] = (c - 1) * 128 if c % 2 == 1 else NC * 128
        pc_u[0, PCU_TABROW] = (c * TPC) // 32
        for gg in range(NG):
            pc_u[0, 8 + gg] = 0 if c % 2 == 1 else gg + 1
            pc_u[0, 12 + gg] = gg + (c % 2)
            pc_u[0, 16 + gg] = (gg + (c % 2)) * 16
            pc_u[0, 20 + gg] = (c % 2) * (gg + 1) * 16
        in_maps.append({
            "x_buf": np.ascontiguousarray(x[idx]),
            "winT": winT, "wdtT": wdtT, "woutT": woutT, "wxT": wxT,
            "convw": np.ascontiguousarray(conv_w.reshape(16, 128, 4).transpose(1, 2, 0)),
            "convb": np.ascontiguousarray(conv_b.reshape(16, 128).T),
            "bdt": np.ascontiguousarray(b_dt.reshape(16, 128).T),
            "alog": np.ascontiguousarray(A_log.reshape(16, 128, DS).transpose(1, 0, 2)),
            "dvec": np.ascontiguousarray(Dvec.reshape(16, 128).T),
            "wg": Wg, "bgv": bg.reshape(1, E),
            "w1": np.ascontiguousarray(W1[2 * c:2 * c + 2]),
            "w2": np.ascontiguousarray(W2[2 * c:2 * c + 2]),
            "b1v": np.ascontiguousarray(b1[2 * c:2 * c + 2].reshape(2, 32, 128).transpose(0, 2, 1)),
            "b2v": np.ascontiguousarray(b2[2 * c:2 * c + 2]),
            "pc_f": pc_f, "pc_u": pc_u,
        })
    return in_maps


def build_program(phases=("w", "f1", "m1", "m2", "m3", "f2"), dbg=()):
    nc = bass.Bass("TRN2", target_bir_lowering=False)
    ctx = Ctx(nc)
    g = {}
    _dum_cm = nc.sbuf_tensor("bar_dum", [1, 64], F32)
    ctx.dum = _dum_cm.__enter__()
    @ctx.on("scalar")
    def _(sc):
        sc.memzero(ctx.dum[:]).then_inc(ctx.sem("dumz"), 1)
    ctx.bump("dumz", 1)

    def ein(name, shape, dt=F32):
        g[name] = nc.dram_tensor(name, shape, dt, kind="ExternalInput").ap()
    ein("x_buf", [TPC + 3, D])
    ein("winT", [D, 2 * DI]); ein("wdtT", [DI, DI]); ein("woutT", [DI, D])
    ein("wxT", [DI, 2 * DS])
    ein("convw", [128, 4, 16]); ein("convb", [128, 16]); ein("bdt", [128, 16])
    ein("alog", [128, 16, DS]); ein("dvec", [128, 16])
    ein("wg", [D, E]); ein("bgv", [1, E])
    ein("w1", [2, D, H]); ein("w2", [2, H, D])
    ein("b1v", [2, 128, 32]); ein("b2v", [2, D])
    ein("pc_f", [1, 32]); ein("pc_u", [1, 32], U32)

    g["out"] = nc.dram_tensor("out", [TPC, D], F32, kind="ExternalOutput").ap()

    def dscr(name, shape, dt):
        g[name] = nc.dram_tensor(name, shape, dt, kind="Internal").ap()
    dscr("winT_b", [32, 128, 8 * 128], BF16)
    dscr("wdtT_b", [16, 128, 16 * 128], BF16)
    dscr("woutT_b", [DI, D], BF16)
    dscr("wxT_b", [16, 128, 2 * DS], BF16)
    dscr("wg_b", [8, 128, E], BF16)
    dscr("w1_b", [2, 32, 128, 8 * 128], BF16)
    dscr("w2_b", [2, H, D], BF16)
    dscr("xln_ag_in", [TPC, D], BF16); dscr("xln_full", [NTOK, D], BF16)
    dscr("tab_in", [TL, 1], U32); dscr("tab_out", [(NC * TL) // 16, 16], U32)
    dscr("cnt_in", [1, 32], F32); dscr("cnt_out", [NC, 32], F32)
    dscr("y_in", [TPC, D], F32); dscr("y_full", [NTOK, D], F32)
    dscr("xg", [CAP, D], BF16)
    dscr("x1b", [TPC + 3, D], F32)
    dscr("xln1", [TPC + 3, D], BF16)
    dscr("u_h", [DI, TPC], F32); dscr("sg_h", [DI, TPC], F32)
    dscr("ubf_h", [DI, TPC], BF16); dscr("yg_h", [DI, TPC], BF16)
    dscr("x2b", [TPC, D], F32)
    dscr("hmx", [1, 6], F32)
    dscr("route_dbg", [128, 4 * NTILE], F32)
    dscr("hmst", [2, 128], F32)
    for gg in range(NG):
        dscr(f"st_in_{gg}", [128, DT_PER_G * DS], F32)
        dscr(f"st_out_{gg}", [(NC + 1) * 128, DT_PER_G * DS], F32)

    for name, shape, dt in dbg:
        g["dbg_" + name] = nc.dram_tensor("dbg_" + name, shape, dt, kind="ExternalOutput").ap()

    sb_keep = []
    def SB(name, shape, dt):
        h, cm = ctx.sb(name, shape, dt)
        sb_keep.append(cm)
        g[name] = h
        return h

    SB("pcf", [1, 32], F32)
    SB("pcu", [1, 32], U32)
    SB("pcf_col", [128, 32], F32)
    SB("iotaP", [128, 1], F32)
    SB("iotaE", [128, E], F32)
    SB("ltU", [128, 128], BF16)
    SB("ones128", [128, 1], BF16)
    SB("onesK1", [1, 128], BF16)
    SB("idN", [128, 128], BF16)
    SB("seln", [E, 128 * E], BF16)
    SB("trashc", [128, NTILE], F32)
    SB("Acols", [128, 16 * DS], F32)
    SB("dcol", [128, 16], F32)
    SB("bdtcol", [128, 16], F32)
    SB("cbcol", [128, 16], F32)
    SB("cwcol", [128, 16 * 4], F32)
    SB("zero128", [128, 64], F32)
    SB("cstF", [128, 8], F32)

    g["r_ntok"] = nc.gpsimd.alloc_register("r_ntok")

    def emit_setup():
        @ctx.on("gpsimd")
        def _(gp):
            gp.reg_mov(g["r_ntok"], NTOK - 1)
        consts = [1e-5, 1.0, -0.5, -1.0, 0.5, 2.0]
        @ctx.on("vector")
        def _(v):
            for j, cv in enumerate(consts):
                v.memset(g["cstF"][:, j:j + 1], float(cv))
            v.engine_nop().then_inc(ctx.sem("setup_v"), 1)
        ctx.bump("setup_v", 1)
        for j, cv in enumerate(consts):
            nc.const_aps.register(F32, float(cv), g["cstF"][:, j:j + 1])

        ld = "s_ld"
        @ctx.on("sync")
        def _(s):
            s.dma_start(out=g["pcf"][:], in_=g["pc_f"]).then_inc(ctx.sem(ld), 16)
            s.dma_start(out=g["pcu"][:], in_=g["pc_u"]).then_inc(ctx.sem(ld), 16)
            s.dma_start(out=g["pcf_col"][:], in_=g["pc_f"].broadcast_to([128, 32])).then_inc(ctx.sem(ld), 16)
            s.dma_start(out=g["bdtcol"][:], in_=g["bdt"]).then_inc(ctx.sem(ld), 16)
            s.dma_start(out=g["cbcol"][:], in_=g["convb"]).then_inc(ctx.sem(ld), 16)
            s.dma_start(out=g["dcol"][:], in_=g["dvec"]).then_inc(ctx.sem(ld), 16)
            s.dma_start(out=g["cwcol"][:].rearrange("p (k a) -> p k a", k=4), in_=g["convw"]).then_inc(ctx.sem(ld), 16)
            s.dma_start(out=g["Acols"][:].rearrange("p (a n) -> p a n", a=16), in_=g["alog"]).then_inc(ctx.sem(ld), 16)
        ctx.bump(ld, 16 * 8)
        ldv = ctx.val(ld)

        @ctx.on("gpsimd")
        def _(gp):
            gp.iota(g["iotaP"][:], pattern=[[0, 1]], base=0, channel_multiplier=1,
                    allow_small_or_imprecise_dtypes=True)
            gp.iota(g["iotaE"][:], pattern=[[1, E]], base=0, channel_multiplier=0,
                    allow_small_or_imprecise_dtypes=True)
            gp.iota(g["trashc"][:], pattern=[[128, NTILE]], base=TRASH, channel_multiplier=1,
                    allow_small_or_imprecise_dtypes=True)
            gp.memset(g["zero128"][:], 0.0)
            gp.engine_nop().then_inc(ctx.sem("setup_g"), 1)
        ctx.bump("setup_g", 1)

        tmpi, cmi = ctx.sb("tmp_ii", [128, 128], F32)
        tmpj, cmj = ctx.sb("tmp_jj", [128, 128], F32)
        @ctx.on("gpsimd")
        def _(gp):
            gp.iota(tmpi[:], pattern=[[0, 128]], base=0, channel_multiplier=1,
                    allow_small_or_imprecise_dtypes=True)
            gp.iota(tmpj[:], pattern=[[1, 128]], base=0, channel_multiplier=0,
                    allow_small_or_imprecise_dtypes=True)
            gp.engine_nop().then_inc(ctx.sem("setup_g"), 1)
        ctx.bump("setup_g", 1)
        sg = ctx.val("setup_g")

        @ctx.on("vector")
        def _(v):
            v.wait_ge(ctx.sem("setup_g"), sg)
            v.wait_ge(ctx.sem(ld), ldv)
            v.tensor_tensor(out=g["idN"][:], in0=tmpi[:], in1=tmpj[:], op=ALU.is_equal)
            v.tensor_tensor(out=g["ltU"][:], in0=tmpi[:], in1=tmpj[:], op=ALU.is_lt)
            v.memset(g["ones128"][:], 1.0)
            v.memset(g["onesK1"][:], 1.0)
            v.engine_nop().then_inc(ctx.sem("setup_v"), 1)
        ctx.bump("setup_v", 1)
        si, cmsi = ctx.sb("tmp_si", [16, 128 * E], F32)
        sj, cmsj = ctx.sb("tmp_sj", [16, 128 * E], F32)
        @ctx.on("gpsimd")
        def _(gp):
            gp.iota(si[:], pattern=[[0, 128 * E]], base=0, channel_multiplier=1,
                    allow_small_or_imprecise_dtypes=True)
            gp.iota(sj[:], pattern=[[1, E], [0, 128]], base=0, channel_multiplier=0,
                    allow_small_or_imprecise_dtypes=True)
            gp.engine_nop().then_inc(ctx.sem("setup_g"), 1)
        ctx.bump("setup_g", 1)
        sg2 = ctx.val("setup_g")
        @ctx.on("vector")
        def _(v):
            v.wait_ge(ctx.sem("setup_g"), sg2)
            v.tensor_tensor(out=g["seln"][:], in0=si[:], in1=sj[:], op=ALU.is_equal)
            v.engine_nop().then_inc(ctx.sem("setup_v"), 1)
        ctx.bump("setup_v", 1)
        ctx.close_tensors([cmsi, cmsj])

        @ctx.on("scalar")
        def _(sc):
            sc.wait_ge(ctx.sem(ld), ldv)
            sc.activation(out=g["Acols"][:], in_=g["Acols"][:], func=AF.Exp).then_inc(ctx.sem("setup_a"), 1)
        ctx.bump("setup_a", 1)
        sa = ctx.val("setup_a")
        @ctx.on("vector")
        def _(v):
            v.wait_ge(ctx.sem("setup_a"), sa)
            v.tensor_scalar_mul(g["Acols"][:], g["Acols"][:], -1.0)
            v.engine_nop().then_inc(ctx.sem("setup_v"), 1)
        ctx.bump("setup_v", 1)

        n_sg_for_zero = ctx.val("setup_g")
        @ctx.on("sync")
        def _(s):
            s.wait_ge(ctx.sem("setup_g"), n_sg_for_zero)
            for gg in range(NG):
                s.dma_start(out=g[f"st_out_{gg}"][NC * 128:(NC + 1) * 128, :], in_=g["zero128"][:])\
                    .then_inc(ctx.sem("setup_z"), 16)
        ctx.bump("setup_z", 16 * NG)
        ctx.close_tensors([cmi, cmj])

    def emit_wconv():
        wtile, cmw = ctx.sb("wconv_tile", [128, 2, 4096], BF16)
        jobs = []
        jobs.append((g["wg"].rearrange("(a p) n -> p a n", p=128),
                     g["wg_b"].rearrange("a p n -> p a n")))
        jobs.append((g["wxT"].rearrange("(a p) n -> p a n", p=128),
                     g["wxT_b"].rearrange("a p n -> p a n")))
        for o in range(32):
            jobs.append((g["winT"][:, o * 128:(o + 1) * 128].rearrange("(a p) h -> p a h", p=128),
                         g["winT_b"][o].rearrange("p (a h) -> p a h", a=8)))
        for o in range(16):
            jobs.append((g["wdtT"][:, o * 128:(o + 1) * 128].rearrange("(a p) h -> p a h", p=128),
                         g["wdtT_b"][o].rearrange("p (a h) -> p a h", a=16)))
        for r0 in range(0, DI, 128):
            jobs.append((g["woutT"][r0:r0 + 128, :], g["woutT_b"][r0:r0 + 128, :]))
        for e in range(2):
            for hc in range(32):
                jobs.append((g["w1"][e][:, hc * 128:(hc + 1) * 128].rearrange("(a p) h -> p a h", p=128),
                             g["w1_b"][e, hc].rearrange("p (a h) -> p a h", a=8)))
            for r0 in range(0, H, 128):
                jobs.append((g["w2"][e][r0:r0 + 128, :], g["w2_b"][e][r0:r0 + 128, :]))
        buf = 0
        for (src, dst) in jobs:
            fs = int(np.prod(src.shape[1:]))
            sview = wtile[:, buf % 2, 0:fs]
            if len(src.shape) == 3:
                sview = sview.rearrange("p (a n) -> p a n", a=src.shape[1])
            bi = buf % 2
            j = buf
            buf += 1
            st_need = (j // 2) * 16
            n_cast = ctx.bump(f"wc_cast{bi}", 16)
            @ctx.on("gpsimd")
            def _(gp, src=src, sview=sview, st_need=st_need, bi=bi):
                gp.wait_ge(ctx.sem(f"wc_st{bi}"), st_need)
                gp.dma_start(out=sview, in_=src).then_inc(ctx.sem(f"wc_cast{bi}"), 16)
            ctx.bump(f"wc_st{bi}", 16)
            @ctx.on("sync")
            def _(s, dst=dst, sview=sview, n_cast=n_cast, bi=bi):
                s.wait_ge(ctx.sem(f"wc_cast{bi}"), n_cast)
                s.dma_start(out=dst, in_=sview).then_inc(ctx.sem(f"wc_st{bi}"), 16)
        @ctx.on("sync")
        def _(s, a=ctx.val("wc_st0"), b=ctx.val("wc_st1")):
            s.wait_ge(ctx.sem("wc_st0"), a)
            s.wait_ge(ctx.sem("wc_st1"), b)
        @ctx.on("gpsimd")
        def _(gp, a=ctx.val("wc_st0"), b=ctx.val("wc_st1")):
            gp.wait_ge(ctx.sem("wc_st0"), a)
            gp.wait_ge(ctx.sem("wc_st1"), b)
        ctx.close_tensors([cmw])

    emit_setup()
    ctx.barrier("after_setup")
    if "w" in phases:
        emit_wconv()
    ctx.barrier("after_w")

    if "f1" in phases:
        emit_ffn(ctx, g, call=0)
    ctx.barrier("after_f1")
    if "m1" in phases:
        emit_m1(ctx, g)
    ctx.barrier("after_m1")
    if "m2" in phases:
        emit_m2(ctx, g)
    ctx.barrier("after_m2")
    if "m3" in phases:
        emit_m3(ctx, g)
    ctx.barrier("after_m3")
    if "f2" in phases:
        emit_ffn(ctx, g, call=1)
    ctx.barrier("after_all")

    if dbg:
        for name, shape, dt in dbg:
            ctx.bump("dbgs", 16)
            @ctx.on("sync")
            def _(s, name=name):
                s.dma_start(out=g["dbg_" + name], in_=g[name]).then_inc(ctx.sem("dbgs"), 16)
        n_dbg = ctx.val("dbgs")
        @ctx.on("sync")
        def _(s):
            s.wait_ge(ctx.sem("dbgs"), n_dbg)

    with nc.Block() as block:
        @block.sync
        def _(eng):
            for e, fn in ctx.prog:
                if e == "sync":
                    fn(eng)
        @block.vector
        def _(eng):
            for e, fn in ctx.prog:
                if e == "vector":
                    fn(eng)
        @block.scalar
        def _(eng):
            for e, fn in ctx.prog:
                if e == "scalar":
                    fn(eng)
        @block.tensor
        def _(eng):
            for e, fn in ctx.prog:
                if e == "tensor":
                    fn(eng)
        @block.gpsimd
        def _(eng):
            for e, fn in ctx.prog:
                if e == "gpsimd":
                    fn(eng)

    ctx.finalize()
    return nc


def emit_ffn(ctx, g, call):
    nc = ctx.nc
    q = f"f{call}"
    src = g["x_buf"] if call == 0 else g["x2b"]
    soff = 3 if call == 0 else 0
    dst = g["x1b"] if call == 0 else g["out"]
    doff = 3 if call == 0 else 0

    cms = []
    cms2 = []
    def sb(name, shape, dt, grp=None):
        h, cm = ctx.sb(q + name, shape, dt)
        (cms if grp is None else grp).append(cm)
        return h

    idxf_a = sb("idxf_a", [128, NTILE], F32)
    pos2_a = sb("pos2_a", [128, NTILE], F32)
    gate_a = sb("gate_a", [128, NTILE], F32)
    comb_a = sb("comb_a", [128, NTILE], F32)
    slkf_a = sb("slkf_a", [128, NTILE], F32)
    slot_u = sb("slot_u", [128, NTILE], U32)
    tabs_u = sb("tabs_u", [128, NTILE], U32)
    payl_u = sb("payl_u", [128, NTILE], U32)
    payl_f = sb("payl_f", [128, NTILE], F32)
    cnt_sb = sb("cnt_sb", [1, 32], F32)
    iota8 = sb("iota8", [8, 1], F32)
    cld = sb("cld", [8, 32], F32)
    mask8 = sb("mask8", [8, 2], F32)
    baser = sb("baser", [1, E], F32)
    bases = sb("bases", [128, E], F32)
    mgi = sb("mgi", [128, 2, 8], U32)
    mgf = sb("mgf", [128, 2, 8, 16], F32)
    tos_f = sb("tos_f", [128, 2, 16], F32)
    tos_u = sb("tos_u", [128, 2, 16], U32)
    hcol = sb("hcol", [128, 2], F32)
    hcu = sb("hcu", [128, 1], U32)
    gbuf = sb("gbuf", [128, 2, D], BF16)
    yrow = sb("yrow", [128, 2, D], F32)
    x1row = sb("x1row", [128, 2, D], F32)
    xr = sb("xr", [128, 2, D], F32, cms2)
    scr = sb("scr", [128, D], F32, cms2)
    col = sb("col", [128, 2, 8], F32, cms2)
    xlt = sb("xlt", [128, 2, D], BF16, cms2)
    xlnT = sb("xlnT", [128, 8, TPC], BF16, cms2)
    wgb = sb("wgb", [128, 8, E], BF16, cms2)
    bgbc = sb("bgbc", [128, E], F32, cms2)
    lg = sb("lg", [128, 2, E], F32, cms2)
    tcol = sb("tcol", [128, 2, 16], F32, cms2)
    idx8 = sb("idx8", [128, 8], U32, cms2)
    ex = sb("ex", [128, 2, E], F32, cms2)
    oh = sb("oh", [128, 2, E], F32, cms2)
    base_r = sb("base_r", [1, 2, E], F32, cms2)
    posS = sb("posS", [128, E], F32, cms2)
    ohs = sb("ohs", [128, E], F32, cms2)
    ltUf = sb("ltUf", [128, 128], F32, cms2)
    base_run = sb("base_run", [1, E], F32, cms2)
    onesK1f = sb("onesK1f", [1, 128], F32, cms2)
    ones128f = sb("ones128f", [128, 1], F32, cms2)
    ffff = sb("ffff", [128, TL // 128], U32, cms2)
    tbf = sb("tbf", [128, NTILE], F32, cms2)

    S_LD = q + "ld"; S_VS = q + "vs"; S_AC = q + "ac"; S_PE = q + "pe"
    S_LG = q + "lg"; S_ST = q + "st"; S_AGX = q + "agx"; S_TRX = q + "trx"
    S_SC = q + "sc"; S_AGT = q + "agt"; S_AGC = q + "agc"; S_MG = q + "mg"
    S_AGY = q + "agy"; S_GA = q + "ga"; S_EX = q + "ex"; S_YS = q + "ys"
    S_OUT = q + "out"
    base_st = [ctx.val(S_ST + "0"), ctx.val(S_ST + "1")]
    base_gst = [ctx.val(q + "gst0"), ctx.val(q + "gst1")]
    base_yst = [ctx.val(q + "yst0"), ctx.val(q + "yst1")]
    base_out = [ctx.val(S_OUT + "0"), ctx.val(S_OUT + "1")]
    base_ev = ctx.val(q + "ev")
    base_peB = ctx.val(S_PE)

    @ctx.on("sync")
    def _(s):
        s.dma_start(out=wgb[:], in_=g["wg_b"].rearrange("a p n -> p a n")).then_inc(ctx.sem(q + "cl2"), 16)
        s.dma_start(out=bgbc[:], in_=g["bgv"].broadcast_to([128, E])).then_inc(ctx.sem(q + "cl2"), 16)
    ctx.bump(q + "cl2", 32)

    @ctx.on("gpsimd")
    def _(gp):
        gp.iota(ffff[:], pattern=[[0, TL // 128]], base=-1, channel_multiplier=0,
                allow_small_or_imprecise_dtypes=True)
        gp.iota(iota8[:], pattern=[[0, 1]], base=0, channel_multiplier=1,
                allow_small_or_imprecise_dtypes=True)
        gp.iota(payl_f[:], pattern=[[128, NTILE]], base=0, channel_multiplier=1,
                allow_small_or_imprecise_dtypes=True)
        gp.engine_nop().then_inc(ctx.sem(q + "setg"), 1)
    ctx.bump(q + "setg", 1)
    v_setg = ctx.val(q + "setg")

    @ctx.on("vector")
    def _(v):
        v.wait_ge(ctx.sem(q + "setg"), v_setg)
        v.tensor_copy(ltUf[:], g["ltU"][:])
        v.memset(onesK1f[:], 1.0)
        v.memset(ones128f[:], 1.0)
        v.tensor_scalar_add(payl_f[:], payl_f[:], g["pcf_col"][:, PCF_COREBASE:PCF_COREBASE + 1])
        v.engine_nop().then_inc(ctx.sem(S_VS), 1)
    n_vs0 = ctx.bump(S_VS, 1)
    @ctx.on("vector")
    def _(v):
        v.wait_ge(ctx.sem(S_VS), n_vs0)
        v.tensor_copy(payl_u[:], payl_f[:])
        v.engine_nop().then_inc(ctx.sem(S_VS), 1)
    ctx.bump(S_VS, 1)

    @ctx.on("sync")
    def _(s):
        s.wait_ge(ctx.sem(q + "setg"), v_setg)
        s.dma_start(out=g["tab_in"].rearrange("(p f) 1 -> p f", p=128), in_=ffff[:])\
            .then_inc(ctx.sem(q + "tz"), 16)
    ctx.bump(q + "tz", 16)

    for i in range(NTILE):
        b = i % 2
        n_ld = ctx.bump(S_LD + str(b), 16)
        @ctx.on("sync")
        def _(s, i=i, b=b):
            if i >= 2:
                s.wait_ge(ctx.sem(S_AC), ac_rel[i - 2])
            s.dma_start(out=xr[:, b, :], in_=src[soff + i * 128: soff + (i + 1) * 128, :])\
                .then_inc(ctx.sem(S_LD + str(b)), 16)
        if i == 0:
            ac_rel = {}
        n_ac = ctx.bump(S_AC, 1)
        @ctx.on("scalar")
        def _(sc, i=i, b=b, n_ld=n_ld):
            sc.wait_ge(ctx.sem(S_LD + str(b)), n_ld)
            sc.activation(out=scr[:], in_=xr[:, b, :], func=AF.Square,
                          accum_out=col[:, b, 1:2]).then_inc(ctx.sem(S_AC), 1)
        n_v1 = ctx.bump(S_VS, 1)
        @ctx.on("vector")
        def _(v, i=i, b=b, n_ld=n_ld, n_ac=n_ac):
            v.wait_ge(ctx.sem(S_LD + str(b)), n_ld)
            v.wait_ge(ctx.sem(S_AC), n_ac)
            v.reduce_sum(col[:, b, 0:1], xr[:, b, :], axis=AX.X)
            v.engine_nop().then_inc(ctx.sem(S_VS), 1)
        n_v2 = ctx.bump(S_VS, 1)
        @ctx.on("vector")
        def _(v, b=b, n_v1=n_v1):
            v.wait_ge(ctx.sem(S_VS), n_v1)
            v.tensor_scalar_mul(col[:, b, 2:3], col[:, b, 0:1], 1.0 / D)
            v.tensor_scalar_mul(col[:, b, 3:4], col[:, b, 1:2], 1.0 / D)
            v.engine_nop().then_inc(ctx.sem(S_VS), 1)
        n_v3 = ctx.bump(S_VS, 1)
        @ctx.on("vector")
        def _(v, b=b, n_v2=n_v2):
            v.wait_ge(ctx.sem(S_VS), n_v2)
            v.tensor_tensor(col[:, b, 6:7], col[:, b, 2:3], col[:, b, 2:3], op=ALU.mult)
            v.engine_nop().then_inc(ctx.sem(S_VS), 1)
        n_v4 = ctx.bump(S_VS, 1)
        @ctx.on("vector")
        def _(v, b=b, n_v3=n_v3):
            v.wait_ge(ctx.sem(S_VS), n_v3)
            v.tensor_sub(col[:, b, 3:4], col[:, b, 3:4], col[:, b, 6:7])
            v.engine_nop().then_inc(ctx.sem(S_VS), 1)
        n_ac1b = ctx.bump(S_AC, 1)
        n_ac2 = ctx.bump(S_AC, 1)
        @ctx.on("scalar")
        def _(sc, b=b, n_v4=n_v4, n_ac1b=n_ac1b):
            sc.wait_ge(ctx.sem(S_VS), n_v4)
            sc.activation(out=col[:, b, 4:5], in_=col[:, b, 3:4], func=AF.Ln, bias=EPS)\
                .then_inc(ctx.sem(S_AC), 1)
            sc.wait_ge(ctx.sem(S_AC), n_ac1b)
            sc.activation(out=col[:, b, 4:5], in_=col[:, b, 4:5], func=AF.Exp, scale=-0.5)\
                .then_inc(ctx.sem(S_AC), 1)
        n_v5 = ctx.bump(S_VS, 1)
        @ctx.on("vector")
        def _(v, b=b, n_ac2=n_ac2):
            v.wait_ge(ctx.sem(S_AC), n_ac2)
            v.scalar_tensor_tensor(col[:, b, 5:6], col[:, b, 2:3], -1.0, col[:, b, 4:5],
                                   op0=ALU.mult, op1=ALU.mult)
            v.engine_nop().then_inc(ctx.sem(S_VS), 1)
        n_ac3 = ctx.bump(S_AC, 1)
        ac_rel[i] = n_ac3
        @ctx.on("scalar")
        def _(sc, i=i, b=b, n_v5=n_v5):
            sc.wait_ge(ctx.sem(S_VS), n_v5)
            if i >= 2:
                sc.wait_ge(ctx.sem(S_ST + str(b)), base_st[b] + (i // 2) * 16)
            sc.activation(out=xlt[:, b, :], in_=xr[:, b, :], func=AF.Identity,
                          scale=col[:, b, 4:5], bias=col[:, b, 5:6]).then_inc(ctx.sem(S_AC), 1)
        ctx.bump(S_ST + str(b), 16)
        @ctx.on("sync")
        def _(s, i=i, b=b, n_ac3=n_ac3):
            s.wait_ge(ctx.sem(S_AC), n_ac3)
            s.dma_start(out=g["xln_ag_in"][i * 128:(i + 1) * 128, :], in_=xlt[:, b, :])\
                .then_inc(ctx.sem(S_ST + str(b)), 16)

    n_st_all0 = ctx.val(S_ST + "0")
    n_st_all1 = ctx.val(S_ST + "1")


    @ctx.on("gpsimd")
    def _(gp):
        gp.wait_ge(ctx.sem(S_ST + "0"), n_st_all0)
        gp.wait_ge(ctx.sem(S_ST + "1"), n_st_all1)
        gp.collective_compute("AllGather", ALU.bypass,
                              replica_groups=[list(range(NC))],
                              ins=[g["xln_ag_in"].opt()], outs=[g["xln_full"].opt()])\
            .then_inc(ctx.sem(S_AGX), 1)
    ctx.bump(S_AGX, 1)
    n_agx = ctx.val(S_AGX)

    for kc in range(8):
        ctx.bump(S_TRX, 16)
        @ctx.on("scalar")
        def _(sc, kc=kc):
            if kc == 0:
                sc.wait_ge(ctx.sem(S_ST + "0"), n_st_all0)
                sc.wait_ge(ctx.sem(S_ST + "1"), n_st_all1)
            sc.dma_start_transpose(out=xlnT[:, kc, :],
                                   in_=g["xln_ag_in"][:, kc * 128:(kc + 1) * 128])\
                .then_inc(ctx.sem(S_TRX), 16)
    n_trx = ctx.val(S_TRX)

    lgP, cm1 = ctx.ps(q + "lgP", [128, 2, E], F32)
    posP, cm2 = ctx.ps(q + "posP", [128, E], F32)
    cntP, cm3 = ctx.ps(q + "cntP", [1, 2, E], F32)
    cms += [cm1, cm2, cm3]
    n_vbr = ctx.bump(S_VS, 1)
    @ctx.on("vector")
    def _(v):
        v.memset(base_run[:], 0.0)
        v.engine_nop().then_inc(ctx.sem(S_VS), 1)

    n_cl2 = ctx.val(q + "cl2")
    for i in range(NTILE):
        b = i % 2
        n_lg = ctx.bump(S_LG, 1)
        @ctx.on("tensor")
        def _(t, i=i, b=b):
            if i == 0:
                t.wait_ge(ctx.sem(S_TRX), n_trx)
                t.wait_ge(ctx.sem(q + "cl2"), n_cl2)
            if i >= 2:
                t.wait_ge(ctx.sem(S_VS), lg_rel[i - 2])
            last = None
            for kc in range(8):
                last = t.matmul(lgP[:, b, :], xlnT[:, kc, i * 128:(i + 1) * 128], wgb[:, kc, :],
                                start=(kc == 0), stop=(kc == 7))
            last.then_inc(ctx.sem(S_LG), 1)
        if i == 0:
            lg_rel = {}
        n_v1 = ctx.bump(S_VS, 1)
        @ctx.on("vector")
        def _(v, i=i, b=b, n_lg=n_lg):
            v.wait_ge(ctx.sem(S_LG), n_lg)
            v.scalar_tensor_tensor(lg[:, b, :], lgP[:, b, :], 0.0, bgbc[:],
                                   op0=ALU.bypass, op1=ALU.add)
            v.engine_nop().then_inc(ctx.sem(S_VS), 1)
        lg_rel[i] = n_v1
        n_v2 = ctx.bump(S_VS, 1)
        @ctx.on("vector")
        def _(v, b=b, n_v1=n_v1):
            v.wait_ge(ctx.sem(S_VS), n_v1)
            v.reduce_max(tcol[:, b, 0:1], lg[:, b, :], axis=AX.X)
            v.engine_nop().then_inc(ctx.sem(S_VS), 1)
        n_v3 = ctx.bump(S_VS, 1)
        @ctx.on("vector")
        def _(v, i=i, b=b, n_v2=n_v2):
            v.wait_ge(ctx.sem(S_VS), n_v2)
            v.max_index(idx8[:], tcol[:, b, 0:1].broadcast_to([128, 8]), lg[:, b, :])
            v.tensor_scalar_mul(tcol[:, b, 1:2], tcol[:, b, 0:1], -1.0)
            v.engine_nop().then_inc(ctx.sem(S_VS), 1)
        n_v4 = ctx.bump(S_VS, 1)
        @ctx.on("vector")
        def _(v, i=i, n_v3=n_v3):
            v.wait_ge(ctx.sem(S_VS), n_v3)
            v.tensor_copy(idxf_a[:, i:i + 1], idx8[:, 0:1])
            v.engine_nop().then_inc(ctx.sem(S_VS), 1)
        n_ac = ctx.bump(S_AC, 1)
        @ctx.on("scalar")
        def _(sc, b=b, n_v3=n_v3):
            sc.wait_ge(ctx.sem(S_VS), n_v3)
            sc.activation(out=ex[:, b, :], in_=lg[:, b, :], func=AF.Exp,
                          bias=tcol[:, b, 1:2]).then_inc(ctx.sem(S_AC), 1)
        n_v5 = ctx.bump(S_VS, 1)
        @ctx.on("vector")
        def _(v, i=i, b=b, n_ac=n_ac, n_v4=n_v4):
            v.wait_ge(ctx.sem(S_AC), n_ac)
            v.wait_ge(ctx.sem(S_VS), n_v4)
            v.reduce_sum(tcol[:, b, 2:3], ex[:, b, :], axis=AX.X)
            v.tensor_tensor(oh[:, b, :], idxf_a[:, i:i + 1].broadcast_to([128, E]),
                            g["iotaE"][:], op=ALU.is_equal)
            v.engine_nop().then_inc(ctx.sem(S_VS), 1)
        n_v6 = ctx.bump(S_VS, 1)
        @ctx.on("vector")
        def _(v, i=i, b=b, n_v5=n_v5):
            v.wait_ge(ctx.sem(S_VS), n_v5)
            v.reciprocal(gate_a[:, i:i + 1], tcol[:, b, 2:3])
            v.tensor_copy(base_r[:, b, :], base_run[:])
            v.engine_nop().then_inc(ctx.sem(S_VS), 1)
        ctx.bump(S_PE, 2)
        @ctx.on("tensor")
        def _(t, i=i, b=b, n_v6=n_v6):
            t.wait_ge(ctx.sem(S_VS), n_v6)
            t.matmul(posP[:], onesK1f[:], base_r[:, b, :], start=True, stop=False)
            t.matmul(posP[:], ltUf[:], oh[:, b, :], start=False, stop=True)\
                .then_inc(ctx.sem(S_PE), 1)
            t.matmul(cntP[:, b, :], ones128f[:], oh[:, b, :], start=True,
                     stop=True).then_inc(ctx.sem(S_PE), 1)
        n_v7 = ctx.bump(S_VS, 1)
        @ctx.on("vector")
        def _(v, i=i, n_pe=base_peB + 2 * i + 1):
            v.wait_ge(ctx.sem(S_PE), n_pe)
            v.tensor_copy(posS[:], posP[:])
            v.engine_nop().then_inc(ctx.sem(S_VS), 1)
        n_v8 = ctx.bump(S_VS, 1)
        @ctx.on("vector")
        def _(v, b=b, n_v7=n_v7):
            v.wait_ge(ctx.sem(S_VS), n_v7)
            v.tensor_mul(posS[:], posS[:], oh[:, b, :])
            v.engine_nop().then_inc(ctx.sem(S_VS), 1)
        n_v9x = ctx.bump(S_VS, 1)
        @ctx.on("vector")
        def _(v, i=i, n_v8=n_v8):
            v.wait_ge(ctx.sem(S_VS), n_v8)
            v.reduce_sum(pos2_a[:, i:i + 1], posS[:], axis=AX.X)
            v.engine_nop().then_inc(ctx.sem(S_VS), 1)
        ctx.bump(S_VS, 1)
        @ctx.on("vector")
        def _(v, i=i, b=b, n_v9x=n_v9x, n_pe=base_peB + 2 * i + 2):
            v.wait_ge(ctx.sem(S_VS), n_v9x)
            v.wait_ge(ctx.sem(S_PE), n_pe)
            v.tensor_add(base_run[:], base_run[:], cntP[:, b, :])
            v.engine_nop().then_inc(ctx.sem(S_VS), 1)

    n_vcnt = ctx.bump(S_VS, 1)
    @ctx.on("vector")
    def _(v, w=ctx.val(S_VS) - 1):
        v.wait_ge(ctx.sem(S_VS), w)
        v.tensor_copy(cnt_sb[:, 0:E], base_run[:])
        v.engine_nop().then_inc(ctx.sem(S_VS), 1)

    @ctx.on("sync")
    def _(s, n_vcnt=n_vcnt):
        s.wait_ge(ctx.sem(S_VS), n_vcnt)
        s.dma_start(out=g["cnt_in"][:, 0:E], in_=cnt_sb[:, 0:E]).then_inc(ctx.sem(q + "cst"), 16)
        s.dma_start(out=g["cnt_in"][:, 16:32], in_=g["zero128"][0:1, 0:16]).then_inc(ctx.sem(q + "cst"), 16)
    ctx.bump(q + "cst", 32)
    n_cst = ctx.val(q + "cst")
    @ctx.on("gpsimd")
    def _(gp):
        gp.wait_ge(ctx.sem(q + "cst"), n_cst)
        gp.collective_compute("AllGather", ALU.bypass,
                              replica_groups=[list(range(NC))],
                              ins=[g["cnt_in"].opt()], outs=[g["cnt_out"].opt()])\
            .then_inc(ctx.sem(S_AGC), 1)
    ctx.bump(S_AGC, 1)
    n_agc = ctx.val(S_AGC)

    @ctx.on("sync")
    def _(s):
        s.wait_ge(ctx.sem(S_AGC), n_agc)
        s.dma_start(out=cld[:], in_=g["cnt_out"][:, :]).then_inc(ctx.sem(q + "cl"), 16)
    ctx.bump(q + "cl", 16)
    n_cl = ctx.val(q + "cl")
    n_v9 = ctx.bump(S_VS, 1)
    @ctx.on("vector")
    def _(v):
        v.wait_ge(ctx.sem(q + "cl"), n_cl)
        v.scalar_tensor_tensor(mask8[:, 0:1], g["pcf_col"][0:8, PCF_COREBASE:PCF_COREBASE + 1],
                               1.0 / TPC, iota8[:], op0=ALU.mult, op1=ALU.is_gt)
        v.scalar_tensor_tensor(mask8[:, 1:2], g["pcf_col"][0:8, PCF_COREBASE:PCF_COREBASE + 1],
                               1.0 / TPC, iota8[:], op0=ALU.mult, op1=ALU.is_equal)
        v.engine_nop().then_inc(ctx.sem(S_VS), 1)
    ctx.bump(S_PE, 1)
    @ctx.on("tensor")
    def _(t, n_v9=n_v9):
        t.wait_ge(ctx.sem(S_VS), n_v9)
        t.matmul(posP[0:1, 0:E], mask8[:, 0:1], cld[0:8, 0:E], start=True, stop=True)\
            .then_inc(ctx.sem(S_PE), 1)
    n_v10 = ctx.bump(S_VS, 1)
    @ctx.on("vector")
    def _(v, n_pe=ctx.val(S_PE)):
        v.wait_ge(ctx.sem(S_PE), n_pe)
        v.tensor_copy(baser[:], posP[0:1, 0:E])
        v.engine_nop().then_inc(ctx.sem(S_VS), 1)
    ctx.bump(S_PE, 1)
    @ctx.on("tensor")
    def _(t, n_v10=n_v10):
        t.wait_ge(ctx.sem(S_VS), n_v10)
        t.matmul(posP[:, 0:E], onesK1f[:], baser[:], start=True, stop=True)\
            .then_inc(ctx.sem(S_PE), 1)
    n_v11 = ctx.bump(S_VS, 1)
    @ctx.on("vector")
    def _(v, n_pe=ctx.val(S_PE)):
        v.wait_ge(ctx.sem(S_PE), n_pe)
        v.tensor_copy(bases[:], posP[:, 0:E])
        v.engine_nop().then_inc(ctx.sem(S_VS), 1)

    for i in range(NTILE):
        b = i % 2
        n_c1 = ctx.bump(S_VS, 1)
        @ctx.on("vector")
        def _(v, i=i, n_v11=n_v11):
            if i == 0:
                v.wait_ge(ctx.sem(S_VS), n_v11)
            v.tensor_tensor(ohs[:], idxf_a[:, i:i + 1].broadcast_to([128, E]),
                            g["iotaE"][:], op=ALU.is_equal)
            v.engine_nop().then_inc(ctx.sem(S_VS), 1)
        n_c2 = ctx.bump(S_VS, 1)
        @ctx.on("vector")
        def _(v, n_c1=n_c1):
            v.wait_ge(ctx.sem(S_VS), n_c1)
            v.tensor_mul(ohs[:], ohs[:], bases[:])
            v.engine_nop().then_inc(ctx.sem(S_VS), 1)
        n_c3 = ctx.bump(S_VS, 1)
        @ctx.on("vector")
        def _(v, b=b, n_c2=n_c2):
            v.wait_ge(ctx.sem(S_VS), n_c2)
            v.reduce_sum(tcol[:, b, 3:4], ohs[:], axis=AX.X)
            v.engine_nop().then_inc(ctx.sem(S_VS), 1)
        n_c4 = ctx.bump(S_VS, 1)
        @ctx.on("vector")
        def _(v, i=i, b=b, n_c3=n_c3):
            v.wait_ge(ctx.sem(S_VS), n_c3)
            v.tensor_add(tcol[:, b, 4:5], tcol[:, b, 3:4], pos2_a[:, i:i + 1])
            v.engine_nop().then_inc(ctx.sem(S_VS), 1)
        n_c5 = ctx.bump(S_VS, 1)
        @ctx.on("vector")
        def _(v, b=b, n_c4=n_c4):
            v.wait_ge(ctx.sem(S_VS), n_c4)
            v.tensor_single_scalar(tcol[:, b, 5:6], tcol[:, b, 4:5], float(CAP), op=ALU.is_lt)
            v.engine_nop().then_inc(ctx.sem(S_VS), 1)
        n_c6 = ctx.bump(S_VS, 1)
        @ctx.on("vector")
        def _(v, i=i, b=b, n_c5=n_c5):
            v.wait_ge(ctx.sem(S_VS), n_c5)
            v.scalar_tensor_tensor(tcol[:, b, 6:7], idxf_a[:, i:i + 1], float(CAP),
                                   tcol[:, b, 4:5], op0=ALU.mult, op1=ALU.add)
            v.scalar_tensor_tensor(comb_a[:, i:i + 1], gate_a[:, i:i + 1], 0.5,
                                   tcol[:, b, 5:6], op0=ALU.mult, op1=ALU.mult)
            v.engine_nop().then_inc(ctx.sem(S_VS), 1)
        n_c7 = ctx.bump(S_VS, 1)
        @ctx.on("vector")
        def _(v, i=i, b=b, n_c6=n_c6):
            v.wait_ge(ctx.sem(S_VS), n_c6)
            v.tensor_mul(slkf_a[:, i:i + 1], tcol[:, b, 6:7], tcol[:, b, 5:6])
            v.tensor_sub(tcol[:, b, 7:8], tcol[:, b, 6:7], g["trashc"][:, i:i + 1])
            v.engine_nop().then_inc(ctx.sem(S_VS), 1)
        n_c8 = ctx.bump(S_VS, 1)
        @ctx.on("vector")
        def _(v, i=i, b=b, n_c7=n_c7):
            v.wait_ge(ctx.sem(S_VS), n_c7)
            v.scalar_tensor_tensor(tbf[:, i:i + 1], tcol[:, b, 7:8], tcol[:, b, 5:6],
                                   g["trashc"][:, i:i + 1], op0=ALU.mult, op1=ALU.add)
            v.engine_nop().then_inc(ctx.sem(S_VS), 1)
        ctx.bump(S_VS, 1)
        @ctx.on("vector")
        def _(v, i=i, n_c8=n_c8):
            v.wait_ge(ctx.sem(S_VS), n_c8)
            v.tensor_copy(slot_u[:, i:i + 1], slkf_a[:, i:i + 1])
            v.tensor_copy(tabs_u[:, i:i + 1], tbf[:, i:i + 1])
            v.engine_nop().then_inc(ctx.sem(S_VS), 1)
    n_vs_passc = ctx.val(S_VS)

    if call == 0:
        @ctx.on("sync")
        def _(s):
            s.wait_ge(ctx.sem(S_VS), n_vs_passc)
            s.dma_start(out=g["route_dbg"][:, 0:NTILE], in_=idxf_a[:]).then_inc(ctx.sem(q + "rd"), 16)
            s.dma_start(out=g["route_dbg"][:, NTILE:2 * NTILE], in_=pos2_a[:]).then_inc(ctx.sem(q + "rd"), 16)
            s.dma_start(out=g["route_dbg"][:, 2 * NTILE:3 * NTILE], in_=slkf_a[:]).then_inc(ctx.sem(q + "rd"), 16)
            s.dma_start(out=g["route_dbg"][:, 3 * NTILE:4 * NTILE], in_=comb_a[:]).then_inc(ctx.sem(q + "rd"), 16)
            s.wait_ge(ctx.sem(q + "rd"), 64)
        ctx.bump(q + "rd", 64)

    n_tz = ctx.val(q + "tz")
    for i in range(NTILE):
        ctx.bump(S_SC, 16)
        @ctx.on("gpsimd")
        def _(gp, i=i):
            if i == 0:
                gp.wait_ge(ctx.sem(S_VS), n_vs_passc)
                gp.wait_ge(ctx.sem(q + "tz"), n_tz)
            gp.indirect_dma_start(
                out=g["tab_in"],
                out_offset=IndirectOffsetOnAxis(ap=tabs_u[:, i:i + 1], axis=0),
                in_=payl_u[:, i:i + 1],
                in_offset=None,
            ).then_inc(ctx.sem(S_SC), 16)
    n_sc = ctx.val(S_SC)

    @ctx.on("sync")
    def _(s):
        s.wait_ge(ctx.sem(S_VS), n_vs_passc)
        s.dma_start(out=g["hmst"][0:1, :].rearrange("1 p -> p 1"),
                    in_=slkf_a[:, NTILE - 1:NTILE]).then_inc(ctx.sem(q + "hm"), 16)
        s.dma_start(out=g["hmst"][1:2, :].rearrange("1 p -> p 1"),
                    in_=comb_a[:, NTILE - 1:NTILE]).then_inc(ctx.sem(q + "hm"), 16)
    ctx.bump(q + "hm", 32)
    n_hm0 = ctx.val(q + "hm")
    @ctx.on("sync")
    def _(s):
        s.wait_ge(ctx.sem(q + "hm"), n_hm0)
        s.dma_start(out=g["cnt_in"][0:1, 16:19], in_=g["hmst"][0:1, 125:128])\
            .then_inc(ctx.sem(q + "hm"), 16)
        s.dma_start(out=g["cnt_in"][0:1, 19:22], in_=g["hmst"][1:2, 125:128])\
            .then_inc(ctx.sem(q + "hm"), 16)
    ctx.bump(q + "hm", 32)

    @ctx.on("gpsimd")
    def _(gp):
        gp.wait_ge(ctx.sem(S_SC), n_sc)
        gp.collective_compute("AllGather", ALU.bypass,
                              replica_groups=[list(range(NC))],
                              ins=[g["tab_in"].opt()], outs=[g["tab_out"].opt()])\
            .then_inc(ctx.sem(S_AGT), 1)
    ctx.bump(S_AGT, 1)
    n_agt = ctx.val(S_AGT)

    n_vmg = ctx.bump(S_VS, 2)
    @ctx.on("gpsimd")
    def _(gp):
        gp.iota(tbf[:, 0:16].rearrange("p (a b) -> p a b", a=2),
                pattern=[[128, 2], [TL // 16, 8]], base=0,
                channel_multiplier=1, allow_small_or_imprecise_dtypes=True)
        gp.engine_nop().then_inc(ctx.sem(q + "mgi"), 1)
    ctx.bump(q + "mgi", 1)
    n_mgi = ctx.val(q + "mgi")
    @ctx.on("vector")
    def _(v):
        v.wait_ge(ctx.sem(q + "mgi"), n_mgi)
        v.scalar_tensor_tensor(tbf[:, 0:16],
                               g["pcf_col"][:, PCF_SLOTBASE:PCF_SLOTBASE + 1].broadcast_to([128, 16]),
                               1.0 / 16.0, tbf[:, 0:16], op0=ALU.mult, op1=ALU.add)
        v.engine_nop().then_inc(ctx.sem(S_VS), 1)
    @ctx.on("vector")
    def _(v, tgt=n_vmg - 1):
        v.wait_ge(ctx.sem(S_VS), tgt)
        v.tensor_copy(mgi[:].rearrange("p a b -> p (a b)"), tbf[:, 0:16])
        v.engine_nop().then_inc(ctx.sem(S_VS), 1)
    n_vmg_done = ctx.val(S_VS)

    for e in range(2):
        for k in range(NC):
            ctx.bump(S_MG, 16)
            @ctx.on("gpsimd")
            def _(gp, e=e, k=k):
                if e == 0 and k == 0:
                    gp.wait_ge(ctx.sem(S_AGT), n_agt)
                    gp.wait_ge(ctx.sem(S_VS), n_vmg_done)
                gp.indirect_dma_start(
                    out=mgf[:, e, k, :],
                    out_offset=None,
                    in_=g["tab_out"],
                    in_offset=IndirectOffsetOnAxis(ap=mgi[:, e, k:k + 1], axis=0),
                ).then_inc(ctx.sem(S_MG), 16)
    n_mg = ctx.val(S_MG)
    n_vm1 = ctx.bump(S_VS, 1)
    @ctx.on("vector")
    def _(v):
        v.wait_ge(ctx.sem(S_MG), n_mg)
        v.tensor_tensor(tos_f[:], mgf[:, :, 0, :], mgf[:, :, 1, :], op=ALU.min)
        v.engine_nop().then_inc(ctx.sem(S_VS), 1)
    for k in range(2, NC):
        n_vmk = ctx.bump(S_VS, 1)
        @ctx.on("vector")
        def _(v, k=k, w=n_vm1 + k - 2):
            v.wait_ge(ctx.sem(S_VS), w)
            v.tensor_tensor(tos_f[:], tos_f[:], mgf[:, :, k, :], op=ALU.min)
            v.engine_nop().then_inc(ctx.sem(S_VS), 1)
    n_vm2 = ctx.bump(S_VS, 1)
    @ctx.on("vector")
    def _(v, w=ctx.val(S_VS) - 1):
        v.wait_ge(ctx.sem(S_VS), w)
        v.tensor_scalar_min(tos_f[:], tos_f[:], BIGF)
        v.engine_nop().then_inc(ctx.sem(S_VS), 1)
    ctx.bump(S_VS, 1)
    @ctx.on("vector")
    def _(v, w=n_vm2):
        v.wait_ge(ctx.sem(S_VS), w)
        v.tensor_copy(tos_u[:], tos_f[:])
        v.engine_nop().then_inc(ctx.sem(S_VS), 1)
    n_tos = ctx.val(S_VS)

    ctx.close_tensors([cm1, cm2, cm3])
    cms.remove(cm1); cms.remove(cm2); cms.remove(cm3)
    ctx.close_tensors(cms2)

    cms3 = []
    bufT = sb("bufT", [128, 8, CAP], BF16, cms3)
    w1t = sb("w1t", [128, 2, 8, 128], BF16, cms3)
    w2sb = sb("w2sb", [128, 32, D], BF16, cms3)
    hbf = sb("hbf", [128, 32, CAP // 2], BF16, cms3)
    b1sb = sb("b1sb", [128, 32], F32, cms3)
    b2bc = sb("b2bc", [128, D], F32, cms3)
    ysb = sb("ysb", [128, 2, D], F32, cms3)

    hP, cm4 = ctx.ps(q + "hP", [128, 2, 1024], F32)
    yP, cm5 = ctx.ps(q + "yP", [128, 4, 512], F32)
    cms += [cm4, cm5]

    for e in range(2):
        eq = f"{q}e{e}"
        eb_peh = ctx.val(eq + "peh")
        eb_hev = ctx.val(eq + "hev")
        eb_yev = ctx.val(eq + "yev")
        for f in range(16):
            fg = e * 16 + f
            fb = fg % 2
            n_ga = ctx.bump(q + "ga" + str(fb), 16)
            @ctx.on("gpsimd")
            def _(gp, f=f, fg=fg, fb=fb, e=e):
                if fg == 0:
                    gp.wait_ge(ctx.sem(S_VS), n_tos)
                    gp.wait_ge(ctx.sem(S_AGX), n_agx)
                gp.wait_ge(ctx.sem(q + "gst" + str(fb)), base_gst[fb] + (fg // 2) * 16)
                gp.indirect_dma_start(
                    out=gbuf[:, fb, :],
                    out_offset=None,
                    in_=g["xln_full"],
                    in_offset=IndirectOffsetOnAxis(ap=tos_u[:, e, f:f + 1], axis=0),
                    bounds_check=g["r_ntok"],
                    oob_is_err=False,
                ).then_inc(ctx.sem(q + "ga" + str(fb)), 16)
            ctx.bump(q + "gst" + str(fb), 16)
            @ctx.on("sync")
            def _(s, f=f, fb=fb, n_ga=n_ga):
                s.wait_ge(ctx.sem(q + "ga" + str(fb)), n_ga)
                s.dma_start(out=g["xg"][f * 128:(f + 1) * 128, :], in_=gbuf[:, fb, :])\
                    .then_inc(ctx.sem(q + "gst" + str(fb)), 16)
        n_gst0 = ctx.val(q + "gst0")
        n_gst1 = ctx.val(q + "gst1")
        for kc in range(8):
            ctx.bump(eq + "trb", 16)
            @ctx.on("scalar")
            def _(sc, kc=kc, eq=eq, n_gst0=n_gst0, n_gst1=n_gst1):
                if kc == 0:
                    sc.wait_ge(ctx.sem(q + "gst0"), n_gst0)
                    sc.wait_ge(ctx.sem(q + "gst1"), n_gst1)
                sc.dma_start_transpose(out=bufT[:, kc, :],
                                       in_=g["xg"][:, kc * 128:(kc + 1) * 128])\
                    .then_inc(ctx.sem(eq + "trb"), 16)
        n_trb = ctx.val(eq + "trb")
        @ctx.on("sync")
        def _(s, e=e, eq=eq):
            s.dma_start(out=b1sb[:], in_=g["b1v"][e]).then_inc(ctx.sem(eq + "wl"), 16)
            s.dma_start(out=b2bc[:], in_=g["b2v"][e:e + 1, :].broadcast_to([128, D]))\
                .then_inc(ctx.sem(eq + "wl"), 16)
            for hc in range(32):
                s.dma_start(out=w2sb[:, hc, :], in_=g["w2_b"][e][hc * 128:(hc + 1) * 128, :])\
                    .then_inc(ctx.sem(eq + "wl"), 16)
        n_wl = ctx.bump(eq + "wl", 16 * 34)

        for half in range(2):
            s0 = half * 1024
            for hc in range(32):
                j = half * 32 + hc
                wb = hc % 2
                n_w1l = ctx.bump(eq + "w1l" + str(wb), 16)
                @ctx.on("sync")
                def _(s, e=e, hc=hc, wb=wb, j=j, eq=eq, eb_peh=eb_peh):
                    s.wait_ge(ctx.sem(eq + "peh"), eb_peh + max(0, j - 1))
                    s.dma_start(out=w1t[:, wb, :, :],
                                in_=g["w1_b"][e, hc].rearrange("p (a h) -> p a h", a=8))\
                        .then_inc(ctx.sem(eq + "w1l" + str(wb)), 16)
                n_peh = ctx.bump(eq + "peh", 1)
                @ctx.on("tensor")
                def _(t, hc=hc, wb=wb, s0=s0, j=j, n_w1l=n_w1l, first=(j == 0),
                      eq=eq, n_trb=n_trb, eb_hev=eb_hev):
                    t.wait_ge(ctx.sem(eq + "w1l" + str(wb)), n_w1l)
                    if first:
                        t.wait_ge(ctx.sem(eq + "trb"), n_trb)
                    t.wait_ge(ctx.sem(eq + "hev"), eb_hev + max(0, j - 1))
                    last = None
                    for kc in range(8):
                        for nb in range(2):
                            last = t.matmul(hP[:, wb, nb * 512:(nb + 1) * 512],
                                            w1t[:, wb, kc, :],
                                            bufT[:, kc, s0 + nb * 512: s0 + (nb + 1) * 512],
                                            start=(kc == 0), stop=(kc == 7))
                    last.then_inc(ctx.sem(eq + "peh"), 1)
                ctx.bump(eq + "hev", 1)
                @ctx.on("scalar")
                def _(sc, hc=hc, wb=wb, n_peh=n_peh, n_wl=n_wl, eq=eq):
                    sc.wait_ge(ctx.sem(eq + "peh"), n_peh)
                    sc.wait_ge(ctx.sem(eq + "wl"), n_wl)
                    sc.activation(out=hbf[:, hc, :], in_=hP[:, wb, :], func=AF.Relu,
                                  bias=b1sb[:, hc:hc + 1]).then_inc(ctx.sem(eq + "hev"), 1)
            n_hev_half = ctx.val(eq + "hev")
            for st in range(8):
                jj = half * 8 + st
                sb2 = st % 2
                for dh in range(2):
                    yb = (jj * 2 + dh) % 4
                    n_pey = ctx.bump(eq + "pey", 1)
                    @ctx.on("tensor")
                    def _(t, st=st, dh=dh, yb=yb, jj=jj, n_hev_half=n_hev_half, eq=eq, eb_yev=eb_yev):
                        if st == 0 and dh == 0:
                            t.wait_ge(ctx.sem(eq + "hev"), n_hev_half)
                        t.wait_ge(ctx.sem(eq + "yev"), eb_yev + max(0, jj * 2 + dh - 3))
                        last = None
                        for hc in range(32):
                            last = t.matmul(yP[:, yb, :],
                                            hbf[:, hc, st * 128:(st + 1) * 128],
                                            w2sb[:, hc, dh * 512:(dh + 1) * 512],
                                            start=(hc == 0), stop=(hc == 31))
                        last.then_inc(ctx.sem(eq + "pey"), 1)
                    ctx.bump(eq + "yev", 1)
                    @ctx.on("vector")
                    def _(v, st=st, dh=dh, yb=yb, sb2=sb2, jj=jj, n_pey=n_pey, n_wl=n_wl, eq=eq, e=e, base_yst=base_yst):
                        v.wait_ge(ctx.sem(eq + "pey"), n_pey)
                        if jj == 0 and dh == 0:
                            v.wait_ge(ctx.sem(eq + "wl"), n_wl)
                        jjg = e * 16 + jj
                        if jjg >= 2 and dh == 0:
                            v.wait_ge(ctx.sem(q + "yst" + str(sb2)), base_yst[sb2] + (jjg // 2) * 16)
                        v.scalar_tensor_tensor(ysb[:, sb2, dh * 512:(dh + 1) * 512],
                                               yP[:, yb, :], 0.0, b2bc[:, dh * 512:(dh + 1) * 512],
                                               op0=ALU.bypass, op1=ALU.add)
                        v.engine_nop().then_inc(ctx.sem(eq + "yev"), 1)
                n_yev = ctx.val(eq + "yev")
                fl = half * 8 + st
                ctx.bump(q + "yst" + str(sb2), 16)
                @ctx.on("sync")
                def _(s, sb2=sb2, fl=fl, e=e, n_yev=n_yev, eq=eq):
                    s.wait_ge(ctx.sem(eq + "yev"), n_yev)
                    s.dma_start(out=g["y_in"][e * CAP:(e + 1) * CAP, :]
                                .rearrange("(p f) d -> p f d", f=16)[:, fl, :],
                                in_=ysb[:, sb2, :]).then_inc(ctx.sem(q + "yst" + str(sb2)), 16)
    ywaits = [(q + f"yst{p}", ctx.val(q + f"yst{p}")) for p in range(2)]
    @ctx.on("gpsimd")
    def _(gp):
        for nm, vv in ywaits:
            gp.wait_ge(ctx.sem(nm), vv)
        gp.collective_compute("AllGather", ALU.bypass,
                              replica_groups=[list(range(NC))],
                              ins=[g["y_in"].opt()], outs=[g["y_full"].opt()])\
            .then_inc(ctx.sem(S_AGY), 1)
    ctx.bump(S_AGY, 1)
    n_agy = ctx.val(S_AGY)

    ctx.close_tensors([cm4, cm5])
    cms.remove(cm4); cms.remove(cm5)
    ctx.close_tensors(cms3)

    if call == 0:
        n_hm = ctx.val(q + "hm")
        @ctx.on("gpsimd")
        def _(gp):
            gp.wait_ge(ctx.sem(q + "hm"), n_hm)
            gp.collective_compute("AllGather", ALU.bypass,
                                  replica_groups=[list(range(NC))],
                                  ins=[g["cnt_in"].opt()], outs=[g["cnt_out"].opt()])\
                .then_inc(ctx.sem(q + "agh"), 1)
        ctx.bump(q + "agh", 1)
        n_agh = ctx.val(q + "agh")
        @ctx.on("sync")
        def _(s):
            s.wait_ge(ctx.sem(q + "agh"), n_agh)
            s.dma_start(out=cld[:], in_=g["cnt_out"][:, :]).then_inc(ctx.sem(q + "hl"), 16)
        ctx.bump(q + "hl", 16)
        n_hl = ctx.val(q + "hl")
        n_hv2 = ctx.bump(S_VS, 1)
        @ctx.on("vector")
        def _(v):
            v.wait_ge(ctx.sem(q + "hl"), n_hl)
            v.tensor_scalar_add(mask8[:, 1:2], iota8[:], 1.0)
            v.engine_nop().then_inc(ctx.sem(S_VS), 1)
        n_hv3 = ctx.bump(S_VS, 1)
        @ctx.on("vector")
        def _(v, n_hv2=n_hv2):
            v.wait_ge(ctx.sem(S_VS), n_hv2)
            v.tensor_scalar_mul(mask8[:, 1:2], mask8[:, 1:2], float(TPC))
            v.engine_nop().then_inc(ctx.sem(S_VS), 1)
        n_hv4 = ctx.bump(S_VS, 1)
        @ctx.on("vector")
        def _(v, n_hv3=n_hv3):
            v.wait_ge(ctx.sem(S_VS), n_hv3)
            v.tensor_tensor(mask8[:, 1:2], mask8[:, 1:2],
                            g["pcf_col"][0:8, PCF_COREBASE:PCF_COREBASE + 1], op=ALU.is_equal)
            v.engine_nop().then_inc(ctx.sem(S_VS), 1)
        hmP, cmh = ctx.ps(q + "hmP", [1, 8], F32)
        cms.append(cmh)
        ctx.bump(S_PE, 1)
        @ctx.on("tensor")
        def _(t, n_hv4=n_hv4):
            t.wait_ge(ctx.sem(S_VS), n_hv4)
            t.matmul(hmP[:, 0:6], mask8[:, 1:2], cld[0:8, 16:22], start=True, stop=True)\
                .then_inc(ctx.sem(S_PE), 1)
        n_hv5 = ctx.bump(S_VS, 1)
        @ctx.on("vector")
        def _(v, n_pe=ctx.val(S_PE)):
            v.wait_ge(ctx.sem(S_PE), n_pe)
            v.tensor_copy(cnt_sb[:, 24:30], hmP[:, 0:6])
            v.engine_nop().then_inc(ctx.sem(S_VS), 1)
        @ctx.on("sync")
        def _(s, n_hv5=n_hv5):
            s.wait_ge(ctx.sem(S_VS), n_hv5)
            s.dma_start(out=g["hmx"], in_=cnt_sb[:, 24:30]).then_inc(ctx.sem(q + "hx"), 16)
        ctx.bump(q + "hx", 16)
        n_hx = ctx.val(q + "hx")
        @ctx.on("sync")
        def _(s):
            s.wait_ge(ctx.sem(q + "hx"), n_hx)
            s.dma_start(out=hcol[0:3, 0:1], in_=g["hmx"][0:1, 0:3].rearrange("1 a -> a 1"))\
                .then_inc(ctx.sem(q + "hx"), 16)
            s.dma_start(out=hcol[0:3, 1:2], in_=g["hmx"][0:1, 3:6].rearrange("1 a -> a 1"))\
                .then_inc(ctx.sem(q + "hx"), 16)
        ctx.bump(q + "hx", 32)
        n_hx2 = ctx.val(q + "hx")
        n_hv6 = ctx.bump(S_VS, 2)
        @ctx.on("vector")
        def _(v):
            v.wait_ge(ctx.sem(q + "hx"), n_hx2)
            v.memset(hcu[:], 0)
            v.engine_nop().then_inc(ctx.sem(S_VS), 1)
        @ctx.on("vector")
        def _(v, w=n_hv6 - 1):
            v.wait_ge(ctx.sem(S_VS), w)
            v.tensor_copy(hcu[0:3, :], hcol[0:3, 0:1])
            v.engine_nop().then_inc(ctx.sem(S_VS), 1)
        n_hcu = ctx.val(S_VS)
        ctx.bump(q + "hg", 16)
        @ctx.on("gpsimd")
        def _(gp):
            gp.wait_ge(ctx.sem(S_VS), n_hcu)
            gp.wait_ge(ctx.sem(S_AGY), n_agy)
            gp.indirect_dma_start(
                out=yrow[:, 0, :], out_offset=None,
                in_=g["y_full"],
                in_offset=IndirectOffsetOnAxis(ap=hcu[:], axis=0),
                bounds_check=g["r_ntok"],
                oob_is_err=False,
            ).then_inc(ctx.sem(q + "hg"), 16)
        n_hg = ctx.val(q + "hg")
        @ctx.on("sync")
        def _(s):
            s.dma_start(out=x1row[:, 0, :][0:3, :], in_=src[0:3, :]).then_inc(ctx.sem(q + "hgx"), 16)
        ctx.bump(q + "hgx", 16)
        n_hg2 = ctx.val(q + "hgx")
        n_hv7 = ctx.bump(S_VS, 2)
        @ctx.on("vector")
        def _(v):
            v.wait_ge(ctx.sem(q + "hg"), n_hg)
            v.wait_ge(ctx.sem(q + "hgx"), n_hg2)
            v.scalar_tensor_tensor(yrow[0:3, 0, :], yrow[0:3, 0, :], hcol[0:3, 1:2],
                                   x1row[0:3, 0, :], op0=ALU.mult, op1=ALU.add)
            v.engine_nop().then_inc(ctx.sem(S_VS), 1)
        @ctx.on("vector")
        def _(v, w=n_hv7 - 1):
            v.wait_ge(ctx.sem(S_VS), w)
            v.tensor_scalar_mul(yrow[0:3, 0, :], yrow[0:3, 0, :],
                                g["pcf_col"][0:3, PCF_PARITY:PCF_PARITY + 1])
            v.engine_nop().then_inc(ctx.sem(S_VS), 1)
        n_hv8 = ctx.val(S_VS)
        ctx.bump(q + "outh", 16)
        @ctx.on("sync")
        def _(s, n_hv8=n_hv8):
            s.wait_ge(ctx.sem(S_VS), n_hv8)
            s.dma_start(out=dst[0:3, :], in_=yrow[0:3, 0, :]).then_inc(ctx.sem(q + "outh"), 16)

    for i in range(NTILE):
        b = i % 2
        n_yg = ctx.bump(q + "yg" + str(b), 16)
        @ctx.on("gpsimd")
        def _(gp, i=i, b=b):
            if i == 0:
                gp.wait_ge(ctx.sem(S_AGY), n_agy)
                if call == 0:
                    gp.wait_ge(ctx.sem(q + "outh"), 16)
            if i >= 2:
                gp.wait_ge(ctx.sem(S_OUT + str(b)), base_out[b] + (i // 2) * 16)
            gp.indirect_dma_start(
                out=yrow[:, b, :], out_offset=None,
                in_=g["y_full"],
                in_offset=IndirectOffsetOnAxis(ap=slot_u[:, i:i + 1], axis=0),
                bounds_check=g["r_ntok"],
                oob_is_err=False,
            ).then_inc(ctx.sem(q + "yg" + str(b)), 16)
        n_xl = ctx.bump(q + "xl" + str(b), 16)
        @ctx.on("sync")
        def _(s, i=i, b=b):
            s.wait_ge(ctx.sem(q + "ev"), base_ev + max(0, i - 1))
            s.dma_start(out=x1row[:, b, :], in_=src[soff + i * 128: soff + (i + 1) * 128, :])\
                .then_inc(ctx.sem(q + "xl" + str(b)), 16)
        n_ev = ctx.bump(q + "ev", 1)
        @ctx.on("vector")
        def _(v, i=i, b=b, n_yg=n_yg, n_xl=n_xl):
            v.wait_ge(ctx.sem(q + "yg" + str(b)), n_yg)
            v.wait_ge(ctx.sem(q + "xl" + str(b)), n_xl)
            v.scalar_tensor_tensor(yrow[:, b, :], yrow[:, b, :], comb_a[:, i:i + 1],
                                   x1row[:, b, :], op0=ALU.mult, op1=ALU.add)
            v.engine_nop().then_inc(ctx.sem(q + "ev"), 1)
        ctx.bump(S_OUT + str(b), 16)
        @ctx.on("sync")
        def _(s, i=i, b=b, n_ev=n_ev):
            s.wait_ge(ctx.sem(q + "ev"), n_ev)
            s.dma_start(out=dst[doff + i * 128: doff + (i + 1) * 128, :], in_=yrow[:, b, :])\
                .then_inc(ctx.sem(S_OUT + str(b)), 16)
    @ctx.on("sync")
    def _(s, a=ctx.val(S_OUT + "0"), bb=ctx.val(S_OUT + "1"),
          hh=(ctx.val(q + "outh") if call == 0 else 0)):
        s.wait_ge(ctx.sem(S_OUT + "0"), a)
        s.wait_ge(ctx.sem(S_OUT + "1"), bb)
        if call == 0:
            s.wait_ge(ctx.sem(q + "outh"), hh)

    ctx.close_tensors(cms)


def emit_m1(ctx, g):
    q = "m1"
    cms = []
    def sb(name, shape, dt):
        h, cm = ctx.sb(q + name, shape, dt)
        cms.append(cm)
        return h

    xr = sb("xr", [128, 2, D], F32)
    scr = sb("scr", [128, D], F32)
    col = sb("col", [128, 2, 8], F32)
    xlt = sb("xlt", [128, 2, D], BF16)
    winb = sb("winb", [128, 32, 8, 128], BF16)
    xlnT = sb("xlnT", [128, 2, 8, TC], BF16)
    xlnTh = sb("xlnTh", [128, 8, 16], BF16)
    xi1 = sb("xi1", [128, 2, TC + 3], F32)
    carry = sb("carry", [128, 16, 3], F32)
    u1 = sb("u1", [128, 2, TC], F32)
    ub1 = sb("ub1", [128, 2, TC], BF16)
    sg1 = sb("sg1", [128, 2, TC], F32)
    xc_sb = sb("xc_sb", [128, TC], F32)

    S_AC = q + "ac"; S_VS = q + "vs"
    base_st = [ctx.val(q + "st0"), ctx.val(q + "st1")]

    def ln_tile(i, nrow, src_off, dst_off):
        b = i % 2
        n_ld = ctx.bump(q + "ld" + str(b), 16)
        @ctx.on("sync")
        def _(s, i=i, b=b, nrow=nrow, src_off=src_off):
            if i >= 2:
                s.wait_ge(ctx.sem(S_AC), lnac_rel[i - 2])
            s.dma_start(out=xr[0:nrow, b, :], in_=g["x1b"][src_off:src_off + nrow, :])\
                .then_inc(ctx.sem(q + "ld" + str(b)), 16)
        n_ac = ctx.bump(S_AC, 1)
        @ctx.on("scalar")
        def _(sc, b=b, nrow=nrow, n_ld=n_ld):
            sc.wait_ge(ctx.sem(q + "ld" + str(b)), n_ld)
            sc.activation(out=scr[0:nrow, :], in_=xr[0:nrow, b, :], func=AF.Square,
                          accum_out=col[0:nrow, b, 1:2]).then_inc(ctx.sem(S_AC), 1)
        n_v1 = ctx.bump(S_VS, 1)
        @ctx.on("vector")
        def _(v, b=b, nrow=nrow, n_ld=n_ld, n_ac=n_ac):
            v.wait_ge(ctx.sem(q + "ld" + str(b)), n_ld)
            v.wait_ge(ctx.sem(S_AC), n_ac)
            v.reduce_sum(col[0:nrow, b, 0:1], xr[0:nrow, b, :], axis=AX.X)
            v.engine_nop().then_inc(ctx.sem(S_VS), 1)
        n_v2 = ctx.bump(S_VS, 1)
        @ctx.on("vector")
        def _(v, b=b, nrow=nrow, n_v1=n_v1):
            v.wait_ge(ctx.sem(S_VS), n_v1)
            v.tensor_scalar_mul(col[0:nrow, b, 2:3], col[0:nrow, b, 0:1], 1.0 / D)
            v.tensor_scalar_mul(col[0:nrow, b, 3:4], col[0:nrow, b, 1:2], 1.0 / D)
            v.engine_nop().then_inc(ctx.sem(S_VS), 1)
        n_v3 = ctx.bump(S_VS, 1)
        @ctx.on("vector")
        def _(v, b=b, nrow=nrow, n_v2=n_v2):
            v.wait_ge(ctx.sem(S_VS), n_v2)
            v.tensor_tensor(col[0:nrow, b, 6:7], col[0:nrow, b, 2:3], col[0:nrow, b, 2:3], op=ALU.mult)
            v.engine_nop().then_inc(ctx.sem(S_VS), 1)
        n_v4 = ctx.bump(S_VS, 1)
        @ctx.on("vector")
        def _(v, b=b, nrow=nrow, n_v3=n_v3):
            v.wait_ge(ctx.sem(S_VS), n_v3)
            v.tensor_sub(col[0:nrow, b, 3:4], col[0:nrow, b, 3:4], col[0:nrow, b, 6:7])
            v.engine_nop().then_inc(ctx.sem(S_VS), 1)
        n_ac1 = ctx.bump(S_AC, 1)
        n_ac2 = ctx.bump(S_AC, 1)
        @ctx.on("scalar")
        def _(sc, b=b, nrow=nrow, n_v4=n_v4, n_ac1=n_ac1):
            sc.wait_ge(ctx.sem(S_VS), n_v4)
            sc.activation(out=col[0:nrow, b, 4:5], in_=col[0:nrow, b, 3:4], func=AF.Ln, bias=EPS)\
                .then_inc(ctx.sem(S_AC), 1)
            sc.wait_ge(ctx.sem(S_AC), n_ac1)
            sc.activation(out=col[0:nrow, b, 4:5], in_=col[0:nrow, b, 4:5], func=AF.Exp, scale=-0.5)\
                .then_inc(ctx.sem(S_AC), 1)
        n_v5 = ctx.bump(S_VS, 1)
        @ctx.on("vector")
        def _(v, b=b, nrow=nrow, n_ac2=n_ac2):
            v.wait_ge(ctx.sem(S_AC), n_ac2)
            v.scalar_tensor_tensor(col[0:nrow, b, 5:6], col[0:nrow, b, 2:3], -1.0,
                                   col[0:nrow, b, 4:5], op0=ALU.mult, op1=ALU.mult)
            v.engine_nop().then_inc(ctx.sem(S_VS), 1)
        n_ac3 = ctx.bump(S_AC, 1)
        lnac_rel[i] = n_ac3
        @ctx.on("scalar")
        def _(sc, i=i, b=b, nrow=nrow, n_v5=n_v5):
            sc.wait_ge(ctx.sem(S_VS), n_v5)
            if i >= 2:
                sc.wait_ge(ctx.sem(q + "st" + str(b)), base_st[b] + (i // 2) * 16)
            sc.activation(out=xlt[0:nrow, b, :], in_=xr[0:nrow, b, :], func=AF.Identity,
                          scale=col[0:nrow, b, 4:5], bias=col[0:nrow, b, 5:6]).then_inc(ctx.sem(S_AC), 1)
        ctx.bump(q + "st" + str(b), 16)
        @ctx.on("sync")
        def _(s, i=i, b=b, nrow=nrow, dst_off=dst_off, n_ac3=n_ac3):
            s.wait_ge(ctx.sem(S_AC), n_ac3)
            s.dma_start(out=g["xln1"][dst_off:dst_off + nrow, :], in_=xlt[0:nrow, b, :])\
                .then_inc(ctx.sem(q + "st" + str(b)), 16)

    lnac_rel = {}
    for i in range(NTILE):
        ln_tile(i, 128, 3 + i * 128, 3 + i * 128)
    ln_tile(NTILE, 3, 0, 0)
    n_st0 = ctx.val(q + "st0")
    n_st1 = ctx.val(q + "st1")

    @ctx.on("sync")
    def _(s):
        for o in range(32):
            s.dma_start(out=winb[:, o, :, :], in_=g["winT_b"][o].rearrange("p (a h) -> p a h", a=8))\
                .then_inc(ctx.sem(q + "wl"), 16)
    n_wl = ctx.bump(q + "wl", 16 * 32)

    @ctx.on("scalar")
    def _(sc):
        sc.wait_ge(ctx.sem(q + "st0"), n_st0)
        sc.wait_ge(ctx.sem(q + "st1"), n_st1)
        for kc in range(8):
            sc.dma_start_transpose(out=xlnTh[:, kc, :], in_=g["xln1"][0:16, kc * 128:(kc + 1) * 128])\
                .then_inc(ctx.sem(q + "trh"), 16)
    n_trh = ctx.bump(q + "trh", 16 * 8)

    xzP, cmp1 = ctx.ps(q + "xzP", [128, 4, TC], F32)
    hP, cmp2 = ctx.ps(q + "hP", [128, 16], F32)
    cms += [cmp1, cmp2]

    for o in range(16):
        n_hpe = ctx.bump(q + "hpe", 1)
        @ctx.on("tensor")
        def _(t, o=o):
            if o == 0:
                t.wait_ge(ctx.sem(q + "trh"), n_trh)
                t.wait_ge(ctx.sem(q + "wl"), n_wl)
            if o >= 1:
                t.wait_ge(ctx.sem(q + "hev"), o)
            last = None
            for kc in range(8):
                last = t.matmul(hP[:, 0:16], winb[:, o, kc, :], xlnTh[:, kc, :],
                                start=(kc == 0), stop=(kc == 7))
            last.then_inc(ctx.sem(q + "hpe"), 1)
        ctx.bump(q + "hev", 1)
        @ctx.on("scalar")
        def _(sc, o=o, n_hpe=n_hpe):
            sc.wait_ge(ctx.sem(q + "hpe"), n_hpe)
            sc.activation(out=carry[:, o, :], in_=hP[:, 0:3], func=AF.Copy)\
                .then_inc(ctx.sem(q + "hev"), 1)
    n_hev = ctx.val(q + "hev")

    cv_rel = {}
    last_uac = [0]
    for c in range(NCH):
        b = c % 2
        ctx.bump(q + "tr" + str(b), 16 * 8)
        @ctx.on("scalar")
        def _(sc, c=c, b=b):
            if c >= 2:
                sc.wait_ge(ctx.sem(q + "pe"), pe_rel[c - 2])
            for kc in range(8):
                sc.dma_start_transpose(out=xlnT[:, b, kc, :],
                                       in_=g["xln1"][3 + c * TC: 3 + (c + 1) * TC,
                                                     kc * 128:(kc + 1) * 128])\
                    .then_inc(ctx.sem(q + "tr" + str(b)), 16)
        if c == 0:
            pe_rel = {}
        n_tr = ctx.val(q + "tr" + str(b))
        for o in range(32):
            pb = o % 4
            ob = o % 2
            n_pe = ctx.bump(q + "pe", 1)
            @ctx.on("tensor")
            def _(t, c=c, b=b, o=o, pb=pb, n_tr=n_tr):
                if o == 0:
                    t.wait_ge(ctx.sem(q + "tr" + str(b)), n_tr)
                    if c == 0:
                        t.wait_ge(ctx.sem(q + "hev"), n_hev)
                j = c * 32 + o
                if j >= 4:
                    t.wait_ge(ctx.sem(q + "ev"), j - 3)
                last = None
                for kc in range(8):
                    last = t.matmul(xzP[:, pb, :], winb[:, o, kc, :], xlnT[:, b, kc, :],
                                    start=(kc == 0), stop=(kc == 7))
                last.then_inc(ctx.sem(q + "pe"), 1)
            pe_rel[c] = ctx.val(q + "pe")
            n_ev = ctx.bump(q + "ev", 1)
            @ctx.on("scalar")
            def _(sc, c=c, o=o, pb=pb, ob=ob, n_pe=n_pe):
                sc.wait_ge(ctx.sem(q + "pe"), n_pe)
                if o < 16:
                    jx = c * 16 + o
                    if jx >= 2:
                        sc.wait_ge(ctx.sem(S_VS), cv_rel[jx - 2])
                    sc.activation(out=xi1[:, ob, 3:3 + TC], in_=xzP[:, pb, :], func=AF.Copy)\
                        .then_inc(ctx.sem(q + "ev"), 1)
                else:
                    jz = c * 16 + (o - 16)
                    if jz >= 2:
                        sc.wait_ge(ctx.sem(q + "zs" + str(ob)), (jz // 2) * 16)
                    sc.activation(out=sg1[:, ob, :], in_=xzP[:, pb, :], func=AF.Silu)\
                        .then_inc(ctx.sem(q + "ev"), 1)
            if o >= 16:
                ctx.bump(q + "zs" + str(ob), 16)
                @ctx.on("sync")
                def _(s, c=c, o=o, ob=ob, n_ev=n_ev):
                    s.wait_ge(ctx.sem(q + "ev"), n_ev)
                    s.dma_start(out=g["sg_h"][(o - 16) * 128:(o - 15) * 128, c * TC:(c + 1) * TC],
                                in_=sg1[:, ob, :]).then_inc(ctx.sem(q + "zs" + str(ob)), 16)
            else:
                n_hd = ctx.bump(S_VS, 1)
                @ctx.on("vector")
                def _(v, o=o, ob=ob, n_ev=n_ev):
                    v.wait_ge(ctx.sem(q + "ev"), n_ev)
                    v.tensor_copy(xi1[:, ob, 0:3], carry[:, o, :])
                    v.engine_nop().then_inc(ctx.sem(S_VS), 1)
                n_vc = ctx.bump(S_VS, 4)
                @ctx.on("vector")
                def _(v, c=c, o=o, ob=ob, base=n_vc - 4, n_hd=n_hd, w_uac=last_uac[0]):
                    v.wait_ge(ctx.sem(S_VS), n_hd)
                    v.wait_ge(ctx.sem(S_AC), w_uac)
                    v.scalar_tensor_tensor(xc_sb[:], xi1[:, ob, 0:TC], g["cwcol"][:, 0 * 16 + o:0 * 16 + o + 1],
                                           g["cbcol"][:, o:o + 1].broadcast_to([128, TC]),
                                           op0=ALU.mult, op1=ALU.add)
                    v.engine_nop().then_inc(ctx.sem(S_VS), 1)
                    for k in range(1, 4):
                        v.wait_ge(ctx.sem(S_VS), base + k)
                        v.scalar_tensor_tensor(xc_sb[:], xi1[:, ob, k:k + TC], g["cwcol"][:, k * 16 + o:k * 16 + o + 1],
                                               xc_sb[:], op0=ALU.mult, op1=ALU.add)
                        v.engine_nop().then_inc(ctx.sem(S_VS), 1)
                n_cr = ctx.bump(S_VS, 1)
                cv_rel[c * 16 + o] = n_cr
                @ctx.on("vector")
                def _(v, o=o, ob=ob, n_vc=n_vc):
                    v.wait_ge(ctx.sem(S_VS), n_vc)
                    v.tensor_copy(carry[:, o, :], xi1[:, ob, TC:TC + 3])
                    v.engine_nop().then_inc(ctx.sem(S_VS), 1)
                n_uac = ctx.bump(S_AC, 1)
                last_uac[0] = n_uac
                @ctx.on("scalar")
                def _(sc, c=c, o=o, ob=ob, n_vc=n_vc):
                    sc.wait_ge(ctx.sem(S_VS), n_vc)
                    jx = c * 16 + o
                    if jx >= 2:
                        sc.wait_ge(ctx.sem(q + "us" + str(ob)), (jx // 2) * 32)
                    sc.activation(out=u1[:, ob, :], in_=xc_sb[:], func=AF.Silu).then_inc(ctx.sem(S_AC), 1)
                n_vcc = ctx.bump(S_VS, 1)
                @ctx.on("vector")
                def _(v, o=o, ob=ob, n_uac=n_uac):
                    v.wait_ge(ctx.sem(S_AC), n_uac)
                    v.tensor_copy(ub1[:, ob, :], u1[:, ob, :])
                    v.engine_nop().then_inc(ctx.sem(S_VS), 1)
                ctx.bump(q + "us" + str(ob), 32)
                @ctx.on("sync")
                def _(s, c=c, o=o, ob=ob, n_vcc=n_vcc):
                    s.wait_ge(ctx.sem(S_VS), n_vcc)
                    s.dma_start(out=g["u_h"][o * 128:(o + 1) * 128, c * TC:(c + 1) * TC],
                                in_=u1[:, ob, :]).then_inc(ctx.sem(q + "us" + str(ob)), 16)
                    s.dma_start(out=g["ubf_h"][o * 128:(o + 1) * 128, c * TC:(c + 1) * TC],
                                in_=ub1[:, ob, :]).then_inc(ctx.sem(q + "us" + str(ob)), 16)
    @ctx.on("sync")
    def _(s, a=ctx.val(q + "us0"), bb=ctx.val(q + "us1"),
          cz=ctx.val(q + "zs0"), dz=ctx.val(q + "zs1")):
        s.wait_ge(ctx.sem(q + "us0"), a)
        s.wait_ge(ctx.sem(q + "us1"), bb)
        s.wait_ge(ctx.sem(q + "zs0"), cz)
        s.wait_ge(ctx.sem(q + "zs1"), dz)
    ctx.close_tensors(cms)


def emit_m2(ctx, g):
    q = "m2"
    cms = []
    def sb(name, shape, dt):
        h, cm = ctx.sb(q + name, shape, dt)
        cms.append(cm)
        return h

    wxb = sb("wxb", [128, 16, 2 * DS], BF16)
    wdtg = sb("wdtg", [128, 4, 16, 128], BF16)
    ub2 = sb("ub2", [128, 2, 16, TC], BF16)
    u2 = sb("u2", [128, 2, 4, TC], F32)
    sg2 = sb("sg2", [128, 2, 4, TC], F32)
    dt_sb = sb("dt_sb", [128, 4, TC], F32)
    du_sb = sb("du_sb", [128, 4, TC], F32)
    a_sb = sb("a_sb", [128, TC], F32)
    bhi = sb("bhi", [16, TPC], BF16)
    blo = sb("blo", [16, TPC], BF16)
    chi = sb("chi", [16, TPC], BF16)
    clo = sb("clo", [16, TPC], BF16)
    Bbq = sb("Bbq", [128, 2, 4, TC], F32)
    Cbq = sb("Cbq", [128, 4, TC], F32)
    dAq = sb("dAq", [128, 2, 4, TC], F32)
    dBq = sb("dBq", [128, 4, TC], F32)
    sq = sb("sq", [128, 4, TC], F32)
    y_acc = sb("y_acc", [128, 4, TC], F32)
    ygb = sb("ygb", [128, 2, TC], BF16)
    state_sb = sb("state_sb", [128, 256], F32)
    st_imp = sb("st_imp", [128, 256], F32)
    pr_f = sb("pr_f", [128, 1], F32)
    pr_u = sb("pr_u", [128, 1], U32)

    S_VS = q + "vs"; S_AC = q + "ac"; S_PE = q + "pe"; S_PL = q + "pl"

    r_ag = ctx.nc.gpsimd.alloc_register(q + "rag")
    r_gp = ctx.nc.gpsimd.alloc_register(q + "rgp")
    r_stg = ctx.nc.vector.alloc_register(q + "rstg")
    r_si = ctx.nc.vector.alloc_register(q + "rsi")

    @ctx.on("sync")
    def _(s):
        for kc in range(16):
            s.dma_start(out=wxb[:, kc, :], in_=g["wxT_b"][kc]).then_inc(ctx.sem(q + "wx"), 16)
    n_wx = ctx.bump(q + "wx", 16 * 16)

    n_vz = ctx.bump(S_VS, 1)
    @ctx.on("vector")
    def _(v):
        v.memset(state_sb[:], 0.0)
        v.memset(st_imp[:], 0.0)
        v.engine_nop().then_inc(ctx.sem(S_VS), 1)
    @ctx.on("gpsimd")
    def _(gp):
        gp.iota(pr_f[:], pattern=[[0, 1]], base=0, channel_multiplier=1,
                allow_small_or_imprecise_dtypes=True)
        gp.engine_nop().then_inc(ctx.sem(S_PL), 1)
    n_pl0 = ctx.bump(S_PL, 1)
    n_vpr1 = ctx.bump(S_VS, 1)
    @ctx.on("vector")
    def _(v):
        v.wait_ge(ctx.sem(S_PL), n_pl0)
        v.tensor_scalar_add(pr_f[:], pr_f[:], g["pcf_col"][:, 5:6])
        v.engine_nop().then_inc(ctx.sem(S_VS), 1)
    ctx.bump(S_VS, 1)
    @ctx.on("vector")
    def _(v, w=n_vpr1):
        v.wait_ge(ctx.sem(S_VS), w)
        v.tensor_copy(pr_u[:], pr_f[:])
        v.engine_nop().then_inc(ctx.sem(S_VS), 1)
    n_pr = ctx.val(S_VS)

    dtP, cmp1 = ctx.ps(q + "dtP", [128, 2, TC], F32)
    bP, cmp2 = ctx.ps(q + "bP", [16, TC], F32)
    cP, cmp3 = ctx.ps(q + "cP", [16, TC], F32)
    bcP, cmp4 = ctx.ps(q + "bcP", [128, 4, TC], F32)
    cms += [cmp1, cmp2, cmp3, cmp4]

    scan_hist = {}
    dbu_hist = {}
    cm_hist = {}
    pe_group_rel = {}
    gt_idx = [0]

    for gi in range(NG):
        @ctx.on("gpsimd")
        def _(gp, gi=gi):
            if gi == 0:
                gp.wait_ge(ctx.sem(S_VS), n_pr)
            gp.reg_load(r_gp, g["pcu"][0:1, 8 + gi:9 + gi])
            gp.wait_ge(ctx.sem(q + "gp"), r_gp)
            gp.dma_start(out=g[f"st_in_{gi}"], in_=st_imp[:, gi * 64:(gi + 1) * 64])\
                .then_inc(ctx.sem(q + f"stg{gi}"), 16)
        n_stg = ctx.bump(q + f"stg{gi}", 16)
        @ctx.on("gpsimd")
        def _(gp, gi=gi, n_stg=n_stg):
            gp.wait_ge(ctx.sem(q + f"stg{gi}"), n_stg)
            gp.collective_compute(
                "AllGather", ALU.bypass,
                replica_groups=[list(range(NC))],
                ins=[g[f"st_in_{gi}"].opt()],
                outs=[g[f"st_out_{gi}"][0:NC * 128, :].opt()],
            ).then_inc(ctx.sem(q + f"ag{gi}"), 1)
        ctx.bump(q + f"ag{gi}", 1)
        @ctx.on("gpsimd")
        def _(gp, gi=gi):
            gp.wait_ge(ctx.sem("setup_z"), 16 * NG)
            gp.wait_ge(ctx.sem(q + f"si{gi}"), gi * 16)
            gp.reg_load(r_ag, g["pcu"][0:1, 12 + gi:13 + gi])
            gp.wait_ge(ctx.sem(q + f"ag{gi}"), r_ag)
            gp.indirect_dma_start(
                out=st_imp[:, gi * 64:(gi + 1) * 64],
                out_offset=None,
                in_=g[f"st_out_{gi}"],
                in_offset=IndirectOffsetOnAxis(ap=pr_u[:], axis=0),
            ).then_inc(ctx.sem(q + f"si{gi}"), 16)
        n_si = ctx.bump(q + f"si{gi}", 16)

        @ctx.on("sync")
        def _(s, gi=gi):
            if gi >= 1:
                s.wait_ge(ctx.sem(S_PE), pe_group_rel[gi - 1])
            for ot in range(4):
                s.dma_start(out=wdtg[:, ot, :, :],
                            in_=g["wdtT_b"][gi * 4 + ot].rearrange("p (a h) -> p a h", a=16))\
                    .then_inc(ctx.sem(q + "wd"), 16)
        n_wd = ctx.bump(q + "wd", 16 * 4)

        for c in range(NCH):
            b = c % 2
            pair = gi * NCH + c
            n_ub = ctx.bump(q + "ub" + str(b), 16 * 16)
            n_ug = ctx.bump(q + "ug" + str(b), 16 * 8)
            @ctx.on("sync")
            def _(s, gi=gi, c=c, b=b, pair=pair):
                if pair >= 2:
                    s.wait_ge(ctx.sem(q + "gt"), pair - 1)
                for kc in range(16):
                    s.dma_start(out=ub2[:, b, kc, :],
                                in_=g["ubf_h"][kc * 128:(kc + 1) * 128, c * TC:(c + 1) * TC])\
                        .then_inc(ctx.sem(q + "ub" + str(b)), 16)
                for dt_i in range(4):
                    r0 = (gi * 4 + dt_i) * 128
                    s.dma_start(out=u2[:, b, dt_i, :],
                                in_=g["u_h"][r0:r0 + 128, c * TC:(c + 1) * TC])\
                        .then_inc(ctx.sem(q + "ug" + str(b)), 16)
                    s.dma_start(out=sg2[:, b, dt_i, :],
                                in_=g["sg_h"][r0:r0 + 128, c * TC:(c + 1) * TC])\
                        .then_inc(ctx.sem(q + "ug" + str(b)), 16)
            for dt_i in range(4):
                pb = (c * 4 + dt_i) % 2
                j = pair * 4 + dt_i
                n_pe = ctx.bump(S_PE, 1)
                @ctx.on("tensor")
                def _(t, gi=gi, c=c, b=b, dt_i=dt_i, pb=pb, j=j, n_ub=n_ub, n_wd=n_wd):
                    if c == 0 and dt_i == 0:
                        t.wait_ge(ctx.sem(q + "wd"), n_wd)
                        if gi == 0:
                            t.wait_ge(ctx.sem(q + "wx"), n_wx)
                    if dt_i == 0:
                        t.wait_ge(ctx.sem(q + "ub" + str(b)), n_ub)
                    if j >= 2:
                        t.wait_ge(ctx.sem(S_AC), spab_hist[j - 2])
                    last = None
                    for kc in range(16):
                        last = t.matmul(dtP[:, pb, :], wdtg[:, dt_i, kc, :], ub2[:, b, kc, :],
                                        start=(kc == 0), stop=(kc == 15))
                    last.then_inc(ctx.sem(S_PE), 1)
                n_sp1 = ctx.bump(S_AC, 2)
                if not (gi == 0 and c == 0 and dt_i == 0):
                    pass
                else:
                    spab_hist = {}
                spab_hist[j] = n_sp1
                w_spl = spl_hist[-1] if (gi + c + dt_i) > 0 else 0
                @ctx.on("scalar")
                def _(sc, gi=gi, dt_i=dt_i, pb=pb, n_pe=n_pe, j=j, w_spl=w_spl):
                    sc.wait_ge(ctx.sem(S_PE), n_pe)
                    if j >= 1:
                        sc.wait_ge(ctx.sem(S_VS), w_spl)
                    bcol = g["bdtcol"][:, gi * 4 + dt_i:gi * 4 + dt_i + 1]
                    sc.activation(out=dt_sb[:, dt_i, :], in_=dtP[:, pb, :], func=AF.Relu, bias=bcol)
                    sc.activation(out=a_sb[:], in_=dtP[:, pb, :], func=AF.Abs, bias=bcol)\
                        .then_inc(ctx.sem(S_AC), 2)
                n_sp2 = ctx.bump(S_AC, 1)
                n_sp3 = ctx.bump(S_AC, 1)
                @ctx.on("scalar")
                def _(sc, n_sp1=n_sp1, n_sp2=n_sp2):
                    sc.wait_ge(ctx.sem(S_AC), n_sp1)
                    sc.activation(out=a_sb[:], in_=a_sb[:], func=AF.Exp, scale=-1.0)\
                        .then_inc(ctx.sem(S_AC), 1)
                    sc.wait_ge(ctx.sem(S_AC), n_sp2)
                    sc.activation(out=a_sb[:], in_=a_sb[:], func=AF.Ln, bias=1.0)\
                        .then_inc(ctx.sem(S_AC), 1)
                if pair == 0 and dt_i == 0:
                    spl_hist = []
                    spl_v_hist = {}
                    da_chunk_hist = {}
                n_vsp = ctx.bump(S_VS, 2)
                spl_v_hist[(gi, c, dt_i)] = n_vsp - 1
                @ctx.on("vector")
                def _(v, gi=gi, c=c, b=b, dt_i=dt_i, pair=pair, n_sp3=n_sp3, n_ug=n_ug,
                      base=n_vsp - 2):
                    v.wait_ge(ctx.sem(S_AC), n_sp3)
                    if pair >= 1:
                        v.wait_ge(ctx.sem(S_AC), da_chunk_hist[pair - 1])
                    v.tensor_add(dt_sb[:, dt_i, :], dt_sb[:, dt_i, :], a_sb[:])\
                        .then_inc(ctx.sem(S_VS), 1)
                    v.wait_ge(ctx.sem(S_VS), base + 1)
                    v.wait_ge(ctx.sem(q + "ug" + str(b)), n_ug)
                    v.tensor_mul(du_sb[:, dt_i, :], dt_sb[:, dt_i, :], u2[:, b, dt_i, :])\
                        .then_inc(ctx.sem(S_VS), 1)
                spl_hist.append(n_vsp - 1)
            n_du_all = ctx.val(S_VS)

            if gi == 0:
                n_bc = ctx.bump(S_PE, 2)
                @ctx.on("tensor")
                def _(t, c=c, b=b):
                    if c >= 2:
                        t.wait_ge(ctx.sem(q + "bcev"), (c - 1) * 4)
                    last = None
                    for kc in range(16):
                        last = t.matmul(bP[:], wxb[:, kc, 0:DS], ub2[:, b, kc, :],
                                        start=(kc == 0), stop=(kc == 15))
                    last.then_inc(ctx.sem(S_PE), 1)
                    last = None
                    for kc in range(16):
                        last = t.matmul(cP[:], wxb[:, kc, DS:2 * DS], ub2[:, b, kc, :],
                                        start=(kc == 0), stop=(kc == 15))
                    last.then_inc(ctx.sem(S_PE), 1)
                n_bhi = ctx.bump(S_AC, 2)
                @ctx.on("scalar")
                def _(sc, c=c, n_bc=n_bc):
                    sc.wait_ge(ctx.sem(S_PE), n_bc)
                    sc.activation(out=bhi[:, c * TC:(c + 1) * TC], in_=bP[:], func=AF.Copy)
                    sc.activation(out=chi[:, c * TC:(c + 1) * TC], in_=cP[:], func=AF.Copy)\
                        .then_inc(ctx.sem(S_AC), 2)
                n_bcev = ctx.bump(q + "bcev", 4)
                @ctx.on("vector")
                def _(v, c=c, n_bhi=n_bhi):
                    v.wait_ge(ctx.sem(S_AC), n_bhi)
                    v.scalar_tensor_tensor(blo[:, c * TC:(c + 1) * TC], bP[:], 0.0,
                                           bhi[:, c * TC:(c + 1) * TC], op0=ALU.bypass, op1=ALU.subtract)
                    v.scalar_tensor_tensor(clo[:, c * TC:(c + 1) * TC], cP[:], 0.0,
                                           chi[:, c * TC:(c + 1) * TC], op0=ALU.bypass, op1=ALU.subtract)
                    v.engine_nop().then_inc(ctx.sem(q + "bcev"), 4)
            n_bcev_need = 4 * (c + 1) if gi == 0 else 4 * NCH

            for qq in range(4):
                it = (pair * 4 + qq)
                n_bb = ctx.bump(S_PE, 2)
                bb_par = it % 2
                @ctx.on("tensor")
                def _(t, gi=gi, c=c, qq=qq, bb_par=bb_par, n_bcev_need=n_bcev_need, it=it):
                    if gi == 0:
                        t.wait_ge(ctx.sem(q + "bcev"), n_bcev_need)
                    elif c == 0 and qq == 0:
                        t.wait_ge(ctx.sem(q + "bcev"), 4 * NCH)
                    if it >= 1:
                        t.wait_ge(ctx.sem(q + "bbev"), it * 2)
                    last = None
                    for nl in range(4):
                        n = qq * 4 + nl
                        t.matmul(bcP[:, nl, :], g["seln"][:, n * 128:(n + 1) * 128],
                                 bhi[:, c * TC:(c + 1) * TC], start=True, stop=False)
                        last = t.matmul(bcP[:, nl, :], g["seln"][:, n * 128:(n + 1) * 128],
                                        blo[:, c * TC:(c + 1) * TC], start=False, stop=True)
                    last.then_inc(ctx.sem(S_PE), 1)
                    t.wait_ge(ctx.sem(q + "bbev"), it * 2 + 1)
                    last = None
                    for nl in range(4):
                        n = qq * 4 + nl
                        t.matmul(bcP[:, nl, :], g["seln"][:, n * 128:(n + 1) * 128],
                                 chi[:, c * TC:(c + 1) * TC], start=True, stop=False)
                        last = t.matmul(bcP[:, nl, :], g["seln"][:, n * 128:(n + 1) * 128],
                                        clo[:, c * TC:(c + 1) * TC], start=False, stop=True)
                    last.then_inc(ctx.sem(S_PE), 1)
                n_bbev1 = ctx.bump(q + "bbev", 1)
                n_bbev2 = ctx.bump(q + "bbev", 1)
                @ctx.on("scalar")
                def _(sc, qb=it % 2, w1=n_bb - 1, w2=n_bb, it=it):
                    sc.wait_ge(ctx.sem(S_PE), w1)
                    if it >= 2:
                        sc.wait_ge(ctx.sem(S_VS), dbu_hist[(it - 2) * 4 + 3])
                    sc.activation(out=Bbq[:, qb], in_=bcP[:, :, :], func=AF.Copy)\
                        .then_inc(ctx.sem(q + "bbev"), 1)
                    sc.wait_ge(ctx.sem(S_PE), w2)
                    if it >= 1:
                        sc.wait_ge(ctx.sem(S_VS), cm_hist[(it - 1) * 4 + 3])
                    sc.activation(out=Cbq[:], in_=bcP[:, :, :], func=AF.Copy)\
                        .then_inc(ctx.sem(q + "bbev"), 1)

                for dt_i in range(4):
                    dcol = gi * 4 + dt_i
                    itd = it * 4 + dt_i
                    ib = itd % 2
                    n_da = ctx.bump(S_AC, 4)
                    @ctx.on("scalar")
                    def _(sc, gi=gi, c=c, qq=qq, dt_i=dt_i, ib=ib, dcol=dcol, itd=itd):
                        sc.wait_ge(ctx.sem(S_VS), spl_v_hist[(gi, c, dt_i)])
                        if itd >= 2:
                            sc.wait_ge(ctx.sem(S_VS), scan_hist[itd - 2])
                        for nl in range(4):
                            acol = g["Acols"][:, dcol * 16 + qq * 4 + nl: dcol * 16 + qq * 4 + nl + 1]
                            sc.activation(out=dAq[:, ib, nl, :], in_=dt_sb[:, dt_i, :],
                                          func=AF.Exp, scale=acol).then_inc(ctx.sem(S_AC), 1)
                    n_db = ctx.bump(S_VS, 1)
                    dbu_hist[itd] = n_db
                    @ctx.on("vector")
                    def _(v, qb=it % 2, dt_i=dt_i, n_bbev1=n_bbev1):
                        v.wait_ge(ctx.sem(q + "bbev"), n_bbev1)
                        v.tensor_tensor(dBq[:],
                                        du_sb[:, dt_i, :].unsqueeze(1).broadcast_to([128, 4, TC]),
                                        Bbq[:, qb], op=ALU.mult)\
                            .then_inc(ctx.sem(S_VS), 1)
                    n_sc = ctx.bump(S_VS, 4)
                    scan_hist[itd] = n_sc
                    @ctx.on("vector")
                    def _(v, gi=gi, c=c, qq=qq, dt_i=dt_i, ib=ib, dcol=dcol,
                          n_da=n_da, n_db=n_db, n_si=n_si):
                        v.wait_ge(ctx.sem(S_AC), n_da)
                        v.wait_ge(ctx.sem(S_VS), n_db)
                        if c == 0 and qq == 0 and dt_i == 0:
                            v.reg_load(r_si, g["pcu"][0:1, 20 + gi:21 + gi])
                            v.wait_ge(ctx.sem(q + f"si{gi}"), r_si)
                        for nl in range(4):
                            scol = dcol * 16 + qq * 4 + nl
                            init = st_imp[:, scol:scol + 1] if c == 0 else state_sb[:, scol:scol + 1]
                            v.tensor_tensor_scan(out=sq[:, nl, :], data0=dAq[:, ib, nl, :],
                                                 data1=dBq[:, nl, :], initial=init,
                                                 op0=ALU.mult, op1=ALU.add)\
                                .then_inc(ctx.sem(S_VS), 1)
                    n_st = ctx.bump(S_VS, 1)
                    @ctx.on("vector")
                    def _(v, gi=gi, c=c, qq=qq, dt_i=dt_i, dcol=dcol, n_sc=n_sc):
                        v.wait_ge(ctx.sem(S_VS), n_sc)
                        scol = dcol * 16 + qq * 4
                        if c == NCH - 1:
                            v.reg_load(r_stg, g["pcu"][0:1, 16 + gi:17 + gi])
                            v.wait_ge(ctx.sem(q + f"stg{gi}"), r_stg)
                            v.tensor_copy(st_imp[:, scol:scol + 4],
                                          sq[:, 0:4, TC - 1:TC].rearrange("p a 1 -> p a"))\
                                .then_inc(ctx.sem(S_VS), 1)
                        else:
                            v.tensor_copy(state_sb[:, scol:scol + 4],
                                          sq[:, 0:4, TC - 1:TC].rearrange("p a 1 -> p a"))\
                                .then_inc(ctx.sem(S_VS), 1)
                    n_cm = ctx.bump(S_VS, 1)
                    cm_hist[itd] = n_cm
                    @ctx.on("vector")
                    def _(v, n_st=n_st):
                        v.wait_ge(ctx.sem(S_VS), n_st)
                        v.tensor_mul(dBq[:], sq[:], Cbq[:])\
                            .then_inc(ctx.sem(S_VS), 1)
                    n_red = ctx.bump(S_VS, 3 if qq == 0 else 4)
                    @ctx.on("vector")
                    def _(v, qq=qq, dt_i=dt_i, n_cm=n_cm, base=n_red - (3 if qq == 0 else 4)):
                        v.wait_ge(ctx.sem(S_VS), n_cm)
                        v.tensor_add(sq[:, 0, :], dBq[:, 0, :], dBq[:, 1, :])\
                            .then_inc(ctx.sem(S_VS), 1)
                        v.tensor_add(sq[:, 1, :], dBq[:, 2, :], dBq[:, 3, :])\
                            .then_inc(ctx.sem(S_VS), 1)
                        v.wait_ge(ctx.sem(S_VS), base + 2)
                        if qq == 0:
                            v.tensor_add(y_acc[:, dt_i, :], sq[:, 0, :], sq[:, 1, :])\
                                .then_inc(ctx.sem(S_VS), 1)
                        else:
                            v.tensor_add(sq[:, 0, :], sq[:, 0, :], sq[:, 1, :])\
                                .then_inc(ctx.sem(S_VS), 1)
                            v.wait_ge(ctx.sem(S_VS), base + 3)
                            v.tensor_add(y_acc[:, dt_i, :], y_acc[:, dt_i, :], sq[:, 0, :])\
                                .then_inc(ctx.sem(S_VS), 1)
            da_chunk_hist[pair] = ctx.val(S_AC)
            if c == NCH - 1:
                ctx.bump(q + "gp", 1)
                @ctx.on("vector")
                def _(v, w=ctx.val(S_VS)):
                    v.wait_ge(ctx.sem(S_VS), w)
                    v.engine_nop().then_inc(ctx.sem(q + "gp"), 1)
            for dt_i in range(4):
                n_g1 = ctx.bump(S_VS, 3)
                ysn = pair * 4 + dt_i
                @ctx.on("vector")
                def _(v, gi=gi, c=c, b=b, dt_i=dt_i, dcol=gi * 4 + dt_i, ysn=ysn,
                      base=n_g1 - 3, n_ug=n_ug):
                    v.wait_ge(ctx.sem(q + "ug" + str(b)), n_ug)
                    if ysn >= 2:
                        v.wait_ge(ctx.sem(q + "ys" + str(ysn % 2)), (ysn // 2) * 16)
                    v.scalar_tensor_tensor(y_acc[:, dt_i, :], u2[:, b, dt_i, :],
                                           g["dcol"][:, dcol:dcol + 1], y_acc[:, dt_i, :],
                                           op0=ALU.mult, op1=ALU.add)\
                        .then_inc(ctx.sem(S_VS), 1)
                    v.wait_ge(ctx.sem(S_VS), base + 1)
                    v.tensor_mul(y_acc[:, dt_i, :], y_acc[:, dt_i, :], sg2[:, b, dt_i, :])\
                        .then_inc(ctx.sem(S_VS), 1)
                    v.wait_ge(ctx.sem(S_VS), base + 2)
                    v.tensor_copy(ygb[:, ysn % 2, :], y_acc[:, dt_i, :])\
                        .then_inc(ctx.sem(S_VS), 1)
                ctx.bump(q + "ys" + str(ysn % 2), 16)
                @ctx.on("sync")
                def _(s, gi=gi, c=c, dt_i=dt_i, ysn=ysn, n_g1=n_g1):
                    s.wait_ge(ctx.sem(S_VS), n_g1)
                    r0 = (gi * 4 + dt_i) * 128
                    s.dma_start(out=g["yg_h"][r0:r0 + 128, c * TC:(c + 1) * TC],
                                in_=ygb[:, ysn % 2, :]).then_inc(ctx.sem(q + "ys" + str(ysn % 2)), 16)
            ctx.bump(q + "gt", 1)
            @ctx.on("vector")
            def _(v, w=ctx.val(S_VS)):
                v.wait_ge(ctx.sem(S_VS), w)
                v.engine_nop().then_inc(ctx.sem(q + "gt"), 1)
        pe_group_rel[gi] = ctx.val(S_PE)

    n_ys0_all = ctx.val(q + "ys0")
    n_ys1_all = ctx.val(q + "ys1")
    n_si_all = ctx.val(q + "si3")
    n_stg_all = ctx.val(q + "stg3")
    n_ag_all = ctx.val(q + "ag3")
    @ctx.on("sync")
    def _(s):
        s.wait_ge(ctx.sem(q + "ys0"), n_ys0_all)
        s.wait_ge(ctx.sem(q + "ys1"), n_ys1_all)
    @ctx.on("gpsimd")
    def _(gp):
        gp.wait_ge(ctx.sem(q + "si3"), n_si_all)
        gp.wait_ge(ctx.sem(q + "stg3"), n_stg_all)
        gp.wait_ge(ctx.sem(q + "ag3"), n_ag_all)
    ctx.close_tensors(cms)


def emit_m3(ctx, g):
    q = "m3"
    cms = []
    def sb(name, shape, dt):
        h, cm = ctx.sb(q + name, shape, dt)
        cms.append(cm)
        return h
    woutb = sb("woutb", [128, 16, D], BF16)
    ygc = sb("ygc", [128, 2, 16, TC], BF16)
    x1r = sb("x1r", [128, 2, D], F32)
    x2r = sb("x2r", [128, 2, D], F32)
    S_VS = q + "vs"

    @ctx.on("sync")
    def _(s):
        for kc in range(16):
            s.dma_start(out=woutb[:, kc, :], in_=g["woutT_b"][kc * 128:(kc + 1) * 128, :])\
                .then_inc(ctx.sem(q + "wl"), 16)
    n_wl = ctx.bump(q + "wl", 16 * 16)

    yoP, cmp = ctx.ps(q + "yoP", [128, 4, 512], F32)
    cms.append(cmp)

    ev_rel = {}
    for c in range(NCH):
        b = c % 2
        n_yl = ctx.bump(q + "yl" + str(b), 16 * 16)
        @ctx.on("sync")
        def _(s, c=c, b=b):
            if c >= 2:
                s.wait_ge(ctx.sem(q + "pe"), pe_rel[c - 2])
            for kc in range(16):
                s.dma_start(out=ygc[:, b, kc, :],
                            in_=g["yg_h"][kc * 128:(kc + 1) * 128, c * TC:(c + 1) * TC])\
                    .then_inc(ctx.sem(q + "yl" + str(b)), 16)
        if c == 0:
            pe_rel = {}
        for ts in range(4):
            tb = ts % 2
            jt = c * 4 + ts
            row0 = c * TC + ts * 128
            n_xl = ctx.bump(q + "xl" + str(tb), 16)
            @ctx.on("sync")
            def _(s, row0=row0, tb=tb, jt=jt):
                if jt >= 2:
                    s.wait_ge(ctx.sem(S_VS), ev_rel[jt - 2])
                s.dma_start(out=x1r[:, tb, :], in_=g["x1b"][3 + row0:3 + row0 + 128, :])\
                    .then_inc(ctx.sem(q + "xl" + str(tb)), 16)
            n_pe = ctx.bump(q + "pe", 2)
            @ctx.on("tensor")
            def _(t, c=c, b=b, ts=ts, jt=jt, n_yl=n_yl, base=n_pe - 2):
                if c == 0 and ts == 0:
                    t.wait_ge(ctx.sem(q + "wl"), n_wl)
                if ts == 0:
                    t.wait_ge(ctx.sem(q + "yl" + str(b)), n_yl)
                for dh in range(2):
                    pb = (jt * 2 + dh) % 4
                    if jt * 2 + dh >= 4:
                        t.wait_ge(ctx.sem(S_VS), ev_rel2[jt * 2 + dh - 4])
                    last = None
                    for kc in range(16):
                        last = t.matmul(yoP[:, pb, :],
                                        ygc[:, b, kc, ts * 128:(ts + 1) * 128],
                                        woutb[:, kc, dh * 512:(dh + 1) * 512],
                                        start=(kc == 0), stop=(kc == 15))
                    last.then_inc(ctx.sem(q + "pe"), 1)
            if c == 0 and ts == 0:
                ev_rel2 = {}
            n_ev = ctx.bump(S_VS, 2)
            @ctx.on("vector")
            def _(v, jt=jt, tb=tb, n_pe=n_pe, n_xl=n_xl, base=n_ev - 2):
                v.wait_ge(ctx.sem(q + "xl" + str(tb)), n_xl)
                for dh in range(2):
                    pb = (jt * 2 + dh) % 4
                    v.wait_ge(ctx.sem(q + "pe"), n_pe - 1 + dh)
                    if jt >= 2:
                        v.wait_ge(ctx.sem(q + "xs" + str(tb)), (jt // 2) * 16)
                    v.scalar_tensor_tensor(x2r[:, tb, dh * 512:(dh + 1) * 512],
                                           yoP[:, pb, :], 0.0, x1r[:, tb, dh * 512:(dh + 1) * 512],
                                           op0=ALU.bypass, op1=ALU.add)
                    v.engine_nop().then_inc(ctx.sem(S_VS), 1)
            ev_rel[jt] = n_ev
            ev_rel2[jt * 2] = n_ev - 1
            ev_rel2[jt * 2 + 1] = n_ev
            ctx.bump(q + "xs" + str(tb), 16)
            @ctx.on("sync")
            def _(s, row0=row0, tb=tb, n_ev=n_ev):
                s.wait_ge(ctx.sem(S_VS), n_ev)
                s.dma_start(out=g["x2b"][row0:row0 + 128, :], in_=x2r[:, tb, :])\
                    .then_inc(ctx.sem(q + "xs" + str(tb)), 16)
        pe_rel[c] = ctx.val(q + "pe")
    @ctx.on("sync")
    def _(s, a=ctx.val(q + "xs0"), bb=ctx.val(q + "xs1")):
        s.wait_ge(ctx.sem(q + "xs0"), a)
        s.wait_ge(ctx.sem(q + "xs1"), bb)
    ctx.close_tensors(cms)


def kernel(**inputs):
    in_maps = stage_inputs(inputs)
    nc = build_program()
    res = run_bass_kernel_spmd(nc, in_maps, core_ids=list(range(NC)))
    shards = [np.asarray(r["out"], np.float32) for r in res.results]
    return np.concatenate(shards, 0).reshape(B, S, D)


# version 06 (final)
# speedup vs baseline: 1.0737x; 1.0422x over previous
import re
import numpy as np
import ml_dtypes
import concourse.bass as bass
import concourse.mybir as mybir
from concourse.bass_utils import run_bass_kernel_spmd
from concourse.bass_core_rust import IndirectOffsetOnAxis

F32 = mybir.dt.float32
BF16 = mybir.dt.bfloat16
U32 = mybir.dt.uint32
AF = mybir.ActivationFunctionType
ALU = mybir.AluOpType
AX = mybir.AxisListType

B, S, D, DI, DS, E, H = 4, 8192, 1024, 2048, 16, 16, 4096
NTOK = B * S
CAP = NTOK // E
NC = 8
TPC = NTOK // NC
NTILE = TPC // 128
TC = 512
NCH = TPC // TC
NG = 4
DT_PER_G = 4
TRASH = NTOK
TL = 36992
BIGF = 65535.0
EPS = 1e-5

PCF_COREBASE = 0
PCF_PARITY = 1
PCF_SLOTBASE = 2
PCF_G0 = 4
PCU_PARTNER_ROW = 0
PCU_TABROW = 1


def _u32(x):
    return np.asarray(x, dtype=np.uint32)


class Ctx:

    def __init__(self, nc):
        self.nc = nc
        self.prog = []
        self.sems = {}
        self.cnt = {}
        self._cm = []
        self.bar_k = 0

    _ALIAS = {
        "m1ld0": "Fld0", "m1ld1": "Fld1", "m1ac": "Fac", "m1vs": "Fvs",
        "m1st0": "Fst0", "m1st1": "Fst1", "m1wl": "Fcl2", "m1trh": "Ftrx",
        "m2ac": "Fac", "m2vs": "Fvs", "m2pe": "Fpe",
        "m3vs": "Fvs", "m3pe": "Fpe", "m3wl": "Fcl2",
        "m3yl0": "Fld0", "m3yl1": "Fld1", "m3xl0": "Fxl0", "m3xl1": "Fxl1",
    }

    @staticmethod
    def canon(name):
        n2 = name
        if n2.startswith("f0") or n2.startswith("f1"):
            n2 = "F" + n2[2:]
            if n2.startswith("Fe0") or n2.startswith("Fe1"):
                n2 = "FE" + n2[3:]
        m = re.fullmatch(r"m2(stg|ag|si)\d", n2)
        if m:
            n2 = "m2" + m.group(1)
        return Ctx._ALIAS.get(n2, n2)

    def sem(self, name):
        name = self.canon(name)
        if name not in self.sems:
            cm = self.nc.semaphore(name)
            h = cm.__enter__()
            self._cm.append(cm)
            self.sems[name] = h
            self.cnt[name] = 0
        return self.sems[name]

    def bump(self, name, k):
        name = self.canon(name)
        self.sem(name)
        self.cnt[name] += k
        return self.cnt[name]

    def val(self, name):
        name = self.canon(name)
        self.sem(name)
        return self.cnt[name]

    def on(self, eng):
        def deco(fn):
            self.prog.append((eng, fn))
            return None
        return deco

    def sb(self, name, shape, dtype):
        cm = self.nc.sbuf_tensor(name, shape, dtype)
        h = cm.__enter__()
        return h, cm

    def ps(self, name, shape, dtype):
        cm = self.nc.psum_tensor(name, shape, dtype)
        h = cm.__enter__()
        return h, cm

    def close_tensors(self, cms):
        for cm in reversed(cms):
            cm.__exit__(None, None, None)

    def barrier(self, tag, engines=("sync", "vector", "scalar", "tensor", "gpsimd")):
        bar = self.sem("bar")
        n = len(engines)
        base = self.cnt["bar"]
        tgt = base + n
        self.cnt["bar"] = tgt
        for e in engines:
            k = self.bar_k
            if e == "scalar":
                self.bar_k += 1
            @self.on(e)
            def _(eng, bar=bar, tgt=tgt, e=e, k=k):
                if e in ("vector", "gpsimd"):
                    eng.engine_nop().then_inc(bar, 1)
                elif e == "scalar":
                    eng.wait_ge(self.sem("dumz"), 1)
                    eng.activation(out=self.dum[0:1, 2 + k:3 + k], in_=self.dum[0:1, 1:2],
                                   func=mybir.ActivationFunctionType.Copy).then_inc(bar, 1)
                else:
                    eng.nop().then_inc(bar, 1)
                eng.wait_ge(bar, tgt)

    def finalize(self):
        for cm in reversed(self._cm):
            cm.__exit__(None, None, None)


def stage_inputs(inp):
    x = np.asarray(inp["L0_attn_x"], np.float32).reshape(NTOK, D)
    W_in = np.asarray(inp["L0_attn_W_in"], np.float32)
    conv_w = np.asarray(inp["L0_attn_conv_w"], np.float32).reshape(DI, 4)
    conv_b = np.asarray(inp["L0_attn_conv_b"], np.float32)
    W_x = np.asarray(inp["L0_attn_W_x"], np.float32)
    W_dt = np.asarray(inp["L0_attn_W_dt"], np.float32)
    b_dt = np.asarray(inp["L0_attn_b_dt"], np.float32)
    A_log = np.asarray(inp["L0_attn_A_log"], np.float32)
    Dvec = np.asarray(inp["L0_attn_D"], np.float32)
    W_out = np.asarray(inp["L0_attn_W_out"], np.float32)
    Wg = np.asarray(inp["L0_ffn_Wg"], np.float32)
    bg = np.asarray(inp["L0_ffn_bg"], np.float32)
    W1 = np.asarray(inp["L0_ffn_W1"], np.float32)
    b1 = np.asarray(inp["L0_ffn_b1"], np.float32)
    W2 = np.asarray(inp["L0_ffn_W2"], np.float32)
    b2 = np.asarray(inp["L0_ffn_b2"], np.float32)

    winT = np.ascontiguousarray(W_in.T)
    wdtT = np.ascontiguousarray(W_dt.T)
    woutT = np.ascontiguousarray(W_out.T)
    wxT = np.ascontiguousarray(W_x.T)

    in_maps = []
    for c in range(NC):
        t0 = c * TPC
        idx = np.clip(np.arange(t0 - 3, t0 + TPC), 0, NTOK - 1)
        pc_f = np.zeros((1, 32), np.float32)
        pc_f[0, PCF_COREBASE] = t0
        pc_f[0, PCF_PARITY] = c % 2
        pc_f[0, PCF_SLOTBASE] = c * TPC
        pc_f[0, 5] = (c - 1) * 128 if c % 2 == 1 else NC * 128
        pc_u = np.zeros((1, 32), np.uint32)
        pc_u[0, PCU_PARTNER_ROW] = (c - 1) * 128 if c % 2 == 1 else NC * 128
        pc_u[0, PCU_TABROW] = (c * TPC) // 32
        for gg in range(NG):
            pc_u[0, 8 + gg] = 0 if c % 2 == 1 else gg + 1
            pc_u[0, 12 + gg] = gg + (c % 2)
            pc_u[0, 16 + gg] = (gg + (c % 2)) * 16
            pc_u[0, 20 + gg] = (c % 2) * (gg + 1) * 16
        in_maps.append({
            "x_buf": np.ascontiguousarray(x[idx]),
            "winT": winT, "wdtT": wdtT, "woutT": woutT, "wxT": wxT,
            "convw": np.ascontiguousarray(conv_w.reshape(16, 128, 4).transpose(1, 2, 0)),
            "convb": np.ascontiguousarray(conv_b.reshape(16, 128).T),
            "bdt": np.ascontiguousarray(b_dt.reshape(16, 128).T),
            "alog": np.ascontiguousarray(A_log.reshape(16, 128, DS).transpose(1, 0, 2)),
            "dvec": np.ascontiguousarray(Dvec.reshape(16, 128).T),
            "wg": Wg, "bgv": bg.reshape(1, E),
            "w1": np.ascontiguousarray(W1[2 * c:2 * c + 2]),
            "w2": np.ascontiguousarray(W2[2 * c:2 * c + 2]),
            "b1v": np.ascontiguousarray(b1[2 * c:2 * c + 2].reshape(2, 32, 128).transpose(0, 2, 1)),
            "b2v": np.ascontiguousarray(b2[2 * c:2 * c + 2]),
            "pc_f": pc_f, "pc_u": pc_u,
        })
    return in_maps


def build_program(phases=("w", "f1", "m1", "m2", "m3", "f2"), dbg=()):
    nc = bass.Bass("TRN2", target_bir_lowering=False)
    ctx = Ctx(nc)
    g = {}
    _dum_cm = nc.sbuf_tensor("bar_dum", [1, 64], F32)
    ctx.dum = _dum_cm.__enter__()
    @ctx.on("scalar")
    def _(sc):
        sc.memzero(ctx.dum[:]).then_inc(ctx.sem("dumz"), 1)
    ctx.bump("dumz", 1)

    def ein(name, shape, dt=F32):
        g[name] = nc.dram_tensor(name, shape, dt, kind="ExternalInput").ap()
    ein("x_buf", [TPC + 3, D])
    ein("winT", [D, 2 * DI]); ein("wdtT", [DI, DI]); ein("woutT", [DI, D])
    ein("wxT", [DI, 2 * DS])
    ein("convw", [128, 4, 16]); ein("convb", [128, 16]); ein("bdt", [128, 16])
    ein("alog", [128, 16, DS]); ein("dvec", [128, 16])
    ein("wg", [D, E]); ein("bgv", [1, E])
    ein("w1", [2, D, H]); ein("w2", [2, H, D])
    ein("b1v", [2, 128, 32]); ein("b2v", [2, D])
    ein("pc_f", [1, 32]); ein("pc_u", [1, 32], U32)

    g["out"] = nc.dram_tensor("out", [TPC, D], F32, kind="ExternalOutput").ap()

    def dscr(name, shape, dt):
        g[name] = nc.dram_tensor(name, shape, dt, kind="Internal").ap()
    dscr("winT_b", [32, 8, 128, 128], BF16)
    dscr("wdtT_b", [16, 16, 128, 128], BF16)
    dscr("woutT_b", [DI, D], BF16)
    dscr("wxT_b", [16, 128, 2 * DS], BF16)
    dscr("wg_b", [8, 128, E], BF16)
    dscr("w1_b", [2, 32, 8, 128, 128], BF16)
    dscr("w2_b", [2, H, D], BF16)
    dscr("xln_ag_in", [TPC, D], BF16); dscr("xln_full", [NTOK, D], BF16)
    dscr("tab_in", [TL, 1], U32); dscr("tab_out", [(NC * TL) // 16, 16], U32)
    dscr("cnt_in", [1, 32], F32); dscr("cnt_out", [NC, 32], F32)
    dscr("y_in", [TPC, D], F32); dscr("y_full", [NTOK, D], F32)
    dscr("xg", [CAP, D], BF16)
    dscr("x1b", [TPC + 3, D], F32)
    dscr("xln1", [TPC + 3, D], BF16)
    dscr("u_h", [DI, TPC], F32); dscr("sg_h", [DI, TPC], F32)
    dscr("ubf_h", [DI, TPC], BF16); dscr("yg_h", [DI, TPC], BF16)
    dscr("x2b", [TPC, D], F32)
    dscr("hmx", [1, 6], F32)
    dscr("route_dbg", [128, 4 * NTILE], F32)
    dscr("hmst", [2, 128], F32)
    for gg in range(NG):
        dscr(f"st_in_{gg}", [128, DT_PER_G * DS], F32)
        dscr(f"st_out_{gg}", [(NC + 1) * 128, DT_PER_G * DS], F32)

    for name, shape, dt in dbg:
        g["dbg_" + name] = nc.dram_tensor("dbg_" + name, shape, dt, kind="ExternalOutput").ap()

    sb_keep = []
    def SB(name, shape, dt):
        h, cm = ctx.sb(name, shape, dt)
        sb_keep.append(cm)
        g[name] = h
        return h

    SB("pcf", [1, 32], F32)
    SB("pcu", [1, 32], U32)
    SB("pcf_col", [128, 32], F32)
    SB("iotaP", [128, 1], F32)
    SB("iotaE", [128, E], F32)
    SB("ltU", [128, 128], BF16)
    SB("ones128", [128, 1], BF16)
    SB("onesK1", [1, 128], BF16)
    SB("idN", [128, 128], BF16)
    SB("seln", [E, 128 * E], BF16)
    SB("trashc", [128, NTILE], F32)
    SB("Acols", [128, 16 * DS], F32)
    SB("dcol", [128, 16], F32)
    SB("bdtcol", [128, 16], F32)
    SB("cbcol", [128, 16], F32)
    SB("cwcol", [128, 16 * 4], F32)
    SB("zero128", [128, 64], F32)
    SB("cstF", [128, 8], F32)

    g["r_ntok"] = nc.gpsimd.alloc_register("r_ntok")

    def emit_setup():
        @ctx.on("gpsimd")
        def _(gp):
            gp.reg_mov(g["r_ntok"], NTOK - 1)
        consts = [1e-5, 1.0, -0.5, -1.0, 0.5, 2.0]
        @ctx.on("vector")
        def _(v):
            for j, cv in enumerate(consts):
                v.memset(g["cstF"][:, j:j + 1], float(cv))
            v.engine_nop().then_inc(ctx.sem("setup_v"), 1)
        ctx.bump("setup_v", 1)
        for j, cv in enumerate(consts):
            nc.const_aps.register(F32, float(cv), g["cstF"][:, j:j + 1])

        ld = "s_ld"
        @ctx.on("sync")
        def _(s):
            s.dma_start(out=g["pcf"][:], in_=g["pc_f"]).then_inc(ctx.sem(ld), 16)
            s.dma_start(out=g["pcu"][:], in_=g["pc_u"]).then_inc(ctx.sem(ld), 16)
            s.dma_start(out=g["pcf_col"][:], in_=g["pc_f"].broadcast_to([128, 32])).then_inc(ctx.sem(ld), 16)
            s.dma_start(out=g["bdtcol"][:], in_=g["bdt"]).then_inc(ctx.sem(ld), 16)
            s.dma_start(out=g["cbcol"][:], in_=g["convb"]).then_inc(ctx.sem(ld), 16)
            s.dma_start(out=g["dcol"][:], in_=g["dvec"]).then_inc(ctx.sem(ld), 16)
            s.dma_start(out=g["cwcol"][:].rearrange("p (k a) -> p k a", k=4), in_=g["convw"]).then_inc(ctx.sem(ld), 16)
            s.dma_start(out=g["Acols"][:].rearrange("p (a n) -> p a n", a=16), in_=g["alog"]).then_inc(ctx.sem(ld), 16)
        ctx.bump(ld, 16 * 8)
        ldv = ctx.val(ld)

        @ctx.on("gpsimd")
        def _(gp):
            gp.iota(g["iotaP"][:], pattern=[[0, 1]], base=0, channel_multiplier=1,
                    allow_small_or_imprecise_dtypes=True)
            gp.iota(g["iotaE"][:], pattern=[[1, E]], base=0, channel_multiplier=0,
                    allow_small_or_imprecise_dtypes=True)
            gp.iota(g["trashc"][:], pattern=[[128, NTILE]], base=TRASH, channel_multiplier=1,
                    allow_small_or_imprecise_dtypes=True)
            gp.memset(g["zero128"][:], 0.0)
            gp.engine_nop().then_inc(ctx.sem("setup_g"), 1)
        ctx.bump("setup_g", 1)

        tmpi, cmi = ctx.sb("tmp_ii", [128, 128], F32)
        tmpj, cmj = ctx.sb("tmp_jj", [128, 128], F32)
        @ctx.on("gpsimd")
        def _(gp):
            gp.iota(tmpi[:], pattern=[[0, 128]], base=0, channel_multiplier=1,
                    allow_small_or_imprecise_dtypes=True)
            gp.iota(tmpj[:], pattern=[[1, 128]], base=0, channel_multiplier=0,
                    allow_small_or_imprecise_dtypes=True)
            gp.engine_nop().then_inc(ctx.sem("setup_g"), 1)
        ctx.bump("setup_g", 1)
        sg = ctx.val("setup_g")

        @ctx.on("vector")
        def _(v):
            v.wait_ge(ctx.sem("setup_g"), sg)
            v.wait_ge(ctx.sem(ld), ldv)
            v.tensor_tensor(out=g["idN"][:], in0=tmpi[:], in1=tmpj[:], op=ALU.is_equal)
            v.tensor_tensor(out=g["ltU"][:], in0=tmpi[:], in1=tmpj[:], op=ALU.is_lt)
            v.memset(g["ones128"][:], 1.0)
            v.memset(g["onesK1"][:], 1.0)
            v.engine_nop().then_inc(ctx.sem("setup_v"), 1)
        ctx.bump("setup_v", 1)
        si, cmsi = ctx.sb("tmp_si", [16, 128 * E], F32)
        sj, cmsj = ctx.sb("tmp_sj", [16, 128 * E], F32)
        @ctx.on("gpsimd")
        def _(gp):
            gp.iota(si[:], pattern=[[0, 128 * E]], base=0, channel_multiplier=1,
                    allow_small_or_imprecise_dtypes=True)
            gp.iota(sj[:], pattern=[[1, E], [0, 128]], base=0, channel_multiplier=0,
                    allow_small_or_imprecise_dtypes=True)
            gp.engine_nop().then_inc(ctx.sem("setup_g"), 1)
        ctx.bump("setup_g", 1)
        sg2 = ctx.val("setup_g")
        @ctx.on("vector")
        def _(v):
            v.wait_ge(ctx.sem("setup_g"), sg2)
            v.tensor_tensor(out=g["seln"][:], in0=si[:], in1=sj[:], op=ALU.is_equal)
            v.engine_nop().then_inc(ctx.sem("setup_v"), 1)
        ctx.bump("setup_v", 1)
        ctx.close_tensors([cmsi, cmsj])

        @ctx.on("scalar")
        def _(sc):
            sc.wait_ge(ctx.sem(ld), ldv)
            sc.activation(out=g["Acols"][:], in_=g["Acols"][:], func=AF.Exp).then_inc(ctx.sem("setup_a"), 1)
        ctx.bump("setup_a", 1)
        sa = ctx.val("setup_a")
        @ctx.on("vector")
        def _(v):
            v.wait_ge(ctx.sem("setup_a"), sa)
            v.tensor_scalar_mul(g["Acols"][:], g["Acols"][:], -1.0)
            v.engine_nop().then_inc(ctx.sem("setup_v"), 1)
        ctx.bump("setup_v", 1)

        n_sg_for_zero = ctx.val("setup_g")
        @ctx.on("sync")
        def _(s):
            s.wait_ge(ctx.sem("setup_g"), n_sg_for_zero)
            for gg in range(NG):
                s.dma_start(out=g[f"st_out_{gg}"][NC * 128:(NC + 1) * 128, :], in_=g["zero128"][:])\
                    .then_inc(ctx.sem("setup_z"), 16)
        ctx.bump("setup_z", 16 * NG)
        ctx.close_tensors([cmi, cmj])

    def emit_wconv():
        wtile, cmw = ctx.sb("wconv_tile", [128, 2, 4096], BF16)
        jobs = []
        jobs.append((g["wg"].rearrange("(a p) n -> p a n", p=128),
                     g["wg_b"].rearrange("a p n -> p a n")))
        jobs.append((g["wxT"].rearrange("(a p) n -> p a n", p=128),
                     g["wxT_b"].rearrange("a p n -> p a n")))
        for o in range(32):
            jobs.append((g["winT"][:, o * 128:(o + 1) * 128].rearrange("(a p) h -> p a h", p=128),
                         g["winT_b"][o].rearrange("a p h -> p a h")))
        for o in range(16):
            jobs.append((g["wdtT"][:, o * 128:(o + 1) * 128].rearrange("(a p) h -> p a h", p=128),
                         g["wdtT_b"][o].rearrange("a p h -> p a h")))
        for r0 in range(0, DI, 128):
            jobs.append((g["woutT"][r0:r0 + 128, :], g["woutT_b"][r0:r0 + 128, :]))
        for e in range(2):
            for hc in range(32):
                jobs.append((g["w1"][e][:, hc * 128:(hc + 1) * 128].rearrange("(a p) h -> p a h", p=128),
                             g["w1_b"][e, hc].rearrange("a p h -> p a h")))
            for r0 in range(0, H, 128):
                jobs.append((g["w2"][e][r0:r0 + 128, :], g["w2_b"][e][r0:r0 + 128, :]))
        buf = 0
        for (src, dst) in jobs:
            fs = int(np.prod(src.shape[1:]))
            sview = wtile[:, buf % 2, 0:fs]
            if len(src.shape) == 3:
                sview = sview.rearrange("p (a n) -> p a n", a=src.shape[1])
            bi = buf % 2
            j = buf
            buf += 1
            st_need = (j // 2) * 16
            n_cast = ctx.bump(f"wc_cast{bi}", 16)
            @ctx.on("gpsimd")
            def _(gp, src=src, sview=sview, st_need=st_need, bi=bi):
                gp.wait_ge(ctx.sem(f"wc_st{bi}"), st_need)
                gp.dma_start(out=sview, in_=src).then_inc(ctx.sem(f"wc_cast{bi}"), 16)
            ctx.bump(f"wc_st{bi}", 16)
            @ctx.on("sync")
            def _(s, dst=dst, sview=sview, n_cast=n_cast, bi=bi):
                s.wait_ge(ctx.sem(f"wc_cast{bi}"), n_cast)
                s.dma_start(out=dst, in_=sview).then_inc(ctx.sem(f"wc_st{bi}"), 16)
        @ctx.on("sync")
        def _(s, a=ctx.val("wc_st0"), b=ctx.val("wc_st1")):
            s.wait_ge(ctx.sem("wc_st0"), a)
            s.wait_ge(ctx.sem("wc_st1"), b)
        @ctx.on("gpsimd")
        def _(gp, a=ctx.val("wc_st0"), b=ctx.val("wc_st1")):
            gp.wait_ge(ctx.sem("wc_st0"), a)
            gp.wait_ge(ctx.sem("wc_st1"), b)
        ctx.close_tensors([cmw])

    emit_setup()
    ctx.barrier("after_setup")
    if "w" in phases:
        emit_wconv()
    ctx.barrier("after_w")

    if "f1" in phases:
        emit_ffn(ctx, g, call=0)
    ctx.barrier("after_f1")
    if "m1" in phases:
        emit_m1(ctx, g)
    ctx.barrier("after_m1")
    if "m2" in phases:
        emit_m2(ctx, g)
    ctx.barrier("after_m2")
    if "m3" in phases:
        emit_m3(ctx, g)
    ctx.barrier("after_m3")
    if "f2" in phases:
        emit_ffn(ctx, g, call=1)
    ctx.barrier("after_all")

    if dbg:
        for name, shape, dt in dbg:
            ctx.bump("dbgs", 16)
            @ctx.on("sync")
            def _(s, name=name):
                s.dma_start(out=g["dbg_" + name], in_=g[name]).then_inc(ctx.sem("dbgs"), 16)
        n_dbg = ctx.val("dbgs")
        @ctx.on("sync")
        def _(s):
            s.wait_ge(ctx.sem("dbgs"), n_dbg)

    with nc.Block() as block:
        @block.sync
        def _(eng):
            for e, fn in ctx.prog:
                if e == "sync":
                    fn(eng)
        @block.vector
        def _(eng):
            for e, fn in ctx.prog:
                if e == "vector":
                    fn(eng)
        @block.scalar
        def _(eng):
            for e, fn in ctx.prog:
                if e == "scalar":
                    fn(eng)
        @block.tensor
        def _(eng):
            for e, fn in ctx.prog:
                if e == "tensor":
                    fn(eng)
        @block.gpsimd
        def _(eng):
            for e, fn in ctx.prog:
                if e == "gpsimd":
                    fn(eng)

    ctx.finalize()
    return nc


def emit_ffn(ctx, g, call):
    nc = ctx.nc
    q = f"f{call}"
    src = g["x_buf"] if call == 0 else g["x2b"]
    soff = 3 if call == 0 else 0
    dst = g["x1b"] if call == 0 else g["out"]
    doff = 3 if call == 0 else 0

    cms = []
    cms2 = []
    def sb(name, shape, dt, grp=None):
        h, cm = ctx.sb(q + name, shape, dt)
        (cms if grp is None else grp).append(cm)
        return h

    idxf_a = sb("idxf_a", [128, NTILE], F32)
    pos2_a = sb("pos2_a", [128, NTILE], F32)
    gate_a = sb("gate_a", [128, NTILE], F32)
    comb_a = sb("comb_a", [128, NTILE], F32)
    slkf_a = sb("slkf_a", [128, NTILE], F32)
    slot_u = sb("slot_u", [128, NTILE], U32)
    tabs_u = sb("tabs_u", [128, NTILE], U32)
    payl_u = sb("payl_u", [128, NTILE], U32)
    payl_f = sb("payl_f", [128, NTILE], F32)
    cnt_sb = sb("cnt_sb", [1, 32], F32)
    iota8 = sb("iota8", [8, 1], F32)
    cld = sb("cld", [8, 32], F32)
    mask8 = sb("mask8", [8, 2], F32)
    baser = sb("baser", [1, E], F32)
    bases = sb("bases", [128, E], F32)
    mgi = sb("mgi", [128, 2, 8], U32)
    mgf = sb("mgf", [128, 2, 8, 16], F32)
    tos_f = sb("tos_f", [128, 2, 16], F32)
    tos_u = sb("tos_u", [128, 2, 16], U32)
    hcol = sb("hcol", [128, 2], F32)
    hcu = sb("hcu", [128, 1], U32)
    gbuf = sb("gbuf", [128, 2, D], BF16)
    yrow = sb("yrow", [128, 2, D], F32)
    x1row = sb("x1row", [128, 2, D], F32)
    xr = sb("xr", [128, 2, D], F32, cms2)
    scr = sb("scr", [128, D], F32, cms2)
    col = sb("col", [128, 2, 8], F32, cms2)
    xlt = sb("xlt", [128, 2, D], BF16, cms2)
    xlnT = sb("xlnT", [128, 8, TPC], BF16, cms2)
    wgb = sb("wgb", [128, 8, E], BF16, cms2)
    bgbc = sb("bgbc", [128, E], F32, cms2)
    lg = sb("lg", [128, 2, E], F32, cms2)
    tcol = sb("tcol", [128, 2, 16], F32, cms2)
    idx8 = sb("idx8", [128, 8], U32, cms2)
    ex = sb("ex", [128, 2, E], F32, cms2)
    oh = sb("oh", [128, 2, E], F32, cms2)
    base_r = sb("base_r", [1, 2, E], F32, cms2)
    posS = sb("posS", [128, E], F32, cms2)
    ohs = sb("ohs", [128, E], F32, cms2)
    ltUf = sb("ltUf", [128, 128], F32, cms2)
    base_run = sb("base_run", [1, E], F32, cms2)
    onesK1f = sb("onesK1f", [1, 128], F32, cms2)
    ones128f = sb("ones128f", [128, 1], F32, cms2)
    ffff = sb("ffff", [128, TL // 128], U32, cms2)
    tbf = sb("tbf", [128, NTILE], F32, cms2)

    S_LD = q + "ld"; S_VS = q + "vs"; S_AC = q + "ac"; S_PE = q + "pe"
    S_LG = q + "lg"; S_ST = q + "st"; S_AGX = q + "agx"; S_TRX = q + "trx"
    S_SC = q + "sc"; S_AGT = q + "agt"; S_AGC = q + "agc"; S_MG = q + "mg"
    S_AGY = q + "agy"; S_GA = q + "ga"; S_EX = q + "ex"; S_YS = q + "ys"
    S_OUT = q + "out"
    base_st = [ctx.val(S_ST + "0"), ctx.val(S_ST + "1")]
    base_gst = [ctx.val(q + "gst0"), ctx.val(q + "gst1")]
    base_yst = [ctx.val(q + "yst0"), ctx.val(q + "yst1")]
    base_out = [ctx.val(S_OUT + "0"), ctx.val(S_OUT + "1")]
    base_ev = ctx.val(q + "ev")
    base_peB = ctx.val(S_PE)

    @ctx.on("sync")
    def _(s):
        s.dma_start(out=wgb[:], in_=g["wg_b"].rearrange("a p n -> p a n")).then_inc(ctx.sem(q + "cl2"), 16)
        s.dma_start(out=bgbc[:], in_=g["bgv"].broadcast_to([128, E])).then_inc(ctx.sem(q + "cl2"), 16)
    ctx.bump(q + "cl2", 32)

    @ctx.on("gpsimd")
    def _(gp):
        gp.iota(ffff[:], pattern=[[0, TL // 128]], base=-1, channel_multiplier=0,
                allow_small_or_imprecise_dtypes=True)
        gp.iota(iota8[:], pattern=[[0, 1]], base=0, channel_multiplier=1,
                allow_small_or_imprecise_dtypes=True)
        gp.iota(payl_f[:], pattern=[[128, NTILE]], base=0, channel_multiplier=1,
                allow_small_or_imprecise_dtypes=True)
        gp.engine_nop().then_inc(ctx.sem(q + "setg"), 1)
    ctx.bump(q + "setg", 1)
    v_setg = ctx.val(q + "setg")

    @ctx.on("vector")
    def _(v):
        v.wait_ge(ctx.sem(q + "setg"), v_setg)
        v.tensor_copy(ltUf[:], g["ltU"][:])
        v.memset(onesK1f[:], 1.0)
        v.memset(ones128f[:], 1.0)
        v.tensor_scalar_add(payl_f[:], payl_f[:], g["pcf_col"][:, PCF_COREBASE:PCF_COREBASE + 1])
        v.engine_nop().then_inc(ctx.sem(S_VS), 1)
    n_vs0 = ctx.bump(S_VS, 1)
    @ctx.on("vector")
    def _(v):
        v.wait_ge(ctx.sem(S_VS), n_vs0)
        v.tensor_copy(payl_u[:], payl_f[:])
        v.engine_nop().then_inc(ctx.sem(S_VS), 1)
    ctx.bump(S_VS, 1)

    @ctx.on("sync")
    def _(s):
        s.wait_ge(ctx.sem(q + "setg"), v_setg)
        s.dma_start(out=g["tab_in"].rearrange("(p f) 1 -> p f", p=128), in_=ffff[:])\
            .then_inc(ctx.sem(q + "tz"), 16)
    ctx.bump(q + "tz", 16)

    for i in range(NTILE):
        b = i % 2
        n_ld = ctx.bump(S_LD + str(b), 16)
        @ctx.on("sync")
        def _(s, i=i, b=b):
            if i >= 2:
                s.wait_ge(ctx.sem(S_AC), ac_rel[i - 2])
            s.dma_start(out=xr[:, b, :], in_=src[soff + i * 128: soff + (i + 1) * 128, :])\
                .then_inc(ctx.sem(S_LD + str(b)), 16)
        if i == 0:
            ac_rel = {}
        n_ac = ctx.bump(S_AC, 1)
        @ctx.on("scalar")
        def _(sc, i=i, b=b, n_ld=n_ld):
            sc.wait_ge(ctx.sem(S_LD + str(b)), n_ld)
            sc.activation(out=scr[:], in_=xr[:, b, :], func=AF.Square,
                          accum_out=col[:, b, 1:2]).then_inc(ctx.sem(S_AC), 1)
        n_v1 = ctx.bump(S_VS, 1)
        @ctx.on("vector")
        def _(v, i=i, b=b, n_ld=n_ld, n_ac=n_ac):
            v.wait_ge(ctx.sem(S_LD + str(b)), n_ld)
            v.wait_ge(ctx.sem(S_AC), n_ac)
            v.reduce_sum(col[:, b, 0:1], xr[:, b, :], axis=AX.X)
            v.engine_nop().then_inc(ctx.sem(S_VS), 1)
        n_v2 = ctx.bump(S_VS, 1)
        @ctx.on("vector")
        def _(v, b=b, n_v1=n_v1):
            v.wait_ge(ctx.sem(S_VS), n_v1)
            v.tensor_scalar_mul(col[:, b, 2:3], col[:, b, 0:1], 1.0 / D)
            v.tensor_scalar_mul(col[:, b, 3:4], col[:, b, 1:2], 1.0 / D)
            v.engine_nop().then_inc(ctx.sem(S_VS), 1)
        n_v3 = ctx.bump(S_VS, 1)
        @ctx.on("vector")
        def _(v, b=b, n_v2=n_v2):
            v.wait_ge(ctx.sem(S_VS), n_v2)
            v.tensor_tensor(col[:, b, 6:7], col[:, b, 2:3], col[:, b, 2:3], op=ALU.mult)
            v.engine_nop().then_inc(ctx.sem(S_VS), 1)
        n_v4 = ctx.bump(S_VS, 1)
        @ctx.on("vector")
        def _(v, b=b, n_v3=n_v3):
            v.wait_ge(ctx.sem(S_VS), n_v3)
            v.tensor_sub(col[:, b, 3:4], col[:, b, 3:4], col[:, b, 6:7])
            v.engine_nop().then_inc(ctx.sem(S_VS), 1)
        n_ac1b = ctx.bump(S_AC, 1)
        n_ac2 = ctx.bump(S_AC, 1)
        @ctx.on("scalar")
        def _(sc, b=b, n_v4=n_v4, n_ac1b=n_ac1b):
            sc.wait_ge(ctx.sem(S_VS), n_v4)
            sc.activation(out=col[:, b, 4:5], in_=col[:, b, 3:4], func=AF.Ln, bias=EPS)\
                .then_inc(ctx.sem(S_AC), 1)
            sc.wait_ge(ctx.sem(S_AC), n_ac1b)
            sc.activation(out=col[:, b, 4:5], in_=col[:, b, 4:5], func=AF.Exp, scale=-0.5)\
                .then_inc(ctx.sem(S_AC), 1)
        n_v5 = ctx.bump(S_VS, 1)
        @ctx.on("vector")
        def _(v, b=b, n_ac2=n_ac2):
            v.wait_ge(ctx.sem(S_AC), n_ac2)
            v.scalar_tensor_tensor(col[:, b, 5:6], col[:, b, 2:3], -1.0, col[:, b, 4:5],
                                   op0=ALU.mult, op1=ALU.mult)
            v.engine_nop().then_inc(ctx.sem(S_VS), 1)
        n_ac3 = ctx.bump(S_AC, 1)
        ac_rel[i] = n_ac3
        @ctx.on("scalar")
        def _(sc, i=i, b=b, n_v5=n_v5):
            sc.wait_ge(ctx.sem(S_VS), n_v5)
            if i >= 2:
                sc.wait_ge(ctx.sem(S_ST + str(b)), base_st[b] + (i // 2) * 16)
            sc.activation(out=xlt[:, b, :], in_=xr[:, b, :], func=AF.Identity,
                          scale=col[:, b, 4:5], bias=col[:, b, 5:6]).then_inc(ctx.sem(S_AC), 1)
        ctx.bump(S_ST + str(b), 16)
        @ctx.on("sync")
        def _(s, i=i, b=b, n_ac3=n_ac3):
            s.wait_ge(ctx.sem(S_AC), n_ac3)
            s.dma_start(out=g["xln_ag_in"][i * 128:(i + 1) * 128, :], in_=xlt[:, b, :])\
                .then_inc(ctx.sem(S_ST + str(b)), 16)

    n_st_all0 = ctx.val(S_ST + "0")
    n_st_all1 = ctx.val(S_ST + "1")


    @ctx.on("gpsimd")
    def _(gp):
        gp.wait_ge(ctx.sem(S_ST + "0"), n_st_all0)
        gp.wait_ge(ctx.sem(S_ST + "1"), n_st_all1)
        gp.collective_compute("AllGather", ALU.bypass,
                              replica_groups=[list(range(NC))],
                              ins=[g["xln_ag_in"].opt()], outs=[g["xln_full"].opt()])\
            .then_inc(ctx.sem(S_AGX), 1)
    ctx.bump(S_AGX, 1)
    n_agx = ctx.val(S_AGX)

    for kc in range(8):
        ctx.bump(S_TRX, 16)
        @ctx.on("scalar")
        def _(sc, kc=kc):
            if kc == 0:
                sc.wait_ge(ctx.sem(S_ST + "0"), n_st_all0)
                sc.wait_ge(ctx.sem(S_ST + "1"), n_st_all1)
            sc.dma_start_transpose(out=xlnT[:, kc, :],
                                   in_=g["xln_ag_in"][:, kc * 128:(kc + 1) * 128])\
                .then_inc(ctx.sem(S_TRX), 16)
    n_trx = ctx.val(S_TRX)

    lgP, cm1 = ctx.ps(q + "lgP", [128, 2, E], F32)
    posP, cm2 = ctx.ps(q + "posP", [128, E], F32)
    cntP, cm3 = ctx.ps(q + "cntP", [1, 2, E], F32)
    cms += [cm1, cm2, cm3]
    n_vbr = ctx.bump(S_VS, 1)
    @ctx.on("vector")
    def _(v):
        v.memset(base_run[:], 0.0)
        v.engine_nop().then_inc(ctx.sem(S_VS), 1)

    n_cl2 = ctx.val(q + "cl2")
    for i in range(NTILE):
        b = i % 2
        n_lg = ctx.bump(S_LG, 1)
        @ctx.on("tensor")
        def _(t, i=i, b=b):
            if i == 0:
                t.wait_ge(ctx.sem(S_TRX), n_trx)
                t.wait_ge(ctx.sem(q + "cl2"), n_cl2)
            if i >= 2:
                t.wait_ge(ctx.sem(S_VS), lg_rel[i - 2])
            last = None
            for kc in range(8):
                last = t.matmul(lgP[:, b, :], xlnT[:, kc, i * 128:(i + 1) * 128], wgb[:, kc, :],
                                start=(kc == 0), stop=(kc == 7))
            last.then_inc(ctx.sem(S_LG), 1)
        if i == 0:
            lg_rel = {}
        n_v1 = ctx.bump(S_VS, 1)
        @ctx.on("vector")
        def _(v, i=i, b=b, n_lg=n_lg):
            v.wait_ge(ctx.sem(S_LG), n_lg)
            v.scalar_tensor_tensor(lg[:, b, :], lgP[:, b, :], 0.0, bgbc[:],
                                   op0=ALU.bypass, op1=ALU.add)
            v.engine_nop().then_inc(ctx.sem(S_VS), 1)
        lg_rel[i] = n_v1
        n_v2 = ctx.bump(S_VS, 1)
        @ctx.on("vector")
        def _(v, b=b, n_v1=n_v1):
            v.wait_ge(ctx.sem(S_VS), n_v1)
            v.reduce_max(tcol[:, b, 0:1], lg[:, b, :], axis=AX.X)
            v.engine_nop().then_inc(ctx.sem(S_VS), 1)
        n_v3 = ctx.bump(S_VS, 1)
        @ctx.on("vector")
        def _(v, i=i, b=b, n_v2=n_v2):
            v.wait_ge(ctx.sem(S_VS), n_v2)
            v.max_index(idx8[:], tcol[:, b, 0:1].broadcast_to([128, 8]), lg[:, b, :])
            v.tensor_scalar_mul(tcol[:, b, 1:2], tcol[:, b, 0:1], -1.0)
            v.engine_nop().then_inc(ctx.sem(S_VS), 1)
        n_v4 = ctx.bump(S_VS, 1)
        @ctx.on("vector")
        def _(v, i=i, n_v3=n_v3):
            v.wait_ge(ctx.sem(S_VS), n_v3)
            v.tensor_copy(idxf_a[:, i:i + 1], idx8[:, 0:1])
            v.engine_nop().then_inc(ctx.sem(S_VS), 1)
        n_ac = ctx.bump(S_AC, 1)
        @ctx.on("scalar")
        def _(sc, b=b, n_v3=n_v3):
            sc.wait_ge(ctx.sem(S_VS), n_v3)
            sc.activation(out=ex[:, b, :], in_=lg[:, b, :], func=AF.Exp,
                          bias=tcol[:, b, 1:2]).then_inc(ctx.sem(S_AC), 1)
        n_v5 = ctx.bump(S_VS, 1)
        @ctx.on("vector")
        def _(v, i=i, b=b, n_ac=n_ac, n_v4=n_v4):
            v.wait_ge(ctx.sem(S_AC), n_ac)
            v.wait_ge(ctx.sem(S_VS), n_v4)
            v.reduce_sum(tcol[:, b, 2:3], ex[:, b, :], axis=AX.X)
            v.tensor_tensor(oh[:, b, :], idxf_a[:, i:i + 1].broadcast_to([128, E]),
                            g["iotaE"][:], op=ALU.is_equal)
            v.engine_nop().then_inc(ctx.sem(S_VS), 1)
        n_v6 = ctx.bump(S_VS, 1)
        @ctx.on("vector")
        def _(v, i=i, b=b, n_v5=n_v5):
            v.wait_ge(ctx.sem(S_VS), n_v5)
            v.reciprocal(gate_a[:, i:i + 1], tcol[:, b, 2:3])
            v.tensor_copy(base_r[:, b, :], base_run[:])
            v.engine_nop().then_inc(ctx.sem(S_VS), 1)
        ctx.bump(S_PE, 2)
        @ctx.on("tensor")
        def _(t, i=i, b=b, n_v6=n_v6):
            t.wait_ge(ctx.sem(S_VS), n_v6)
            t.matmul(posP[:], onesK1f[:], base_r[:, b, :], start=True, stop=False)
            t.matmul(posP[:], ltUf[:], oh[:, b, :], start=False, stop=True)\
                .then_inc(ctx.sem(S_PE), 1)
            t.matmul(cntP[:, b, :], ones128f[:], oh[:, b, :], start=True,
                     stop=True).then_inc(ctx.sem(S_PE), 1)
        n_v7 = ctx.bump(S_VS, 1)
        @ctx.on("vector")
        def _(v, i=i, n_pe=base_peB + 2 * i + 1):
            v.wait_ge(ctx.sem(S_PE), n_pe)
            v.tensor_copy(posS[:], posP[:])
            v.engine_nop().then_inc(ctx.sem(S_VS), 1)
        n_v8 = ctx.bump(S_VS, 1)
        @ctx.on("vector")
        def _(v, b=b, n_v7=n_v7):
            v.wait_ge(ctx.sem(S_VS), n_v7)
            v.tensor_mul(posS[:], posS[:], oh[:, b, :])
            v.engine_nop().then_inc(ctx.sem(S_VS), 1)
        n_v9x = ctx.bump(S_VS, 1)
        @ctx.on("vector")
        def _(v, i=i, n_v8=n_v8):
            v.wait_ge(ctx.sem(S_VS), n_v8)
            v.reduce_sum(pos2_a[:, i:i + 1], posS[:], axis=AX.X)
            v.engine_nop().then_inc(ctx.sem(S_VS), 1)
        ctx.bump(S_VS, 1)
        @ctx.on("vector")
        def _(v, i=i, b=b, n_v9x=n_v9x, n_pe=base_peB + 2 * i + 2):
            v.wait_ge(ctx.sem(S_VS), n_v9x)
            v.wait_ge(ctx.sem(S_PE), n_pe)
            v.tensor_add(base_run[:], base_run[:], cntP[:, b, :])
            v.engine_nop().then_inc(ctx.sem(S_VS), 1)

    n_vcnt = ctx.bump(S_VS, 1)
    @ctx.on("vector")
    def _(v, w=ctx.val(S_VS) - 1):
        v.wait_ge(ctx.sem(S_VS), w)
        v.tensor_copy(cnt_sb[:, 0:E], base_run[:])
        v.engine_nop().then_inc(ctx.sem(S_VS), 1)

    @ctx.on("sync")
    def _(s, n_vcnt=n_vcnt):
        s.wait_ge(ctx.sem(S_VS), n_vcnt)
        s.dma_start(out=g["cnt_in"][:, 0:E], in_=cnt_sb[:, 0:E]).then_inc(ctx.sem(q + "cst"), 16)
        s.dma_start(out=g["cnt_in"][:, 16:32], in_=g["zero128"][0:1, 0:16]).then_inc(ctx.sem(q + "cst"), 16)
    ctx.bump(q + "cst", 32)
    n_cst = ctx.val(q + "cst")
    @ctx.on("gpsimd")
    def _(gp):
        gp.wait_ge(ctx.sem(q + "cst"), n_cst)
        gp.collective_compute("AllGather", ALU.bypass,
                              replica_groups=[list(range(NC))],
                              ins=[g["cnt_in"].opt()], outs=[g["cnt_out"].opt()])\
            .then_inc(ctx.sem(S_AGC), 1)
    ctx.bump(S_AGC, 1)
    n_agc = ctx.val(S_AGC)

    @ctx.on("sync")
    def _(s):
        s.wait_ge(ctx.sem(S_AGC), n_agc)
        s.dma_start(out=cld[:], in_=g["cnt_out"][:, :]).then_inc(ctx.sem(q + "cl"), 16)
    ctx.bump(q + "cl", 16)
    n_cl = ctx.val(q + "cl")
    n_v9 = ctx.bump(S_VS, 1)
    @ctx.on("vector")
    def _(v):
        v.wait_ge(ctx.sem(q + "cl"), n_cl)
        v.scalar_tensor_tensor(mask8[:, 0:1], g["pcf_col"][0:8, PCF_COREBASE:PCF_COREBASE + 1],
                               1.0 / TPC, iota8[:], op0=ALU.mult, op1=ALU.is_gt)
        v.scalar_tensor_tensor(mask8[:, 1:2], g["pcf_col"][0:8, PCF_COREBASE:PCF_COREBASE + 1],
                               1.0 / TPC, iota8[:], op0=ALU.mult, op1=ALU.is_equal)
        v.engine_nop().then_inc(ctx.sem(S_VS), 1)
    ctx.bump(S_PE, 1)
    @ctx.on("tensor")
    def _(t, n_v9=n_v9):
        t.wait_ge(ctx.sem(S_VS), n_v9)
        t.matmul(posP[0:1, 0:E], mask8[:, 0:1], cld[0:8, 0:E], start=True, stop=True)\
            .then_inc(ctx.sem(S_PE), 1)
    n_v10 = ctx.bump(S_VS, 1)
    @ctx.on("vector")
    def _(v, n_pe=ctx.val(S_PE)):
        v.wait_ge(ctx.sem(S_PE), n_pe)
        v.tensor_copy(baser[:], posP[0:1, 0:E])
        v.engine_nop().then_inc(ctx.sem(S_VS), 1)
    ctx.bump(S_PE, 1)
    @ctx.on("tensor")
    def _(t, n_v10=n_v10):
        t.wait_ge(ctx.sem(S_VS), n_v10)
        t.matmul(posP[:, 0:E], onesK1f[:], baser[:], start=True, stop=True)\
            .then_inc(ctx.sem(S_PE), 1)
    n_v11 = ctx.bump(S_VS, 1)
    @ctx.on("vector")
    def _(v, n_pe=ctx.val(S_PE)):
        v.wait_ge(ctx.sem(S_PE), n_pe)
        v.tensor_copy(bases[:], posP[:, 0:E])
        v.engine_nop().then_inc(ctx.sem(S_VS), 1)

    for i in range(NTILE):
        b = i % 2
        n_c1 = ctx.bump(S_VS, 1)
        @ctx.on("vector")
        def _(v, i=i, n_v11=n_v11):
            if i == 0:
                v.wait_ge(ctx.sem(S_VS), n_v11)
            v.tensor_tensor(ohs[:], idxf_a[:, i:i + 1].broadcast_to([128, E]),
                            g["iotaE"][:], op=ALU.is_equal)
            v.engine_nop().then_inc(ctx.sem(S_VS), 1)
        n_c2 = ctx.bump(S_VS, 1)
        @ctx.on("vector")
        def _(v, n_c1=n_c1):
            v.wait_ge(ctx.sem(S_VS), n_c1)
            v.tensor_mul(ohs[:], ohs[:], bases[:])
            v.engine_nop().then_inc(ctx.sem(S_VS), 1)
        n_c3 = ctx.bump(S_VS, 1)
        @ctx.on("vector")
        def _(v, b=b, n_c2=n_c2):
            v.wait_ge(ctx.sem(S_VS), n_c2)
            v.reduce_sum(tcol[:, b, 3:4], ohs[:], axis=AX.X)
            v.engine_nop().then_inc(ctx.sem(S_VS), 1)
        n_c4 = ctx.bump(S_VS, 1)
        @ctx.on("vector")
        def _(v, i=i, b=b, n_c3=n_c3):
            v.wait_ge(ctx.sem(S_VS), n_c3)
            v.tensor_add(tcol[:, b, 4:5], tcol[:, b, 3:4], pos2_a[:, i:i + 1])
            v.engine_nop().then_inc(ctx.sem(S_VS), 1)
        n_c5 = ctx.bump(S_VS, 1)
        @ctx.on("vector")
        def _(v, b=b, n_c4=n_c4):
            v.wait_ge(ctx.sem(S_VS), n_c4)
            v.tensor_single_scalar(tcol[:, b, 5:6], tcol[:, b, 4:5], float(CAP), op=ALU.is_lt)
            v.engine_nop().then_inc(ctx.sem(S_VS), 1)
        n_c6 = ctx.bump(S_VS, 1)
        @ctx.on("vector")
        def _(v, i=i, b=b, n_c5=n_c5):
            v.wait_ge(ctx.sem(S_VS), n_c5)
            v.scalar_tensor_tensor(tcol[:, b, 6:7], idxf_a[:, i:i + 1], float(CAP),
                                   tcol[:, b, 4:5], op0=ALU.mult, op1=ALU.add)
            v.scalar_tensor_tensor(comb_a[:, i:i + 1], gate_a[:, i:i + 1], 0.5,
                                   tcol[:, b, 5:6], op0=ALU.mult, op1=ALU.mult)
            v.engine_nop().then_inc(ctx.sem(S_VS), 1)
        n_c7 = ctx.bump(S_VS, 1)
        @ctx.on("vector")
        def _(v, i=i, b=b, n_c6=n_c6):
            v.wait_ge(ctx.sem(S_VS), n_c6)
            v.tensor_mul(slkf_a[:, i:i + 1], tcol[:, b, 6:7], tcol[:, b, 5:6])
            v.tensor_sub(tcol[:, b, 7:8], tcol[:, b, 6:7], g["trashc"][:, i:i + 1])
            v.engine_nop().then_inc(ctx.sem(S_VS), 1)
        n_c8 = ctx.bump(S_VS, 1)
        @ctx.on("vector")
        def _(v, i=i, b=b, n_c7=n_c7):
            v.wait_ge(ctx.sem(S_VS), n_c7)
            v.scalar_tensor_tensor(tbf[:, i:i + 1], tcol[:, b, 7:8], tcol[:, b, 5:6],
                                   g["trashc"][:, i:i + 1], op0=ALU.mult, op1=ALU.add)
            v.engine_nop().then_inc(ctx.sem(S_VS), 1)
        ctx.bump(S_VS, 1)
        @ctx.on("vector")
        def _(v, i=i, n_c8=n_c8):
            v.wait_ge(ctx.sem(S_VS), n_c8)
            v.tensor_copy(slot_u[:, i:i + 1], slkf_a[:, i:i + 1])
            v.tensor_copy(tabs_u[:, i:i + 1], tbf[:, i:i + 1])
            v.engine_nop().then_inc(ctx.sem(S_VS), 1)
    n_vs_passc = ctx.val(S_VS)

    if call == 0:
        @ctx.on("sync")
        def _(s):
            s.wait_ge(ctx.sem(S_VS), n_vs_passc)
            s.dma_start(out=g["route_dbg"][:, 0:NTILE], in_=idxf_a[:]).then_inc(ctx.sem(q + "rd"), 16)
            s.dma_start(out=g["route_dbg"][:, NTILE:2 * NTILE], in_=pos2_a[:]).then_inc(ctx.sem(q + "rd"), 16)
            s.dma_start(out=g["route_dbg"][:, 2 * NTILE:3 * NTILE], in_=slkf_a[:]).then_inc(ctx.sem(q + "rd"), 16)
            s.dma_start(out=g["route_dbg"][:, 3 * NTILE:4 * NTILE], in_=comb_a[:]).then_inc(ctx.sem(q + "rd"), 16)
            s.wait_ge(ctx.sem(q + "rd"), 64)
        ctx.bump(q + "rd", 64)

    n_tz = ctx.val(q + "tz")
    for i in range(NTILE):
        ctx.bump(S_SC, 16)
        @ctx.on("gpsimd")
        def _(gp, i=i):
            if i == 0:
                gp.wait_ge(ctx.sem(S_VS), n_vs_passc)
                gp.wait_ge(ctx.sem(q + "tz"), n_tz)
            gp.indirect_dma_start(
                out=g["tab_in"],
                out_offset=IndirectOffsetOnAxis(ap=tabs_u[:, i:i + 1], axis=0),
                in_=payl_u[:, i:i + 1],
                in_offset=None,
            ).then_inc(ctx.sem(S_SC), 16)
    n_sc = ctx.val(S_SC)

    @ctx.on("sync")
    def _(s):
        s.wait_ge(ctx.sem(S_VS), n_vs_passc)
        s.dma_start(out=g["hmst"][0:1, :].rearrange("1 p -> p 1"),
                    in_=slkf_a[:, NTILE - 1:NTILE]).then_inc(ctx.sem(q + "hm"), 16)
        s.dma_start(out=g["hmst"][1:2, :].rearrange("1 p -> p 1"),
                    in_=comb_a[:, NTILE - 1:NTILE]).then_inc(ctx.sem(q + "hm"), 16)
    ctx.bump(q + "hm", 32)
    n_hm0 = ctx.val(q + "hm")
    @ctx.on("sync")
    def _(s):
        s.wait_ge(ctx.sem(q + "hm"), n_hm0)
        s.dma_start(out=g["cnt_in"][0:1, 16:19], in_=g["hmst"][0:1, 125:128])\
            .then_inc(ctx.sem(q + "hm"), 16)
        s.dma_start(out=g["cnt_in"][0:1, 19:22], in_=g["hmst"][1:2, 125:128])\
            .then_inc(ctx.sem(q + "hm"), 16)
    ctx.bump(q + "hm", 32)

    @ctx.on("gpsimd")
    def _(gp):
        gp.wait_ge(ctx.sem(S_SC), n_sc)
        gp.collective_compute("AllGather", ALU.bypass,
                              replica_groups=[list(range(NC))],
                              ins=[g["tab_in"].opt()], outs=[g["tab_out"].opt()])\
            .then_inc(ctx.sem(S_AGT), 1)
    ctx.bump(S_AGT, 1)
    n_agt = ctx.val(S_AGT)

    n_vmg = ctx.bump(S_VS, 2)
    @ctx.on("gpsimd")
    def _(gp):
        gp.iota(tbf[:, 0:16].rearrange("p (a b) -> p a b", a=2),
                pattern=[[128, 2], [TL // 16, 8]], base=0,
                channel_multiplier=1, allow_small_or_imprecise_dtypes=True)
        gp.engine_nop().then_inc(ctx.sem(q + "mgi"), 1)
    ctx.bump(q + "mgi", 1)
    n_mgi = ctx.val(q + "mgi")
    @ctx.on("vector")
    def _(v):
        v.wait_ge(ctx.sem(q + "mgi"), n_mgi)
        v.scalar_tensor_tensor(tbf[:, 0:16],
                               g["pcf_col"][:, PCF_SLOTBASE:PCF_SLOTBASE + 1].broadcast_to([128, 16]),
                               1.0 / 16.0, tbf[:, 0:16], op0=ALU.mult, op1=ALU.add)
        v.engine_nop().then_inc(ctx.sem(S_VS), 1)
    @ctx.on("vector")
    def _(v, tgt=n_vmg - 1):
        v.wait_ge(ctx.sem(S_VS), tgt)
        v.tensor_copy(mgi[:].rearrange("p a b -> p (a b)"), tbf[:, 0:16])
        v.engine_nop().then_inc(ctx.sem(S_VS), 1)
    n_vmg_done = ctx.val(S_VS)

    for e in range(2):
        for k in range(NC):
            ctx.bump(S_MG, 16)
            @ctx.on("gpsimd")
            def _(gp, e=e, k=k):
                if e == 0 and k == 0:
                    gp.wait_ge(ctx.sem(S_AGT), n_agt)
                    gp.wait_ge(ctx.sem(S_VS), n_vmg_done)
                gp.indirect_dma_start(
                    out=mgf[:, e, k, :],
                    out_offset=None,
                    in_=g["tab_out"],
                    in_offset=IndirectOffsetOnAxis(ap=mgi[:, e, k:k + 1], axis=0),
                ).then_inc(ctx.sem(S_MG), 16)
    n_mg = ctx.val(S_MG)
    n_vm1 = ctx.bump(S_VS, 1)
    @ctx.on("vector")
    def _(v):
        v.wait_ge(ctx.sem(S_MG), n_mg)
        v.tensor_tensor(tos_f[:], mgf[:, :, 0, :], mgf[:, :, 1, :], op=ALU.min)
        v.engine_nop().then_inc(ctx.sem(S_VS), 1)
    for k in range(2, NC):
        n_vmk = ctx.bump(S_VS, 1)
        @ctx.on("vector")
        def _(v, k=k, w=n_vm1 + k - 2):
            v.wait_ge(ctx.sem(S_VS), w)
            v.tensor_tensor(tos_f[:], tos_f[:], mgf[:, :, k, :], op=ALU.min)
            v.engine_nop().then_inc(ctx.sem(S_VS), 1)
    n_vm2 = ctx.bump(S_VS, 1)
    @ctx.on("vector")
    def _(v, w=ctx.val(S_VS) - 1):
        v.wait_ge(ctx.sem(S_VS), w)
        v.tensor_scalar_min(tos_f[:], tos_f[:], BIGF)
        v.engine_nop().then_inc(ctx.sem(S_VS), 1)
    ctx.bump(S_VS, 1)
    @ctx.on("vector")
    def _(v, w=n_vm2):
        v.wait_ge(ctx.sem(S_VS), w)
        v.tensor_copy(tos_u[:], tos_f[:])
        v.engine_nop().then_inc(ctx.sem(S_VS), 1)
    n_tos = ctx.val(S_VS)

    ctx.close_tensors([cm1, cm2, cm3])
    cms.remove(cm1); cms.remove(cm2); cms.remove(cm3)
    ctx.close_tensors(cms2)

    cms3 = []
    bufT = sb("bufT", [128, 8, CAP], BF16, cms3)
    w1t = sb("w1t", [128, 2, 8, 128], BF16, cms3)
    w2sb = sb("w2sb", [128, 32, D], BF16, cms3)
    hbf = sb("hbf", [128, 32, CAP // 2], BF16, cms3)
    b1sb = sb("b1sb", [128, 32], F32, cms3)
    b2bc = sb("b2bc", [128, D], F32, cms3)
    ysb = sb("ysb", [128, 2, D], F32, cms3)

    hP, cm4 = ctx.ps(q + "hP", [128, 2, 1024], F32)
    yP, cm5 = ctx.ps(q + "yP", [128, 4, 512], F32)
    cms += [cm4, cm5]

    for e in range(2):
        eq = f"{q}e{e}"
        eb_peh = ctx.val(eq + "peh")
        eb_hev = ctx.val(eq + "hev")
        eb_yev = ctx.val(eq + "yev")
        for f in range(16):
            fg = e * 16 + f
            fb = fg % 2
            n_ga = ctx.bump(q + "ga" + str(fb), 16)
            @ctx.on("gpsimd")
            def _(gp, f=f, fg=fg, fb=fb, e=e):
                if fg == 0:
                    gp.wait_ge(ctx.sem(S_VS), n_tos)
                    gp.wait_ge(ctx.sem(S_AGX), n_agx)
                gp.wait_ge(ctx.sem(q + "gst" + str(fb)), base_gst[fb] + (fg // 2) * 16)
                gp.indirect_dma_start(
                    out=gbuf[:, fb, :],
                    out_offset=None,
                    in_=g["xln_full"],
                    in_offset=IndirectOffsetOnAxis(ap=tos_u[:, e, f:f + 1], axis=0),
                    bounds_check=g["r_ntok"],
                    oob_is_err=False,
                ).then_inc(ctx.sem(q + "ga" + str(fb)), 16)
            ctx.bump(q + "gst" + str(fb), 16)
            @ctx.on("sync")
            def _(s, f=f, fb=fb, n_ga=n_ga):
                s.wait_ge(ctx.sem(q + "ga" + str(fb)), n_ga)
                s.dma_start(out=g["xg"][f * 128:(f + 1) * 128, :], in_=gbuf[:, fb, :])\
                    .then_inc(ctx.sem(q + "gst" + str(fb)), 16)
        n_gst0 = ctx.val(q + "gst0")
        n_gst1 = ctx.val(q + "gst1")
        for kc in range(8):
            ctx.bump(eq + "trb", 16)
            @ctx.on("scalar")
            def _(sc, kc=kc, eq=eq, n_gst0=n_gst0, n_gst1=n_gst1):
                if kc == 0:
                    sc.wait_ge(ctx.sem(q + "gst0"), n_gst0)
                    sc.wait_ge(ctx.sem(q + "gst1"), n_gst1)
                sc.dma_start_transpose(out=bufT[:, kc, :],
                                       in_=g["xg"][:, kc * 128:(kc + 1) * 128])\
                    .then_inc(ctx.sem(eq + "trb"), 16)
        n_trb = ctx.val(eq + "trb")
        @ctx.on("sync")
        def _(s, e=e, eq=eq):
            s.dma_start(out=b1sb[:], in_=g["b1v"][e]).then_inc(ctx.sem(eq + "wl"), 16)
            s.dma_start(out=b2bc[:], in_=g["b2v"][e:e + 1, :].broadcast_to([128, D]))\
                .then_inc(ctx.sem(eq + "wl"), 16)
            for hc in range(32):
                s.dma_start(out=w2sb[:, hc, :], in_=g["w2_b"][e][hc * 128:(hc + 1) * 128, :])\
                    .then_inc(ctx.sem(eq + "wl"), 16)
        n_wl = ctx.bump(eq + "wl", 16 * 34)

        for half in range(2):
            s0 = half * 1024
            for hc in range(32):
                j = half * 32 + hc
                wb = hc % 2
                n_w1l = ctx.bump(eq + "w1l" + str(wb), 16)
                @ctx.on("sync")
                def _(s, e=e, hc=hc, wb=wb, j=j, eq=eq, eb_peh=eb_peh):
                    s.wait_ge(ctx.sem(eq + "peh"), eb_peh + max(0, j - 1))
                    s.dma_start(out=w1t[:, wb, :, :],
                                in_=g["w1_b"][e, hc].rearrange("a p h -> p a h"))\
                        .then_inc(ctx.sem(eq + "w1l" + str(wb)), 16)
                n_peh = ctx.bump(eq + "peh", 1)
                @ctx.on("tensor")
                def _(t, hc=hc, wb=wb, s0=s0, j=j, n_w1l=n_w1l, first=(j == 0),
                      eq=eq, n_trb=n_trb, eb_hev=eb_hev):
                    t.wait_ge(ctx.sem(eq + "w1l" + str(wb)), n_w1l)
                    if first:
                        t.wait_ge(ctx.sem(eq + "trb"), n_trb)
                    t.wait_ge(ctx.sem(eq + "hev"), eb_hev + max(0, j - 1))
                    last = None
                    for kc in range(8):
                        for nb in range(2):
                            last = t.matmul(hP[:, wb, nb * 512:(nb + 1) * 512],
                                            w1t[:, wb, kc, :],
                                            bufT[:, kc, s0 + nb * 512: s0 + (nb + 1) * 512],
                                            start=(kc == 0), stop=(kc == 7))
                    last.then_inc(ctx.sem(eq + "peh"), 1)
                ctx.bump(eq + "hev", 1)
                @ctx.on("scalar")
                def _(sc, hc=hc, wb=wb, n_peh=n_peh, n_wl=n_wl, eq=eq):
                    sc.wait_ge(ctx.sem(eq + "peh"), n_peh)
                    sc.wait_ge(ctx.sem(eq + "wl"), n_wl)
                    sc.activation(out=hbf[:, hc, :], in_=hP[:, wb, :], func=AF.Relu,
                                  bias=b1sb[:, hc:hc + 1]).then_inc(ctx.sem(eq + "hev"), 1)
            n_hev_half = ctx.val(eq + "hev")
            for st in range(8):
                jj = half * 8 + st
                sb2 = st % 2
                for dh in range(2):
                    yb = (jj * 2 + dh) % 4
                    n_pey = ctx.bump(eq + "pey", 1)
                    @ctx.on("tensor")
                    def _(t, st=st, dh=dh, yb=yb, jj=jj, n_hev_half=n_hev_half, eq=eq, eb_yev=eb_yev):
                        if st == 0 and dh == 0:
                            t.wait_ge(ctx.sem(eq + "hev"), n_hev_half)
                        t.wait_ge(ctx.sem(eq + "yev"), eb_yev + max(0, jj * 2 + dh - 3))
                        last = None
                        for hc in range(32):
                            last = t.matmul(yP[:, yb, :],
                                            hbf[:, hc, st * 128:(st + 1) * 128],
                                            w2sb[:, hc, dh * 512:(dh + 1) * 512],
                                            start=(hc == 0), stop=(hc == 31))
                        last.then_inc(ctx.sem(eq + "pey"), 1)
                    ctx.bump(eq + "yev", 1)
                    @ctx.on("vector")
                    def _(v, st=st, dh=dh, yb=yb, sb2=sb2, jj=jj, n_pey=n_pey, n_wl=n_wl, eq=eq, e=e, base_yst=base_yst):
                        v.wait_ge(ctx.sem(eq + "pey"), n_pey)
                        if jj == 0 and dh == 0:
                            v.wait_ge(ctx.sem(eq + "wl"), n_wl)
                        jjg = e * 16 + jj
                        if jjg >= 2 and dh == 0:
                            v.wait_ge(ctx.sem(q + "yst" + str(sb2)), base_yst[sb2] + (jjg // 2) * 16)
                        v.scalar_tensor_tensor(ysb[:, sb2, dh * 512:(dh + 1) * 512],
                                               yP[:, yb, :], 0.0, b2bc[:, dh * 512:(dh + 1) * 512],
                                               op0=ALU.bypass, op1=ALU.add)
                        v.engine_nop().then_inc(ctx.sem(eq + "yev"), 1)
                n_yev = ctx.val(eq + "yev")
                fl = half * 8 + st
                ctx.bump(q + "yst" + str(sb2), 16)
                @ctx.on("sync")
                def _(s, sb2=sb2, fl=fl, e=e, n_yev=n_yev, eq=eq):
                    s.wait_ge(ctx.sem(eq + "yev"), n_yev)
                    s.dma_start(out=g["y_in"][e * CAP:(e + 1) * CAP, :]
                                .rearrange("(p f) d -> p f d", f=16)[:, fl, :],
                                in_=ysb[:, sb2, :]).then_inc(ctx.sem(q + "yst" + str(sb2)), 16)
    ywaits = [(q + f"yst{p}", ctx.val(q + f"yst{p}")) for p in range(2)]
    @ctx.on("gpsimd")
    def _(gp):
        for nm, vv in ywaits:
            gp.wait_ge(ctx.sem(nm), vv)
        gp.collective_compute("AllGather", ALU.bypass,
                              replica_groups=[list(range(NC))],
                              ins=[g["y_in"].opt()], outs=[g["y_full"].opt()])\
            .then_inc(ctx.sem(S_AGY), 1)
    ctx.bump(S_AGY, 1)
    n_agy = ctx.val(S_AGY)

    ctx.close_tensors([cm4, cm5])
    cms.remove(cm4); cms.remove(cm5)
    ctx.close_tensors(cms3)

    if call == 0:
        n_hm = ctx.val(q + "hm")
        @ctx.on("gpsimd")
        def _(gp):
            gp.wait_ge(ctx.sem(q + "hm"), n_hm)
            gp.collective_compute("AllGather", ALU.bypass,
                                  replica_groups=[list(range(NC))],
                                  ins=[g["cnt_in"].opt()], outs=[g["cnt_out"].opt()])\
                .then_inc(ctx.sem(q + "agh"), 1)
        ctx.bump(q + "agh", 1)
        n_agh = ctx.val(q + "agh")
        @ctx.on("sync")
        def _(s):
            s.wait_ge(ctx.sem(q + "agh"), n_agh)
            s.dma_start(out=cld[:], in_=g["cnt_out"][:, :]).then_inc(ctx.sem(q + "hl"), 16)
        ctx.bump(q + "hl", 16)
        n_hl = ctx.val(q + "hl")
        n_hv2 = ctx.bump(S_VS, 1)
        @ctx.on("vector")
        def _(v):
            v.wait_ge(ctx.sem(q + "hl"), n_hl)
            v.tensor_scalar_add(mask8[:, 1:2], iota8[:], 1.0)
            v.engine_nop().then_inc(ctx.sem(S_VS), 1)
        n_hv3 = ctx.bump(S_VS, 1)
        @ctx.on("vector")
        def _(v, n_hv2=n_hv2):
            v.wait_ge(ctx.sem(S_VS), n_hv2)
            v.tensor_scalar_mul(mask8[:, 1:2], mask8[:, 1:2], float(TPC))
            v.engine_nop().then_inc(ctx.sem(S_VS), 1)
        n_hv4 = ctx.bump(S_VS, 1)
        @ctx.on("vector")
        def _(v, n_hv3=n_hv3):
            v.wait_ge(ctx.sem(S_VS), n_hv3)
            v.tensor_tensor(mask8[:, 1:2], mask8[:, 1:2],
                            g["pcf_col"][0:8, PCF_COREBASE:PCF_COREBASE + 1], op=ALU.is_equal)
            v.engine_nop().then_inc(ctx.sem(S_VS), 1)
        hmP, cmh = ctx.ps(q + "hmP", [1, 8], F32)
        cms.append(cmh)
        ctx.bump(S_PE, 1)
        @ctx.on("tensor")
        def _(t, n_hv4=n_hv4):
            t.wait_ge(ctx.sem(S_VS), n_hv4)
            t.matmul(hmP[:, 0:6], mask8[:, 1:2], cld[0:8, 16:22], start=True, stop=True)\
                .then_inc(ctx.sem(S_PE), 1)
        n_hv5 = ctx.bump(S_VS, 1)
        @ctx.on("vector")
        def _(v, n_pe=ctx.val(S_PE)):
            v.wait_ge(ctx.sem(S_PE), n_pe)
            v.tensor_copy(cnt_sb[:, 24:30], hmP[:, 0:6])
            v.engine_nop().then_inc(ctx.sem(S_VS), 1)
        @ctx.on("sync")
        def _(s, n_hv5=n_hv5):
            s.wait_ge(ctx.sem(S_VS), n_hv5)
            s.dma_start(out=g["hmx"], in_=cnt_sb[:, 24:30]).then_inc(ctx.sem(q + "hx"), 16)
        ctx.bump(q + "hx", 16)
        n_hx = ctx.val(q + "hx")
        @ctx.on("sync")
        def _(s):
            s.wait_ge(ctx.sem(q + "hx"), n_hx)
            s.dma_start(out=hcol[0:3, 0:1], in_=g["hmx"][0:1, 0:3].rearrange("1 a -> a 1"))\
                .then_inc(ctx.sem(q + "hx"), 16)
            s.dma_start(out=hcol[0:3, 1:2], in_=g["hmx"][0:1, 3:6].rearrange("1 a -> a 1"))\
                .then_inc(ctx.sem(q + "hx"), 16)
        ctx.bump(q + "hx", 32)
        n_hx2 = ctx.val(q + "hx")
        n_hv6 = ctx.bump(S_VS, 2)
        @ctx.on("vector")
        def _(v):
            v.wait_ge(ctx.sem(q + "hx"), n_hx2)
            v.memset(hcu[:], 0)
            v.engine_nop().then_inc(ctx.sem(S_VS), 1)
        @ctx.on("vector")
        def _(v, w=n_hv6 - 1):
            v.wait_ge(ctx.sem(S_VS), w)
            v.tensor_copy(hcu[0:3, :], hcol[0:3, 0:1])
            v.engine_nop().then_inc(ctx.sem(S_VS), 1)
        n_hcu = ctx.val(S_VS)
        ctx.bump(q + "hg", 16)
        @ctx.on("gpsimd")
        def _(gp):
            gp.wait_ge(ctx.sem(S_VS), n_hcu)
            gp.wait_ge(ctx.sem(S_AGY), n_agy)
            gp.indirect_dma_start(
                out=yrow[:, 0, :], out_offset=None,
                in_=g["y_full"],
                in_offset=IndirectOffsetOnAxis(ap=hcu[:], axis=0),
                bounds_check=g["r_ntok"],
                oob_is_err=False,
            ).then_inc(ctx.sem(q + "hg"), 16)
        n_hg = ctx.val(q + "hg")
        @ctx.on("sync")
        def _(s):
            s.dma_start(out=x1row[:, 0, :][0:3, :], in_=src[0:3, :]).then_inc(ctx.sem(q + "hgx"), 16)
        ctx.bump(q + "hgx", 16)
        n_hg2 = ctx.val(q + "hgx")
        n_hv7 = ctx.bump(S_VS, 2)
        @ctx.on("vector")
        def _(v):
            v.wait_ge(ctx.sem(q + "hg"), n_hg)
            v.wait_ge(ctx.sem(q + "hgx"), n_hg2)
            v.scalar_tensor_tensor(yrow[0:3, 0, :], yrow[0:3, 0, :], hcol[0:3, 1:2],
                                   x1row[0:3, 0, :], op0=ALU.mult, op1=ALU.add)
            v.engine_nop().then_inc(ctx.sem(S_VS), 1)
        @ctx.on("vector")
        def _(v, w=n_hv7 - 1):
            v.wait_ge(ctx.sem(S_VS), w)
            v.tensor_scalar_mul(yrow[0:3, 0, :], yrow[0:3, 0, :],
                                g["pcf_col"][0:3, PCF_PARITY:PCF_PARITY + 1])
            v.engine_nop().then_inc(ctx.sem(S_VS), 1)
        n_hv8 = ctx.val(S_VS)
        ctx.bump(q + "outh", 16)
        @ctx.on("sync")
        def _(s, n_hv8=n_hv8):
            s.wait_ge(ctx.sem(S_VS), n_hv8)
            s.dma_start(out=dst[0:3, :], in_=yrow[0:3, 0, :]).then_inc(ctx.sem(q + "outh"), 16)

    for i in range(NTILE):
        b = i % 2
        n_yg = ctx.bump(q + "yg" + str(b), 16)
        @ctx.on("gpsimd")
        def _(gp, i=i, b=b):
            if i == 0:
                gp.wait_ge(ctx.sem(S_AGY), n_agy)
                if call == 0:
                    gp.wait_ge(ctx.sem(q + "outh"), 16)
            if i >= 2:
                gp.wait_ge(ctx.sem(S_OUT + str(b)), base_out[b] + (i // 2) * 16)
            gp.indirect_dma_start(
                out=yrow[:, b, :], out_offset=None,
                in_=g["y_full"],
                in_offset=IndirectOffsetOnAxis(ap=slot_u[:, i:i + 1], axis=0),
                bounds_check=g["r_ntok"],
                oob_is_err=False,
            ).then_inc(ctx.sem(q + "yg" + str(b)), 16)
        n_xl = ctx.bump(q + "xl" + str(b), 16)
        @ctx.on("sync")
        def _(s, i=i, b=b):
            s.wait_ge(ctx.sem(q + "ev"), base_ev + max(0, i - 1))
            s.dma_start(out=x1row[:, b, :], in_=src[soff + i * 128: soff + (i + 1) * 128, :])\
                .then_inc(ctx.sem(q + "xl" + str(b)), 16)
        n_ev = ctx.bump(q + "ev", 1)
        @ctx.on("vector")
        def _(v, i=i, b=b, n_yg=n_yg, n_xl=n_xl):
            v.wait_ge(ctx.sem(q + "yg" + str(b)), n_yg)
            v.wait_ge(ctx.sem(q + "xl" + str(b)), n_xl)
            v.scalar_tensor_tensor(yrow[:, b, :], yrow[:, b, :], comb_a[:, i:i + 1],
                                   x1row[:, b, :], op0=ALU.mult, op1=ALU.add)
            v.engine_nop().then_inc(ctx.sem(q + "ev"), 1)
        ctx.bump(S_OUT + str(b), 16)
        @ctx.on("sync")
        def _(s, i=i, b=b, n_ev=n_ev):
            s.wait_ge(ctx.sem(q + "ev"), n_ev)
            s.dma_start(out=dst[doff + i * 128: doff + (i + 1) * 128, :], in_=yrow[:, b, :])\
                .then_inc(ctx.sem(S_OUT + str(b)), 16)
    @ctx.on("sync")
    def _(s, a=ctx.val(S_OUT + "0"), bb=ctx.val(S_OUT + "1"),
          hh=(ctx.val(q + "outh") if call == 0 else 0)):
        s.wait_ge(ctx.sem(S_OUT + "0"), a)
        s.wait_ge(ctx.sem(S_OUT + "1"), bb)
        if call == 0:
            s.wait_ge(ctx.sem(q + "outh"), hh)

    ctx.close_tensors(cms)


def emit_m1(ctx, g):
    q = "m1"
    cms = []
    def sb(name, shape, dt):
        h, cm = ctx.sb(q + name, shape, dt)
        cms.append(cm)
        return h

    xr = sb("xr", [128, 2, D], F32)
    scr = sb("scr", [128, D], F32)
    col = sb("col", [128, 2, 8], F32)
    xlt = sb("xlt", [128, 2, D], BF16)
    winb = sb("winb", [128, 32, 8, 128], BF16)
    xlnT = sb("xlnT", [128, 2, 8, TC], BF16)
    xlnTh = sb("xlnTh", [128, 8, 16], BF16)
    xi1 = sb("xi1", [128, 2, TC + 3], F32)
    carry = sb("carry", [128, 16, 3], F32)
    u1 = sb("u1", [128, 2, TC], F32)
    ub1 = sb("ub1", [128, 2, TC], BF16)
    sg1 = sb("sg1", [128, 2, TC], F32)
    xc_sb = sb("xc_sb", [128, TC], F32)

    S_AC = q + "ac"; S_VS = q + "vs"
    base_st = [ctx.val(q + "st0"), ctx.val(q + "st1")]

    def ln_tile(i, nrow, src_off, dst_off):
        b = i % 2
        n_ld = ctx.bump(q + "ld" + str(b), 16)
        @ctx.on("sync")
        def _(s, i=i, b=b, nrow=nrow, src_off=src_off):
            if i >= 2:
                s.wait_ge(ctx.sem(S_AC), lnac_rel[i - 2])
            s.dma_start(out=xr[0:nrow, b, :], in_=g["x1b"][src_off:src_off + nrow, :])\
                .then_inc(ctx.sem(q + "ld" + str(b)), 16)
        n_ac = ctx.bump(S_AC, 1)
        @ctx.on("scalar")
        def _(sc, b=b, nrow=nrow, n_ld=n_ld):
            sc.wait_ge(ctx.sem(q + "ld" + str(b)), n_ld)
            sc.activation(out=scr[0:nrow, :], in_=xr[0:nrow, b, :], func=AF.Square,
                          accum_out=col[0:nrow, b, 1:2]).then_inc(ctx.sem(S_AC), 1)
        n_v1 = ctx.bump(S_VS, 1)
        @ctx.on("vector")
        def _(v, b=b, nrow=nrow, n_ld=n_ld, n_ac=n_ac):
            v.wait_ge(ctx.sem(q + "ld" + str(b)), n_ld)
            v.wait_ge(ctx.sem(S_AC), n_ac)
            v.reduce_sum(col[0:nrow, b, 0:1], xr[0:nrow, b, :], axis=AX.X)
            v.engine_nop().then_inc(ctx.sem(S_VS), 1)
        n_v2 = ctx.bump(S_VS, 1)
        @ctx.on("vector")
        def _(v, b=b, nrow=nrow, n_v1=n_v1):
            v.wait_ge(ctx.sem(S_VS), n_v1)
            v.tensor_scalar_mul(col[0:nrow, b, 2:3], col[0:nrow, b, 0:1], 1.0 / D)
            v.tensor_scalar_mul(col[0:nrow, b, 3:4], col[0:nrow, b, 1:2], 1.0 / D)
            v.engine_nop().then_inc(ctx.sem(S_VS), 1)
        n_v3 = ctx.bump(S_VS, 1)
        @ctx.on("vector")
        def _(v, b=b, nrow=nrow, n_v2=n_v2):
            v.wait_ge(ctx.sem(S_VS), n_v2)
            v.tensor_tensor(col[0:nrow, b, 6:7], col[0:nrow, b, 2:3], col[0:nrow, b, 2:3], op=ALU.mult)
            v.engine_nop().then_inc(ctx.sem(S_VS), 1)
        n_v4 = ctx.bump(S_VS, 1)
        @ctx.on("vector")
        def _(v, b=b, nrow=nrow, n_v3=n_v3):
            v.wait_ge(ctx.sem(S_VS), n_v3)
            v.tensor_sub(col[0:nrow, b, 3:4], col[0:nrow, b, 3:4], col[0:nrow, b, 6:7])
            v.engine_nop().then_inc(ctx.sem(S_VS), 1)
        n_ac1 = ctx.bump(S_AC, 1)
        n_ac2 = ctx.bump(S_AC, 1)
        @ctx.on("scalar")
        def _(sc, b=b, nrow=nrow, n_v4=n_v4, n_ac1=n_ac1):
            sc.wait_ge(ctx.sem(S_VS), n_v4)
            sc.activation(out=col[0:nrow, b, 4:5], in_=col[0:nrow, b, 3:4], func=AF.Ln, bias=EPS)\
                .then_inc(ctx.sem(S_AC), 1)
            sc.wait_ge(ctx.sem(S_AC), n_ac1)
            sc.activation(out=col[0:nrow, b, 4:5], in_=col[0:nrow, b, 4:5], func=AF.Exp, scale=-0.5)\
                .then_inc(ctx.sem(S_AC), 1)
        n_v5 = ctx.bump(S_VS, 1)
        @ctx.on("vector")
        def _(v, b=b, nrow=nrow, n_ac2=n_ac2):
            v.wait_ge(ctx.sem(S_AC), n_ac2)
            v.scalar_tensor_tensor(col[0:nrow, b, 5:6], col[0:nrow, b, 2:3], -1.0,
                                   col[0:nrow, b, 4:5], op0=ALU.mult, op1=ALU.mult)
            v.engine_nop().then_inc(ctx.sem(S_VS), 1)
        n_ac3 = ctx.bump(S_AC, 1)
        lnac_rel[i] = n_ac3
        @ctx.on("scalar")
        def _(sc, i=i, b=b, nrow=nrow, n_v5=n_v5):
            sc.wait_ge(ctx.sem(S_VS), n_v5)
            if i >= 2:
                sc.wait_ge(ctx.sem(q + "st" + str(b)), base_st[b] + (i // 2) * 16)
            sc.activation(out=xlt[0:nrow, b, :], in_=xr[0:nrow, b, :], func=AF.Identity,
                          scale=col[0:nrow, b, 4:5], bias=col[0:nrow, b, 5:6]).then_inc(ctx.sem(S_AC), 1)
        ctx.bump(q + "st" + str(b), 16)
        @ctx.on("sync")
        def _(s, i=i, b=b, nrow=nrow, dst_off=dst_off, n_ac3=n_ac3):
            s.wait_ge(ctx.sem(S_AC), n_ac3)
            s.dma_start(out=g["xln1"][dst_off:dst_off + nrow, :], in_=xlt[0:nrow, b, :])\
                .then_inc(ctx.sem(q + "st" + str(b)), 16)

    lnac_rel = {}
    for i in range(NTILE):
        ln_tile(i, 128, 3 + i * 128, 3 + i * 128)
    ln_tile(NTILE, 3, 0, 0)
    n_st0 = ctx.val(q + "st0")
    n_st1 = ctx.val(q + "st1")

    @ctx.on("sync")
    def _(s):
        for o in range(32):
            s.dma_start(out=winb[:, o, :, :], in_=g["winT_b"][o].rearrange("a p h -> p a h"))\
                .then_inc(ctx.sem(q + "wl"), 16)
    n_wl = ctx.bump(q + "wl", 16 * 32)

    @ctx.on("scalar")
    def _(sc):
        sc.wait_ge(ctx.sem(q + "st0"), n_st0)
        sc.wait_ge(ctx.sem(q + "st1"), n_st1)
        for kc in range(8):
            sc.dma_start_transpose(out=xlnTh[:, kc, :], in_=g["xln1"][0:16, kc * 128:(kc + 1) * 128])\
                .then_inc(ctx.sem(q + "trh"), 16)
    n_trh = ctx.bump(q + "trh", 16 * 8)

    xzP, cmp1 = ctx.ps(q + "xzP", [128, 4, TC], F32)
    hP, cmp2 = ctx.ps(q + "hP", [128, 16], F32)
    cms += [cmp1, cmp2]

    for o in range(16):
        n_hpe = ctx.bump(q + "hpe", 1)
        @ctx.on("tensor")
        def _(t, o=o):
            if o == 0:
                t.wait_ge(ctx.sem(q + "trh"), n_trh)
                t.wait_ge(ctx.sem(q + "wl"), n_wl)
            if o >= 1:
                t.wait_ge(ctx.sem(q + "hev"), o)
            last = None
            for kc in range(8):
                last = t.matmul(hP[:, 0:16], winb[:, o, kc, :], xlnTh[:, kc, :],
                                start=(kc == 0), stop=(kc == 7))
            last.then_inc(ctx.sem(q + "hpe"), 1)
        ctx.bump(q + "hev", 1)
        @ctx.on("scalar")
        def _(sc, o=o, n_hpe=n_hpe):
            sc.wait_ge(ctx.sem(q + "hpe"), n_hpe)
            sc.activation(out=carry[:, o, :], in_=hP[:, 0:3], func=AF.Copy)\
                .then_inc(ctx.sem(q + "hev"), 1)
    n_hev = ctx.val(q + "hev")

    cv_rel = {}
    last_uac = [0]
    for c in range(NCH):
        b = c % 2
        ctx.bump(q + "tr" + str(b), 16 * 8)
        @ctx.on("scalar")
        def _(sc, c=c, b=b):
            if c >= 2:
                sc.wait_ge(ctx.sem(q + "pe"), pe_rel[c - 2])
            for kc in range(8):
                sc.dma_start_transpose(out=xlnT[:, b, kc, :],
                                       in_=g["xln1"][3 + c * TC: 3 + (c + 1) * TC,
                                                     kc * 128:(kc + 1) * 128])\
                    .then_inc(ctx.sem(q + "tr" + str(b)), 16)
        if c == 0:
            pe_rel = {}
        n_tr = ctx.val(q + "tr" + str(b))
        for o in range(32):
            pb = o % 4
            ob = o % 2
            n_pe = ctx.bump(q + "pe", 1)
            @ctx.on("tensor")
            def _(t, c=c, b=b, o=o, pb=pb, n_tr=n_tr):
                if o == 0:
                    t.wait_ge(ctx.sem(q + "tr" + str(b)), n_tr)
                    if c == 0:
                        t.wait_ge(ctx.sem(q + "hev"), n_hev)
                j = c * 32 + o
                if j >= 4:
                    t.wait_ge(ctx.sem(q + "ev"), j - 3)
                last = None
                for kc in range(8):
                    last = t.matmul(xzP[:, pb, :], winb[:, o, kc, :], xlnT[:, b, kc, :],
                                    start=(kc == 0), stop=(kc == 7))
                last.then_inc(ctx.sem(q + "pe"), 1)
            pe_rel[c] = ctx.val(q + "pe")
            n_ev = ctx.bump(q + "ev", 1)
            @ctx.on("scalar")
            def _(sc, c=c, o=o, pb=pb, ob=ob, n_pe=n_pe):
                sc.wait_ge(ctx.sem(q + "pe"), n_pe)
                if o < 16:
                    jx = c * 16 + o
                    if jx >= 2:
                        sc.wait_ge(ctx.sem(S_VS), cv_rel[jx - 2])
                    sc.activation(out=xi1[:, ob, 3:3 + TC], in_=xzP[:, pb, :], func=AF.Copy)\
                        .then_inc(ctx.sem(q + "ev"), 1)
                else:
                    jz = c * 16 + (o - 16)
                    if jz >= 2:
                        sc.wait_ge(ctx.sem(q + "zs" + str(ob)), (jz // 2) * 16)
                    sc.activation(out=sg1[:, ob, :], in_=xzP[:, pb, :], func=AF.Silu)\
                        .then_inc(ctx.sem(q + "ev"), 1)
            if o >= 16:
                ctx.bump(q + "zs" + str(ob), 16)
                @ctx.on("sync")
                def _(s, c=c, o=o, ob=ob, n_ev=n_ev):
                    s.wait_ge(ctx.sem(q + "ev"), n_ev)
                    s.dma_start(out=g["sg_h"][(o - 16) * 128:(o - 15) * 128, c * TC:(c + 1) * TC],
                                in_=sg1[:, ob, :]).then_inc(ctx.sem(q + "zs" + str(ob)), 16)
            else:
                n_hd = ctx.bump(S_VS, 1)
                @ctx.on("vector")
                def _(v, o=o, ob=ob, n_ev=n_ev):
                    v.wait_ge(ctx.sem(q + "ev"), n_ev)
                    v.tensor_copy(xi1[:, ob, 0:3], carry[:, o, :])
                    v.engine_nop().then_inc(ctx.sem(S_VS), 1)
                n_vc = ctx.bump(S_VS, 4)
                @ctx.on("vector")
                def _(v, c=c, o=o, ob=ob, base=n_vc - 4, n_hd=n_hd, w_uac=last_uac[0]):
                    v.wait_ge(ctx.sem(S_VS), n_hd)
                    v.wait_ge(ctx.sem(S_AC), w_uac)
                    v.scalar_tensor_tensor(xc_sb[:], xi1[:, ob, 0:TC], g["cwcol"][:, 0 * 16 + o:0 * 16 + o + 1],
                                           g["cbcol"][:, o:o + 1].broadcast_to([128, TC]),
                                           op0=ALU.mult, op1=ALU.add)
                    v.engine_nop().then_inc(ctx.sem(S_VS), 1)
                    for k in range(1, 4):
                        v.wait_ge(ctx.sem(S_VS), base + k)
                        v.scalar_tensor_tensor(xc_sb[:], xi1[:, ob, k:k + TC], g["cwcol"][:, k * 16 + o:k * 16 + o + 1],
                                               xc_sb[:], op0=ALU.mult, op1=ALU.add)
                        v.engine_nop().then_inc(ctx.sem(S_VS), 1)
                n_cr = ctx.bump(S_VS, 1)
                cv_rel[c * 16 + o] = n_cr
                @ctx.on("vector")
                def _(v, o=o, ob=ob, n_vc=n_vc):
                    v.wait_ge(ctx.sem(S_VS), n_vc)
                    v.tensor_copy(carry[:, o, :], xi1[:, ob, TC:TC + 3])
                    v.engine_nop().then_inc(ctx.sem(S_VS), 1)
                n_uac = ctx.bump(S_AC, 1)
                last_uac[0] = n_uac
                @ctx.on("scalar")
                def _(sc, c=c, o=o, ob=ob, n_vc=n_vc):
                    sc.wait_ge(ctx.sem(S_VS), n_vc)
                    jx = c * 16 + o
                    if jx >= 2:
                        sc.wait_ge(ctx.sem(q + "us" + str(ob)), (jx // 2) * 32)
                    sc.activation(out=u1[:, ob, :], in_=xc_sb[:], func=AF.Silu).then_inc(ctx.sem(S_AC), 1)
                n_vcc = ctx.bump(S_VS, 1)
                @ctx.on("vector")
                def _(v, o=o, ob=ob, n_uac=n_uac):
                    v.wait_ge(ctx.sem(S_AC), n_uac)
                    v.tensor_copy(ub1[:, ob, :], u1[:, ob, :])
                    v.engine_nop().then_inc(ctx.sem(S_VS), 1)
                ctx.bump(q + "us" + str(ob), 32)
                @ctx.on("sync")
                def _(s, c=c, o=o, ob=ob, n_vcc=n_vcc):
                    s.wait_ge(ctx.sem(S_VS), n_vcc)
                    s.dma_start(out=g["u_h"][o * 128:(o + 1) * 128, c * TC:(c + 1) * TC],
                                in_=u1[:, ob, :]).then_inc(ctx.sem(q + "us" + str(ob)), 16)
                    s.dma_start(out=g["ubf_h"][o * 128:(o + 1) * 128, c * TC:(c + 1) * TC],
                                in_=ub1[:, ob, :]).then_inc(ctx.sem(q + "us" + str(ob)), 16)
    @ctx.on("sync")
    def _(s, a=ctx.val(q + "us0"), bb=ctx.val(q + "us1"),
          cz=ctx.val(q + "zs0"), dz=ctx.val(q + "zs1")):
        s.wait_ge(ctx.sem(q + "us0"), a)
        s.wait_ge(ctx.sem(q + "us1"), bb)
        s.wait_ge(ctx.sem(q + "zs0"), cz)
        s.wait_ge(ctx.sem(q + "zs1"), dz)
    ctx.close_tensors(cms)


def emit_m2(ctx, g):
    q = "m2"
    cms = []
    def sb(name, shape, dt):
        h, cm = ctx.sb(q + name, shape, dt)
        cms.append(cm)
        return h

    wxb = sb("wxb", [128, 16, 2 * DS], BF16)
    wdtg = sb("wdtg", [128, 4, 16, 128], BF16)
    ub2 = sb("ub2", [128, 2, 16, TC], BF16)
    u2 = sb("u2", [128, 2, 4, TC], F32)
    sg2 = sb("sg2", [128, 2, 4, TC], F32)
    dt_sb = sb("dt_sb", [128, 4, TC], F32)
    du_sb = sb("du_sb", [128, 4, TC], F32)
    a_sb = sb("a_sb", [128, TC], F32)
    bhi = sb("bhi", [16, TPC], BF16)
    blo = sb("blo", [16, TPC], BF16)
    chi = sb("chi", [16, TPC], BF16)
    clo = sb("clo", [16, TPC], BF16)
    Bbq = sb("Bbq", [128, 2, 4, TC], F32)
    Cbq = sb("Cbq", [128, 4, TC], F32)
    dAq = sb("dAq", [128, 2, 4, TC], F32)
    dBq = sb("dBq", [128, 4, TC], F32)
    sq = sb("sq", [128, 4, TC], F32)
    y_acc = sb("y_acc", [128, 4, TC], F32)
    ygb = sb("ygb", [128, 2, TC], BF16)
    state_sb = sb("state_sb", [128, 256], F32)
    st_imp = sb("st_imp", [128, 256], F32)
    pr_f = sb("pr_f", [128, 1], F32)
    pr_u = sb("pr_u", [128, 1], U32)

    S_VS = q + "vs"; S_AC = q + "ac"; S_PE = q + "pe"; S_PL = q + "pl"

    r_ag = ctx.nc.gpsimd.alloc_register(q + "rag")
    r_gp = ctx.nc.gpsimd.alloc_register(q + "rgp")
    r_stg = ctx.nc.vector.alloc_register(q + "rstg")
    r_si = ctx.nc.vector.alloc_register(q + "rsi")

    @ctx.on("sync")
    def _(s):
        for kc in range(16):
            s.dma_start(out=wxb[:, kc, :], in_=g["wxT_b"][kc]).then_inc(ctx.sem(q + "wx"), 16)
    n_wx = ctx.bump(q + "wx", 16 * 16)

    n_vz = ctx.bump(S_VS, 1)
    @ctx.on("vector")
    def _(v):
        v.memset(state_sb[:], 0.0)
        v.memset(st_imp[:], 0.0)
        v.engine_nop().then_inc(ctx.sem(S_VS), 1)
    @ctx.on("gpsimd")
    def _(gp):
        gp.iota(pr_f[:], pattern=[[0, 1]], base=0, channel_multiplier=1,
                allow_small_or_imprecise_dtypes=True)
        gp.engine_nop().then_inc(ctx.sem(S_PL), 1)
    n_pl0 = ctx.bump(S_PL, 1)
    n_vpr1 = ctx.bump(S_VS, 1)
    @ctx.on("vector")
    def _(v):
        v.wait_ge(ctx.sem(S_PL), n_pl0)
        v.tensor_scalar_add(pr_f[:], pr_f[:], g["pcf_col"][:, 5:6])
        v.engine_nop().then_inc(ctx.sem(S_VS), 1)
    ctx.bump(S_VS, 1)
    @ctx.on("vector")
    def _(v, w=n_vpr1):
        v.wait_ge(ctx.sem(S_VS), w)
        v.tensor_copy(pr_u[:], pr_f[:])
        v.engine_nop().then_inc(ctx.sem(S_VS), 1)
    n_pr = ctx.val(S_VS)

    dtP, cmp1 = ctx.ps(q + "dtP", [128, 2, TC], F32)
    bP, cmp2 = ctx.ps(q + "bP", [16, TC], F32)
    cP, cmp3 = ctx.ps(q + "cP", [16, TC], F32)
    bcP, cmp4 = ctx.ps(q + "bcP", [128, 4, TC], F32)
    cms += [cmp1, cmp2, cmp3, cmp4]

    scan_hist = {}
    dbu_hist = {}
    cm_hist = {}
    pe_group_rel = {}
    gt_idx = [0]

    for gi in range(NG):
        @ctx.on("gpsimd")
        def _(gp, gi=gi):
            if gi == 0:
                gp.wait_ge(ctx.sem(S_VS), n_pr)
            gp.reg_load(r_gp, g["pcu"][0:1, 8 + gi:9 + gi])
            gp.wait_ge(ctx.sem(q + "gp"), r_gp)
            gp.dma_start(out=g[f"st_in_{gi}"], in_=st_imp[:, gi * 64:(gi + 1) * 64])\
                .then_inc(ctx.sem(q + f"stg{gi}"), 16)
        n_stg = ctx.bump(q + f"stg{gi}", 16)
        @ctx.on("gpsimd")
        def _(gp, gi=gi, n_stg=n_stg):
            gp.wait_ge(ctx.sem(q + f"stg{gi}"), n_stg)
            gp.collective_compute(
                "AllGather", ALU.bypass,
                replica_groups=[list(range(NC))],
                ins=[g[f"st_in_{gi}"].opt()],
                outs=[g[f"st_out_{gi}"][0:NC * 128, :].opt()],
            ).then_inc(ctx.sem(q + f"ag{gi}"), 1)
        ctx.bump(q + f"ag{gi}", 1)
        @ctx.on("gpsimd")
        def _(gp, gi=gi):
            gp.wait_ge(ctx.sem("setup_z"), 16 * NG)
            gp.wait_ge(ctx.sem(q + f"si{gi}"), gi * 16)
            gp.reg_load(r_ag, g["pcu"][0:1, 12 + gi:13 + gi])
            gp.wait_ge(ctx.sem(q + f"ag{gi}"), r_ag)
            gp.indirect_dma_start(
                out=st_imp[:, gi * 64:(gi + 1) * 64],
                out_offset=None,
                in_=g[f"st_out_{gi}"],
                in_offset=IndirectOffsetOnAxis(ap=pr_u[:], axis=0),
            ).then_inc(ctx.sem(q + f"si{gi}"), 16)
        n_si = ctx.bump(q + f"si{gi}", 16)

        @ctx.on("sync")
        def _(s, gi=gi):
            if gi >= 1:
                s.wait_ge(ctx.sem(S_PE), pe_group_rel[gi - 1])
            for ot in range(4):
                s.dma_start(out=wdtg[:, ot, :, :],
                            in_=g["wdtT_b"][gi * 4 + ot].rearrange("a p h -> p a h"))\
                    .then_inc(ctx.sem(q + "wd"), 16)
        n_wd = ctx.bump(q + "wd", 16 * 4)

        for c in range(NCH):
            b = c % 2
            pair = gi * NCH + c
            n_ub = ctx.bump(q + "ub" + str(b), 16 * 16)
            n_ug = ctx.bump(q + "ug" + str(b), 16 * 8)
            @ctx.on("sync")
            def _(s, gi=gi, c=c, b=b, pair=pair):
                if pair >= 2:
                    s.wait_ge(ctx.sem(q + "gt"), pair - 1)
                for kc in range(16):
                    s.dma_start(out=ub2[:, b, kc, :],
                                in_=g["ubf_h"][kc * 128:(kc + 1) * 128, c * TC:(c + 1) * TC])\
                        .then_inc(ctx.sem(q + "ub" + str(b)), 16)
                for dt_i in range(4):
                    r0 = (gi * 4 + dt_i) * 128
                    s.dma_start(out=u2[:, b, dt_i, :],
                                in_=g["u_h"][r0:r0 + 128, c * TC:(c + 1) * TC])\
                        .then_inc(ctx.sem(q + "ug" + str(b)), 16)
                    s.dma_start(out=sg2[:, b, dt_i, :],
                                in_=g["sg_h"][r0:r0 + 128, c * TC:(c + 1) * TC])\
                        .then_inc(ctx.sem(q + "ug" + str(b)), 16)
            for dt_i in range(4):
                pb = (c * 4 + dt_i) % 2
                j = pair * 4 + dt_i
                n_pe = ctx.bump(S_PE, 1)
                @ctx.on("tensor")
                def _(t, gi=gi, c=c, b=b, dt_i=dt_i, pb=pb, j=j, n_ub=n_ub, n_wd=n_wd):
                    if c == 0 and dt_i == 0:
                        t.wait_ge(ctx.sem(q + "wd"), n_wd)
                        if gi == 0:
                            t.wait_ge(ctx.sem(q + "wx"), n_wx)
                    if dt_i == 0:
                        t.wait_ge(ctx.sem(q + "ub" + str(b)), n_ub)
                    if j >= 2:
                        t.wait_ge(ctx.sem(S_AC), spab_hist[j - 2])
                    last = None
                    for kc in range(16):
                        last = t.matmul(dtP[:, pb, :], wdtg[:, dt_i, kc, :], ub2[:, b, kc, :],
                                        start=(kc == 0), stop=(kc == 15))
                    last.then_inc(ctx.sem(S_PE), 1)
                n_sp1 = ctx.bump(S_AC, 2)
                if not (gi == 0 and c == 0 and dt_i == 0):
                    pass
                else:
                    spab_hist = {}
                spab_hist[j] = n_sp1
                w_spl = spl_hist[-1] if (gi + c + dt_i) > 0 else 0
                @ctx.on("scalar")
                def _(sc, gi=gi, dt_i=dt_i, pb=pb, n_pe=n_pe, j=j, w_spl=w_spl):
                    sc.wait_ge(ctx.sem(S_PE), n_pe)
                    if j >= 1:
                        sc.wait_ge(ctx.sem(S_VS), w_spl)
                    bcol = g["bdtcol"][:, gi * 4 + dt_i:gi * 4 + dt_i + 1]
                    sc.activation(out=dt_sb[:, dt_i, :], in_=dtP[:, pb, :], func=AF.Relu, bias=bcol)
                    sc.activation(out=a_sb[:], in_=dtP[:, pb, :], func=AF.Abs, bias=bcol)\
                        .then_inc(ctx.sem(S_AC), 2)
                n_sp2 = ctx.bump(S_AC, 1)
                n_sp3 = ctx.bump(S_AC, 1)
                @ctx.on("scalar")
                def _(sc, n_sp1=n_sp1, n_sp2=n_sp2):
                    sc.wait_ge(ctx.sem(S_AC), n_sp1)
                    sc.activation(out=a_sb[:], in_=a_sb[:], func=AF.Exp, scale=-1.0)\
                        .then_inc(ctx.sem(S_AC), 1)
                    sc.wait_ge(ctx.sem(S_AC), n_sp2)
                    sc.activation(out=a_sb[:], in_=a_sb[:], func=AF.Ln, bias=1.0)\
                        .then_inc(ctx.sem(S_AC), 1)
                if pair == 0 and dt_i == 0:
                    spl_hist = []
                    spl_v_hist = {}
                    da_chunk_hist = {}
                n_vsp = ctx.bump(S_VS, 2)
                spl_v_hist[(gi, c, dt_i)] = n_vsp - 1
                @ctx.on("vector")
                def _(v, gi=gi, c=c, b=b, dt_i=dt_i, pair=pair, n_sp3=n_sp3, n_ug=n_ug,
                      base=n_vsp - 2):
                    v.wait_ge(ctx.sem(S_AC), n_sp3)
                    if pair >= 1:
                        v.wait_ge(ctx.sem(S_AC), da_chunk_hist[pair - 1])
                    v.tensor_add(dt_sb[:, dt_i, :], dt_sb[:, dt_i, :], a_sb[:])\
                        .then_inc(ctx.sem(S_VS), 1)
                    v.wait_ge(ctx.sem(S_VS), base + 1)
                    v.wait_ge(ctx.sem(q + "ug" + str(b)), n_ug)
                    v.tensor_mul(du_sb[:, dt_i, :], dt_sb[:, dt_i, :], u2[:, b, dt_i, :])\
                        .then_inc(ctx.sem(S_VS), 1)
                spl_hist.append(n_vsp - 1)
            n_du_all = ctx.val(S_VS)

            if gi == 0:
                n_bc = ctx.bump(S_PE, 2)
                @ctx.on("tensor")
                def _(t, c=c, b=b):
                    if c >= 2:
                        t.wait_ge(ctx.sem(q + "bcev"), (c - 1) * 4)
                    last = None
                    for kc in range(16):
                        last = t.matmul(bP[:], wxb[:, kc, 0:DS], ub2[:, b, kc, :],
                                        start=(kc == 0), stop=(kc == 15))
                    last.then_inc(ctx.sem(S_PE), 1)
                    last = None
                    for kc in range(16):
                        last = t.matmul(cP[:], wxb[:, kc, DS:2 * DS], ub2[:, b, kc, :],
                                        start=(kc == 0), stop=(kc == 15))
                    last.then_inc(ctx.sem(S_PE), 1)
                n_bhi = ctx.bump(S_AC, 2)
                @ctx.on("scalar")
                def _(sc, c=c, n_bc=n_bc):
                    sc.wait_ge(ctx.sem(S_PE), n_bc)
                    sc.activation(out=bhi[:, c * TC:(c + 1) * TC], in_=bP[:], func=AF.Copy)
                    sc.activation(out=chi[:, c * TC:(c + 1) * TC], in_=cP[:], func=AF.Copy)\
                        .then_inc(ctx.sem(S_AC), 2)
                n_bcev = ctx.bump(q + "bcev", 4)
                @ctx.on("vector")
                def _(v, c=c, n_bhi=n_bhi):
                    v.wait_ge(ctx.sem(S_AC), n_bhi)
                    v.scalar_tensor_tensor(blo[:, c * TC:(c + 1) * TC], bP[:], 0.0,
                                           bhi[:, c * TC:(c + 1) * TC], op0=ALU.bypass, op1=ALU.subtract)
                    v.scalar_tensor_tensor(clo[:, c * TC:(c + 1) * TC], cP[:], 0.0,
                                           chi[:, c * TC:(c + 1) * TC], op0=ALU.bypass, op1=ALU.subtract)
                    v.engine_nop().then_inc(ctx.sem(q + "bcev"), 4)
            n_bcev_need = 4 * (c + 1) if gi == 0 else 4 * NCH

            for qq in range(4):
                it = (pair * 4 + qq)
                n_bb = ctx.bump(S_PE, 2)
                bb_par = it % 2
                @ctx.on("tensor")
                def _(t, gi=gi, c=c, qq=qq, bb_par=bb_par, n_bcev_need=n_bcev_need, it=it):
                    if gi == 0:
                        t.wait_ge(ctx.sem(q + "bcev"), n_bcev_need)
                    elif c == 0 and qq == 0:
                        t.wait_ge(ctx.sem(q + "bcev"), 4 * NCH)
                    if it >= 1:
                        t.wait_ge(ctx.sem(q + "bbev"), it * 2)
                    last = None
                    for nl in range(4):
                        n = qq * 4 + nl
                        t.matmul(bcP[:, nl, :], g["seln"][:, n * 128:(n + 1) * 128],
                                 bhi[:, c * TC:(c + 1) * TC], start=True, stop=False)
                        last = t.matmul(bcP[:, nl, :], g["seln"][:, n * 128:(n + 1) * 128],
                                        blo[:, c * TC:(c + 1) * TC], start=False, stop=True)
                    last.then_inc(ctx.sem(S_PE), 1)
                    t.wait_ge(ctx.sem(q + "bbev"), it * 2 + 1)
                    last = None
                    for nl in range(4):
                        n = qq * 4 + nl
                        t.matmul(bcP[:, nl, :], g["seln"][:, n * 128:(n + 1) * 128],
                                 chi[:, c * TC:(c + 1) * TC], start=True, stop=False)
                        last = t.matmul(bcP[:, nl, :], g["seln"][:, n * 128:(n + 1) * 128],
                                        clo[:, c * TC:(c + 1) * TC], start=False, stop=True)
                    last.then_inc(ctx.sem(S_PE), 1)
                n_bbev1 = ctx.bump(q + "bbev", 1)
                n_bbev2 = ctx.bump(q + "bbev", 1)
                @ctx.on("scalar")
                def _(sc, qb=it % 2, w1=n_bb - 1, w2=n_bb, it=it):
                    sc.wait_ge(ctx.sem(S_PE), w1)
                    if it >= 2:
                        sc.wait_ge(ctx.sem(S_VS), dbu_hist[(it - 2) * 4 + 3])
                    sc.activation(out=Bbq[:, qb], in_=bcP[:, :, :], func=AF.Copy)\
                        .then_inc(ctx.sem(q + "bbev"), 1)
                    sc.wait_ge(ctx.sem(S_PE), w2)
                    if it >= 1:
                        sc.wait_ge(ctx.sem(S_VS), cm_hist[(it - 1) * 4 + 3])
                    sc.activation(out=Cbq[:], in_=bcP[:, :, :], func=AF.Copy)\
                        .then_inc(ctx.sem(q + "bbev"), 1)

                for dt_i in range(4):
                    dcol = gi * 4 + dt_i
                    itd = it * 4 + dt_i
                    ib = itd % 2
                    n_da = ctx.bump(S_AC, 4)
                    @ctx.on("scalar")
                    def _(sc, gi=gi, c=c, qq=qq, dt_i=dt_i, ib=ib, dcol=dcol, itd=itd):
                        sc.wait_ge(ctx.sem(S_VS), spl_v_hist[(gi, c, dt_i)])
                        if itd >= 2:
                            sc.wait_ge(ctx.sem(S_VS), scan_hist[itd - 2])
                        for nl in range(4):
                            acol = g["Acols"][:, dcol * 16 + qq * 4 + nl: dcol * 16 + qq * 4 + nl + 1]
                            sc.activation(out=dAq[:, ib, nl, :], in_=dt_sb[:, dt_i, :],
                                          func=AF.Exp, scale=acol).then_inc(ctx.sem(S_AC), 1)
                    n_db = ctx.bump(S_VS, 1)
                    dbu_hist[itd] = n_db
                    @ctx.on("vector")
                    def _(v, qb=it % 2, dt_i=dt_i, n_bbev1=n_bbev1):
                        v.wait_ge(ctx.sem(q + "bbev"), n_bbev1)
                        v.tensor_tensor(dBq[:],
                                        du_sb[:, dt_i, :].unsqueeze(1).broadcast_to([128, 4, TC]),
                                        Bbq[:, qb], op=ALU.mult)\
                            .then_inc(ctx.sem(S_VS), 1)
                    n_sc = ctx.bump(S_VS, 4)
                    scan_hist[itd] = n_sc
                    @ctx.on("vector")
                    def _(v, gi=gi, c=c, qq=qq, dt_i=dt_i, ib=ib, dcol=dcol,
                          n_da=n_da, n_db=n_db, n_si=n_si):
                        v.wait_ge(ctx.sem(S_AC), n_da)
                        if c == 0 and qq == 0 and dt_i == 0:
                            v.reg_load(r_si, g["pcu"][0:1, 20 + gi:21 + gi])
                            v.wait_ge(ctx.sem(q + f"si{gi}"), r_si)
                        for nl in range(4):
                            scol = dcol * 16 + qq * 4 + nl
                            init = st_imp[:, scol:scol + 1] if c == 0 else state_sb[:, scol:scol + 1]
                            v.tensor_tensor_scan(out=sq[:, nl, :], data0=dAq[:, ib, nl, :],
                                                 data1=dBq[:, nl, :], initial=init,
                                                 op0=ALU.mult, op1=ALU.add)\
                                .then_inc(ctx.sem(S_VS), 1)
                    n_st = ctx.bump(S_VS, 1)
                    @ctx.on("vector")
                    def _(v, gi=gi, c=c, qq=qq, dt_i=dt_i, dcol=dcol, n_sc=n_sc):
                        scol = dcol * 16 + qq * 4
                        if c == NCH - 1:
                            v.reg_load(r_stg, g["pcu"][0:1, 16 + gi:17 + gi])
                            v.wait_ge(ctx.sem(q + f"stg{gi}"), r_stg)
                            v.tensor_copy(st_imp[:, scol:scol + 4],
                                          sq[:, 0:4, TC - 1:TC].rearrange("p a 1 -> p a"))\
                                .then_inc(ctx.sem(S_VS), 1)
                        else:
                            v.tensor_copy(state_sb[:, scol:scol + 4],
                                          sq[:, 0:4, TC - 1:TC].rearrange("p a 1 -> p a"))\
                                .then_inc(ctx.sem(S_VS), 1)
                    n_cm = ctx.bump(S_VS, 1)
                    cm_hist[itd] = n_cm
                    @ctx.on("vector")
                    def _(v, n_st=n_st):
                        v.tensor_mul(dBq[:], sq[:], Cbq[:])\
                            .then_inc(ctx.sem(S_VS), 1)
                    n_red = ctx.bump(S_VS, 3 if qq == 0 else 4)
                    @ctx.on("vector")
                    def _(v, qq=qq, dt_i=dt_i, n_cm=n_cm, base=n_red - (3 if qq == 0 else 4)):
                        v.tensor_add(sq[:, 0, :], dBq[:, 0, :], dBq[:, 1, :])\
                            .then_inc(ctx.sem(S_VS), 1)
                        v.tensor_add(sq[:, 1, :], dBq[:, 2, :], dBq[:, 3, :])\
                            .then_inc(ctx.sem(S_VS), 1)
                        if qq == 0:
                            v.tensor_add(y_acc[:, dt_i, :], sq[:, 0, :], sq[:, 1, :])\
                                .then_inc(ctx.sem(S_VS), 1)
                        else:
                            v.tensor_add(sq[:, 0, :], sq[:, 0, :], sq[:, 1, :])\
                                .then_inc(ctx.sem(S_VS), 1)
                            v.tensor_add(y_acc[:, dt_i, :], y_acc[:, dt_i, :], sq[:, 0, :])\
                                .then_inc(ctx.sem(S_VS), 1)
            da_chunk_hist[pair] = ctx.val(S_AC)
            if c == NCH - 1:
                ctx.bump(q + "gp", 1)
                @ctx.on("vector")
                def _(v, w=ctx.val(S_VS)):
                    v.wait_ge(ctx.sem(S_VS), w)
                    v.engine_nop().then_inc(ctx.sem(q + "gp"), 1)
            for dt_i in range(4):
                n_g1 = ctx.bump(S_VS, 3)
                ysn = pair * 4 + dt_i
                @ctx.on("vector")
                def _(v, gi=gi, c=c, b=b, dt_i=dt_i, dcol=gi * 4 + dt_i, ysn=ysn,
                      base=n_g1 - 3, n_ug=n_ug):
                    v.wait_ge(ctx.sem(q + "ug" + str(b)), n_ug)
                    if ysn >= 2:
                        v.wait_ge(ctx.sem(q + "ys" + str(ysn % 2)), (ysn // 2) * 16)
                    v.scalar_tensor_tensor(y_acc[:, dt_i, :], u2[:, b, dt_i, :],
                                           g["dcol"][:, dcol:dcol + 1], y_acc[:, dt_i, :],
                                           op0=ALU.mult, op1=ALU.add)\
                        .then_inc(ctx.sem(S_VS), 1)
                    v.tensor_mul(y_acc[:, dt_i, :], y_acc[:, dt_i, :], sg2[:, b, dt_i, :])\
                        .then_inc(ctx.sem(S_VS), 1)
                    v.tensor_copy(ygb[:, ysn % 2, :], y_acc[:, dt_i, :])\
                        .then_inc(ctx.sem(S_VS), 1)
                ctx.bump(q + "ys" + str(ysn % 2), 16)
                @ctx.on("sync")
                def _(s, gi=gi, c=c, dt_i=dt_i, ysn=ysn, n_g1=n_g1):
                    s.wait_ge(ctx.sem(S_VS), n_g1)
                    r0 = (gi * 4 + dt_i) * 128
                    s.dma_start(out=g["yg_h"][r0:r0 + 128, c * TC:(c + 1) * TC],
                                in_=ygb[:, ysn % 2, :]).then_inc(ctx.sem(q + "ys" + str(ysn % 2)), 16)
            ctx.bump(q + "gt", 1)
            @ctx.on("vector")
            def _(v, w=ctx.val(S_VS)):
                v.wait_ge(ctx.sem(S_VS), w)
                v.engine_nop().then_inc(ctx.sem(q + "gt"), 1)
        pe_group_rel[gi] = ctx.val(S_PE)

    n_ys0_all = ctx.val(q + "ys0")
    n_ys1_all = ctx.val(q + "ys1")
    n_si_all = ctx.val(q + "si3")
    n_stg_all = ctx.val(q + "stg3")
    n_ag_all = ctx.val(q + "ag3")
    @ctx.on("sync")
    def _(s):
        s.wait_ge(ctx.sem(q + "ys0"), n_ys0_all)
        s.wait_ge(ctx.sem(q + "ys1"), n_ys1_all)
    @ctx.on("gpsimd")
    def _(gp):
        gp.wait_ge(ctx.sem(q + "si3"), n_si_all)
        gp.wait_ge(ctx.sem(q + "stg3"), n_stg_all)
        gp.wait_ge(ctx.sem(q + "ag3"), n_ag_all)
    ctx.close_tensors(cms)


def emit_m3(ctx, g):
    q = "m3"
    cms = []
    def sb(name, shape, dt):
        h, cm = ctx.sb(q + name, shape, dt)
        cms.append(cm)
        return h
    woutb = sb("woutb", [128, 16, D], BF16)
    ygc = sb("ygc", [128, 2, 16, TC], BF16)
    x1r = sb("x1r", [128, 2, D], F32)
    x2r = sb("x2r", [128, 2, D], F32)
    S_VS = q + "vs"

    @ctx.on("sync")
    def _(s):
        for kc in range(16):
            s.dma_start(out=woutb[:, kc, :], in_=g["woutT_b"][kc * 128:(kc + 1) * 128, :])\
                .then_inc(ctx.sem(q + "wl"), 16)
    n_wl = ctx.bump(q + "wl", 16 * 16)

    yoP, cmp = ctx.ps(q + "yoP", [128, 4, 512], F32)
    cms.append(cmp)

    ev_rel = {}
    for c in range(NCH):
        b = c % 2
        n_yl = ctx.bump(q + "yl" + str(b), 16 * 16)
        @ctx.on("sync")
        def _(s, c=c, b=b):
            if c >= 2:
                s.wait_ge(ctx.sem(q + "pe"), pe_rel[c - 2])
            for kc in range(16):
                s.dma_start(out=ygc[:, b, kc, :],
                            in_=g["yg_h"][kc * 128:(kc + 1) * 128, c * TC:(c + 1) * TC])\
                    .then_inc(ctx.sem(q + "yl" + str(b)), 16)
        if c == 0:
            pe_rel = {}
        for ts in range(4):
            tb = ts % 2
            jt = c * 4 + ts
            row0 = c * TC + ts * 128
            n_xl = ctx.bump(q + "xl" + str(tb), 16)
            @ctx.on("sync")
            def _(s, row0=row0, tb=tb, jt=jt):
                if jt >= 2:
                    s.wait_ge(ctx.sem(S_VS), ev_rel[jt - 2])
                s.dma_start(out=x1r[:, tb, :], in_=g["x1b"][3 + row0:3 + row0 + 128, :])\
                    .then_inc(ctx.sem(q + "xl" + str(tb)), 16)
            n_pe = ctx.bump(q + "pe", 2)
            @ctx.on("tensor")
            def _(t, c=c, b=b, ts=ts, jt=jt, n_yl=n_yl, base=n_pe - 2):
                if c == 0 and ts == 0:
                    t.wait_ge(ctx.sem(q + "wl"), n_wl)
                if ts == 0:
                    t.wait_ge(ctx.sem(q + "yl" + str(b)), n_yl)
                for dh in range(2):
                    pb = (jt * 2 + dh) % 4
                    if jt * 2 + dh >= 4:
                        t.wait_ge(ctx.sem(S_VS), ev_rel2[jt * 2 + dh - 4])
                    last = None
                    for kc in range(16):
                        last = t.matmul(yoP[:, pb, :],
                                        ygc[:, b, kc, ts * 128:(ts + 1) * 128],
                                        woutb[:, kc, dh * 512:(dh + 1) * 512],
                                        start=(kc == 0), stop=(kc == 15))
                    last.then_inc(ctx.sem(q + "pe"), 1)
            if c == 0 and ts == 0:
                ev_rel2 = {}
            n_ev = ctx.bump(S_VS, 2)
            @ctx.on("vector")
            def _(v, jt=jt, tb=tb, n_pe=n_pe, n_xl=n_xl, base=n_ev - 2):
                v.wait_ge(ctx.sem(q + "xl" + str(tb)), n_xl)
                for dh in range(2):
                    pb = (jt * 2 + dh) % 4
                    v.wait_ge(ctx.sem(q + "pe"), n_pe - 1 + dh)
                    if jt >= 2:
                        v.wait_ge(ctx.sem(q + "xs" + str(tb)), (jt // 2) * 16)
                    v.scalar_tensor_tensor(x2r[:, tb, dh * 512:(dh + 1) * 512],
                                           yoP[:, pb, :], 0.0, x1r[:, tb, dh * 512:(dh + 1) * 512],
                                           op0=ALU.bypass, op1=ALU.add)
                    v.engine_nop().then_inc(ctx.sem(S_VS), 1)
            ev_rel[jt] = n_ev
            ev_rel2[jt * 2] = n_ev - 1
            ev_rel2[jt * 2 + 1] = n_ev
            ctx.bump(q + "xs" + str(tb), 16)
            @ctx.on("sync")
            def _(s, row0=row0, tb=tb, n_ev=n_ev):
                s.wait_ge(ctx.sem(S_VS), n_ev)
                s.dma_start(out=g["x2b"][row0:row0 + 128, :], in_=x2r[:, tb, :])\
                    .then_inc(ctx.sem(q + "xs" + str(tb)), 16)
        pe_rel[c] = ctx.val(q + "pe")
    @ctx.on("sync")
    def _(s, a=ctx.val(q + "xs0"), bb=ctx.val(q + "xs1")):
        s.wait_ge(ctx.sem(q + "xs0"), a)
        s.wait_ge(ctx.sem(q + "xs1"), bb)
    ctx.close_tensors(cms)


def kernel(**inputs):
    in_maps = stage_inputs(inputs)
    nc = build_program()
    res = run_bass_kernel_spmd(nc, in_maps, core_ids=list(range(NC)))
    shards = [np.asarray(r["out"], np.float32) for r in res.results]
    return np.concatenate(shards, 0).reshape(B, S, D)
